# version 08
# baseline (speedup 1.0000x reference)
import numpy as np
import ml_dtypes
import contextlib
import concourse.bass as bass
import concourse.mybir as mybir
from concourse.bass_utils import run_bass_kernel_spmd

F32 = mybir.dt.float32
BF16 = mybir.dt.bfloat16
AF = mybir.ActivationFunctionType
ALU = mybir.AluOpType
AX = mybir.AxisListType.X
NPBF = np.dtype(ml_dtypes.bfloat16)

NC = 8
B = 16
C = 1024
TFULL = 2048
PPC = 256
TP = 32
NT = 8
CH = 8
FF = 4096
NE = 8
SEG = TP + 1
TILE_COLS = CH * B * TP
HALO_COLS = CH * B * SEG
EPS = 1e-5


class SemT:
    def __init__(self, h):
        self.h = h
        self.n = 0

    def inc(self, amt):
        self.n += amt
        return self.n


class Ctx:
    pass


def build_program(dump=()):
    nc = bass.Bass("TRN2", target_bir_lowering=False, debug=False)
    g = Ctx()
    g.nc = nc

    def din(name, shape, dt=F32):
        return nc.dram_tensor(name, list(shape), dt, kind="ExternalInput").ap()

    g.x_t = din("x_t", [(1 + NT) * 128, HALO_COLS])
    g.wk_t = din("wk_t", [128, CH * C])
    g.wv_t = din("wv_t", [128, CH * C])
    g.wr_t = din("wr_t", [128, CH * C])
    g.wo_t = din("wo_t", [128, CH * C])
    g.wck_t = din("wck_t", [8 * 128, CH * 512])
    g.wcv_t = din("wcv_t", [8 * 128, 4 * C])
    g.wcr_t = din("wcr_t", [128, CH * C])
    g.wip_t = din("wip_t", [2 * 128, 12288])
    g.wow_t = din("wow_t", [128, CH * C])
    g.wrt_t = din("wrt_t", [128, CH * 8])
    g.w1_t = din("w1_t", [NE * 128, CH * C])
    g.w2_t = din("w2_t", [NE * 128, CH * C])
    g.vec32 = din("vec32", [128, 256])
    g.c8f = din("c8f", [8, 64])
    g.b2f = din("b2f", [8, C])
    g.eidb = din("eidb", [8, 512], BF16)
    g.ones1f = din("ones1f", [1, 128])
    g.ones8b = din("ones8b", [1, 8], BF16)
    g.neg9b = din("neg9b", [8, 1024], BF16)
    g.identb = din("identb", [128, 128], BF16)
    g.identf = din("identf", [128, 128])
    g.iotarep = din("iotarep", [128, 257])
    g.iota8 = din("iota8", [128, 8])
    g.gates = din("gates", [128, 2])
    g.gpre = din("gpre", [128, 8])
    g.gpre256 = din("gpre256", [128, 8])
    g.maskA = din("maskA", [8, 128], BF16)
    g.maskB = din("maskB", [8, 128], BF16)

    g.out_d = nc.dram_tensor("out_d", [NT * 128, TILE_COLS], F32, kind="ExternalOutput").ap()
    g.dumps = {}
    for nm in dump:
        shape = {"hm0": [128, TILE_COLS], "m80": [8, 512], "he0": [128, 4096],
                 "asg0": [1, 512], "bc0": [8, 512],
                 "hk0": [128, 2048], "x1p": [128, 128], "wkvp": [128, 128],
                 "x1": [128, TILE_COLS], "xc": [128, HALO_COLS],
                 "kv": [128, TILE_COLS], "rr": [128, TILE_COLS],
                 "yp": [128, TILE_COLS], "wkv": [128, TILE_COLS],
                 "dbgP": [128, HALO_COLS], "dbgF": [128, HALO_COLS],
                 "dbgN": [128, HALO_COLS], "dbgD": [128, HALO_COLS],
                 "num": [128, 256 + NT * TILE_COLS], "den": [128, 256 + NT * TILE_COLS],
                 "r": [128, 256 + NT * TILE_COLS], "y1": [128, NT * TILE_COLS],
                 "h": [128, NT * TILE_COLS], "seeds": [128, 256],
                 "assign": [1, 4096]}[nm]
        g.dumps[nm] = nc.dram_tensor("dump_" + nm, shape, F32, kind="ExternalOutput").ap()

    g.num_d = nc.dram_tensor("num_d", [(1 + NT) * 128, TILE_COLS], F32).ap()
    g.den_d = nc.dram_tensor("den_d", [(1 + NT) * 128, TILE_COLS], F32).ap()
    g.r_d = nc.dram_tensor("r_d", [(1 + NT) * 128, TILE_COLS], F32).ap()
    g.y1_d = nc.dram_tensor("y1_d", [NT * 128, TILE_COLS], F32).ap()
    g.h_d = nc.dram_tensor("h_d", [NT * 128, TILE_COLS], F32).ap()
    g.wob_d = nc.dram_tensor("wob_d", [128, CH * C], BF16).ap()
    g.wcrb_d = nc.dram_tensor("wcrb_d", [128, CH * C], BF16).ap()
    g.wckb_d = nc.dram_tensor("wckb_d", [8 * 128, CH * 512], BF16).ap()
    g.wcvb_d = nc.dram_tensor("wcvb_d", [8 * 128, 4 * C], BF16).ap()
    g.wipb_d = nc.dram_tensor("wipb_d", [2 * 128, 12288], BF16).ap()
    g.wowb_d = nc.dram_tensor("wowb_d", [128, CH * C], BF16).ap()
    g.w1b_d = nc.dram_tensor("w1b_d", [NE * 128, CH * C], BF16).ap()
    g.w2b_d = nc.dram_tensor("w2b_d", [NE * 128, CH * C], BF16).ap()
    g.asg_d = nc.dram_tensor("asg_d", [1, 4096], F32).ap()
    g.cc_in = nc.dram_tensor("cc_in", [128, 256], F32).ap()
    g.cc_out = nc.dram_tensor("cc_out", [1024, 256], F32).ap()

    es = contextlib.ExitStack()
    with es:
        def sbt(name, shape, dt=F32):
            return es.enter_context(nc.sbuf_tensor(name, list(shape), dt))

        g.sVEC = sbt("sVEC", [128, 256])
        g.sC8 = sbt("sC8", [8, 64])
        g.sB2 = sbt("sB2", [8, C], BF16)
        g.sEID = sbt("sEID", [8, 512], BF16)
        g.sONES1 = sbt("sONES1", [1, 128])
        g.sONES8 = sbt("sONES8", [1, 8], BF16)
        g.sNEG9 = sbt("sNEG9", [8, 1024], BF16)
        g.sIDB = sbt("sIDB", [128, 128], BF16)
        g.sIDF = sbt("sIDF", [128, 128])
        g.sIOTA = sbt("sIOTA", [128, 257])
        g.sIOTA8 = sbt("sIOTA8", [128, 8])
        g.sGATE = sbt("sGATE", [128, 2])
        g.sGPRE = sbt("sGPRE", [128, 8])
        g.sGP256 = sbt("sGP256", [128, 8])
        g.sMA = sbt("sMA", [8, 128], BF16)
        g.sMB = sbt("sMB", [8, 128], BF16)
        g.sONE1c = sbt("sONE1c", [128, 1])
        g.sDRV = sbt("sDRV", [128, 104])
        g.sROW = sbt("sROW", [1, 1600])
        g.sSEED = sbt("sSEED", [128, 512])
        g.sGATH = sbt("sGATH", [128, 512])
        g.sSNSD = sbt("sSNSD", [128, 256])
        g.sXC = sbt("sXC", [128, 128])
        g.sASG = sbt("sASG", [1, 512], BF16)
        g.sM8 = sbt("sM8", [8, 512], BF16)
        g.sNM8 = sbt("sNM8", [8, 512], BF16)
        g.sSC = sbt("sSC", [128, 1088])
        g.sLREP = sbt("sLREP", [128, CH * SEG * B])
        g.sLAM = g.sLREP
        g.sLPRE = sbt("sLPRE", [128, CH * B * 2])
        g.ARF = sbt("ARF", [128, 21120])
        g.ARB = sbt("ARB", [128, 36864], BF16)
        g.PS = [es.enter_context(nc.psum_tensor("ps%d" % i, [128, 512], F32))
                for i in range(8)]

        g.s_ld = SemT(es.enter_context(nc.semaphore("s_ld")))
        g.s_gc = SemT(es.enter_context(nc.semaphore("s_gc")))
        g.s_t = SemT(es.enter_context(nc.semaphore("s_t")))
        g.s_v = SemT(es.enter_context(nc.semaphore("s_v")))
        g.s_s = SemT(es.enter_context(nc.semaphore("s_s")))
        g.s_st = SemT(es.enter_context(nc.semaphore("s_st")))
        g.s_cc = SemT(es.enter_context(nc.semaphore("s_cc")))

        g.P = {"sync": [], "tensor": [], "vector": [], "scalar": [], "gpsimd": []}

        def em(eng, fn):
            g.P[eng].append(fn)

        def wait(eng, sem, v=None):
            vv = sem.n if v is None else v
            em(eng, lambda e, h=sem.h, w=vv: e.wait_ge(h, w))

        def dma(eng, dst, src, sem, slow=False):
            sem.inc(16)
            em(eng, lambda e, d=dst, s=src, h=sem.h, sl=slow: e.dma_start(
                out=d, in_=s, allow_slow_non_contiguous=sl).then_inc(h, 16))

        def run(eng, fn, sem=None, amt=1):
            if sem is None:
                em(eng, fn)
            else:
                sem.inc(amt)
                em(eng, lambda e, f=fn, h=sem.h, a=amt: f(e).then_inc(h, a))

        g.em, g.wait, g.dma, g.run = em, wait, dma, run

        def vf(off, n):
            return g.ARF[:, off:off + n]

        def vb(off, n):
            return g.ARB[:, off:off + n]

        g.vf, g.vb = vf, vb

        VOFF = dict(g0=0, b0=8, g1=16, b1=24, tmk=32, tmv=40, tmr=48, cmk=56,
                    cmr=64, td=72, tf=80, ob=88, fg1=96, fb1=104, fg2=112,
                    fb2=120, ipb=128, b1e=152)
        DOFF = dict(w=0, lam=8, eu=16, bq8=24, fpre=32)

        def vcol(name, j=0):
            o = VOFF[name] + j
            return g.sVEC[:, o:o + 1]

        def dcol(name, j=0):
            o = DOFF[name] + j
            return g.sDRV[:, o:o + 1]

        g.vcol, g.dcol = vcol, dcol

        def emit_ln(get_src, put_out, sub_ranges, gcol=None, bcol=None):
            for (o, w) in sub_ranges:
                wait("tensor", g.s_v)
                wait("tensor", g.s_s)
                wait("scalar", g.s_v)
                for c in range(CH):
                    run("tensor", lambda e, ap=get_src(c, o, w), ww=w, cc=c: e.matmul(
                        g.PS[4][0:1, 0:ww], g.sONE1c[:], ap,
                        start=(cc == 0), stop=(cc == CH - 1)), g.s_t)
                for c in range(CH):
                    if c >= 2:
                        wait("scalar", g.s_t)
                    run("scalar", lambda e, ap=get_src(c, o, w), cc=c, ww=w: e.activation(
                        g.sSC[:, (cc % 2) * 544:(cc % 2) * 544 + ww],
                        ap, AF.Square), g.s_s)
                    wait("tensor", g.s_s)
                    run("tensor", lambda e, cc=c, ww=w: e.matmul(
                        g.PS[5][0:1, 0:ww], g.sONE1c[:],
                        g.sSC[:, (cc % 2) * 544:(cc % 2) * 544 + ww],
                        start=(cc == 0), stop=(cc == CH - 1)), g.s_t)
                tmm = g.s_t.n
                wait("vector", g.s_t, tmm)
                run("vector", lambda e, ww=w: e.tensor_copy(
                    g.sROW[0:1, 0:ww], g.PS[4][0:1, 0:ww]), g.s_v)
                em("vector", lambda e: e.drain())
                run("vector", lambda e, ww=w: e.tensor_tensor(
                    g.sROW[0:1, 1024:1024 + ww], g.sROW[0:1, 0:ww],
                    g.sROW[0:1, 0:ww], ALU.mult), g.s_v)
                em("vector", lambda e: e.drain())
                run("vector", lambda e, ww=w: e.tensor_tensor(
                    g.sROW[0:1, 512:512 + ww], g.PS[5][0:1, 0:ww],
                    g.sROW[0:1, 1024:1024 + ww], ALU.subtract), g.s_v)
                vrow = g.s_v.n
                wait("scalar", g.s_v, vrow)
                run("scalar", lambda e, ww=w: e.activation(
                    g.sROW[0:1, 512:512 + ww], g.sROW[0:1, 512:512 + ww],
                    AF.Ln, bias=g.sDRV[0:1, 96:97]), g.s_s)
                em("scalar", lambda e: e.drain())
                run("scalar", lambda e, ww=w: e.activation(
                    g.sROW[0:1, 512:512 + ww], g.sROW[0:1, 512:512 + ww],
                    AF.Exp, scale=-0.5), g.s_s)
                srow = g.s_s.n
                wait("tensor", g.s_s, srow)
                run("tensor", lambda e, ww=w: e.matmul(
                    g.PS[6][:, 0:ww], g.sONES1[:], g.sROW[0:1, 0:ww],
                    start=True, stop=True), g.s_t)
                run("tensor", lambda e, ww=w: e.matmul(
                    g.PS[7][:, 0:ww], g.sONES1[:], g.sROW[0:1, 512:512 + ww],
                    start=True, stop=True), g.s_t)
                tb = g.s_t.n
                wait("vector", g.s_t, tb)
                for c in range(CH):
                    run("vector", lambda e, po=put_out(c, o, w), gs=get_src(c, o, w), ww=w: e.tensor_tensor(
                        po, gs, g.PS[6][:, 0:ww], ALU.subtract), g.s_v)
                for c in range(CH):
                    run("vector", lambda e, po=put_out(c, o, w), ww=w: e.tensor_tensor(
                        po, po, g.PS[7][:, 0:ww], ALU.mult), g.s_v)
                if gcol is not None:
                    for c in range(CH):
                        run("vector", lambda e, po=put_out(c, o, w), gc=gcol(c), bc=bcol(c): e.tensor_scalar(
                            po, po, gc, bc, ALU.mult, ALU.add), g.s_v)

        g.emit_ln = emit_ln

        _preamble(g)
        _s1(g)
        _collective(g)
        _s2(g)
        _s3(g)
        _s4(g)

        wait("sync", g.s_st, g.s_st.n)
        wait("sync", g.s_gc, g.s_gc.n)

        with nc.Block() as block:
            @block.sync
            def _(q):
                for fn in g.P["sync"]:
                    fn(q)

            @block.tensor
            def _(q):
                for fn in g.P["tensor"]:
                    fn(q)

            @block.vector
            def _(q):
                for fn in g.P["vector"]:
                    fn(q)

            @block.scalar
            def _(q):
                for fn in g.P["scalar"]:
                    fn(q)

            @block.gpsimd
            def _(q):
                for fn in g.P["gpsimd"]:
                    fn(q)
    return nc


def _preamble(g):
    em, wait, dma, run = g.em, g.wait, g.dma, g.run
    for dst, src in [(g.sVEC, g.vec32), (g.sC8, g.c8f), (g.sONES1, g.ones1f),
                     (g.sONES8, g.ones8b), (g.sNEG9, g.neg9b),
                     (g.sIDB, g.identb), (g.sIDF, g.identf),
                     (g.sIOTA, g.iotarep), (g.sIOTA8, g.iota8),
                     (g.sGATE, g.gates), (g.sGPRE, g.gpre),
                     (g.sGP256, g.gpre256), (g.sMA, g.maskA),
                     (g.sMB, g.maskB), (g.sEID, g.eidb)]:
        dma("sync", dst[:], src, g.s_ld)
    dma("gpsimd", g.sB2[:], g.b2f, g.s_gc)

    run("vector", lambda e: e.memset(g.sONE1c[:], 1.0 / 1024.0), g.s_v)
    run("vector", lambda e: e.memset(g.sDRV[:, 96:97], float(EPS)), g.s_v)
    wait("scalar", g.s_ld)
    run("scalar", lambda e: e.activation(g.sDRV[:, 0:8], g.sVEC[:, 72:80], AF.Exp), g.s_s)
    wait("vector", g.s_s)
    run("vector", lambda e: e.tensor_scalar_mul(g.sDRV[:, 0:8], g.sDRV[:, 0:8], -1.0), g.s_v)
    em("vector", lambda e: e.drain())
    wait("scalar", g.s_v)
    run("scalar", lambda e: e.activation(g.sDRV[:, 8:16], g.sDRV[:, 0:8], AF.Exp), g.s_s)
    run("scalar", lambda e: e.activation(g.sDRV[:, 16:24], g.sVEC[:, 80:88], AF.Exp), g.s_s)
    for j in range(NC):
        run("scalar", lambda e, jj=j: e.activation(
            g.sDRV[:, 32 + 8 * jj:40 + 8 * jj], g.sDRV[:, 0:8], AF.Exp,
            scale=g.sGP256[:, jj:jj + 1]), g.s_s)
    sd = g.s_s.n
    wait("vector", g.s_s, sd)
    run("vector", lambda e: e.tensor_scalar_mul(g.sDRV[:, 24:32], g.sVEC[:, 128:136], 0.125), g.s_v)
    run("vector", lambda e: e.memset(g.sLREP[:], 1.0), g.s_v)
    run("vector", lambda e: e.memset(g.sLPRE[:], 1.0), g.s_v)
    em("vector", lambda e: e.drain())
    for chi in range(CH):
        run("vector", lambda e, c=chi: e.tensor_scalar_mul(
            g.sLREP[:, c * 528:(c + 1) * 528], g.sLREP[:, c * 528:(c + 1) * 528],
            g.sDRV[:, 8 + c:9 + c]), g.s_v)
        run("vector", lambda e, c=chi: e.tensor_scalar_mul(
            g.sLPRE[:, c * 32:(c + 1) * 32], g.sLPRE[:, c * 32:(c + 1) * 32],
            g.sDRV[:, 8 + c:9 + c]), g.s_v)
    em("vector", lambda e: e.drain())
    run("vector", lambda e: e.memset(
        g.sLREP[:].rearrange("p (c b t) -> p c b t", c=CH, b=B)[:, :, :, 0:1], 0.0), g.s_v)
    run("vector", lambda e: e.memset(
        g.sLPRE[:].rearrange("p (c b t) -> p c b t", c=CH, b=B, t=2)[:, :, :, 0:1], 0.0), g.s_v)
    run("vector", lambda e: e.memset(g.sSEED[:], 0.0), g.s_v)
    em("vector", lambda e: e.drain())
    WKB_, WVB_, WRB_ = 2 * TILE_COLS, 2 * TILE_COLS + CH * C, 2 * TILE_COLS + 2 * CH * C
    for dst_off, src in [(WKB_, g.wk_t), (WVB_, g.wv_t), (WRB_, g.wr_t)]:
        dma("gpsimd", g.vb(dst_off, CH * C), src, g.s_gc)
    g.s1w = g.s_gc.n
    dma("gpsimd", g.wob_d, g.wo_t, g.s_gc)
    dma("gpsimd", g.wcrb_d, g.wcr_t, g.s_gc)
    dma("gpsimd", g.wckb_d, g.wck_t, g.s_gc)
    dma("gpsimd", g.wcvb_d, g.wcv_t, g.s_gc)
    g.prep_s2 = g.s_gc.n
    dma("gpsimd", g.wipb_d, g.wip_t, g.s_gc)
    dma("gpsimd", g.wowb_d, g.wow_t, g.s_gc)
    g.prep_s3 = g.s_gc.n
    dma("gpsimd", g.w1b_d, g.w1_t, g.s_gc)
    dma("gpsimd", g.w2b_d, g.w2_t, g.s_gc)
    g.prep_done = g.s_gc.n


def _s1(g):
    em, wait, dma, run, vf, vb = g.em, g.wait, g.dma, g.run, g.vf, g.vb
    A_, Bf_, Cp_, Dn_, Ed_ = 0, HALO_COLS, 2 * HALO_COLS, 3 * HALO_COLS, 4 * HALO_COLS
    MX0_, MX1_, MX2_ = 0, TILE_COLS, 8 * TILE_COLS
    WKB_, WVB_, WRB_ = 2 * TILE_COLS, 2 * TILE_COLS + CH * C, 2 * TILE_COLS + 2 * CH * C

    dma("sync", vf(A_, CH * B * 2), g.x_t[0:128, 0:256], g.s_ld)
    g.ldx_next = g.s_ld.n

    for kk in range(-1, NT):
        pre = (kk == -1)
        seg = 2 if pre else SEG
        cols = CH * B * seg
        tokw = B * (seg - 1)

        def r4(base, _c=cols, _s=seg):
            return g.ARF[:, base:base + _c].rearrange("p (c b t) -> p c b t", c=CH, b=B)

        def chsl(base, c, a, bb, _s=seg):
            return g.ARF[:, base + c * (B * _s): base + (c + 1) * (B * _s)].rearrange(
                "p (b t) -> p b t", b=B)[:, :, a:bb]

        ldx = g.ldx_next
        rt_prev = g.s_t.n
        st_prev = g.s_st.n

        wait("vector", g.s_ld, ldx)
        wait("scalar", g.s_ld, ldx)
        wait("tensor", g.s_ld, ldx)
        if kk == 0:
            wait("scalar", g.s_st, st_prev)
        subs = [(0, B * seg)] if pre else [(0, 512), (512, 16)]

        def src_xa(c, o, w, _base=A_, _s=seg):
            return g.ARF[:, _base + c * (B * _s) + o: _base + c * (B * _s) + o + w]
        g.emit_ln(src_xa, src_xa, subs, lambda c: g.vcol("g0", c), lambda c: g.vcol("b0", c))
        if pre:
            run("vector", lambda e, cc=cols: e.tensor_scalar_mul(vf(A_, cc), vf(A_, cc), g.sGATE[:, 0:1]), g.s_v)
        elif kk == 0:
            run("vector", lambda e, ap=r4(A_)[:, :, :, 0:1]: e.tensor_scalar_mul(
                ap, ap, g.sGATE[:, 0:1]), g.s_v)
        em("vector", lambda e: e.drain())
        run("vector", lambda e, ad=r4(Cp_)[:, :, :, 1:seg], ac=r4(A_)[:, :, :, 1:seg],
                ap_=r4(A_)[:, :, :, 0:seg - 1]: e.tensor_tensor(ad, ac, ap_, ALU.subtract), g.s_v)
        em("vector", lambda e: e.drain())

        wait("vector", g.s_t, rt_prev)
        for mi, (nm, moff) in enumerate([("tmr", MX2_), ("tmk", MX0_), ("tmv", MX1_)]):
            for c in range(CH):
                run("vector", lambda e, ot=vb(moff + c * tokw, tokw), i0=chsl(Cp_, c, 1, seg),
                        sc=g.vcol(nm, c), i1=chsl(A_, c, 0, seg - 1): e.scalar_tensor_tensor(
                    ot, i0, sc, i1, ALU.mult, ALU.add), g.s_v)
            mixv = g.s_v.n
            if mi == 1:
                wait("tensor", g.s_v, mixv)
                if kk <= 0:
                    wait("tensor", g.s_gc, g.s1w)
                wait("scalar", g.s_st, st_prev)
                for co in range(CH):
                    bank = co % 2
                    wait("tensor", g.s_s)
                    for ci in range(CH):
                        run("tensor", lambda e, b_=bank, cc=ci, cco=co, tw=tokw: e.matmul(
                            g.PS[b_][:, 0:tw],
                            vb(WKB_ + cc * C + cco * 128, 128),
                            vb(MX0_ + cc * tw, tw),
                            start=(cc == 0), stop=(cc == CH - 1)),
                            g.s_t if ci == CH - 1 else None)
                    wait("scalar", g.s_t)
                    run("scalar", lambda e, b_=bank, ot=chsl(Bf_, co, 1, seg), tw=tokw: e.activation(
                        ot, g.PS[b_][:, 0:tw], AF.Exp), g.s_s)
            elif mi == 2:
                wait("tensor", g.s_v, mixv)
                for co in range(CH):
                    bank = 2 + co % 2
                    wait("tensor", g.s_v)
                    for ci in range(CH):
                        run("tensor", lambda e, b_=bank, cc=ci, cco=co, tw=tokw: e.matmul(
                            g.PS[b_][:, 0:tw],
                            vb(WVB_ + cc * C + cco * 128, 128),
                            vb(MX1_ + cc * tw, tw),
                            start=(cc == 0), stop=(cc == CH - 1)),
                            g.s_t if ci == CH - 1 else None)
                    wait("vector", g.s_t)
                    wait("vector", g.s_s)
                    run("vector", lambda e, b_=bank, ot=chsl(Cp_, co, 1, seg),
                            i0=chsl(Bf_, co, 1, seg), tw=tokw: e.tensor_tensor(
                        ot, i0, g.PS[b_][:, 0:tw], ALU.mult), g.s_v)
        if kk < NT - 1:
            nxt = kk + 1
            wait("sync", g.s_v, mixv)
            dma("sync", vf(A_, HALO_COLS),
                g.x_t[(1 + nxt) * 128:(2 + nxt) * 128, :], g.s_ld)
            g.ldx_next = g.s_ld.n

        if pre:
            wait("vector", g.s_s)
            run("vector", lambda e, cc=cols: e.tensor_scalar_mul(vf(Cp_, cc), vf(Cp_, cc), g.sGATE[:, 0:1]), g.s_v)
            run("vector", lambda e, cc=cols: e.tensor_scalar_mul(vf(Bf_, cc), vf(Bf_, cc), g.sGATE[:, 0:1]), g.s_v)
            em("vector", lambda e: e.drain())
            run("vector", lambda e, ap=r4(Cp_)[:, :, :, 0:1]: e.memset(ap, 0.0), g.s_v)
            run("vector", lambda e, ap=r4(Bf_)[:, :, :, 0:1]: e.memset(ap, 0.0), g.s_v)
        else:
            wait("vector", g.s_s)
            run("vector", lambda e, ap=r4(Cp_)[:, :, :, 0:1]: e.tensor_copy(
                ap, g.sSEED[:, 0:128].rearrange("p (c b o) -> p c b o", c=CH, b=B)), g.s_v)
            run("vector", lambda e, ap=r4(Bf_)[:, :, :, 0:1]: e.tensor_copy(
                ap, g.sSEED[:, 128:256].rearrange("p (c b o) -> p c b o", c=CH, b=B)), g.s_v)
        em("vector", lambda e: e.drain())
        wait("vector", g.s_st, st_prev)
        lrep = g.sLPRE if pre else g.sLREP
        for c in range(CH):
            w_ = B * seg
            run("vector", lambda e, cc=c, ww=w_, lr=lrep: e.tensor_tensor_scan(
                vf(Dn_ + cc * ww, ww), lr[:, cc * ww:(cc + 1) * ww],
                vf(Cp_ + cc * ww, ww), 0.0, ALU.mult, ALU.add), g.s_v)
        for c in range(CH):
            w_ = B * seg
            run("vector", lambda e, cc=c, ww=w_, lr=lrep: e.tensor_tensor_scan(
                vf(Ed_ + cc * ww, ww), lr[:, cc * ww:(cc + 1) * ww],
                vf(Bf_ + cc * ww, ww), 0.0, ALU.mult, ALU.add), g.s_v)
        em("vector", lambda e: e.drain())
        run("vector", lambda e, ap=r4(Dn_)[:, :, :, seg - 1:seg]: e.tensor_copy(
            g.sSEED[:, 0:128].rearrange("p (c b o) -> p c b o", c=CH, b=B), ap), g.s_v)
        run("vector", lambda e, ap=r4(Ed_)[:, :, :, seg - 1:seg]: e.tensor_copy(
            g.sSEED[:, 128:256].rearrange("p (c b o) -> p c b o", c=CH, b=B), ap), g.s_v)
        if kk == NT - 1:
            run("vector", lambda e, ap=r4(Dn_)[:, :, :, seg - 2:seg - 1]: e.tensor_copy(
                g.sSEED[:, 256:384].rearrange("p (c b o) -> p c b o", c=CH, b=B), ap), g.s_v)
            run("vector", lambda e, ap=r4(Ed_)[:, :, :, seg - 2:seg - 1]: e.tensor_copy(
                g.sSEED[:, 384:512].rearrange("p (c b o) -> p c b o", c=CH, b=B), ap), g.s_v)
        em("vector", lambda e: e.drain())
        for c in range(CH):
            run("vector", lambda e, ot=chsl(Dn_, c, 0, seg - 1), i0=chsl(Cp_, c, 1, seg),
                    sc=g.dcol("eu", c): e.scalar_tensor_tensor(
                ot, i0, sc, ot, ALU.mult, ALU.add), g.s_v)
        for c in range(CH):
            run("vector", lambda e, ot=chsl(Ed_, c, 0, seg - 1), i0=chsl(Bf_, c, 1, seg),
                    sc=g.dcol("eu", c): e.scalar_tensor_tensor(
                ot, i0, sc, ot, ALU.mult, ALU.add), g.s_v)
        if pre:
            run("vector", lambda e, ap=r4(Ed_)[:, :, :, 0:1]: e.tensor_scalar(
                ap, ap, g.sGATE[:, 1:2], None, ALU.add), g.s_v)
        numden = g.s_v.n
        wait("tensor", g.s_v, numden)
        for co in range(CH):
            bank = 4 + co % 2
            wait("tensor", g.s_s)
            for ci in range(CH):
                run("tensor", lambda e, b_=bank, cc=ci, cco=co, tw=tokw: e.matmul(
                    g.PS[b_][:, 0:tw],
                    vb(WRB_ + cc * C + cco * 128, 128),
                    vb(MX2_ + cc * tw, tw),
                    start=(cc == 0), stop=(cc == CH - 1)),
                    g.s_t if ci == CH - 1 else None)
            wait("scalar", g.s_t)
            run("scalar", lambda e, b_=bank, ot=chsl(Cp_, co, 1, seg), tw=tokw: e.activation(
                ot, g.PS[b_][:, 0:tw], AF.Exp, scale=-1.0), g.s_s)
        wait("vector", g.s_s)
        run("vector", lambda e, cc=cols: e.tensor_scalar(
            vf(Cp_, cc), vf(Cp_, cc), 1.0, None, ALU.add), g.s_v)
        em("vector", lambda e: e.drain())
        run("vector", lambda e, cc=cols: e.reciprocal(vf(Bf_, cc), vf(Cp_, cc)), g.s_v)
        if pre:
            for si, (base, a) in enumerate([(Dn_, 0), (Ed_, 0), (Bf_, 1)]):
                run("vector", lambda e, aa=si, ap=r4(base)[:, :, :, a:a + 1]: e.tensor_copy(
                    g.sSC[:, aa * 128:(aa + 1) * 128].rearrange("p (c b o) -> p c b o", c=CH, b=B),
                    ap), g.s_v)
            em("vector", lambda e: e.drain())
            vr = g.s_v.n
            wait("sync", g.s_v, vr)
            for si, dram in enumerate([g.num_d, g.den_d, g.r_d]):
                dma("sync", dram[0:128, 0:128], g.sSC[:, si * 128:(si + 1) * 128], g.s_st)
        else:
            vr = g.s_v.n
            r0 = (1 + kk) * 128
            for (dram, base, a, bb), eng in [((g.r_d, Bf_, 1, seg), "sync"),
                                             ((g.num_d, Dn_, 0, seg - 1), "scalar"),
                                             ((g.den_d, Ed_, 0, seg - 1), "sync")]:
                wait(eng, g.s_v, vr)
                dma(eng, dram[r0:r0 + 128, :],
                    r4(base)[:, :, :, a:bb], g.s_st)


def _collective(g):
    em, wait, dma, run = g.em, g.wait, g.dma, g.run
    wait("gpsimd", g.s_v)
    dma("gpsimd", g.cc_in, g.sSEED[:, 256:512], g.s_gc)
    wait("gpsimd", g.s_gc)
    run("gpsimd", lambda e: e.collective_compute(
        "AllGather", ALU.bypass, replica_groups=[list(range(NC))],
        ins=[g.cc_in.opt()], outs=[g.cc_out.opt()]), g.s_cc)
    wait("gpsimd", g.s_cc)
    run("vector", lambda e: e.memset(g.sSNSD[:], 0.0), g.s_v)
    em("vector", lambda e: e.drain())
    for j in range(NC):
        dma("gpsimd", g.sGATH[:, 0:256], g.cc_out[j * 128:(j + 1) * 128, :], g.s_gc)
        gld = g.s_gc.n
        fj = g.sDRV[:, 32 + 8 * j:40 + 8 * j]
        wait("vector", g.s_gc, gld)
        for half in (0, 1):
            run("vector", lambda e, f_=fj, hh=half: e.tensor_tensor(
                g.sSC[:, hh * 128:hh * 128 + 128].rearrange("p (c b) -> p c b", c=CH),
                g.sSNSD[:, hh * 128:hh * 128 + 128].rearrange("p (c b) -> p c b", c=CH),
                f_.rearrange("p (c o) -> p c o", o=1).to_broadcast([128, CH, B]), ALU.mult), g.s_v)
        em("vector", lambda e: e.drain())
        for half in (0, 1):
            run("vector", lambda e, jj=j, hh=half: e.scalar_tensor_tensor(
                g.sSNSD[:, hh * 128:hh * 128 + 128],
                g.sGATH[:, hh * 128:hh * 128 + 128],
                g.sGPRE[:, jj:jj + 1],
                g.sSC[:, hh * 128:hh * 128 + 128], ALU.mult, ALU.add), g.s_v)
        em("vector", lambda e: e.drain())
        wait("gpsimd", g.s_v)
    if "seeds" in g.dumps:
        wait("sync", g.s_v)
        dma("sync", g.dumps["seeds"], g.sSNSD[:], g.s_st)
    for nm, dr in [("num", g.num_d), ("den", g.den_d), ("r", g.r_d)]:
        if nm in g.dumps:
            wait("sync", g.s_st, g.s_st.n)
            dma("sync", g.dumps[nm], dr, g.s_st)


def _s2(g):
    em, wait, dma, run, vf, vb = g.em, g.wait, g.dma, g.run, g.vf, g.vb
    A_, Bd_, Cr_, Dx_, Ekv_ = 0, 4224, 8320, 12416, 16512
    RW_, MXC_, HK_, WOC_, WG_ = 0, 4096, 8192, 10240, 18432
    FG = 4
    NG = 32 // FG

    wait("scalar", g.s_v)
    for chi in range(CH):
        run("scalar", lambda e, c=chi: e.activation(
            g.sLAM[:, c * 257:(c + 1) * 257], g.sIOTA[:], AF.Exp,
            scale=g.sDRV[:, c:c + 1]), g.s_s)
    lam_built = g.s_s.n

    def fg_load(gi, par):
        base = WG_ + par * 8192
        wait("sync", g.s_t)
        dma("sync", vb(base, 4096),
            g.wckb_d[gi * 128:(gi + 1) * 128, :], g.s_ld)
        wait("scalar", g.s_t)
        dma("scalar", vb(base + 4096, 4096),
            g.wcvb_d[gi * 128:(gi + 1) * 128, :], g.s_ld)
        return g.s_ld.n

    wait("sync", g.s_gc, g.prep_s2)
    wait("scalar", g.s_gc, g.prep_s2)
    wait("sync", g.s_st, g.s_st.n)
    wait("scalar", g.s_st, g.s_st.n)
    fg_ready = [None] * NG
    fg_ready[0] = fg_load(0, 0)

    for kk in range(-1, NT):
        pre = (kk == -1)
        ntok = 128 if pre else TILE_COLS
        tokc = B if pre else B * TP
        src_off = 0 if pre else 256 + kk * TILE_COLS

        def csl(base, c, _t=tokc):
            return g.ARF[:, base + c * _t: base + (c + 1) * _t]

        wait("sync", g.s_v)
        wait("sync", g.s_s)
        wait("sync", g.s_t)
        wait("sync", g.s_gc, g.s_gc.n)
        wait("scalar", g.s_v)
        wait("scalar", g.s_st, g.s_st.n)
        rb = (1 + kk) * 128 if not pre else 0
        dma("sync", vf(A_, ntok), g.num_d[rb:rb + 128, 0:ntok], g.s_ld)
        dma("scalar", vf(Bd_, ntok), g.den_d[rb:rb + 128, 0:ntok], g.s_ld)
        dma("sync", vf(Cr_, ntok), g.r_d[rb:rb + 128, 0:ntok], g.s_ld)
        if pre:
            dma("sync", vf(Ekv_, 256), g.x_t[0:128, 0:256], g.s_ld)
            prex = g.s_ld.n
            wait("vector", g.s_ld, prex)
            run("vector", lambda e: e.tensor_copy(
                vf(Dx_, 128).rearrange("p (c b o) -> p c b o", c=CH, b=B),
                vf(Ekv_, 256).rearrange("p (c b t) -> p c b t", c=CH, b=B)[:, :, :, 1:2]), g.s_v)
            em("vector", lambda e: e.drain())
        else:
            xs = g.x_t[(1 + kk) * 128:(2 + kk) * 128, :].rearrange(
                "p (c b t) -> p c b t", c=CH, b=B)[:, :, :, 1:SEG]
            dma("scalar", vf(Dx_, ntok).rearrange("p (c b t) -> p c b t", c=CH, b=B, t=TP),
                xs, g.s_ld)
        wait("scalar", g.s_t)
        dma("scalar", vb(WOC_, CH * C), g.wob_d, g.s_ld)
        ldt = g.s_ld.n

        t0 = 0 if pre else TP * kk + 1
        tw = 1 if pre else TP
        lam4 = g.sLAM.rearrange("p (c t) -> p c t", c=CH, t=528)[:, :, t0:t0 + tw] if False else \
            g.sLAM[:, 0:CH * 257].rearrange("p (c t) -> p c t", c=CH)[:, :, t0:t0 + tw]
        wait("vector", g.s_ld, ldt)
        if pre:
            wait("vector", g.s_s, lam_built)
        for (tgt, sn_off) in [(A_, 0), (Bd_, 128)]:
            run("vector", lambda e, so=sn_off, l4=lam4, _tw=tw: e.tensor_tensor(
                vf(Ekv_, CH * B * _tw).rearrange("p (c b t) -> p c b t", c=CH, b=B),
                g.sSNSD[:, so:so + 128].rearrange("p (c b o) -> p c b o", c=CH, b=B).to_broadcast([128, CH, B, _tw]),
                l4.unsqueeze(2).to_broadcast([128, CH, B, _tw]), ALU.mult), g.s_v)
            em("vector", lambda e: e.drain())
            run("vector", lambda e, tg=tgt, _n=ntok: e.tensor_tensor(
                vf(tg, _n), vf(tg, _n), vf(Ekv_, _n), ALU.add), g.s_v)
            em("vector", lambda e: e.drain())
        run("vector", lambda e, n_=ntok: e.reciprocal(vf(Ekv_, n_), vf(Bd_, n_)), g.s_v)
        em("vector", lambda e: e.drain())
        run("vector", lambda e, n_=ntok: e.tensor_tensor(vf(A_, n_), vf(A_, n_), vf(Ekv_, n_), ALU.mult), g.s_v)
        run("vector", lambda e, n_=ntok: e.tensor_tensor(vb(RW_, n_), vf(A_, n_), vf(Cr_, n_), ALU.mult), g.s_v)
        rwv = g.s_v.n
        if not pre:
            wait("scalar", g.s_v, rwv)
            dma("scalar", vf(Cr_, TILE_COLS).rearrange("p (c b t) -> p c b t", c=CH, b=B),
                g.x_t[(1 + kk) * 128:(2 + kk) * 128, :].rearrange(
                    "p (c b t) -> p c b t", c=CH, b=B)[:, :, :, 1:SEG], g.s_ld)
            g.xre = g.s_ld.n
        if kk == 0 and "wkv" in g.dumps:
            wait("sync", g.s_v, rwv)
            dma("sync", g.dumps["wkv"], vf(A_, TILE_COLS), g.s_st)
            wait("vector", g.s_st, g.s_st.n)
            wait("scalar", g.s_st, g.s_st.n)
        wait("tensor", g.s_v, rwv)
        wait("tensor", g.s_ld, ldt)
        for co in range(CH):
            bank = co % 2
            wait("tensor", g.s_v)
            for ci in range(CH):
                run("tensor", lambda e, b_=bank, cc=ci, cco=co, _tk=tokc: e.matmul(
                    g.PS[b_][:, 0:_tk],
                    vb(WOC_ + cc * C + cco * 128, 128),
                    vb(RW_ + cc * _tk, _tk),
                    start=(cc == 0), stop=(cc == CH - 1)),
                    g.s_t if ci == CH - 1 else None)
            wait("vector", g.s_t)
            run("vector", lambda e, b_=bank, ap=csl(Dx_, co), _tk=tokc: e.tensor_tensor(
                ap, ap, g.PS[b_][:, 0:_tk], ALU.add), g.s_v)
        if kk == 0 and "x1" in g.dumps:
            wait("sync", g.s_v)
            dma("sync", g.dumps["x1"], vf(Dx_, TILE_COLS), g.s_st)
            wait("vector", g.s_st, g.s_st.n)
            wait("scalar", g.s_st, g.s_st.n)
        if pre and "x1p" in g.dumps:
            wait("sync", g.s_v)
            dma("sync", g.dumps["x1p"], vf(Dx_, 128), g.s_st)
            dma("sync", g.dumps["wkvp"], vf(A_, 128), g.s_st)
            wait("vector", g.s_st, g.s_st.n)
            wait("scalar", g.s_st, g.s_st.n)
        if pre:
            def put_pre(c, o, w):
                return g.sXC[:, c * B + o: c * B + o + w]
            g.emit_ln(lambda c, o, w: g.ARF[:, Dx_ + c * tokc + o: Dx_ + c * tokc + o + w],
                      put_pre, [(0, B)],
                      lambda c: g.vcol("g1", c), lambda c: g.vcol("b1", c))
            run("vector", lambda e: e.tensor_scalar_mul(g.sXC[:], g.sXC[:], g.sGATE[:, 0:1]), g.s_v)
            em("vector", lambda e: e.drain())
            continue

        def put_xc(c, o, w):
            ap = g.ARF[:, A_ + c * (B * SEG): A_ + (c + 1) * (B * SEG)].rearrange(
                "p (b t) -> p b t", b=B)[:, :, 1:SEG]
            b0 = o // TP
            b1 = (o + w) // TP
            return ap[:, b0:b1, :]
        g.emit_ln(lambda c, o, w: g.ARF[:, Dx_ + c * tokc + o: Dx_ + c * tokc + o + w],
                  put_xc, [(0, 512)],
                  lambda c: g.vcol("g1", c), lambda c: g.vcol("b1", c))

        def xc4():
            return g.ARF[:, A_:A_ + HALO_COLS].rearrange("p (c b t) -> p c b t", c=CH, b=B)
        run("vector", lambda e, ap=xc4()[:, :, :, 0:1]: e.tensor_copy(
            ap, g.sXC[:].rearrange("p (c b o) -> p c b o", c=CH, b=B)), g.s_v)
        em("vector", lambda e: e.drain())
        run("vector", lambda e, ap=xc4()[:, :, :, SEG - 1:SEG]: e.tensor_copy(
            g.sXC[:].rearrange("p (c b o) -> p c b o", c=CH, b=B), ap), g.s_v)
        em("vector", lambda e: e.drain())
        if kk == 0 and "xc" in g.dumps:
            wait("sync", g.s_v)
            dma("sync", g.dumps["xc"], vf(A_, HALO_COLS), g.s_st)
            wait("vector", g.s_st, g.s_st.n)
            wait("scalar", g.s_st, g.s_st.n)
        run("vector", lambda e, a1=xc4()[:, :, :, 1:SEG], a0=xc4()[:, :, :, 0:SEG - 1]: e.tensor_tensor(
            vf(Bd_, TILE_COLS).rearrange("p (c b t) -> p c b t", c=CH, b=B),
            a1, a0, ALU.subtract), g.s_v)
        em("vector", lambda e: e.drain())
        wait("vector", g.s_t)
        for c in range(CH):
            run("vector", lambda e, cc=c: e.scalar_tensor_tensor(
                vb(RW_ + cc * tokc, tokc),
                g.ARF[:, Bd_ + cc * tokc: Bd_ + (cc + 1) * tokc], g.vcol("cmk", cc),
                g.ARF[:, A_ + cc * (B * SEG): A_ + (cc + 1) * (B * SEG)].rearrange(
                    "p (b t) -> p b t", b=B)[:, :, 0:SEG - 1],
                ALU.mult, ALU.add), g.s_v)
        for c in range(CH):
            run("vector", lambda e, cc=c: e.scalar_tensor_tensor(
                vb(MXC_ + cc * tokc, tokc),
                g.ARF[:, Bd_ + cc * tokc: Bd_ + (cc + 1) * tokc], g.vcol("cmr", cc),
                g.ARF[:, A_ + cc * (B * SEG): A_ + (cc + 1) * (B * SEG)].rearrange(
                    "p (b t) -> p b t", b=B)[:, :, 0:SEG - 1],
                ALU.mult, ALU.add), g.s_v)
        mixc = g.s_v.n
        wait("tensor", g.s_v, mixc)
        first_mm = [True] * CH
        for gi in range(NG):
            if fg_ready[gi] is None:
                wait("sync", g.s_t)
                fg_ready[gi] = fg_load(gi, gi % 2)
            if gi + 1 < NG and fg_ready[gi + 1] is None:
                wait("sync", g.s_t)
                fg_ready[gi + 1] = fg_load(gi + 1, (gi + 1) % 2)
            wait("tensor", g.s_ld, fg_ready[gi])
            wbase = WG_ + (gi % 2) * 8192
            prev_wcv_t = g.s_t.n
            for fi in range(FG):
                hbank = 6 + fi % 2
                wait("tensor", g.s_v)
                for ci in range(CH):
                    run("tensor", lambda e, b_=hbank, cc=ci, ff=fi, wb=wbase: e.matmul(
                        g.PS[b_][:, 0:512],
                        vb(wb + cc * (128 * FG) + ff * 128, 128),
                        vb(RW_ + cc * 512, 512),
                        start=(cc == 0), stop=(cc == CH - 1)),
                        g.s_t if ci == CH - 1 else None)
                wait("vector", g.s_t)
                run("vector", lambda e, b_=hbank, ff=fi: e.tensor_relu(
                    g.sSC[:, 544 * (ff % 2):544 * (ff % 2) + 512],
                    g.PS[b_][:, 0:512]), g.s_v)
                wait("vector", g.s_t, prev_wcv_t)
                run("vector", lambda e, ff=fi: e.tensor_tensor(
                    vb(HK_ + ff * 512, 512),
                    g.sSC[:, 544 * (ff % 2):544 * (ff % 2) + 512],
                    g.sSC[:, 544 * (ff % 2):544 * (ff % 2) + 512], ALU.mult), g.s_v)
            hkv = g.s_v.n
            if kk == 0 and gi == 0 and "hk0" in g.dumps:
                run("vector", lambda e: e.tensor_copy(g.sSC[:, 0:512], vb(HK_, 512)), g.s_v)
                run("vector", lambda e: e.tensor_copy(g.sSC[:, 544:1056], vb(HK_ + 512, 512)), g.s_v)
                em("vector", lambda e: e.drain())
                wait("sync", g.s_v)
                dma("sync", g.dumps["hk0"][:, 0:512], g.sSC[:, 0:512], g.s_st)
                dma("sync", g.dumps["hk0"][:, 512:1024], g.sSC[:, 544:1056], g.s_st)
                wait("vector", g.s_st, g.s_st.n)
                run("vector", lambda e: e.tensor_copy(g.sSC[:, 0:512], vb(HK_ + 1024, 512)), g.s_v)
                run("vector", lambda e: e.tensor_copy(g.sSC[:, 544:1056], vb(HK_ + 1536, 512)), g.s_v)
                em("vector", lambda e: e.drain())
                wait("sync", g.s_v)
                wait("sync", g.s_st, g.s_st.n)
                dma("sync", g.dumps["hk0"][:, 1024:1536], g.sSC[:, 0:512], g.s_st)
                dma("sync", g.dumps["hk0"][:, 1536:2048], g.sSC[:, 544:1056], g.s_st)
                wait("vector", g.s_st, g.s_st.n)
                wait("scalar", g.s_st, g.s_st.n)
            wait("tensor", g.s_v, hkv)
            for co in range(CH):
                if co < 6:
                    last = (gi == NG - 1)
                    for fi in range(FG):
                        run("tensor", lambda e, cc=co, ff=fi, wb=wbase,
                                st=first_mm[co] and fi == 0,
                                sp=(last and fi == FG - 1): e.matmul(
                            g.PS[cc][:, 0:512],
                            vb(wb + 4096 + ff * C + cc * 128, 128),
                            vb(HK_ + ff * 512, 512),
                            start=st, stop=sp),
                            g.s_t if fi == FG - 1 else None)
                    first_mm[co] = False
                else:
                    bank = 6 + (co - 6)
                    wait("tensor", g.s_v)
                    for fi in range(FG):
                        run("tensor", lambda e, b_=bank, cc=co, ff=fi, wb=wbase: e.matmul(
                            g.PS[b_][:, 0:512],
                            vb(wb + 4096 + ff * C + cc * 128, 128),
                            vb(HK_ + ff * 512, 512),
                            start=(ff == 0), stop=(ff == FG - 1)),
                            g.s_t if fi == FG - 1 else None)
                    wait("vector", g.s_t)
                    if gi == 0:
                        run("vector", lambda e, b_=bank, ap=csl(Ekv_, co): e.tensor_copy(
                            ap, g.PS[b_][:, 0:512]), g.s_v)
                    else:
                        run("vector", lambda e, b_=bank, ap=csl(Ekv_, co): e.tensor_tensor(
                            ap, ap, g.PS[b_][:, 0:512], ALU.add), g.s_v)
        fg_ready = [None] * NG
        if kk < NT - 1:
            fg_ready[0] = fg_load(0, 0)
            fg_ready[1] = fg_load(1, 1)
        tall = g.s_t.n
        wait("vector", g.s_t, tall)
        for co in range(6):
            run("vector", lambda e, cc=co, ap=csl(Ekv_, co): e.tensor_copy(ap, g.PS[cc][:, 0:512]), g.s_v)
        kvev = g.s_v.n
        wait("sync", g.s_v, kvev)
        dma("sync", vb(WOC_, CH * C), g.wcrb_d, g.s_ld)
        wcrl = g.s_ld.n
        wait("tensor", g.s_ld, wcrl)
        wait("tensor", g.s_v, kvev)
        for co in range(CH):
            bank = co % 2
            wait("tensor", g.s_s)
            for ci in range(CH):
                run("tensor", lambda e, b_=bank, cc=ci, cco=co: e.matmul(
                    g.PS[b_][:, 0:512],
                    vb(WOC_ + cc * C + cco * 128, 128),
                    vb(MXC_ + cc * 512, 512),
                    start=(cc == 0), stop=(cc == CH - 1)),
                    g.s_t if ci == CH - 1 else None)
            wait("scalar", g.s_t)
            run("scalar", lambda e, b_=bank, ap=csl(A_, co): e.activation(
                ap, g.PS[b_][:, 0:512], AF.Exp, scale=-1.0), g.s_s)
        wait("vector", g.s_s)
        run("vector", lambda e: e.tensor_scalar(vf(A_, TILE_COLS), vf(A_, TILE_COLS), 1.0, None, ALU.add), g.s_v)
        em("vector", lambda e: e.drain())
        run("vector", lambda e: e.reciprocal(vf(Bd_, TILE_COLS), vf(A_, TILE_COLS)), g.s_v)
        em("vector", lambda e: e.drain())
        if kk == 0 and "kv" in g.dumps:
            wait("sync", g.s_v)
            dma("sync", g.dumps["kv"], vf(Ekv_, TILE_COLS), g.s_st)
            dma("sync", g.dumps["rr"], vf(Bd_, TILE_COLS), g.s_st)
            wait("vector", g.s_st, g.s_st.n)
            wait("scalar", g.s_st, g.s_st.n)
        xre = g.xre
        run("vector", lambda e: e.tensor_tensor(vf(Ekv_, TILE_COLS), vf(Ekv_, TILE_COLS), vf(Bd_, TILE_COLS), ALU.mult), g.s_v)
        run("vector", lambda e: e.tensor_tensor(vf(Ekv_, TILE_COLS), vf(Ekv_, TILE_COLS), vf(Dx_, TILE_COLS), ALU.add), g.s_v)
        wait("vector", g.s_ld, xre)
        run("vector", lambda e: e.tensor_tensor(vf(Ekv_, TILE_COLS), vf(Ekv_, TILE_COLS), vf(Cr_, TILE_COLS), ALU.add), g.s_v)
        em("vector", lambda e: e.drain())

        if kk == 0 and "yp" in g.dumps:
            wait("sync", g.s_v)
            dma("sync", g.dumps["yp"], vf(Ekv_, TILE_COLS), g.s_st)
            wait("vector", g.s_st, g.s_st.n)
            wait("scalar", g.s_st, g.s_st.n)
        def src_y(c, o, w):
            return g.ARF[:, Ekv_ + c * tokc + o: Ekv_ + c * tokc + o + w]

        def put_y1(c, o, w):
            b0 = o // TP
            b1 = (o + w) // TP
            return g.ARF[:, Bd_ + c * tokc: Bd_ + (c + 1) * tokc].rearrange(
                "p (t b) -> p t b", t=TP)[:, :, b0:b1].transpose([0, 2, 1])
        for (o, w) in [(0, 512)]:
            wait("tensor", g.s_v)
            wait("tensor", g.s_s)
            wait("scalar", g.s_v)
            for c in range(CH):
                run("tensor", lambda e, cc=c, ap=src_y(c, o, w), ww=w: e.matmul(
                    g.PS[4][0:1, 0:ww], g.sONE1c[:], ap,
                    start=(cc == 0), stop=(cc == CH - 1)), g.s_t)
            for c in range(CH):
                if c >= 2:
                    wait("scalar", g.s_t)
                run("scalar", lambda e, cc=c, ap=src_y(c, o, w), ww=w: e.activation(
                    g.sSC[:, (cc % 2) * 544:(cc % 2) * 544 + ww], ap, AF.Square), g.s_s)
                wait("tensor", g.s_s)
                run("tensor", lambda e, cc=c, ww=w: e.matmul(
                    g.PS[5][0:1, 0:ww], g.sONE1c[:],
                    g.sSC[:, (cc % 2) * 544:(cc % 2) * 544 + ww],
                    start=(cc == 0), stop=(cc == CH - 1)), g.s_t)
            tmm = g.s_t.n
            wait("vector", g.s_t, tmm)
            run("vector", lambda e, ww=w: e.tensor_copy(g.sROW[0:1, 0:ww], g.PS[4][0:1, 0:ww]), g.s_v)
            em("vector", lambda e: e.drain())
            run("vector", lambda e, ww=w: e.tensor_tensor(
                g.sROW[0:1, 1024:1024 + ww], g.sROW[0:1, 0:ww], g.sROW[0:1, 0:ww], ALU.mult), g.s_v)
            em("vector", lambda e: e.drain())
            run("vector", lambda e, ww=w: e.tensor_tensor(
                g.sROW[0:1, 512:512 + ww], g.PS[5][0:1, 0:ww], g.sROW[0:1, 1024:1024 + ww], ALU.subtract), g.s_v)
            vrow = g.s_v.n
            wait("scalar", g.s_v, vrow)
            run("scalar", lambda e, ww=w: e.activation(
                g.sROW[0:1, 512:512 + ww], g.sROW[0:1, 512:512 + ww], AF.Ln, bias=g.sDRV[0:1, 96:97]), g.s_s)
            em("scalar", lambda e: e.drain())
            run("scalar", lambda e, ww=w: e.activation(
                g.sROW[0:1, 512:512 + ww], g.sROW[0:1, 512:512 + ww], AF.Exp, scale=-0.5), g.s_s)
            srow = g.s_s.n
            wait("tensor", g.s_s, srow)
            run("tensor", lambda e, ww=w: e.matmul(
                g.PS[6][:, 0:ww], g.sONES1[:], g.sROW[0:1, 0:ww], start=True, stop=True), g.s_t)
            run("tensor", lambda e, ww=w: e.matmul(
                g.PS[7][:, 0:ww], g.sONES1[:], g.sROW[0:1, 512:512 + ww], start=True, stop=True), g.s_t)
            tb = g.s_t.n
            wait("vector", g.s_t, tb)
            for c in range(CH):
                run("vector", lambda e, ap=src_y(c, o, w), ww=w: e.tensor_tensor(
                    ap, ap, g.PS[6][:, 0:ww], ALU.subtract), g.s_v)
            for c in range(CH):
                run("vector", lambda e, po=put_y1(c, o, w), ap=src_y(c, o, w), ww=w: e.tensor_tensor(
                    po, ap, g.PS[7][:, 0:ww], ALU.mult), g.s_v)
        y1v = g.s_v.n
        wait("sync", g.s_v, y1v)
        dma("sync", g.y1_d[kk * 128:(kk + 1) * 128, :], vf(Bd_, TILE_COLS), g.s_st)
    if "y1" in g.dumps:
        wait("sync", g.s_st, g.s_st.n)
        dma("sync", g.dumps["y1"], g.y1_d, g.s_st)


def _s3(g):
    em, wait, dma, run, vf, vb = g.em, g.wait, g.dma, g.run, g.vf, g.vb
    Y_, Hh_ = 0, TILE_COLS
    QKV_, YB_, CTX_ = 0, 12288, 16384
    WIP_ = 20480
    CTX_ = QKV_
    WIP_ = 16384

    def wip_load(half):
        wait("scalar", g.s_t)
        dma("scalar", vb(WIP_, 12288),
            g.wipb_d[half * 128:(half + 1) * 128, :], g.s_ld)
        return g.s_ld.n

    wait("scalar", g.s_gc, g.prep_s3)
    s3w = wip_load(0)
    dma("sync", g.sGATH[:, 0:64], g.wrt_t, g.s_ld)
    rtw = g.s_ld.n

    for kk in range(NT):
        wait("sync", g.s_v)
        wait("sync", g.s_t)
        wait("sync", g.s_s)
        wait("sync", g.s_st, g.s_st.n)
        dma("sync", vf(Y_, TILE_COLS), g.y1_d[kk * 128:(kk + 1) * 128, :], g.s_ld)
        ldt = g.s_ld.n
        wait("vector", g.s_ld, ldt)
        run("vector", lambda e: e.tensor_copy(vb(YB_, TILE_COLS), vf(Y_, TILE_COLS)), g.s_v)
        em("vector", lambda e: e.drain())
        ybc = g.s_v.n
        wait("tensor", g.s_v, ybc)
        wait("tensor", g.s_ld, s3w)
        for oc in range(24):
            if oc == 12:
                s3w = wip_load(1)
                wait("tensor", g.s_ld, s3w)
            bank = oc % 2
            wait("tensor", g.s_s)
            for ci in range(CH):
                run("tensor", lambda e, b_=bank, cc=ci, oo=oc % 12: e.matmul(
                    g.PS[b_][:, 0:512],
                    vb(WIP_ + cc * 1536 + oo * 128, 128),
                    vb(YB_ + cc * 512, 512),
                    start=(cc == 0), stop=(cc == CH - 1)),
                    g.s_t if ci == CH - 1 else None)
            wait("scalar", g.s_t)
            if oc < 8:
                run("scalar", lambda e, b_=bank, oo=oc: e.activation(
                    vb(QKV_ + oo * 512, 512), g.PS[b_][:, 0:512], AF.Identity,
                    bias=g.sDRV[:, 24 + oo:25 + oo], scale=0.125), g.s_s)
            else:
                run("scalar", lambda e, b_=bank, oo=oc: e.activation(
                    vb(QKV_ + oo * 512, 512), g.PS[b_][:, 0:512], AF.Identity,
                    bias=g.sVEC[:, 128 + oo:129 + oo]), g.s_s)
        qkvs = g.s_s.n
        inproj_t = g.s_t.n
        wait("tensor", g.s_s, qkvs)
        snaps = [dict(t=0, v=0, s=0), dict(t=0, v=0, s=0)]
        SBK = [2, 0]
        TBK = [3, 1]
        CBK = [4, 6]
        AS_ = [YB_ + 2048, YB_ + 2816]
        XR = [1056, 1072]
        gidx = 0
        for pb in range(4):
            colo = pb * 128
            for j in range(CH):
                p = gidx % 2
                sb, tb, cb = SBK[p], TBK[p], CBK[p]
                a0 = AS_[p]
                xr = XR[p]
                qof = QKV_ + j * 512 + colo
                kof = QKV_ + (8 + j) * 512 + colo
                vof = QKV_ + (16 + j) * 512 + colo
                sn = snaps[p]
                wait("tensor", g.s_v, sn["v"])
                wait("tensor", g.s_s, sn["s"])
                for hh in range(2):
                    run("tensor", lambda e, h_=hh, b_=sb: e.matmul(
                        g.PS[b_][:, h_ * 128:h_ * 128 + 128], g.sMA[:], g.sMB[:],
                        start=True, stop=False), None)
                    run("tensor", lambda e, h_=hh, b_=sb, q_=qof, k_=kof: e.matmul(
                        g.PS[b_][:, h_ * 128:h_ * 128 + 128],
                        g.ARB[64 * h_:64 * h_ + 64, q_:q_ + 128],
                        g.ARB[64 * h_:64 * h_ + 64, k_:k_ + 128],
                        start=False, stop=True), g.s_t)
                sco = g.s_t.n
                run("tensor", lambda e, v_=vof, b_=tb: e.transpose(
                    g.PS[b_][:, 128:192].bitcast(BF16), g.ARB[:, v_:v_ + 128],
                    g.sIDB[:]), g.s_t)
                vt = g.s_t.n
                wait("vector", g.s_t, sco)
                run("vector", lambda e, b_=sb, x_=xr: e.reduce_max(
                    g.sSC[:, x_:x_ + 2].rearrange("p (s o) -> p s o", s=2),
                    g.PS[b_][:, 0:256].rearrange("p (s n) -> p s n", s=2),
                    axis=AX, negate=True), g.s_v)
                em("vector", lambda e: e.drain())
                smax = g.s_v.n
                wait("scalar", g.s_v, smax)
                for hh in range(2):
                    run("scalar", lambda e, h_=hh, b_=sb, a_=a0, x_=xr: e.activation(
                        g.ARB[:, a_ + h_ * 128: a_ + h_ * 128 + 128],
                        g.PS[b_][:, h_ * 128:h_ * 128 + 128],
                        AF.Exp, bias=g.sSC[:, x_ + h_:x_ + h_ + 1]), g.s_s)
                sexp = g.s_s.n
                wait("vector", g.s_s, sexp)
                run("vector", lambda e, a_=a0, x_=xr: e.tensor_reduce(
                    g.sSC[:, x_ + 2:x_ + 4].rearrange("p (s o) -> p s o", s=2),
                    g.ARB[:, a_:a_ + 256].rearrange("p (s n) -> p s n", s=2),
                    axis=AX, op=ALU.add), g.s_v)
                em("vector", lambda e: e.drain())
                run("vector", lambda e, x_=xr: e.reciprocal(
                    g.sSC[:, x_ + 4:x_ + 6], g.sSC[:, x_ + 2:x_ + 4]), g.s_v)
                em("vector", lambda e: e.drain())
                for hh in range(2):
                    run("vector", lambda e, h_=hh, a_=a0, x_=xr: e.tensor_scalar_mul(
                        g.ARB[:, a_ + h_ * 128: a_ + h_ * 128 + 128],
                        g.ARB[:, a_ + h_ * 128: a_ + h_ * 128 + 128],
                        g.sSC[:, x_ + 4 + h_:x_ + 5 + h_]), g.s_v)
                atn = g.s_v.n
                wait("tensor", g.s_v, atn)
                for hh in range(2):
                    run("tensor", lambda e, h_=hh, a_=a0, b_=tb: e.transpose(
                        g.PS[b_][:, h_ * 64:h_ * 64 + 64].bitcast(BF16),
                        g.ARB[:, a_ + h_ * 128: a_ + h_ * 128 + 128],
                        g.sIDB[:]), g.s_t)
                att = g.s_t.n
                wait("vector", g.s_t, att)
                run("vector", lambda e, a_=a0, b_=tb: e.tensor_copy(
                    g.ARB[:, a_ + 256: a_ + 512], g.PS[b_][:, 0:128].bitcast(BF16)), g.s_v)
                run("vector", lambda e, a_=a0, b_=tb: e.tensor_copy(
                    g.ARB[:, a_ + 512: a_ + 640], g.PS[b_][:, 128:192].bitcast(BF16)), g.s_v)
                em("vector", lambda e: e.drain())
                tev = g.s_v.n
                wait("tensor", g.s_v, tev)
                run("tensor", lambda e, a_=a0, b_=cb: e.matmul(
                    g.PS[b_][:, 0:256], g.ARB[:, a_ + 512: a_ + 640],
                    g.ARB[:, a_ + 256: a_ + 512], start=True, stop=True), g.s_t)
                ctm = g.s_t.n
                wait("vector", g.s_t, ctm)
                run("vector", lambda e, j_=j, c_=colo, b_=cb: e.tensor_copy(
                    g.ARB[0:64, CTX_ + j_ * 512 + c_: CTX_ + j_ * 512 + c_ + 128],
                    g.PS[b_][0:64, 0:128]), g.s_v)
                run("vector", lambda e, j_=j, c_=colo, b_=cb: e.tensor_copy(
                    g.ARB[64:128, CTX_ + j_ * 512 + c_: CTX_ + j_ * 512 + c_ + 128],
                    g.PS[b_][64:128, 128:256]), g.s_v)
                snaps[p] = dict(t=g.s_t.n, v=g.s_v.n, s=g.s_s.n)
                gidx += 1
        ctxv = g.s_v.n
        wait("scalar", g.s_t, inproj_t)
        dma("scalar", vb(WIP_, CH * C), g.wowb_d, g.s_ld)
        oww = g.s_ld.n
        wait("tensor", g.s_ld, oww)
        wait("tensor", g.s_v, ctxv)
        for co in range(CH):
            bank = co % 2
            wait("tensor", g.s_v)
            for ci in range(CH):
                run("tensor", lambda e, b_=bank, cc=ci, cco=co: e.matmul(
                    g.PS[b_][:, 0:512],
                    vb(WIP_ + cc * C + cco * 128, 128),
                    vb(CTX_ + cc * 512, 512),
                    start=(cc == 0), stop=(cc == CH - 1)),
                    g.s_t if ci == CH - 1 else None)
            wait("vector", g.s_t)
            run("vector", lambda e, b_=bank, cco=co: e.tensor_tensor(
                vf(Hh_ + cco * 512, 512), vf(Y_ + cco * 512, 512),
                g.PS[b_][:, 0:512], ALU.add), g.s_v)
            run("vector", lambda e, cco=co: e.tensor_scalar(
                vf(Hh_ + cco * 512, 512), vf(Hh_ + cco * 512, 512),
                g.vcol("ob", cco), None, ALU.add), g.s_v)
        def src_h(c, o, w):
            return g.ARF[:, Hh_ + c * 512 + o: Hh_ + c * 512 + o + w]
        g.emit_ln(src_h, src_h, [(0, 512)],
                  lambda c: g.vcol("fg1", c), lambda c: g.vcol("fb1", c))
        hv = g.s_v.n
        wait("sync", g.s_v, hv)
        dma("sync", g.h_d[kk * 128:(kk + 1) * 128, :], vf(Hh_, TILE_COLS), g.s_st)
        wait("tensor", g.s_v, hv)
        wait("tensor", g.s_ld, rtw)
        for ci in range(CH):
            run("tensor", lambda e, cc=ci: e.matmul(
                g.PS[6][0:8, 0:512], g.sGATH[:, cc * 8:(cc + 1) * 8],
                vf(Hh_ + cc * 512, 512), start=(cc == 0), stop=(cc == CH - 1)),
                g.s_t if ci == CH - 1 else None)
        lg = g.s_t.n
        wait("vector", g.s_t, lg)
        run("vector", lambda e: e.tensor_scalar(
            g.sSC[0:8, 0:512], g.PS[6][0:8, 0:512], g.sC8[:, 0:1], None, ALU.add), g.s_v)
        em("vector", lambda e: e.drain())
        lgv = g.s_v.n
        wait("tensor", g.s_v, lgv)
        for bq in range(4):
            run("tensor", lambda e, b_=bq: e.transpose(
                g.PS[7][:, b_ * 8:b_ * 8 + 8], g.sSC[0:8, b_ * 128:(b_ + 1) * 128],
                g.sIDF[0:8, 0:8]), g.s_t)
        lgt = g.s_t.n
        wait("vector", g.s_t, lgt)
        run("vector", lambda e: e.tensor_copy(g.sSC[:, 544:544 + 32], g.PS[7][:, 0:32]), g.s_v)
        em("vector", lambda e: e.drain())
        for bq in range(4):
            lgq = g.sSC[:, 544 + bq * 8:544 + bq * 8 + 8]
            m1 = g.sSC[:, 1080:1081]
            msk = g.sSC[:, 0:8]
            xp = g.sSC[:, 16:24]
            m2 = g.sSC[:, 1081:1082]
            i1 = g.sSC[:, 1082:1083]
            run("vector", lambda e, a=lgq, b_=m1: e.reduce_max(b_, a, axis=AX), g.s_v)
            em("vector", lambda e: e.drain())
            run("vector", lambda e, a=lgq, b_=m1, c_=msk: e.tensor_tensor(
                c_, a, b_.to_broadcast([128, 8]), ALU.is_equal), g.s_v)
            em("vector", lambda e: e.drain())
            run("vector", lambda e, a=lgq, c_=msk, d_=xp: e.scalar_tensor_tensor(
                d_, c_, -1e30, a, ALU.mult, ALU.add), g.s_v)
            em("vector", lambda e: e.drain())
            run("vector", lambda e, d_=xp, b_=m2: e.reduce_max(b_, d_, axis=AX), g.s_v)
            em("vector", lambda e: e.drain())
            run("vector", lambda e, a=lgq, b_=m2, c_=msk: e.tensor_tensor(
                c_, b_.to_broadcast([128, 8]), a, ALU.is_le), g.s_v)
            em("vector", lambda e: e.drain())
            run("vector", lambda e, c_=msk: e.tensor_tensor(c_, c_, g.sIOTA8[:], ALU.mult), g.s_v)
            em("vector", lambda e: e.drain())
            run("vector", lambda e, c_=msk, i_=i1: e.reduce_max(i_, c_, axis=AX), g.s_v)
            em("vector", lambda e: e.drain())
            asn = g.s_v.n
            wait("tensor", g.s_v, asn)
            run("tensor", lambda e, i_=i1: e.transpose(
                g.PS[7][0:1, 128:256], i_, g.sIDF[:]), g.s_t)
            att2 = g.s_t.n
            wait("vector", g.s_t, att2)
            run("vector", lambda e: e.tensor_copy(
                g.sSC[0:1, 544:672], g.PS[7][0:1, 128:256]), g.s_v)
            em("vector", lambda e: e.drain())
            wait("sync", g.s_v)
            dma("sync", g.asg_d[0:1, kk * 512 + bq * 128: kk * 512 + (bq + 1) * 128],
                g.sSC[0:1, 544:672], g.s_st)
        if kk < NT - 1:
            s3w = wip_load(0)
    if "h" in g.dumps:
        wait("sync", g.s_st, g.s_st.n)
        dma("sync", g.dumps["h"], g.h_d, g.s_st)
    if "assign" in g.dumps:
        wait("sync", g.s_st, g.s_st.n)
        dma("sync", g.dumps["assign"], g.asg_d, g.s_st)


def _s4(g):
    em, wait, dma, run, vf, vb = g.em, g.wait, g.dma, g.run, g.vf, g.vb
    H_, Yt_, M67_ = 0, TILE_COLS, 2 * TILE_COLS
    HBV = g.ARF[:, 10240:12288].bitcast(BF16)
    HKe_ = 0
    W1_ = [4096, 12288]
    W2_ = [20480, 28672]

    def w_load(e_, par):
        wait("sync", g.s_t)
        dma("sync", vb(W1_[par], CH * C), g.w1b_d[e_ * 128:(e_ + 1) * 128, :], g.s_ld)
        wait("scalar", g.s_t)
        dma("scalar", vb(W2_[par], CH * C), g.w2b_d[e_ * 128:(e_ + 1) * 128, :], g.s_ld)
        return g.s_ld.n

    wait("sync", g.s_gc, g.prep_done)
    wait("scalar", g.s_gc, g.prep_done)
    wready = [None] * NE

    for kk in range(NT):
        wait("sync", g.s_v)
        wait("sync", g.s_t)
        wait("sync", g.s_s)
        wait("sync", g.s_st, g.s_st.n)
        dma("sync", vf(H_, TILE_COLS), g.h_d[kk * 128:(kk + 1) * 128, :], g.s_ld)
        wait("gpsimd", g.s_st, g.s_st.n)
        dma("gpsimd", g.sASG[:], g.asg_d[0:1, kk * 512:(kk + 1) * 512], g.s_gc)
        asgl = g.s_gc.n
        ldh = g.s_ld.n
        wait("vector", g.s_ld, ldh)
        run("vector", lambda e: e.tensor_copy(HBV, vf(H_, TILE_COLS)), g.s_v)
        em("vector", lambda e: e.drain())
        g.hbc = g.s_v.n
        wait("tensor", g.s_v)
        wait("tensor", g.s_gc, asgl)
        run("tensor", lambda e: e.matmul(
            g.PS[6][0:8, 0:512], g.sONES8[:],
            g.sASG[0:1, :], start=True, stop=True), g.s_t)
        mb = g.s_t.n
        wait("vector", g.s_t, mb)
        run("vector", lambda e: e.tensor_tensor(
            g.sM8[:], g.PS[6][0:8, 0:512], g.sEID[:], ALU.is_equal), g.s_v)
        em("vector", lambda e: e.drain())
        if kk == 0 and "m80" in g.dumps:
            run("vector", lambda e: e.tensor_copy(g.sSC[0:8, 0:512], g.sM8[:]), g.s_v)
            em("vector", lambda e: e.drain())
            wait("sync", g.s_v)
            dma("sync", g.dumps["m80"], g.sSC[0:8, 0:512], g.s_st)
            wait("vector", g.s_st, g.s_st.n)
            run("vector", lambda e: e.tensor_copy(g.sSC[0:8, 0:512], g.PS[6][0:8, 0:512]), g.s_v)
            em("vector", lambda e: e.drain())
            wait("sync", g.s_v)
            dma("sync", g.dumps["bc0"], g.sSC[0:8, 0:512], g.s_st)
            wait("vector", g.s_st, g.s_st.n)
            run("vector", lambda e: e.tensor_copy(g.sSC[0:1, 0:512], g.sASG[:]), g.s_v)
            em("vector", lambda e: e.drain())
            wait("sync", g.s_v)
            dma("sync", g.dumps["asg0"], g.sSC[0:1, 0:512], g.s_st)
            wait("vector", g.s_st, g.s_st.n)
        run("vector", lambda e: e.tensor_scalar(
            g.sNM8[:], g.sM8[:], -1.0, 1.0, ALU.mult, ALU.add), g.s_v)
        em("vector", lambda e: e.drain())
        m8v = g.s_v.n
        wait("tensor", g.s_v, m8v)
        for co in range(CH):
            if co < 6:
                run("tensor", lambda e, cc=co: e.matmul(
                    g.PS[cc][:, 0:512], g.sB2[:, cc * 128:(cc + 1) * 128],
                    g.sM8[:], start=True, stop=False), g.s_t)
            else:
                run("tensor", lambda e, cc=co: e.matmul(
                    g.PS[7][:, 0:512], g.sB2[:, cc * 128:(cc + 1) * 128],
                    g.sM8[:], start=True, stop=True), g.s_t)
                wait("vector", g.s_t)
                run("vector", lambda e, cc=co: e.tensor_copy(
                    vf(M67_ + (cc - 6) * 512, 512), g.PS[7][:, 0:512]), g.s_v)
                wait("tensor", g.s_v)
        wait("tensor", g.s_v, g.hbc)
        for e_ in range(NE):
            if wready[e_] is None:
                wready[e_] = w_load(e_, e_ % 2)
            if e_ + 1 < NE and wready[e_ + 1] is None:
                wready[e_ + 1] = w_load(e_ + 1, (e_ + 1) % 2)
            wait("tensor", g.s_ld, wready[e_])
            w1b = W1_[e_ % 2]
            w2b = W2_[e_ % 2]
            for fi in range(CH):
                hbank = 6 + fi % 2
                wait("tensor", g.s_s)
                run("tensor", lambda e, b_=hbank, ee=e_: e.matmul(
                    g.PS[b_][:, 0:512], g.sNEG9[:, ee * 128:(ee + 1) * 128],
                    g.sNM8[:], start=True, stop=False), None)
                for ci in range(CH):
                    run("tensor", lambda e, b_=hbank, cc=ci, ff=fi, wb=w1b: e.matmul(
                        g.PS[b_][:, 0:512],
                        vb(wb + cc * C + ff * 128, 128),
                        HBV[:, cc * 512:(cc + 1) * 512],
                        start=False, stop=(cc == CH - 1)),
                        g.s_t if ci == CH - 1 else None)
                wait("scalar", g.s_t)
                run("scalar", lambda e, b_=hbank, ee=e_, ff=fi: e.activation(
                    vb(HKe_ + ff * 512, 512), g.PS[b_][:, 0:512], AF.Gelu,
                    bias=g.sVEC[:, 152 + ee * 8 + ff:153 + ee * 8 + ff]), g.s_s)
            hkv = g.s_s.n
            if kk == 0 and e_ == 0 and "he0" in g.dumps:
                wait("vector", g.s_s, hkv)
                run("vector", lambda e: e.tensor_copy(vf(Yt_, 4096), vb(HKe_, 4096)), g.s_v)
                em("vector", lambda e: e.drain())
                wait("sync", g.s_v)
                dma("sync", g.dumps["he0"], vf(Yt_, 4096), g.s_st)
                wait("vector", g.s_st, g.s_st.n)
                wait("scalar", g.s_st, g.s_st.n)
                wait("tensor", g.s_st, g.s_st.n)
            wait("tensor", g.s_s, hkv)
            for co in range(CH):
                if co < 6:
                    for fi in range(CH):
                        run("tensor", lambda e, cc=co, ff=fi, wb=w2b,
                                sp=(e_ == NE - 1 and fi == CH - 1): e.matmul(
                            g.PS[cc][:, 0:512],
                            vb(wb + ff * C + cc * 128, 128),
                            vb(HKe_ + ff * 512, 512),
                            start=False, stop=sp),
                            g.s_t if fi == CH - 1 else None)
                else:
                    bank = 6 + (co - 6)
                    wait("tensor", g.s_v)
                    for fi in range(CH):
                        run("tensor", lambda e, b_=bank, cc=co, ff=fi, wb=w2b: e.matmul(
                            g.PS[b_][:, 0:512],
                            vb(wb + ff * C + cc * 128, 128),
                            vb(HKe_ + ff * 512, 512),
                            start=(ff == 0), stop=(ff == CH - 1)),
                            g.s_t if fi == CH - 1 else None)
                    wait("vector", g.s_t)
                    run("vector", lambda e, cc=co: e.tensor_tensor(
                        vf(M67_ + (cc - 6) * 512, 512), vf(M67_ + (cc - 6) * 512, 512),
                        g.PS[6 + (cc - 6)][:, 0:512], ALU.add), g.s_v)
        wready = [None] * NE
        tall = g.s_t.n
        wait("vector", g.s_t, tall)
        for co in range(CH):
            if co < 6:
                run("vector", lambda e, cc=co: e.tensor_tensor(
                    vf(H_ + cc * 512, 512), vf(H_ + cc * 512, 512),
                    g.PS[cc][:, 0:512], ALU.add), g.s_v)
            else:
                run("vector", lambda e, cc=co: e.tensor_tensor(
                    vf(H_ + cc * 512, 512), vf(H_ + cc * 512, 512),
                    vf(M67_ + (cc - 6) * 512, 512), ALU.add), g.s_v)
        em("vector", lambda e: e.drain())

        if kk == 0 and "hm0" in g.dumps:
            wait("sync", g.s_v)
            dma("sync", g.dumps["hm0"], vf(H_, TILE_COLS), g.s_st)
            wait("vector", g.s_st, g.s_st.n)
            wait("scalar", g.s_st, g.s_st.n)
            wait("tensor", g.s_st, g.s_st.n)
        def src_hm(c, o, w):
            return g.ARF[:, H_ + c * 512 + o: H_ + c * 512 + o + w]
        g.emit_ln(src_hm, src_hm, [(0, 512)],
                  lambda c: g.vcol("fg2", c), lambda c: g.vcol("fb2", c))
        dma("sync", vf(Yt_, TILE_COLS), g.y1_d[kk * 128:(kk + 1) * 128, :], g.s_ld)
        ldy = g.s_ld.n
        wait("vector", g.s_ld, ldy)
        run("vector", lambda e: e.tensor_tensor(
            vf(H_, TILE_COLS), vf(H_, TILE_COLS), vf(Yt_, TILE_COLS), ALU.add), g.s_v)
        em("vector", lambda e: e.drain())
        g.emit_ln(src_hm, src_hm, [(0, 512)], None, None)
        ov = g.s_v.n
        wait("sync", g.s_v, ov)
        dma("sync", g.out_d[kk * 128:(kk + 1) * 128, :], vf(H_, TILE_COLS), g.s_st)


def _pack_weight(w):
    kin, nout = w.shape
    return np.ascontiguousarray(
        w.reshape(kin // 128, 128, nout).transpose(1, 0, 2).reshape(128, -1))


def _pack_vec(v):
    return np.ascontiguousarray(np.asarray(v, np.float32).reshape(CH, 128).T)


def _stack_groups_ck(wp):
    blocks = []
    for gi in range(8):
        blocks.append(np.concatenate(
            [wp[:, cc * FF + 512 * gi: cc * FF + 512 * gi + 512] for cc in range(CH)], axis=1))
    return np.ascontiguousarray(np.concatenate(blocks, axis=0))


def _stack_halves_ip(wp):
    blocks = []
    for h in range(2):
        blocks.append(np.concatenate(
            [wp[:, ci * 3 * C + h * 1536: ci * 3 * C + (h + 1) * 1536] for ci in range(CH)], axis=1))
    return np.ascontiguousarray(np.concatenate(blocks, axis=0))


def _host_inputs(inp):
    x = np.asarray(inp["L0_attn_x"], np.float32)
    x_pad = np.concatenate([np.zeros((B, 2, C), np.float32), x], axis=1)
    vec = np.zeros((128, 256), np.float32)
    order = ["L0_attn_ln0_g", "L0_attn_ln0_b", "L0_attn_ln1_g", "L0_attn_ln1_b",
             "L0_attn_tm_k", "L0_attn_tm_v", "L0_attn_tm_r", "L0_attn_cm_k",
             "L0_attn_cm_r", "L0_attn_time_decay", "L0_attn_time_first",
             "L0_ffn_out_b", "L0_ffn_ln1_g", "L0_ffn_ln1_b", "L0_ffn_ln2_g",
             "L0_ffn_ln2_b"]
    for idx, key in enumerate(order):
        vec[:, idx * 8:(idx + 1) * 8] = _pack_vec(inp[key])
    vec[:, 128:152] = np.ascontiguousarray(
        np.asarray(inp["L0_ffn_in_proj_b"], np.float32).reshape(24, 128).T)
    vec[:, 152:216] = np.ascontiguousarray(
        np.asarray(inp["L0_ffn_b1"], np.float32).reshape(NE * CH, 128).T)
    c8 = np.zeros((8, 64), np.float32)
    c8[:, 0] = np.asarray(inp["L0_ffn_router_b"], np.float32)
    common = dict(
        wk_t=_pack_weight(np.asarray(inp["L0_attn_Wk"], np.float32)),
        wv_t=_pack_weight(np.asarray(inp["L0_attn_Wv"], np.float32)),
        wr_t=_pack_weight(np.asarray(inp["L0_attn_Wr"], np.float32)),
        wo_t=_pack_weight(np.asarray(inp["L0_attn_Wo"], np.float32)),
        wck_t=_stack_groups_ck(_pack_weight(np.asarray(inp["L0_attn_Wck"], np.float32))),
        wcv_t=np.concatenate([_pack_weight(np.asarray(inp["L0_attn_Wcv"], np.float32))[:, gi * 4096:(gi + 1) * 4096] for gi in range(8)], axis=0),
        wcr_t=_pack_weight(np.asarray(inp["L0_attn_Wcr"], np.float32)),
        wip_t=_stack_halves_ip(_pack_weight(np.ascontiguousarray(np.asarray(inp["L0_ffn_in_proj_w"], np.float32).T))),
        wow_t=_pack_weight(np.ascontiguousarray(np.asarray(inp["L0_ffn_out_w"], np.float32).T)),
        wrt_t=_pack_weight(np.ascontiguousarray(np.asarray(inp["L0_ffn_router_w"], np.float32).T)),
        w1_t=np.concatenate([_pack_weight(np.ascontiguousarray(np.asarray(inp["L0_ffn_w1"][e], np.float32).T)) for e in range(NE)], axis=0),
        w2_t=np.concatenate([_pack_weight(np.ascontiguousarray(np.asarray(inp["L0_ffn_w2"][e], np.float32).T)) for e in range(NE)], axis=0),
        vec32=vec, c8f=c8,
        b2f=np.asarray(inp["L0_ffn_b2"], np.float32),
        eidb=np.ascontiguousarray(np.broadcast_to(
            np.arange(8, dtype=np.float32)[:, None], (8, 512))).astype(NPBF),
        ones1f=np.ones((1, 128), np.float32),
        ones8b=np.ones((1, 8), NPBF),
        neg9b=np.asarray([[(-1e9 if k2 == (m // 128) else 0.0) for m in range(1024)]
                          for k2 in range(8)], NPBF),
        identb=np.eye(128, dtype=NPBF),
        identf=np.eye(128, dtype=np.float32),
        iotarep=np.ascontiguousarray(np.broadcast_to(np.arange(257, dtype=np.float32), (128, 257))),
        iota8=np.ascontiguousarray(np.broadcast_to(np.arange(8, dtype=np.float32), (128, 8))),
        maskA=np.asarray([[1.0 if m // 16 == gg else 0.0 for m in range(128)] for gg in range(8)], NPBF),
        maskB=np.asarray([[0.0 if n // 16 == gg else -1e30 for n in range(128)] for gg in range(8)], NPBF),
    )
    maps = []
    for i in range(NC):
        s = PPC * i
        slabs = [x_pad[:, s:s + 2, :]]
        for kk in range(NT):
            slabs.append(x_pad[:, s + 1 + TP * kk: s + 1 + TP * kk + SEG, :])
        x_core = np.zeros(((1 + NT) * 128, HALO_COLS), np.float32)
        for si, sl in enumerate(slabs):
            L = sl.shape[1]
            x_core[si * 128:(si + 1) * 128, 0:CH * B * L] = (
                sl.transpose(2, 0, 1).reshape(CH, 128, B * L).transpose(1, 0, 2).reshape(128, CH * B * L))
        gates = np.zeros((128, 2), np.float32)
        gates[:, 0] = 0.0 if i == 0 else 1.0
        gates[:, 1] = 1.0 - gates[:, 0]
        gp = np.zeros((128, 8), np.float32)
        gp[:, :i] = 1.0
        m = dict(common)
        m.update(x_t=x_core, gates=gates, gpre=gp, gpre256=np.ascontiguousarray(gp * 256.0))
        maps.append(m)
    return maps


def _assemble(results):
    out = np.zeros((B, TFULL, C), np.float32)
    for i in range(NC):
        arr = np.asarray(results[i]["out_d"], np.float32)
        a = arr.reshape(NT, 128, CH, TP, B)
        out[:, PPC * i:PPC * (i + 1), :] = a.transpose(4, 0, 3, 2, 1).reshape(B, PPC, C)
    return out


def kernel(**inputs):
    maps = _host_inputs(inputs)
    nc = build_program()
    res = run_bass_kernel_spmd(nc, maps, core_ids=list(range(NC)))
    return _assemble(res.results)


# version 09
# speedup vs baseline: 1.0339x; 1.0339x over previous
import numpy as np
import ml_dtypes
import contextlib
import concourse.bass as bass
import concourse.mybir as mybir
from concourse.bass_utils import run_bass_kernel_spmd

F32 = mybir.dt.float32
BF16 = mybir.dt.bfloat16
AF = mybir.ActivationFunctionType
ALU = mybir.AluOpType
AX = mybir.AxisListType.X
NPBF = np.dtype(ml_dtypes.bfloat16)

NC = 8
B = 16
C = 1024
TFULL = 2048
PPC = 256
TP = 32
NT = 8
CH = 8
FF = 4096
NE = 8
SEG = TP + 1
TILE_COLS = CH * B * TP
HALO_COLS = CH * B * SEG
EPS = 1e-5


class SemT:
    def __init__(self, h):
        self.h = h
        self.n = 0

    def inc(self, amt):
        self.n += amt
        return self.n


class Ctx:
    pass


def build_program(dump=()):
    nc = bass.Bass("TRN2", target_bir_lowering=False, debug=False)
    g = Ctx()
    g.nc = nc

    def din(name, shape, dt=F32):
        return nc.dram_tensor(name, list(shape), dt, kind="ExternalInput").ap()

    g.x_t = din("x_t", [(1 + NT) * 128, HALO_COLS])
    g.wk_t = din("wk_t", [128, CH * C])
    g.wv_t = din("wv_t", [128, CH * C])
    g.wr_t = din("wr_t", [128, CH * C])
    g.wo_t = din("wo_t", [128, CH * C])
    g.wck_t = din("wck_t", [8 * 128, CH * 512])
    g.wcv_t = din("wcv_t", [8 * 128, 4 * C])
    g.wcr_t = din("wcr_t", [128, CH * C])
    g.wip_t = din("wip_t", [2 * 128, 12288])
    g.wow_t = din("wow_t", [128, CH * C])
    g.wrt_t = din("wrt_t", [128, CH * 8])
    g.w1_t = din("w1_t", [NE * 128, CH * C])
    g.w2_t = din("w2_t", [NE * 128, CH * C])
    g.vec32 = din("vec32", [128, 256])
    g.c8f = din("c8f", [8, 64])
    g.b2f = din("b2f", [8, C])
    g.eidb = din("eidb", [8, 512], BF16)
    g.ones1f = din("ones1f", [1, 128])
    g.ones8b = din("ones8b", [1, 8], BF16)
    g.neg9b = din("neg9b", [8, 1024], BF16)
    g.identb = din("identb", [128, 128], BF16)
    g.identf = din("identf", [128, 128])
    g.iotarep = din("iotarep", [128, 257])
    g.iota8 = din("iota8", [128, 8])
    g.gates = din("gates", [128, 2])
    g.gpre = din("gpre", [128, 8])
    g.gpre256 = din("gpre256", [128, 8])
    g.maskA = din("maskA", [8, 128], BF16)
    g.maskB = din("maskB", [8, 128], BF16)

    g.out_d = nc.dram_tensor("out_d", [NT * 128, TILE_COLS], F32, kind="ExternalOutput").ap()
    g.dumps = {}
    for nm in dump:
        shape = {"hm0": [128, TILE_COLS], "m80": [8, 512], "he0": [128, 4096],
                 "asg0": [1, 512], "bc0": [8, 512],
                 "hk0": [128, 2048], "x1p": [128, 128], "wkvp": [128, 128],
                 "x1": [128, TILE_COLS], "xc": [128, HALO_COLS],
                 "kv": [128, TILE_COLS], "rr": [128, TILE_COLS],
                 "yp": [128, TILE_COLS], "wkv": [128, TILE_COLS],
                 "dbgP": [128, HALO_COLS], "dbgF": [128, HALO_COLS],
                 "dbgN": [128, HALO_COLS], "dbgD": [128, HALO_COLS],
                 "num": [128, 256 + NT * TILE_COLS], "den": [128, 256 + NT * TILE_COLS],
                 "r": [128, 256 + NT * TILE_COLS], "y1": [128, NT * TILE_COLS],
                 "h": [128, NT * TILE_COLS], "seeds": [128, 256],
                 "assign": [1, 4096]}[nm]
        g.dumps[nm] = nc.dram_tensor("dump_" + nm, shape, F32, kind="ExternalOutput").ap()

    g.num_d = nc.dram_tensor("num_d", [(1 + NT) * 128, TILE_COLS], F32).ap()
    g.den_d = nc.dram_tensor("den_d", [(1 + NT) * 128, TILE_COLS], F32).ap()
    g.r_d = nc.dram_tensor("r_d", [(1 + NT) * 128, TILE_COLS], F32).ap()
    g.y1_d = nc.dram_tensor("y1_d", [NT * 128, TILE_COLS], F32).ap()
    g.h_d = nc.dram_tensor("h_d", [NT * 128, TILE_COLS], F32).ap()
    g.wob_d = nc.dram_tensor("wob_d", [128, CH * C], BF16).ap()
    g.wcrb_d = nc.dram_tensor("wcrb_d", [128, CH * C], BF16).ap()
    g.wckb_d = nc.dram_tensor("wckb_d", [8 * 128, CH * 512], BF16).ap()
    g.wcvb_d = nc.dram_tensor("wcvb_d", [8 * 128, 4 * C], BF16).ap()
    g.wipb_d = nc.dram_tensor("wipb_d", [2 * 128, 12288], BF16).ap()
    g.wowb_d = nc.dram_tensor("wowb_d", [128, CH * C], BF16).ap()
    g.w1b_d = nc.dram_tensor("w1b_d", [NE * 128, CH * C], BF16).ap()
    g.w2b_d = nc.dram_tensor("w2b_d", [NE * 128, CH * C], BF16).ap()
    g.asg_d = nc.dram_tensor("asg_d", [1, 4096], F32).ap()
    g.cc_in = nc.dram_tensor("cc_in", [128, 256], F32).ap()
    g.cc_out = nc.dram_tensor("cc_out", [1024, 256], F32).ap()

    es = contextlib.ExitStack()
    with es:
        def sbt(name, shape, dt=F32):
            return es.enter_context(nc.sbuf_tensor(name, list(shape), dt))

        g.sVEC = sbt("sVEC", [128, 256])
        g.sC8 = sbt("sC8", [8, 64])
        g.sB2 = sbt("sB2", [8, C], BF16)
        g.sEID = sbt("sEID", [8, 512], BF16)
        g.sONES1 = sbt("sONES1", [1, 128])
        g.sONES8 = sbt("sONES8", [1, 8], BF16)
        g.sNEG9 = sbt("sNEG9", [8, 1024], BF16)
        g.sIDB = sbt("sIDB", [128, 128], BF16)
        g.sIDF = sbt("sIDF", [128, 128])
        g.sIOTA = sbt("sIOTA", [128, 257])
        g.sIOTA8 = sbt("sIOTA8", [128, 8])
        g.sGATE = sbt("sGATE", [128, 2])
        g.sGPRE = sbt("sGPRE", [128, 8])
        g.sGP256 = sbt("sGP256", [128, 8])
        g.sMA = sbt("sMA", [8, 128], BF16)
        g.sMB = sbt("sMB", [8, 128], BF16)
        g.sONE1c = sbt("sONE1c", [128, 1])
        g.sDRV = sbt("sDRV", [128, 104])
        g.sROW = sbt("sROW", [1, 1600])
        g.sSEED = sbt("sSEED", [128, 512])
        g.sGATH = sbt("sGATH", [128, 512])
        g.sSNSD = sbt("sSNSD", [128, 256])
        g.sXC = sbt("sXC", [128, 128])
        g.sASG = sbt("sASG", [1, 512], BF16)
        g.sM8 = sbt("sM8", [8, 512], BF16)
        g.sNM8 = sbt("sNM8", [8, 512], BF16)
        g.sSC = sbt("sSC", [128, 1088])
        g.sLREP = sbt("sLREP", [128, CH * SEG * B])
        g.sLAM = g.sLREP
        g.sLPRE = sbt("sLPRE", [128, CH * B * 2])
        g.ARF = sbt("ARF", [128, 21120])
        g.ARB = sbt("ARB", [128, 36864], BF16)
        g.PS = [es.enter_context(nc.psum_tensor("ps%d" % i, [128, 512], F32))
                for i in range(8)]

        g.s_ld = SemT(es.enter_context(nc.semaphore("s_ld")))
        g.s_gc = SemT(es.enter_context(nc.semaphore("s_gc")))
        g.s_t = SemT(es.enter_context(nc.semaphore("s_t")))
        g.s_v = SemT(es.enter_context(nc.semaphore("s_v")))
        g.s_s = SemT(es.enter_context(nc.semaphore("s_s")))
        g.s_st = SemT(es.enter_context(nc.semaphore("s_st")))
        g.s_cc = SemT(es.enter_context(nc.semaphore("s_cc")))

        g.P = {"sync": [], "tensor": [], "vector": [], "scalar": [], "gpsimd": []}

        def em(eng, fn):
            g.P[eng].append(fn)

        def wait(eng, sem, v=None):
            vv = sem.n if v is None else v
            em(eng, lambda e, h=sem.h, w=vv: e.wait_ge(h, w))

        def dma(eng, dst, src, sem, slow=False):
            sem.inc(16)
            em(eng, lambda e, d=dst, s=src, h=sem.h, sl=slow: e.dma_start(
                out=d, in_=s, allow_slow_non_contiguous=sl).then_inc(h, 16))

        def run(eng, fn, sem=None, amt=1):
            if sem is None:
                em(eng, fn)
            else:
                sem.inc(amt)
                em(eng, lambda e, f=fn, h=sem.h, a=amt: f(e).then_inc(h, a))

        g.em, g.wait, g.dma, g.run = em, wait, dma, run

        def vf(off, n):
            return g.ARF[:, off:off + n]

        def vb(off, n):
            return g.ARB[:, off:off + n]

        g.vf, g.vb = vf, vb

        VOFF = dict(g0=0, b0=8, g1=16, b1=24, tmk=32, tmv=40, tmr=48, cmk=56,
                    cmr=64, td=72, tf=80, ob=88, fg1=96, fb1=104, fg2=112,
                    fb2=120, ipb=128, b1e=152)
        DOFF = dict(w=0, lam=8, eu=16, bq8=24, fpre=32)

        def vcol(name, j=0):
            o = VOFF[name] + j
            return g.sVEC[:, o:o + 1]

        def dcol(name, j=0):
            o = DOFF[name] + j
            return g.sDRV[:, o:o + 1]

        g.vcol, g.dcol = vcol, dcol

        def emit_ln(get_src, put_out, sub_ranges, gcol=None, bcol=None):
            for (o, w) in sub_ranges:
                wait("tensor", g.s_v)
                wait("tensor", g.s_s)
                wait("scalar", g.s_v)
                for c in range(CH):
                    run("tensor", lambda e, ap=get_src(c, o, w), ww=w, cc=c: e.matmul(
                        g.PS[4][0:1, 0:ww], g.sONE1c[:], ap,
                        start=(cc == 0), stop=(cc == CH - 1)), g.s_t)
                for c in range(CH):
                    if c >= 2:
                        wait("scalar", g.s_t)
                    run("scalar", lambda e, ap=get_src(c, o, w), cc=c, ww=w: e.activation(
                        g.sSC[:, (cc % 2) * 544:(cc % 2) * 544 + ww],
                        ap, AF.Square), g.s_s)
                    wait("tensor", g.s_s)
                    run("tensor", lambda e, cc=c, ww=w: e.matmul(
                        g.PS[5][0:1, 0:ww], g.sONE1c[:],
                        g.sSC[:, (cc % 2) * 544:(cc % 2) * 544 + ww],
                        start=(cc == 0), stop=(cc == CH - 1)), g.s_t)
                tmm = g.s_t.n
                wait("vector", g.s_t, tmm)
                run("vector", lambda e, ww=w: e.tensor_copy(
                    g.sROW[0:1, 0:ww], g.PS[4][0:1, 0:ww]), g.s_v)
                em("vector", lambda e: e.drain())
                run("vector", lambda e, ww=w: e.tensor_tensor(
                    g.sROW[0:1, 1024:1024 + ww], g.sROW[0:1, 0:ww],
                    g.sROW[0:1, 0:ww], ALU.mult), g.s_v)
                em("vector", lambda e: e.drain())
                run("vector", lambda e, ww=w: e.tensor_tensor(
                    g.sROW[0:1, 512:512 + ww], g.PS[5][0:1, 0:ww],
                    g.sROW[0:1, 1024:1024 + ww], ALU.subtract), g.s_v)
                vrow = g.s_v.n
                wait("scalar", g.s_v, vrow)
                run("scalar", lambda e, ww=w: e.activation(
                    g.sROW[0:1, 512:512 + ww], g.sROW[0:1, 512:512 + ww],
                    AF.Ln, bias=g.sDRV[0:1, 96:97]), g.s_s)
                em("scalar", lambda e: e.drain())
                run("scalar", lambda e, ww=w: e.activation(
                    g.sROW[0:1, 512:512 + ww], g.sROW[0:1, 512:512 + ww],
                    AF.Exp, scale=-0.5), g.s_s)
                srow = g.s_s.n
                wait("tensor", g.s_s, srow)
                run("tensor", lambda e, ww=w: e.matmul(
                    g.PS[6][:, 0:ww], g.sONES1[:], g.sROW[0:1, 0:ww],
                    start=True, stop=True), g.s_t)
                run("tensor", lambda e, ww=w: e.matmul(
                    g.PS[7][:, 0:ww], g.sONES1[:], g.sROW[0:1, 512:512 + ww],
                    start=True, stop=True), g.s_t)
                tb = g.s_t.n
                wait("vector", g.s_t, tb)
                for c in range(CH):
                    run("vector", lambda e, po=put_out(c, o, w), gs=get_src(c, o, w), ww=w: e.tensor_tensor(
                        po, gs, g.PS[6][:, 0:ww], ALU.subtract), g.s_v)
                for c in range(CH):
                    run("vector", lambda e, po=put_out(c, o, w), ww=w: e.tensor_tensor(
                        po, po, g.PS[7][:, 0:ww], ALU.mult), g.s_v)
                if gcol is not None:
                    for c in range(CH):
                        run("vector", lambda e, po=put_out(c, o, w), gc=gcol(c), bc=bcol(c): e.tensor_scalar(
                            po, po, gc, bc, ALU.mult, ALU.add), g.s_v)

        g.emit_ln = emit_ln

        _preamble(g)
        _s1(g)
        _collective(g)
        _s2(g)
        _s3(g)
        _s4(g)

        wait("sync", g.s_st, g.s_st.n)
        wait("sync", g.s_gc, g.s_gc.n)

        with nc.Block() as block:
            @block.sync
            def _(q):
                for fn in g.P["sync"]:
                    fn(q)

            @block.tensor
            def _(q):
                for fn in g.P["tensor"]:
                    fn(q)

            @block.vector
            def _(q):
                for fn in g.P["vector"]:
                    fn(q)

            @block.scalar
            def _(q):
                for fn in g.P["scalar"]:
                    fn(q)

            @block.gpsimd
            def _(q):
                for fn in g.P["gpsimd"]:
                    fn(q)
    return nc


def _preamble(g):
    em, wait, dma, run = g.em, g.wait, g.dma, g.run
    for dst, src in [(g.sVEC, g.vec32), (g.sC8, g.c8f), (g.sONES1, g.ones1f),
                     (g.sONES8, g.ones8b), (g.sNEG9, g.neg9b),
                     (g.sIDB, g.identb), (g.sIDF, g.identf),
                     (g.sIOTA, g.iotarep), (g.sIOTA8, g.iota8),
                     (g.sGATE, g.gates), (g.sGPRE, g.gpre),
                     (g.sGP256, g.gpre256), (g.sMA, g.maskA),
                     (g.sMB, g.maskB), (g.sEID, g.eidb)]:
        dma("sync", dst[:], src, g.s_ld)
    dma("gpsimd", g.sB2[:], g.b2f, g.s_gc)

    run("vector", lambda e: e.memset(g.sONE1c[:], 1.0 / 1024.0), g.s_v)
    run("vector", lambda e: e.memset(g.sDRV[:, 96:97], float(EPS)), g.s_v)
    wait("scalar", g.s_ld)
    run("scalar", lambda e: e.activation(g.sDRV[:, 0:8], g.sVEC[:, 72:80], AF.Exp), g.s_s)
    wait("vector", g.s_s)
    run("vector", lambda e: e.tensor_scalar_mul(g.sDRV[:, 0:8], g.sDRV[:, 0:8], -1.0), g.s_v)
    em("vector", lambda e: e.drain())
    wait("scalar", g.s_v)
    run("scalar", lambda e: e.activation(g.sDRV[:, 8:16], g.sDRV[:, 0:8], AF.Exp), g.s_s)
    run("scalar", lambda e: e.activation(g.sDRV[:, 16:24], g.sVEC[:, 80:88], AF.Exp), g.s_s)
    for j in range(NC):
        run("scalar", lambda e, jj=j: e.activation(
            g.sDRV[:, 32 + 8 * jj:40 + 8 * jj], g.sDRV[:, 0:8], AF.Exp,
            scale=g.sGP256[:, jj:jj + 1]), g.s_s)
    sd = g.s_s.n
    wait("vector", g.s_s, sd)
    run("vector", lambda e: e.tensor_scalar_mul(g.sDRV[:, 24:32], g.sVEC[:, 128:136], 0.125), g.s_v)
    run("vector", lambda e: e.memset(g.sLREP[:], 1.0), g.s_v)
    run("vector", lambda e: e.memset(g.sLPRE[:], 1.0), g.s_v)
    em("vector", lambda e: e.drain())
    for chi in range(CH):
        run("vector", lambda e, c=chi: e.tensor_scalar_mul(
            g.sLREP[:, c * 528:(c + 1) * 528], g.sLREP[:, c * 528:(c + 1) * 528],
            g.sDRV[:, 8 + c:9 + c]), g.s_v)
        run("vector", lambda e, c=chi: e.tensor_scalar_mul(
            g.sLPRE[:, c * 32:(c + 1) * 32], g.sLPRE[:, c * 32:(c + 1) * 32],
            g.sDRV[:, 8 + c:9 + c]), g.s_v)
    em("vector", lambda e: e.drain())
    run("vector", lambda e: e.memset(
        g.sLREP[:].rearrange("p (c b t) -> p c b t", c=CH, b=B)[:, :, :, 0:1], 0.0), g.s_v)
    run("vector", lambda e: e.memset(
        g.sLPRE[:].rearrange("p (c b t) -> p c b t", c=CH, b=B, t=2)[:, :, :, 0:1], 0.0), g.s_v)
    run("vector", lambda e: e.memset(g.sSEED[:], 0.0), g.s_v)
    em("vector", lambda e: e.drain())
    WKB_, WVB_, WRB_ = 2 * TILE_COLS, 2 * TILE_COLS + CH * C, 2 * TILE_COLS + 2 * CH * C
    for dst_off, src in [(WKB_, g.wk_t), (WVB_, g.wv_t), (WRB_, g.wr_t)]:
        dma("gpsimd", g.vb(dst_off, CH * C), src, g.s_gc)
    g.s1w = g.s_gc.n
    dma("gpsimd", g.wob_d, g.wo_t, g.s_gc)
    dma("gpsimd", g.wcrb_d, g.wcr_t, g.s_gc)
    dma("gpsimd", g.wckb_d, g.wck_t, g.s_gc)
    dma("gpsimd", g.wcvb_d, g.wcv_t, g.s_gc)
    g.prep_s2 = g.s_gc.n
    dma("gpsimd", g.wipb_d, g.wip_t, g.s_gc)
    dma("gpsimd", g.wowb_d, g.wow_t, g.s_gc)
    g.prep_s3 = g.s_gc.n
    dma("gpsimd", g.w1b_d, g.w1_t, g.s_gc)
    dma("gpsimd", g.w2b_d, g.w2_t, g.s_gc)
    g.prep_done = g.s_gc.n


def _s1(g):
    em, wait, dma, run, vf, vb = g.em, g.wait, g.dma, g.run, g.vf, g.vb
    A_, Bf_, Cp_, Dn_, Ed_ = 0, HALO_COLS, 2 * HALO_COLS, 3 * HALO_COLS, 4 * HALO_COLS
    MX0_, MX1_, MX2_ = 0, TILE_COLS, 8 * TILE_COLS
    WKB_, WVB_, WRB_ = 2 * TILE_COLS, 2 * TILE_COLS + CH * C, 2 * TILE_COLS + 2 * CH * C

    dma("sync", vf(A_, CH * B * 2), g.x_t[0:128, 0:256], g.s_ld)
    g.ldx_next = g.s_ld.n

    for kk in range(-1, NT):
        pre = (kk == -1)
        seg = 2 if pre else SEG
        cols = CH * B * seg
        tokw = B * (seg - 1)

        def r4(base, _c=cols, _s=seg):
            return g.ARF[:, base:base + _c].rearrange("p (c b t) -> p c b t", c=CH, b=B)

        def chsl(base, c, a, bb, _s=seg):
            return g.ARF[:, base + c * (B * _s): base + (c + 1) * (B * _s)].rearrange(
                "p (b t) -> p b t", b=B)[:, :, a:bb]

        ldx = g.ldx_next
        rt_prev = g.s_t.n
        st_prev = g.s_st.n

        wait("vector", g.s_ld, ldx)
        wait("scalar", g.s_ld, ldx)
        wait("tensor", g.s_ld, ldx)
        if kk == 0:
            wait("scalar", g.s_st, st_prev)
        subs = [(0, B * seg)] if pre else [(0, 512), (512, 16)]

        def src_xa(c, o, w, _base=A_, _s=seg):
            return g.ARF[:, _base + c * (B * _s) + o: _base + c * (B * _s) + o + w]
        g.emit_ln(src_xa, src_xa, subs, lambda c: g.vcol("g0", c), lambda c: g.vcol("b0", c))
        if pre:
            run("vector", lambda e, cc=cols: e.tensor_scalar_mul(vf(A_, cc), vf(A_, cc), g.sGATE[:, 0:1]), g.s_v)
        elif kk == 0:
            run("vector", lambda e, ap=r4(A_)[:, :, :, 0:1]: e.tensor_scalar_mul(
                ap, ap, g.sGATE[:, 0:1]), g.s_v)
        em("vector", lambda e: e.drain())
        run("vector", lambda e, ad=r4(Cp_)[:, :, :, 1:seg], ac=r4(A_)[:, :, :, 1:seg],
                ap_=r4(A_)[:, :, :, 0:seg - 1]: e.tensor_tensor(ad, ac, ap_, ALU.subtract), g.s_v)
        em("vector", lambda e: e.drain())

        wait("vector", g.s_t, rt_prev)
        for mi, (nm, moff) in enumerate([("tmr", MX2_), ("tmk", MX0_), ("tmv", MX1_)]):
            for c in range(CH):
                run("vector", lambda e, ot=vb(moff + c * tokw, tokw), i0=chsl(Cp_, c, 1, seg),
                        sc=g.vcol(nm, c), i1=chsl(A_, c, 0, seg - 1): e.scalar_tensor_tensor(
                    ot, i0, sc, i1, ALU.mult, ALU.add), g.s_v)
            mixv = g.s_v.n
            if mi == 1:
                wait("tensor", g.s_v, mixv)
                if kk <= 0:
                    wait("tensor", g.s_gc, g.s1w)
                wait("scalar", g.s_st, st_prev)
                for co in range(CH):
                    bank = co % 2
                    wait("tensor", g.s_s)
                    for ci in range(CH):
                        run("tensor", lambda e, b_=bank, cc=ci, cco=co, tw=tokw: e.matmul(
                            g.PS[b_][:, 0:tw],
                            vb(WKB_ + cc * C + cco * 128, 128),
                            vb(MX0_ + cc * tw, tw),
                            start=(cc == 0), stop=(cc == CH - 1)),
                            g.s_t if ci == CH - 1 else None)
                    wait("scalar", g.s_t)
                    run("scalar", lambda e, b_=bank, ot=chsl(Bf_, co, 1, seg), tw=tokw: e.activation(
                        ot, g.PS[b_][:, 0:tw], AF.Exp), g.s_s)
            elif mi == 2:
                wait("tensor", g.s_v, mixv)
                for co in range(CH):
                    bank = 2 + co % 2
                    wait("tensor", g.s_v)
                    for ci in range(CH):
                        run("tensor", lambda e, b_=bank, cc=ci, cco=co, tw=tokw: e.matmul(
                            g.PS[b_][:, 0:tw],
                            vb(WVB_ + cc * C + cco * 128, 128),
                            vb(MX1_ + cc * tw, tw),
                            start=(cc == 0), stop=(cc == CH - 1)),
                            g.s_t if ci == CH - 1 else None)
                    wait("vector", g.s_t)
                    wait("vector", g.s_s)
                    run("vector", lambda e, b_=bank, ot=chsl(Cp_, co, 1, seg),
                            i0=chsl(Bf_, co, 1, seg), tw=tokw: e.tensor_tensor(
                        ot, i0, g.PS[b_][:, 0:tw], ALU.mult), g.s_v)
        if kk < NT - 1:
            nxt = kk + 1
            wait("sync", g.s_v, mixv)
            dma("sync", vf(A_, HALO_COLS),
                g.x_t[(1 + nxt) * 128:(2 + nxt) * 128, :], g.s_ld)
            g.ldx_next = g.s_ld.n

        if pre:
            wait("vector", g.s_s)
            run("vector", lambda e, cc=cols: e.tensor_scalar_mul(vf(Cp_, cc), vf(Cp_, cc), g.sGATE[:, 0:1]), g.s_v)
            run("vector", lambda e, cc=cols: e.tensor_scalar_mul(vf(Bf_, cc), vf(Bf_, cc), g.sGATE[:, 0:1]), g.s_v)
            em("vector", lambda e: e.drain())
            run("vector", lambda e, ap=r4(Cp_)[:, :, :, 0:1]: e.memset(ap, 0.0), g.s_v)
            run("vector", lambda e, ap=r4(Bf_)[:, :, :, 0:1]: e.memset(ap, 0.0), g.s_v)
        else:
            wait("vector", g.s_s)
            run("vector", lambda e, ap=r4(Cp_)[:, :, :, 0:1]: e.tensor_copy(
                ap, g.sSEED[:, 0:128].rearrange("p (c b o) -> p c b o", c=CH, b=B)), g.s_v)
            run("vector", lambda e, ap=r4(Bf_)[:, :, :, 0:1]: e.tensor_copy(
                ap, g.sSEED[:, 128:256].rearrange("p (c b o) -> p c b o", c=CH, b=B)), g.s_v)
        em("vector", lambda e: e.drain())
        wait("vector", g.s_st, st_prev)
        lrep = g.sLPRE if pre else g.sLREP
        for c in range(CH):
            w_ = B * seg
            run("vector", lambda e, cc=c, ww=w_, lr=lrep: e.tensor_tensor_scan(
                vf(Dn_ + cc * ww, ww), lr[:, cc * ww:(cc + 1) * ww],
                vf(Cp_ + cc * ww, ww), 0.0, ALU.mult, ALU.add), g.s_v)
        for c in range(CH):
            w_ = B * seg
            run("vector", lambda e, cc=c, ww=w_, lr=lrep: e.tensor_tensor_scan(
                vf(Ed_ + cc * ww, ww), lr[:, cc * ww:(cc + 1) * ww],
                vf(Bf_ + cc * ww, ww), 0.0, ALU.mult, ALU.add), g.s_v)
        em("vector", lambda e: e.drain())
        run("vector", lambda e, ap=r4(Dn_)[:, :, :, seg - 1:seg]: e.tensor_copy(
            g.sSEED[:, 0:128].rearrange("p (c b o) -> p c b o", c=CH, b=B), ap), g.s_v)
        run("vector", lambda e, ap=r4(Ed_)[:, :, :, seg - 1:seg]: e.tensor_copy(
            g.sSEED[:, 128:256].rearrange("p (c b o) -> p c b o", c=CH, b=B), ap), g.s_v)
        if kk == NT - 1:
            run("vector", lambda e, ap=r4(Dn_)[:, :, :, seg - 2:seg - 1]: e.tensor_copy(
                g.sSEED[:, 256:384].rearrange("p (c b o) -> p c b o", c=CH, b=B), ap), g.s_v)
            run("vector", lambda e, ap=r4(Ed_)[:, :, :, seg - 2:seg - 1]: e.tensor_copy(
                g.sSEED[:, 384:512].rearrange("p (c b o) -> p c b o", c=CH, b=B), ap), g.s_v)
        em("vector", lambda e: e.drain())
        for c in range(CH):
            run("vector", lambda e, ot=chsl(Dn_, c, 0, seg - 1), i0=chsl(Cp_, c, 1, seg),
                    sc=g.dcol("eu", c): e.scalar_tensor_tensor(
                ot, i0, sc, ot, ALU.mult, ALU.add), g.s_v)
        for c in range(CH):
            run("vector", lambda e, ot=chsl(Ed_, c, 0, seg - 1), i0=chsl(Bf_, c, 1, seg),
                    sc=g.dcol("eu", c): e.scalar_tensor_tensor(
                ot, i0, sc, ot, ALU.mult, ALU.add), g.s_v)
        if pre:
            run("vector", lambda e, ap=r4(Ed_)[:, :, :, 0:1]: e.tensor_scalar(
                ap, ap, g.sGATE[:, 1:2], None, ALU.add), g.s_v)
        numden = g.s_v.n
        wait("tensor", g.s_v, numden)
        for co in range(CH):
            bank = 4 + co % 2
            wait("tensor", g.s_s)
            for ci in range(CH):
                run("tensor", lambda e, b_=bank, cc=ci, cco=co, tw=tokw: e.matmul(
                    g.PS[b_][:, 0:tw],
                    vb(WRB_ + cc * C + cco * 128, 128),
                    vb(MX2_ + cc * tw, tw),
                    start=(cc == 0), stop=(cc == CH - 1)),
                    g.s_t if ci == CH - 1 else None)
            wait("scalar", g.s_t)
            run("scalar", lambda e, b_=bank, ot=chsl(Cp_, co, 1, seg), tw=tokw: e.activation(
                ot, g.PS[b_][:, 0:tw], AF.Exp, scale=-1.0), g.s_s)
        wait("vector", g.s_s)
        run("vector", lambda e, cc=cols: e.tensor_scalar(
            vf(Cp_, cc), vf(Cp_, cc), 1.0, None, ALU.add), g.s_v)
        em("vector", lambda e: e.drain())
        run("vector", lambda e, cc=cols: e.reciprocal(vf(Bf_, cc), vf(Cp_, cc)), g.s_v)
        if pre:
            for si, (base, a) in enumerate([(Dn_, 0), (Ed_, 0), (Bf_, 1)]):
                run("vector", lambda e, aa=si, ap=r4(base)[:, :, :, a:a + 1]: e.tensor_copy(
                    g.sSC[:, aa * 128:(aa + 1) * 128].rearrange("p (c b o) -> p c b o", c=CH, b=B),
                    ap), g.s_v)
            em("vector", lambda e: e.drain())
            vr = g.s_v.n
            wait("sync", g.s_v, vr)
            for si, dram in enumerate([g.num_d, g.den_d, g.r_d]):
                dma("sync", dram[0:128, 0:128], g.sSC[:, si * 128:(si + 1) * 128], g.s_st)
        else:
            vr = g.s_v.n
            r0 = (1 + kk) * 128
            for (dram, base, a, bb), eng in [((g.r_d, Bf_, 1, seg), "sync"),
                                             ((g.num_d, Dn_, 0, seg - 1), "scalar"),
                                             ((g.den_d, Ed_, 0, seg - 1), "sync")]:
                wait(eng, g.s_v, vr)
                dma(eng, dram[r0:r0 + 128, :],
                    r4(base)[:, :, :, a:bb], g.s_st)


def _collective(g):
    em, wait, dma, run = g.em, g.wait, g.dma, g.run
    wait("gpsimd", g.s_v)
    dma("gpsimd", g.cc_in, g.sSEED[:, 256:512], g.s_gc)
    wait("gpsimd", g.s_gc)
    run("gpsimd", lambda e: e.collective_compute(
        "AllGather", ALU.bypass, replica_groups=[list(range(NC))],
        ins=[g.cc_in.opt()], outs=[g.cc_out.opt()]), g.s_cc)
    wait("gpsimd", g.s_cc)
    run("vector", lambda e: e.memset(g.sSNSD[:], 0.0), g.s_v)
    em("vector", lambda e: e.drain())
    for j in range(NC):
        dma("gpsimd", g.sGATH[:, 0:256], g.cc_out[j * 128:(j + 1) * 128, :], g.s_gc)
        gld = g.s_gc.n
        fj = g.sDRV[:, 32 + 8 * j:40 + 8 * j]
        wait("vector", g.s_gc, gld)
        for half in (0, 1):
            run("vector", lambda e, f_=fj, hh=half: e.tensor_tensor(
                g.sSC[:, hh * 128:hh * 128 + 128].rearrange("p (c b) -> p c b", c=CH),
                g.sSNSD[:, hh * 128:hh * 128 + 128].rearrange("p (c b) -> p c b", c=CH),
                f_.rearrange("p (c o) -> p c o", o=1).to_broadcast([128, CH, B]), ALU.mult), g.s_v)
        em("vector", lambda e: e.drain())
        for half in (0, 1):
            run("vector", lambda e, jj=j, hh=half: e.scalar_tensor_tensor(
                g.sSNSD[:, hh * 128:hh * 128 + 128],
                g.sGATH[:, hh * 128:hh * 128 + 128],
                g.sGPRE[:, jj:jj + 1],
                g.sSC[:, hh * 128:hh * 128 + 128], ALU.mult, ALU.add), g.s_v)
        em("vector", lambda e: e.drain())
        wait("gpsimd", g.s_v)
    if "seeds" in g.dumps:
        wait("sync", g.s_v)
        dma("sync", g.dumps["seeds"], g.sSNSD[:], g.s_st)
    for nm, dr in [("num", g.num_d), ("den", g.den_d), ("r", g.r_d)]:
        if nm in g.dumps:
            wait("sync", g.s_st, g.s_st.n)
            dma("sync", g.dumps[nm], dr, g.s_st)


def _s2(g):
    em, wait, dma, run, vf, vb = g.em, g.wait, g.dma, g.run, g.vf, g.vb
    A_, Bd_, Cr_, Dx_, Ekv_ = 0, 4224, 8320, 12416, 16512
    RW_, MXC_, HK_, WOC_, WG_ = 0, 4096, 8192, 10240, 18432
    FG = 4
    NG = 32 // FG

    wait("scalar", g.s_v)
    for chi in range(CH):
        run("scalar", lambda e, c=chi: e.activation(
            g.sLAM[:, c * 257:(c + 1) * 257], g.sIOTA[:], AF.Exp,
            scale=g.sDRV[:, c:c + 1]), g.s_s)
    lam_built = g.s_s.n

    def fg_load(gi, par):
        base = WG_ + par * 8192
        wait("sync", g.s_t)
        dma("sync", vb(base, 4096),
            g.wckb_d[gi * 128:(gi + 1) * 128, :], g.s_ld)
        wait("scalar", g.s_t)
        dma("scalar", vb(base + 4096, 4096),
            g.wcvb_d[gi * 128:(gi + 1) * 128, :], g.s_ld)
        return g.s_ld.n

    wait("sync", g.s_gc, g.prep_s2)
    wait("scalar", g.s_gc, g.prep_s2)
    wait("sync", g.s_st, g.s_st.n)
    wait("scalar", g.s_st, g.s_st.n)
    fg_ready = [None] * NG
    fg_ready[0] = fg_load(0, 0)

    for kk in range(-1, NT):
        pre = (kk == -1)
        ntok = 128 if pre else TILE_COLS
        tokc = B if pre else B * TP
        src_off = 0 if pre else 256 + kk * TILE_COLS

        def csl(base, c, _t=tokc):
            return g.ARF[:, base + c * _t: base + (c + 1) * _t]

        wait("sync", g.s_v)
        wait("sync", g.s_s)
        wait("sync", g.s_t)
        wait("sync", g.s_gc, g.s_gc.n)
        wait("scalar", g.s_v)
        wait("scalar", g.s_st, g.s_st.n)
        rb = (1 + kk) * 128 if not pre else 0
        dma("sync", vf(A_, ntok), g.num_d[rb:rb + 128, 0:ntok], g.s_ld)
        dma("scalar", vf(Bd_, ntok), g.den_d[rb:rb + 128, 0:ntok], g.s_ld)
        dma("sync", vf(Cr_, ntok), g.r_d[rb:rb + 128, 0:ntok], g.s_ld)
        if pre:
            dma("sync", vf(Ekv_, 256), g.x_t[0:128, 0:256], g.s_ld)
            prex = g.s_ld.n
            wait("vector", g.s_ld, prex)
            run("vector", lambda e: e.tensor_copy(
                vf(Dx_, 128).rearrange("p (c b o) -> p c b o", c=CH, b=B),
                vf(Ekv_, 256).rearrange("p (c b t) -> p c b t", c=CH, b=B)[:, :, :, 1:2]), g.s_v)
            em("vector", lambda e: e.drain())
        else:
            xs = g.x_t[(1 + kk) * 128:(2 + kk) * 128, :].rearrange(
                "p (c b t) -> p c b t", c=CH, b=B)[:, :, :, 1:SEG]
            dma("scalar", vf(Dx_, ntok).rearrange("p (c b t) -> p c b t", c=CH, b=B, t=TP),
                xs, g.s_ld)
        wait("scalar", g.s_t)
        dma("scalar", vb(WOC_, CH * C), g.wob_d, g.s_ld)
        ldt = g.s_ld.n

        t0 = 0 if pre else TP * kk + 1
        tw = 1 if pre else TP
        lam4 = g.sLAM.rearrange("p (c t) -> p c t", c=CH, t=528)[:, :, t0:t0 + tw] if False else \
            g.sLAM[:, 0:CH * 257].rearrange("p (c t) -> p c t", c=CH)[:, :, t0:t0 + tw]
        wait("vector", g.s_ld, ldt)
        if pre:
            wait("vector", g.s_s, lam_built)
        for (tgt, sn_off) in [(A_, 0), (Bd_, 128)]:
            run("vector", lambda e, so=sn_off, l4=lam4, _tw=tw: e.tensor_tensor(
                vf(Ekv_, CH * B * _tw).rearrange("p (c b t) -> p c b t", c=CH, b=B),
                g.sSNSD[:, so:so + 128].rearrange("p (c b o) -> p c b o", c=CH, b=B).to_broadcast([128, CH, B, _tw]),
                l4.unsqueeze(2).to_broadcast([128, CH, B, _tw]), ALU.mult), g.s_v)
            em("vector", lambda e: e.drain())
            run("vector", lambda e, tg=tgt, _n=ntok: e.tensor_tensor(
                vf(tg, _n), vf(tg, _n), vf(Ekv_, _n), ALU.add), g.s_v)
            em("vector", lambda e: e.drain())
        run("vector", lambda e, n_=ntok: e.reciprocal(vf(Ekv_, n_), vf(Bd_, n_)), g.s_v)
        em("vector", lambda e: e.drain())
        run("vector", lambda e, n_=ntok: e.tensor_tensor(vf(A_, n_), vf(A_, n_), vf(Ekv_, n_), ALU.mult), g.s_v)
        run("vector", lambda e, n_=ntok: e.tensor_tensor(vb(RW_, n_), vf(A_, n_), vf(Cr_, n_), ALU.mult), g.s_v)
        rwv = g.s_v.n
        if not pre:
            wait("scalar", g.s_v, rwv)
            dma("scalar", vf(Cr_, TILE_COLS).rearrange("p (c b t) -> p c b t", c=CH, b=B),
                g.x_t[(1 + kk) * 128:(2 + kk) * 128, :].rearrange(
                    "p (c b t) -> p c b t", c=CH, b=B)[:, :, :, 1:SEG], g.s_ld)
            g.xre = g.s_ld.n
        if kk == 0 and "wkv" in g.dumps:
            wait("sync", g.s_v, rwv)
            dma("sync", g.dumps["wkv"], vf(A_, TILE_COLS), g.s_st)
            wait("vector", g.s_st, g.s_st.n)
            wait("scalar", g.s_st, g.s_st.n)
        wait("tensor", g.s_v, rwv)
        wait("tensor", g.s_ld, ldt)
        for co in range(CH):
            bank = co % 2
            wait("tensor", g.s_v)
            for ci in range(CH):
                run("tensor", lambda e, b_=bank, cc=ci, cco=co, _tk=tokc: e.matmul(
                    g.PS[b_][:, 0:_tk],
                    vb(WOC_ + cc * C + cco * 128, 128),
                    vb(RW_ + cc * _tk, _tk),
                    start=(cc == 0), stop=(cc == CH - 1)),
                    g.s_t if ci == CH - 1 else None)
            wait("vector", g.s_t)
            run("vector", lambda e, b_=bank, ap=csl(Dx_, co), _tk=tokc: e.tensor_tensor(
                ap, ap, g.PS[b_][:, 0:_tk], ALU.add), g.s_v)
        if kk == 0 and "x1" in g.dumps:
            wait("sync", g.s_v)
            dma("sync", g.dumps["x1"], vf(Dx_, TILE_COLS), g.s_st)
            wait("vector", g.s_st, g.s_st.n)
            wait("scalar", g.s_st, g.s_st.n)
        if pre and "x1p" in g.dumps:
            wait("sync", g.s_v)
            dma("sync", g.dumps["x1p"], vf(Dx_, 128), g.s_st)
            dma("sync", g.dumps["wkvp"], vf(A_, 128), g.s_st)
            wait("vector", g.s_st, g.s_st.n)
            wait("scalar", g.s_st, g.s_st.n)
        if pre:
            def put_pre(c, o, w):
                return g.sXC[:, c * B + o: c * B + o + w]
            g.emit_ln(lambda c, o, w: g.ARF[:, Dx_ + c * tokc + o: Dx_ + c * tokc + o + w],
                      put_pre, [(0, B)],
                      lambda c: g.vcol("g1", c), lambda c: g.vcol("b1", c))
            run("vector", lambda e: e.tensor_scalar_mul(g.sXC[:], g.sXC[:], g.sGATE[:, 0:1]), g.s_v)
            em("vector", lambda e: e.drain())
            continue

        def put_xc(c, o, w):
            ap = g.ARF[:, A_ + c * (B * SEG): A_ + (c + 1) * (B * SEG)].rearrange(
                "p (b t) -> p b t", b=B)[:, :, 1:SEG]
            b0 = o // TP
            b1 = (o + w) // TP
            return ap[:, b0:b1, :]
        g.emit_ln(lambda c, o, w: g.ARF[:, Dx_ + c * tokc + o: Dx_ + c * tokc + o + w],
                  put_xc, [(0, 512)],
                  lambda c: g.vcol("g1", c), lambda c: g.vcol("b1", c))

        def xc4():
            return g.ARF[:, A_:A_ + HALO_COLS].rearrange("p (c b t) -> p c b t", c=CH, b=B)
        run("vector", lambda e, ap=xc4()[:, :, :, 0:1]: e.tensor_copy(
            ap, g.sXC[:].rearrange("p (c b o) -> p c b o", c=CH, b=B)), g.s_v)
        em("vector", lambda e: e.drain())
        run("vector", lambda e, ap=xc4()[:, :, :, SEG - 1:SEG]: e.tensor_copy(
            g.sXC[:].rearrange("p (c b o) -> p c b o", c=CH, b=B), ap), g.s_v)
        em("vector", lambda e: e.drain())
        if kk == 0 and "xc" in g.dumps:
            wait("sync", g.s_v)
            dma("sync", g.dumps["xc"], vf(A_, HALO_COLS), g.s_st)
            wait("vector", g.s_st, g.s_st.n)
            wait("scalar", g.s_st, g.s_st.n)
        run("vector", lambda e, a1=xc4()[:, :, :, 1:SEG], a0=xc4()[:, :, :, 0:SEG - 1]: e.tensor_tensor(
            vf(Bd_, TILE_COLS).rearrange("p (c b t) -> p c b t", c=CH, b=B),
            a1, a0, ALU.subtract), g.s_v)
        em("vector", lambda e: e.drain())
        wait("vector", g.s_t)
        for c in range(CH):
            run("vector", lambda e, cc=c: e.scalar_tensor_tensor(
                vb(RW_ + cc * tokc, tokc),
                g.ARF[:, Bd_ + cc * tokc: Bd_ + (cc + 1) * tokc], g.vcol("cmk", cc),
                g.ARF[:, A_ + cc * (B * SEG): A_ + (cc + 1) * (B * SEG)].rearrange(
                    "p (b t) -> p b t", b=B)[:, :, 0:SEG - 1],
                ALU.mult, ALU.add), g.s_v)
        for c in range(CH):
            run("vector", lambda e, cc=c: e.scalar_tensor_tensor(
                vb(MXC_ + cc * tokc, tokc),
                g.ARF[:, Bd_ + cc * tokc: Bd_ + (cc + 1) * tokc], g.vcol("cmr", cc),
                g.ARF[:, A_ + cc * (B * SEG): A_ + (cc + 1) * (B * SEG)].rearrange(
                    "p (b t) -> p b t", b=B)[:, :, 0:SEG - 1],
                ALU.mult, ALU.add), g.s_v)
        mixc = g.s_v.n
        wait("tensor", g.s_v, mixc)
        first_mm = [True] * CH
        for gi in range(NG):
            if fg_ready[gi] is None:
                wait("sync", g.s_t)
                fg_ready[gi] = fg_load(gi, gi % 2)
            if gi + 1 < NG and fg_ready[gi + 1] is None:
                wait("sync", g.s_t)
                fg_ready[gi + 1] = fg_load(gi + 1, (gi + 1) % 2)
            wait("tensor", g.s_ld, fg_ready[gi])
            wbase = WG_ + (gi % 2) * 8192
            prev_wcv_t = g.s_t.n
            for fi in range(FG):
                hbank = 6 + fi % 2
                wait("tensor", g.s_v)
                for ci in range(CH):
                    run("tensor", lambda e, b_=hbank, cc=ci, ff=fi, wb=wbase: e.matmul(
                        g.PS[b_][:, 0:512],
                        vb(wb + cc * (128 * FG) + ff * 128, 128),
                        vb(RW_ + cc * 512, 512),
                        start=(cc == 0), stop=(cc == CH - 1)),
                        g.s_t if ci == CH - 1 else None)
                wait("vector", g.s_t)
                run("vector", lambda e, b_=hbank, ff=fi: e.tensor_relu(
                    g.sSC[:, 544 * (ff % 2):544 * (ff % 2) + 512],
                    g.PS[b_][:, 0:512]), g.s_v)
                wait("vector", g.s_t, prev_wcv_t)
                run("vector", lambda e, ff=fi: e.tensor_tensor(
                    vb(HK_ + ff * 512, 512),
                    g.sSC[:, 544 * (ff % 2):544 * (ff % 2) + 512],
                    g.sSC[:, 544 * (ff % 2):544 * (ff % 2) + 512], ALU.mult), g.s_v)
            hkv = g.s_v.n
            if kk == 0 and gi == 0 and "hk0" in g.dumps:
                run("vector", lambda e: e.tensor_copy(g.sSC[:, 0:512], vb(HK_, 512)), g.s_v)
                run("vector", lambda e: e.tensor_copy(g.sSC[:, 544:1056], vb(HK_ + 512, 512)), g.s_v)
                em("vector", lambda e: e.drain())
                wait("sync", g.s_v)
                dma("sync", g.dumps["hk0"][:, 0:512], g.sSC[:, 0:512], g.s_st)
                dma("sync", g.dumps["hk0"][:, 512:1024], g.sSC[:, 544:1056], g.s_st)
                wait("vector", g.s_st, g.s_st.n)
                run("vector", lambda e: e.tensor_copy(g.sSC[:, 0:512], vb(HK_ + 1024, 512)), g.s_v)
                run("vector", lambda e: e.tensor_copy(g.sSC[:, 544:1056], vb(HK_ + 1536, 512)), g.s_v)
                em("vector", lambda e: e.drain())
                wait("sync", g.s_v)
                wait("sync", g.s_st, g.s_st.n)
                dma("sync", g.dumps["hk0"][:, 1024:1536], g.sSC[:, 0:512], g.s_st)
                dma("sync", g.dumps["hk0"][:, 1536:2048], g.sSC[:, 544:1056], g.s_st)
                wait("vector", g.s_st, g.s_st.n)
                wait("scalar", g.s_st, g.s_st.n)
            wait("tensor", g.s_v, hkv)
            for co in range(CH):
                if co < 6:
                    last = (gi == NG - 1)
                    for fi in range(FG):
                        run("tensor", lambda e, cc=co, ff=fi, wb=wbase,
                                st=first_mm[co] and fi == 0,
                                sp=(last and fi == FG - 1): e.matmul(
                            g.PS[cc][:, 0:512],
                            vb(wb + 4096 + ff * C + cc * 128, 128),
                            vb(HK_ + ff * 512, 512),
                            start=st, stop=sp),
                            g.s_t if fi == FG - 1 else None)
                    first_mm[co] = False
                else:
                    bank = 6 + (co - 6)
                    wait("tensor", g.s_v)
                    for fi in range(FG):
                        run("tensor", lambda e, b_=bank, cc=co, ff=fi, wb=wbase: e.matmul(
                            g.PS[b_][:, 0:512],
                            vb(wb + 4096 + ff * C + cc * 128, 128),
                            vb(HK_ + ff * 512, 512),
                            start=(ff == 0), stop=(ff == FG - 1)),
                            g.s_t if fi == FG - 1 else None)
                    wait("vector", g.s_t)
                    if gi == 0:
                        run("vector", lambda e, b_=bank, ap=csl(Ekv_, co): e.tensor_copy(
                            ap, g.PS[b_][:, 0:512]), g.s_v)
                    else:
                        run("vector", lambda e, b_=bank, ap=csl(Ekv_, co): e.tensor_tensor(
                            ap, ap, g.PS[b_][:, 0:512], ALU.add), g.s_v)
        fg_ready = [None] * NG
        if kk < NT - 1:
            fg_ready[0] = fg_load(0, 0)
            fg_ready[1] = fg_load(1, 1)
        tall = g.s_t.n
        wait("vector", g.s_t, tall)
        for co in range(6):
            run("vector", lambda e, cc=co, ap=csl(Ekv_, co): e.tensor_copy(ap, g.PS[cc][:, 0:512]), g.s_v)
        kvev = g.s_v.n
        wait("sync", g.s_v, kvev)
        dma("sync", vb(WOC_, CH * C), g.wcrb_d, g.s_ld)
        wcrl = g.s_ld.n
        wait("tensor", g.s_ld, wcrl)
        wait("tensor", g.s_v, kvev)
        for co in range(CH):
            bank = co % 2
            wait("tensor", g.s_s)
            for ci in range(CH):
                run("tensor", lambda e, b_=bank, cc=ci, cco=co: e.matmul(
                    g.PS[b_][:, 0:512],
                    vb(WOC_ + cc * C + cco * 128, 128),
                    vb(MXC_ + cc * 512, 512),
                    start=(cc == 0), stop=(cc == CH - 1)),
                    g.s_t if ci == CH - 1 else None)
            wait("scalar", g.s_t)
            run("scalar", lambda e, b_=bank, ap=csl(A_, co): e.activation(
                ap, g.PS[b_][:, 0:512], AF.Exp, scale=-1.0), g.s_s)
        wait("vector", g.s_s)
        run("vector", lambda e: e.tensor_scalar(vf(A_, TILE_COLS), vf(A_, TILE_COLS), 1.0, None, ALU.add), g.s_v)
        em("vector", lambda e: e.drain())
        run("vector", lambda e: e.reciprocal(vf(Bd_, TILE_COLS), vf(A_, TILE_COLS)), g.s_v)
        em("vector", lambda e: e.drain())
        if kk == 0 and "kv" in g.dumps:
            wait("sync", g.s_v)
            dma("sync", g.dumps["kv"], vf(Ekv_, TILE_COLS), g.s_st)
            dma("sync", g.dumps["rr"], vf(Bd_, TILE_COLS), g.s_st)
            wait("vector", g.s_st, g.s_st.n)
            wait("scalar", g.s_st, g.s_st.n)
        xre = g.xre
        run("vector", lambda e: e.tensor_tensor(vf(Ekv_, TILE_COLS), vf(Ekv_, TILE_COLS), vf(Bd_, TILE_COLS), ALU.mult), g.s_v)
        run("vector", lambda e: e.tensor_tensor(vf(Ekv_, TILE_COLS), vf(Ekv_, TILE_COLS), vf(Dx_, TILE_COLS), ALU.add), g.s_v)
        wait("vector", g.s_ld, xre)
        run("vector", lambda e: e.tensor_tensor(vf(Ekv_, TILE_COLS), vf(Ekv_, TILE_COLS), vf(Cr_, TILE_COLS), ALU.add), g.s_v)
        em("vector", lambda e: e.drain())

        if kk == 0 and "yp" in g.dumps:
            wait("sync", g.s_v)
            dma("sync", g.dumps["yp"], vf(Ekv_, TILE_COLS), g.s_st)
            wait("vector", g.s_st, g.s_st.n)
            wait("scalar", g.s_st, g.s_st.n)
        def src_y(c, o, w):
            return g.ARF[:, Ekv_ + c * tokc + o: Ekv_ + c * tokc + o + w]

        def put_y1(c, o, w):
            b0 = o // TP
            b1 = (o + w) // TP
            return g.ARF[:, Bd_ + c * tokc: Bd_ + (c + 1) * tokc].rearrange(
                "p (t b) -> p t b", t=TP)[:, :, b0:b1].transpose([0, 2, 1])
        for (o, w) in [(0, 512)]:
            wait("tensor", g.s_v)
            wait("tensor", g.s_s)
            wait("scalar", g.s_v)
            for c in range(CH):
                run("tensor", lambda e, cc=c, ap=src_y(c, o, w), ww=w: e.matmul(
                    g.PS[4][0:1, 0:ww], g.sONE1c[:], ap,
                    start=(cc == 0), stop=(cc == CH - 1)), g.s_t)
            for c in range(CH):
                if c >= 2:
                    wait("scalar", g.s_t)
                run("scalar", lambda e, cc=c, ap=src_y(c, o, w), ww=w: e.activation(
                    g.sSC[:, (cc % 2) * 544:(cc % 2) * 544 + ww], ap, AF.Square), g.s_s)
                wait("tensor", g.s_s)
                run("tensor", lambda e, cc=c, ww=w: e.matmul(
                    g.PS[5][0:1, 0:ww], g.sONE1c[:],
                    g.sSC[:, (cc % 2) * 544:(cc % 2) * 544 + ww],
                    start=(cc == 0), stop=(cc == CH - 1)), g.s_t)
            tmm = g.s_t.n
            wait("vector", g.s_t, tmm)
            run("vector", lambda e, ww=w: e.tensor_copy(g.sROW[0:1, 0:ww], g.PS[4][0:1, 0:ww]), g.s_v)
            em("vector", lambda e: e.drain())
            run("vector", lambda e, ww=w: e.tensor_tensor(
                g.sROW[0:1, 1024:1024 + ww], g.sROW[0:1, 0:ww], g.sROW[0:1, 0:ww], ALU.mult), g.s_v)
            em("vector", lambda e: e.drain())
            run("vector", lambda e, ww=w: e.tensor_tensor(
                g.sROW[0:1, 512:512 + ww], g.PS[5][0:1, 0:ww], g.sROW[0:1, 1024:1024 + ww], ALU.subtract), g.s_v)
            vrow = g.s_v.n
            wait("scalar", g.s_v, vrow)
            run("scalar", lambda e, ww=w: e.activation(
                g.sROW[0:1, 512:512 + ww], g.sROW[0:1, 512:512 + ww], AF.Ln, bias=g.sDRV[0:1, 96:97]), g.s_s)
            em("scalar", lambda e: e.drain())
            run("scalar", lambda e, ww=w: e.activation(
                g.sROW[0:1, 512:512 + ww], g.sROW[0:1, 512:512 + ww], AF.Exp, scale=-0.5), g.s_s)
            srow = g.s_s.n
            wait("tensor", g.s_s, srow)
            run("tensor", lambda e, ww=w: e.matmul(
                g.PS[6][:, 0:ww], g.sONES1[:], g.sROW[0:1, 0:ww], start=True, stop=True), g.s_t)
            run("tensor", lambda e, ww=w: e.matmul(
                g.PS[7][:, 0:ww], g.sONES1[:], g.sROW[0:1, 512:512 + ww], start=True, stop=True), g.s_t)
            tb = g.s_t.n
            wait("vector", g.s_t, tb)
            for c in range(CH):
                run("vector", lambda e, ap=src_y(c, o, w), ww=w: e.tensor_tensor(
                    ap, ap, g.PS[6][:, 0:ww], ALU.subtract), g.s_v)
            for c in range(CH):
                run("vector", lambda e, po=put_y1(c, o, w), ap=src_y(c, o, w), ww=w: e.tensor_tensor(
                    po, ap, g.PS[7][:, 0:ww], ALU.mult), g.s_v)
        y1v = g.s_v.n
        wait("sync", g.s_v, y1v)
        dma("sync", g.y1_d[kk * 128:(kk + 1) * 128, :], vf(Bd_, TILE_COLS), g.s_st)
    if "y1" in g.dumps:
        wait("sync", g.s_st, g.s_st.n)
        dma("sync", g.dumps["y1"], g.y1_d, g.s_st)


def _s3(g):
    em, wait, dma, run, vf, vb = g.em, g.wait, g.dma, g.run, g.vf, g.vb
    Y_, Hh_ = 0, TILE_COLS
    QKV_, YB_, CTX_ = 0, 12288, 16384
    WIP_ = 20480
    CTX_ = QKV_
    WIP_ = 16384

    def wip_load(half):
        wait("scalar", g.s_t)
        dma("scalar", vb(WIP_, 12288),
            g.wipb_d[half * 128:(half + 1) * 128, :], g.s_ld)
        return g.s_ld.n

    wait("scalar", g.s_gc, g.prep_s3)
    s3w = wip_load(0)
    dma("sync", g.sGATH[:, 0:64], g.wrt_t, g.s_ld)
    rtw = g.s_ld.n

    for kk in range(NT):
        wait("sync", g.s_v)
        wait("sync", g.s_t)
        wait("sync", g.s_s)
        wait("sync", g.s_st, g.s_st.n)
        dma("sync", vf(Y_, TILE_COLS), g.y1_d[kk * 128:(kk + 1) * 128, :], g.s_ld)
        ldt = g.s_ld.n
        wait("vector", g.s_ld, ldt)
        run("vector", lambda e: e.tensor_copy(vb(YB_, TILE_COLS), vf(Y_, TILE_COLS)), g.s_v)
        em("vector", lambda e: e.drain())
        ybc = g.s_v.n
        wait("tensor", g.s_v, ybc)
        wait("tensor", g.s_ld, s3w)
        for oc in range(24):
            if oc == 12:
                s3w = wip_load(1)
                wait("tensor", g.s_ld, s3w)
            bank = oc % 2
            wait("tensor", g.s_s)
            for ci in range(CH):
                run("tensor", lambda e, b_=bank, cc=ci, oo=oc % 12: e.matmul(
                    g.PS[b_][:, 0:512],
                    vb(WIP_ + cc * 1536 + oo * 128, 128),
                    vb(YB_ + cc * 512, 512),
                    start=(cc == 0), stop=(cc == CH - 1)),
                    g.s_t if ci == CH - 1 else None)
            wait("scalar", g.s_t)
            if oc < 8:
                run("scalar", lambda e, b_=bank, oo=oc: e.activation(
                    vb(QKV_ + oo * 512, 512), g.PS[b_][:, 0:512], AF.Identity,
                    bias=g.sDRV[:, 24 + oo:25 + oo], scale=0.125), g.s_s)
            else:
                run("scalar", lambda e, b_=bank, oo=oc: e.activation(
                    vb(QKV_ + oo * 512, 512), g.PS[b_][:, 0:512], AF.Identity,
                    bias=g.sVEC[:, 128 + oo:129 + oo]), g.s_s)
        qkvs = g.s_s.n
        inproj_t = g.s_t.n
        wait("tensor", g.s_s, qkvs)
        snaps = [dict(t=0, v=0, s=0), dict(t=0, v=0, s=0)]
        SBK = [2, 0]
        TBK = [3, 1]
        CBK = [4, 6]
        AS_ = [YB_ + 2048, YB_ + 2816]
        XR = [1056, 1072]
        gidx = 0
        for pb in range(4):
            colo = pb * 128
            for j in range(CH):
                p = gidx % 2
                sb, tb, cb = SBK[p], TBK[p], CBK[p]
                a0 = AS_[p]
                xr = XR[p]
                qof = QKV_ + j * 512 + colo
                kof = QKV_ + (8 + j) * 512 + colo
                vof = QKV_ + (16 + j) * 512 + colo
                sn = snaps[p]
                wait("tensor", g.s_v, sn["v"])
                wait("tensor", g.s_s, sn["s"])
                for hh in range(2):
                    run("tensor", lambda e, h_=hh, b_=sb: e.matmul(
                        g.PS[b_][:, h_ * 128:h_ * 128 + 128], g.sMA[:], g.sMB[:],
                        start=True, stop=False), None)
                    run("tensor", lambda e, h_=hh, b_=sb, q_=qof, k_=kof: e.matmul(
                        g.PS[b_][:, h_ * 128:h_ * 128 + 128],
                        g.ARB[64 * h_:64 * h_ + 64, q_:q_ + 128],
                        g.ARB[64 * h_:64 * h_ + 64, k_:k_ + 128],
                        start=False, stop=True), g.s_t)
                sco = g.s_t.n
                run("tensor", lambda e, v_=vof, b_=tb: e.transpose(
                    g.PS[b_][:, 128:192].bitcast(BF16), g.ARB[:, v_:v_ + 128],
                    g.sIDB[:]), g.s_t)
                vt = g.s_t.n
                wait("vector", g.s_t, sco)
                run("vector", lambda e, b_=sb, x_=xr: e.reduce_max(
                    g.sSC[:, x_:x_ + 2].rearrange("p (s o) -> p s o", s=2),
                    g.PS[b_][:, 0:256].rearrange("p (s n) -> p s n", s=2),
                    axis=AX, negate=True), g.s_v)
                em("vector", lambda e: e.drain())
                smax = g.s_v.n
                wait("scalar", g.s_v, smax)
                for hh in range(2):
                    run("scalar", lambda e, h_=hh, b_=sb, a_=a0, x_=xr: e.activation(
                        g.ARB[:, a_ + h_ * 128: a_ + h_ * 128 + 128],
                        g.PS[b_][:, h_ * 128:h_ * 128 + 128],
                        AF.Exp, bias=g.sSC[:, x_ + h_:x_ + h_ + 1]), g.s_s)
                sexp = g.s_s.n
                wait("vector", g.s_s, sexp)
                run("vector", lambda e, a_=a0, x_=xr: e.tensor_reduce(
                    g.sSC[:, x_ + 2:x_ + 4].rearrange("p (s o) -> p s o", s=2),
                    g.ARB[:, a_:a_ + 256].rearrange("p (s n) -> p s n", s=2),
                    axis=AX, op=ALU.add), g.s_v)
                em("vector", lambda e: e.drain())
                run("vector", lambda e, x_=xr: e.reciprocal(
                    g.sSC[:, x_ + 4:x_ + 6], g.sSC[:, x_ + 2:x_ + 4]), g.s_v)
                em("vector", lambda e: e.drain())
                for hh in range(2):
                    run("vector", lambda e, h_=hh, a_=a0, x_=xr: e.tensor_scalar_mul(
                        g.ARB[:, a_ + h_ * 128: a_ + h_ * 128 + 128],
                        g.ARB[:, a_ + h_ * 128: a_ + h_ * 128 + 128],
                        g.sSC[:, x_ + 4 + h_:x_ + 5 + h_]), g.s_v)
                atn = g.s_v.n
                wait("tensor", g.s_v, atn)
                for hh in range(2):
                    run("tensor", lambda e, h_=hh, a_=a0, b_=tb: e.transpose(
                        g.PS[b_][:, h_ * 64:h_ * 64 + 64].bitcast(BF16),
                        g.ARB[:, a_ + h_ * 128: a_ + h_ * 128 + 128],
                        g.sIDB[:]), g.s_t)
                att = g.s_t.n
                wait("vector", g.s_t, att)
                run("vector", lambda e, a_=a0, b_=tb: e.tensor_copy(
                    g.ARB[:, a_ + 256: a_ + 512], g.PS[b_][:, 0:128].bitcast(BF16)), g.s_v)
                run("vector", lambda e, a_=a0, b_=tb: e.tensor_copy(
                    g.ARB[:, a_ + 512: a_ + 640], g.PS[b_][:, 128:192].bitcast(BF16)), g.s_v)
                em("vector", lambda e: e.drain())
                tev = g.s_v.n
                wait("tensor", g.s_v, tev)
                run("tensor", lambda e, a_=a0, b_=cb: e.matmul(
                    g.PS[b_][:, 0:256], g.ARB[:, a_ + 512: a_ + 640],
                    g.ARB[:, a_ + 256: a_ + 512], start=True, stop=True), g.s_t)
                ctm = g.s_t.n
                wait("vector", g.s_t, ctm)
                run("vector", lambda e, j_=j, c_=colo, b_=cb: e.tensor_copy(
                    g.ARB[0:64, CTX_ + j_ * 512 + c_: CTX_ + j_ * 512 + c_ + 128],
                    g.PS[b_][0:64, 0:128]), g.s_v)
                run("vector", lambda e, j_=j, c_=colo, b_=cb: e.tensor_copy(
                    g.ARB[64:128, CTX_ + j_ * 512 + c_: CTX_ + j_ * 512 + c_ + 128],
                    g.PS[b_][64:128, 128:256]), g.s_v)
                snaps[p] = dict(t=g.s_t.n, v=g.s_v.n, s=g.s_s.n)
                gidx += 1
        ctxv = g.s_v.n
        wait("scalar", g.s_t, inproj_t)
        dma("scalar", vb(WIP_, CH * C), g.wowb_d, g.s_ld)
        oww = g.s_ld.n
        wait("tensor", g.s_ld, oww)
        wait("tensor", g.s_v, ctxv)
        for co in range(CH):
            bank = co % 2
            wait("tensor", g.s_v)
            for ci in range(CH):
                run("tensor", lambda e, b_=bank, cc=ci, cco=co: e.matmul(
                    g.PS[b_][:, 0:512],
                    vb(WIP_ + cc * C + cco * 128, 128),
                    vb(CTX_ + cc * 512, 512),
                    start=(cc == 0), stop=(cc == CH - 1)),
                    g.s_t if ci == CH - 1 else None)
            wait("vector", g.s_t)
            run("vector", lambda e, b_=bank, cco=co: e.tensor_tensor(
                vf(Hh_ + cco * 512, 512), vf(Y_ + cco * 512, 512),
                g.PS[b_][:, 0:512], ALU.add), g.s_v)
            run("vector", lambda e, cco=co: e.tensor_scalar(
                vf(Hh_ + cco * 512, 512), vf(Hh_ + cco * 512, 512),
                g.vcol("ob", cco), None, ALU.add), g.s_v)
        def src_h(c, o, w):
            return g.ARF[:, Hh_ + c * 512 + o: Hh_ + c * 512 + o + w]
        g.emit_ln(src_h, src_h, [(0, 512)],
                  lambda c: g.vcol("fg1", c), lambda c: g.vcol("fb1", c))
        hv = g.s_v.n
        wait("sync", g.s_v, hv)
        dma("sync", g.h_d[kk * 128:(kk + 1) * 128, :], vf(Hh_, TILE_COLS), g.s_st)
        wait("tensor", g.s_v, hv)
        wait("tensor", g.s_ld, rtw)
        for ci in range(CH):
            run("tensor", lambda e, cc=ci: e.matmul(
                g.PS[6][0:8, 0:512], g.sGATH[:, cc * 8:(cc + 1) * 8],
                vf(Hh_ + cc * 512, 512), start=(cc == 0), stop=(cc == CH - 1)),
                g.s_t if ci == CH - 1 else None)
        lg = g.s_t.n
        wait("vector", g.s_t, lg)
        run("vector", lambda e: e.tensor_scalar(
            g.sSC[0:8, 0:512], g.PS[6][0:8, 0:512], g.sC8[:, 0:1], None, ALU.add), g.s_v)
        em("vector", lambda e: e.drain())
        lgv = g.s_v.n
        wait("tensor", g.s_v, lgv)
        for bq in range(4):
            run("tensor", lambda e, b_=bq: e.transpose(
                g.PS[7][:, b_ * 8:b_ * 8 + 8], g.sSC[0:8, b_ * 128:(b_ + 1) * 128],
                g.sIDF[0:8, 0:8]), g.s_t)
        lgt = g.s_t.n
        wait("vector", g.s_t, lgt)
        run("vector", lambda e: e.tensor_copy(g.sSC[:, 544:544 + 32], g.PS[7][:, 0:32]), g.s_v)
        em("vector", lambda e: e.drain())
        for bq in range(4):
            lgq = g.sSC[:, 544 + bq * 8:544 + bq * 8 + 8]
            m1 = g.sSC[:, 1080:1081]
            msk = g.sSC[:, 0:8]
            xp = g.sSC[:, 16:24]
            m2 = g.sSC[:, 1081:1082]
            i1 = g.sSC[:, 1082:1083]
            run("vector", lambda e, a=lgq, b_=m1: e.reduce_max(b_, a, axis=AX), g.s_v)
            em("vector", lambda e: e.drain())
            run("vector", lambda e, a=lgq, b_=m1, c_=msk: e.tensor_tensor(
                c_, a, b_.to_broadcast([128, 8]), ALU.is_equal), g.s_v)
            em("vector", lambda e: e.drain())
            run("vector", lambda e, a=lgq, c_=msk, d_=xp: e.scalar_tensor_tensor(
                d_, c_, -1e30, a, ALU.mult, ALU.add), g.s_v)
            em("vector", lambda e: e.drain())
            run("vector", lambda e, d_=xp, b_=m2: e.reduce_max(b_, d_, axis=AX), g.s_v)
            em("vector", lambda e: e.drain())
            run("vector", lambda e, a=lgq, b_=m2, c_=msk: e.tensor_tensor(
                c_, b_.to_broadcast([128, 8]), a, ALU.is_le), g.s_v)
            em("vector", lambda e: e.drain())
            run("vector", lambda e, c_=msk: e.tensor_tensor(c_, c_, g.sIOTA8[:], ALU.mult), g.s_v)
            em("vector", lambda e: e.drain())
            run("vector", lambda e, c_=msk, i_=i1: e.reduce_max(i_, c_, axis=AX), g.s_v)
            em("vector", lambda e: e.drain())
            asn = g.s_v.n
            wait("tensor", g.s_v, asn)
            run("tensor", lambda e, i_=i1: e.transpose(
                g.PS[7][0:1, 128:256], i_, g.sIDF[:]), g.s_t)
            att2 = g.s_t.n
            wait("vector", g.s_t, att2)
            run("vector", lambda e: e.tensor_copy(
                g.sSC[0:1, 544:672], g.PS[7][0:1, 128:256]), g.s_v)
            em("vector", lambda e: e.drain())
            wait("sync", g.s_v)
            dma("sync", g.asg_d[0:1, kk * 512 + bq * 128: kk * 512 + (bq + 1) * 128],
                g.sSC[0:1, 544:672], g.s_st)
        if kk < NT - 1:
            s3w = wip_load(0)
    if "h" in g.dumps:
        wait("sync", g.s_st, g.s_st.n)
        dma("sync", g.dumps["h"], g.h_d, g.s_st)
    if "assign" in g.dumps:
        wait("sync", g.s_st, g.s_st.n)
        dma("sync", g.dumps["assign"], g.asg_d, g.s_st)


def _s4(g):
    em, wait, dma, run, vf, vb = g.em, g.wait, g.dma, g.run, g.vf, g.vb
    H_, Yt_, M67_ = 0, TILE_COLS, 2 * TILE_COLS
    HBV = g.ARF[:, 10240:12288].bitcast(BF16)
    HKe_ = 0
    W1_ = [4096, 12288]
    W2_ = [20480, 28672]

    def w_load(e_, par):
        wait("sync", g.s_t)
        dma("sync", vb(W1_[par], CH * C), g.w1b_d[e_ * 128:(e_ + 1) * 128, :], g.s_ld)
        wait("scalar", g.s_t)
        dma("scalar", vb(W2_[par], CH * C), g.w2b_d[e_ * 128:(e_ + 1) * 128, :], g.s_ld)
        return g.s_ld.n

    wait("sync", g.s_gc, g.prep_done)
    wait("scalar", g.s_gc, g.prep_done)
    wready = [None] * NE
    gelu_done = {6: 0, 7: 0}
    vadd_done = {6: 0, 7: 0}

    for kk in range(NT):
        wait("sync", g.s_v)
        wait("sync", g.s_t)
        wait("sync", g.s_s)
        wait("sync", g.s_st, g.s_st.n)
        dma("sync", vf(H_, TILE_COLS), g.h_d[kk * 128:(kk + 1) * 128, :], g.s_ld)
        wait("gpsimd", g.s_st, g.s_st.n)
        dma("gpsimd", g.sASG[:], g.asg_d[0:1, kk * 512:(kk + 1) * 512], g.s_gc)
        asgl = g.s_gc.n
        ldh = g.s_ld.n
        wait("vector", g.s_ld, ldh)
        run("vector", lambda e: e.tensor_copy(HBV, vf(H_, TILE_COLS)), g.s_v)
        em("vector", lambda e: e.drain())
        g.hbc = g.s_v.n
        wait("tensor", g.s_v)
        wait("tensor", g.s_gc, asgl)
        run("tensor", lambda e: e.matmul(
            g.PS[6][0:8, 0:512], g.sONES8[:],
            g.sASG[0:1, :], start=True, stop=True), g.s_t)
        mb = g.s_t.n
        wait("vector", g.s_t, mb)
        run("vector", lambda e: e.tensor_tensor(
            g.sM8[:], g.PS[6][0:8, 0:512], g.sEID[:], ALU.is_equal), g.s_v)
        em("vector", lambda e: e.drain())
        if kk == 0 and "m80" in g.dumps:
            run("vector", lambda e: e.tensor_copy(g.sSC[0:8, 0:512], g.sM8[:]), g.s_v)
            em("vector", lambda e: e.drain())
            wait("sync", g.s_v)
            dma("sync", g.dumps["m80"], g.sSC[0:8, 0:512], g.s_st)
            wait("vector", g.s_st, g.s_st.n)
            run("vector", lambda e: e.tensor_copy(g.sSC[0:8, 0:512], g.PS[6][0:8, 0:512]), g.s_v)
            em("vector", lambda e: e.drain())
            wait("sync", g.s_v)
            dma("sync", g.dumps["bc0"], g.sSC[0:8, 0:512], g.s_st)
            wait("vector", g.s_st, g.s_st.n)
            run("vector", lambda e: e.tensor_copy(g.sSC[0:1, 0:512], g.sASG[:]), g.s_v)
            em("vector", lambda e: e.drain())
            wait("sync", g.s_v)
            dma("sync", g.dumps["asg0"], g.sSC[0:1, 0:512], g.s_st)
            wait("vector", g.s_st, g.s_st.n)
        run("vector", lambda e: e.tensor_scalar(
            g.sNM8[:], g.sM8[:], -1.0, 1.0, ALU.mult, ALU.add), g.s_v)
        em("vector", lambda e: e.drain())
        m8v = g.s_v.n
        wait("tensor", g.s_v, m8v)
        for co in range(CH):
            if co < 6:
                run("tensor", lambda e, cc=co: e.matmul(
                    g.PS[cc][:, 0:512], g.sB2[:, cc * 128:(cc + 1) * 128],
                    g.sM8[:], start=True, stop=False), g.s_t)
            else:
                run("tensor", lambda e, cc=co: e.matmul(
                    g.PS[7][:, 0:512], g.sB2[:, cc * 128:(cc + 1) * 128],
                    g.sM8[:], start=True, stop=True), g.s_t)
                wait("vector", g.s_t)
                run("vector", lambda e, cc=co: e.tensor_copy(
                    vf(M67_ + (cc - 6) * 512, 512), g.PS[7][:, 0:512]), g.s_v)
                wait("tensor", g.s_v)
        vadd_done[6] = vadd_done[7] = g.s_v.n
        wait("tensor", g.s_v, g.hbc)
        for e_ in range(NE):
            if wready[e_] is None:
                wready[e_] = w_load(e_, e_ % 2)
            if e_ + 1 < NE and wready[e_ + 1] is None:
                wready[e_ + 1] = w_load(e_ + 1, (e_ + 1) % 2)
            wait("tensor", g.s_ld, wready[e_])
            w1b = W1_[e_ % 2]
            w2b = W2_[e_ % 2]
            gfi = []
            for fi in range(CH):
                hbank = 6 + fi % 2
                wait("tensor", g.s_s, gelu_done[hbank])
                wait("tensor", g.s_v, vadd_done[hbank])
                run("tensor", lambda e, b_=hbank, ee=e_: e.matmul(
                    g.PS[b_][:, 0:512], g.sNEG9[:, ee * 128:(ee + 1) * 128],
                    g.sNM8[:], start=True, stop=False), None)
                for ci in range(CH):
                    run("tensor", lambda e, b_=hbank, cc=ci, ff=fi, wb=w1b: e.matmul(
                        g.PS[b_][:, 0:512],
                        vb(wb + cc * C + ff * 128, 128),
                        HBV[:, cc * 512:(cc + 1) * 512],
                        start=False, stop=(cc == CH - 1)),
                        g.s_t if ci == CH - 1 else None)
                wait("scalar", g.s_t)
                run("scalar", lambda e, b_=hbank, ee=e_, ff=fi: e.activation(
                    vb(HKe_ + ff * 512, 512), g.PS[b_][:, 0:512], AF.Gelu,
                    bias=g.sVEC[:, 152 + ee * 8 + ff:153 + ee * 8 + ff]), g.s_s)
                gelu_done[hbank] = g.s_s.n
                gfi.append(g.s_s.n)
            hkv = g.s_s.n
            if kk == 0 and e_ == 0 and "he0" in g.dumps:
                wait("vector", g.s_s, hkv)
                run("vector", lambda e: e.tensor_copy(vf(Yt_, 4096), vb(HKe_, 4096)), g.s_v)
                em("vector", lambda e: e.drain())
                wait("sync", g.s_v)
                dma("sync", g.dumps["he0"], vf(Yt_, 4096), g.s_st)
                wait("vector", g.s_st, g.s_st.n)
                wait("scalar", g.s_st, g.s_st.n)
                wait("tensor", g.s_st, g.s_st.n)
            for fi in range(CH):
                wait("tensor", g.s_s, gfi[fi])
                for co in range(6):
                    run("tensor", lambda e, cc=co, ff=fi, wb=w2b,
                            sp=(e_ == NE - 1 and fi == CH - 1): e.matmul(
                        g.PS[cc][:, 0:512],
                        vb(wb + ff * C + cc * 128, 128),
                        vb(HKe_ + ff * 512, 512),
                        start=False, stop=sp),
                        g.s_t if co == 5 else None)
            for co in (6, 7):
                bank = co
                wait("tensor", g.s_v)
                for fi in range(CH):
                    run("tensor", lambda e, b_=bank, cc=co, ff=fi, wb=w2b: e.matmul(
                        g.PS[b_][:, 0:512],
                        vb(wb + ff * C + cc * 128, 128),
                        vb(HKe_ + ff * 512, 512),
                        start=(ff == 0), stop=(ff == CH - 1)),
                        g.s_t if fi == CH - 1 else None)
                wait("vector", g.s_t)
                run("vector", lambda e, cc=co: e.tensor_tensor(
                    vf(M67_ + (cc - 6) * 512, 512), vf(M67_ + (cc - 6) * 512, 512),
                    g.PS[6 + (cc - 6)][:, 0:512], ALU.add), g.s_v)
                vadd_done[bank] = g.s_v.n
        wready = [None] * NE
        tall = g.s_t.n
        wait("vector", g.s_t, tall)
        for co in range(CH):
            if co < 6:
                run("vector", lambda e, cc=co: e.tensor_tensor(
                    vf(H_ + cc * 512, 512), vf(H_ + cc * 512, 512),
                    g.PS[cc][:, 0:512], ALU.add), g.s_v)
            else:
                run("vector", lambda e, cc=co: e.tensor_tensor(
                    vf(H_ + cc * 512, 512), vf(H_ + cc * 512, 512),
                    vf(M67_ + (cc - 6) * 512, 512), ALU.add), g.s_v)
        em("vector", lambda e: e.drain())

        if kk == 0 and "hm0" in g.dumps:
            wait("sync", g.s_v)
            dma("sync", g.dumps["hm0"], vf(H_, TILE_COLS), g.s_st)
            wait("vector", g.s_st, g.s_st.n)
            wait("scalar", g.s_st, g.s_st.n)
            wait("tensor", g.s_st, g.s_st.n)
        def src_hm(c, o, w):
            return g.ARF[:, H_ + c * 512 + o: H_ + c * 512 + o + w]
        g.emit_ln(src_hm, src_hm, [(0, 512)],
                  lambda c: g.vcol("fg2", c), lambda c: g.vcol("fb2", c))
        dma("sync", vf(Yt_, TILE_COLS), g.y1_d[kk * 128:(kk + 1) * 128, :], g.s_ld)
        ldy = g.s_ld.n
        wait("vector", g.s_ld, ldy)
        run("vector", lambda e: e.tensor_tensor(
            vf(H_, TILE_COLS), vf(H_, TILE_COLS), vf(Yt_, TILE_COLS), ALU.add), g.s_v)
        em("vector", lambda e: e.drain())
        g.emit_ln(src_hm, src_hm, [(0, 512)], None, None)
        ov = g.s_v.n
        wait("sync", g.s_v, ov)
        dma("sync", g.out_d[kk * 128:(kk + 1) * 128, :], vf(H_, TILE_COLS), g.s_st)


def _pack_weight(w):
    kin, nout = w.shape
    return np.ascontiguousarray(
        w.reshape(kin // 128, 128, nout).transpose(1, 0, 2).reshape(128, -1))


def _pack_vec(v):
    return np.ascontiguousarray(np.asarray(v, np.float32).reshape(CH, 128).T)


def _stack_groups_ck(wp):
    blocks = []
    for gi in range(8):
        blocks.append(np.concatenate(
            [wp[:, cc * FF + 512 * gi: cc * FF + 512 * gi + 512] for cc in range(CH)], axis=1))
    return np.ascontiguousarray(np.concatenate(blocks, axis=0))


def _stack_halves_ip(wp):
    blocks = []
    for h in range(2):
        blocks.append(np.concatenate(
            [wp[:, ci * 3 * C + h * 1536: ci * 3 * C + (h + 1) * 1536] for ci in range(CH)], axis=1))
    return np.ascontiguousarray(np.concatenate(blocks, axis=0))


def _host_inputs(inp):
    x = np.asarray(inp["L0_attn_x"], np.float32)
    x_pad = np.concatenate([np.zeros((B, 2, C), np.float32), x], axis=1)
    vec = np.zeros((128, 256), np.float32)
    order = ["L0_attn_ln0_g", "L0_attn_ln0_b", "L0_attn_ln1_g", "L0_attn_ln1_b",
             "L0_attn_tm_k", "L0_attn_tm_v", "L0_attn_tm_r", "L0_attn_cm_k",
             "L0_attn_cm_r", "L0_attn_time_decay", "L0_attn_time_first",
             "L0_ffn_out_b", "L0_ffn_ln1_g", "L0_ffn_ln1_b", "L0_ffn_ln2_g",
             "L0_ffn_ln2_b"]
    for idx, key in enumerate(order):
        vec[:, idx * 8:(idx + 1) * 8] = _pack_vec(inp[key])
    vec[:, 128:152] = np.ascontiguousarray(
        np.asarray(inp["L0_ffn_in_proj_b"], np.float32).reshape(24, 128).T)
    vec[:, 152:216] = np.ascontiguousarray(
        np.asarray(inp["L0_ffn_b1"], np.float32).reshape(NE * CH, 128).T)
    c8 = np.zeros((8, 64), np.float32)
    c8[:, 0] = np.asarray(inp["L0_ffn_router_b"], np.float32)
    common = dict(
        wk_t=_pack_weight(np.asarray(inp["L0_attn_Wk"], np.float32)),
        wv_t=_pack_weight(np.asarray(inp["L0_attn_Wv"], np.float32)),
        wr_t=_pack_weight(np.asarray(inp["L0_attn_Wr"], np.float32)),
        wo_t=_pack_weight(np.asarray(inp["L0_attn_Wo"], np.float32)),
        wck_t=_stack_groups_ck(_pack_weight(np.asarray(inp["L0_attn_Wck"], np.float32))),
        wcv_t=np.concatenate([_pack_weight(np.asarray(inp["L0_attn_Wcv"], np.float32))[:, gi * 4096:(gi + 1) * 4096] for gi in range(8)], axis=0),
        wcr_t=_pack_weight(np.asarray(inp["L0_attn_Wcr"], np.float32)),
        wip_t=_stack_halves_ip(_pack_weight(np.ascontiguousarray(np.asarray(inp["L0_ffn_in_proj_w"], np.float32).T))),
        wow_t=_pack_weight(np.ascontiguousarray(np.asarray(inp["L0_ffn_out_w"], np.float32).T)),
        wrt_t=_pack_weight(np.ascontiguousarray(np.asarray(inp["L0_ffn_router_w"], np.float32).T)),
        w1_t=np.concatenate([_pack_weight(np.ascontiguousarray(np.asarray(inp["L0_ffn_w1"][e], np.float32).T)) for e in range(NE)], axis=0),
        w2_t=np.concatenate([_pack_weight(np.ascontiguousarray(np.asarray(inp["L0_ffn_w2"][e], np.float32).T)) for e in range(NE)], axis=0),
        vec32=vec, c8f=c8,
        b2f=np.asarray(inp["L0_ffn_b2"], np.float32),
        eidb=np.ascontiguousarray(np.broadcast_to(
            np.arange(8, dtype=np.float32)[:, None], (8, 512))).astype(NPBF),
        ones1f=np.ones((1, 128), np.float32),
        ones8b=np.ones((1, 8), NPBF),
        neg9b=np.asarray([[(-1e9 if k2 == (m // 128) else 0.0) for m in range(1024)]
                          for k2 in range(8)], NPBF),
        identb=np.eye(128, dtype=NPBF),
        identf=np.eye(128, dtype=np.float32),
        iotarep=np.ascontiguousarray(np.broadcast_to(np.arange(257, dtype=np.float32), (128, 257))),
        iota8=np.ascontiguousarray(np.broadcast_to(np.arange(8, dtype=np.float32), (128, 8))),
        maskA=np.asarray([[1.0 if m // 16 == gg else 0.0 for m in range(128)] for gg in range(8)], NPBF),
        maskB=np.asarray([[0.0 if n // 16 == gg else -1e30 for n in range(128)] for gg in range(8)], NPBF),
    )
    maps = []
    for i in range(NC):
        s = PPC * i
        slabs = [x_pad[:, s:s + 2, :]]
        for kk in range(NT):
            slabs.append(x_pad[:, s + 1 + TP * kk: s + 1 + TP * kk + SEG, :])
        x_core = np.zeros(((1 + NT) * 128, HALO_COLS), np.float32)
        for si, sl in enumerate(slabs):
            L = sl.shape[1]
            x_core[si * 128:(si + 1) * 128, 0:CH * B * L] = (
                sl.transpose(2, 0, 1).reshape(CH, 128, B * L).transpose(1, 0, 2).reshape(128, CH * B * L))
        gates = np.zeros((128, 2), np.float32)
        gates[:, 0] = 0.0 if i == 0 else 1.0
        gates[:, 1] = 1.0 - gates[:, 0]
        gp = np.zeros((128, 8), np.float32)
        gp[:, :i] = 1.0
        m = dict(common)
        m.update(x_t=x_core, gates=gates, gpre=gp, gpre256=np.ascontiguousarray(gp * 256.0))
        maps.append(m)
    return maps


def _assemble(results):
    out = np.zeros((B, TFULL, C), np.float32)
    for i in range(NC):
        arr = np.asarray(results[i]["out_d"], np.float32)
        a = arr.reshape(NT, 128, CH, TP, B)
        out[:, PPC * i:PPC * (i + 1), :] = a.transpose(4, 0, 3, 2, 1).reshape(B, PPC, C)
    return out


def kernel(**inputs):
    maps = _host_inputs(inputs)
    nc = build_program()
    res = run_bass_kernel_spmd(nc, maps, core_ids=list(range(NC)))
    return _assemble(res.results)


# version 10
# speedup vs baseline: 1.1049x; 1.0686x over previous
import numpy as np
import ml_dtypes
import contextlib
import concourse.bass as bass
import concourse.mybir as mybir
from concourse.bass_utils import run_bass_kernel_spmd

F32 = mybir.dt.float32
BF16 = mybir.dt.bfloat16
AF = mybir.ActivationFunctionType
ALU = mybir.AluOpType
AX = mybir.AxisListType.X
NPBF = np.dtype(ml_dtypes.bfloat16)

NC = 8
B = 16
C = 1024
TFULL = 2048
PPC = 256
TP = 32
NT = 8
CH = 8
FF = 4096
NE = 8
SEG = TP + 1
TILE_COLS = CH * B * TP
HALO_COLS = CH * B * SEG
EPS = 1e-5


class SemT:
    def __init__(self, h):
        self.h = h
        self.n = 0

    def inc(self, amt):
        self.n += amt
        return self.n


class Ctx:
    pass


def build_program(dump=()):
    nc = bass.Bass("TRN2", target_bir_lowering=False, debug=False)
    g = Ctx()
    g.nc = nc

    def din(name, shape, dt=F32):
        return nc.dram_tensor(name, list(shape), dt, kind="ExternalInput").ap()

    g.x_t = din("x_t", [(1 + NT) * 128, HALO_COLS])
    g.wk_t = din("wk_t", [128, CH * C])
    g.wv_t = din("wv_t", [128, CH * C])
    g.wr_t = din("wr_t", [128, CH * C])
    g.wo_t = din("wo_t", [128, CH * C])
    g.wck_t = din("wck_t", [8 * 128, CH * 512])
    g.wcv_t = din("wcv_t", [8 * 128, 4 * C])
    g.wcr_t = din("wcr_t", [128, CH * C])
    g.wip_t = din("wip_t", [2 * 128, 12288])
    g.wow_t = din("wow_t", [128, CH * C])
    g.wrt_t = din("wrt_t", [128, CH * 8])
    g.w1_t = din("w1_t", [NE * 128, CH * C])
    g.w2_t = din("w2_t", [NE * 128, CH * C])
    g.vec32 = din("vec32", [128, 256])
    g.c8f = din("c8f", [8, 64])
    g.b2f = din("b2f", [8, C])
    g.eidb = din("eidb", [8, 512], BF16)
    g.ones1f = din("ones1f", [1, 128])
    g.ones8b = din("ones8b", [1, 8], BF16)
    g.neg9b = din("neg9b", [8, 1024], BF16)
    g.identb = din("identb", [128, 128], BF16)
    g.identf = din("identf", [128, 128])
    g.iotarep = din("iotarep", [128, 257])
    g.iota8 = din("iota8", [128, 8])
    g.gates = din("gates", [128, 2])
    g.gpre = din("gpre", [128, 8])
    g.gpre256 = din("gpre256", [128, 8])
    g.maskA = din("maskA", [8, 128], BF16)
    g.maskB = din("maskB", [8, 128], BF16)

    g.out_d = nc.dram_tensor("out_d", [NT * 128, TILE_COLS], F32, kind="ExternalOutput").ap()
    g.dumps = {}
    for nm in dump:
        shape = {"hm0": [128, TILE_COLS], "m80": [8, 512], "he0": [128, 4096],
                 "asg0": [1, 512], "bc0": [8, 512],
                 "hk0": [128, 2048], "x1p": [128, 128], "wkvp": [128, 128],
                 "x1": [128, TILE_COLS], "xc": [128, HALO_COLS],
                 "kv": [128, TILE_COLS], "rr": [128, TILE_COLS],
                 "yp": [128, TILE_COLS], "wkv": [128, TILE_COLS],
                 "dbgP": [128, HALO_COLS], "dbgF": [128, HALO_COLS],
                 "dbgN": [128, HALO_COLS], "dbgD": [128, HALO_COLS],
                 "num": [128, 256 + NT * TILE_COLS], "den": [128, 256 + NT * TILE_COLS],
                 "r": [128, 256 + NT * TILE_COLS], "y1": [128, NT * TILE_COLS],
                 "h": [128, NT * TILE_COLS], "seeds": [128, 256],
                 "assign": [1, 4096]}[nm]
        g.dumps[nm] = nc.dram_tensor("dump_" + nm, shape, F32, kind="ExternalOutput").ap()

    g.num_d = nc.dram_tensor("num_d", [(1 + NT) * 128, TILE_COLS], F32).ap()
    g.den_d = nc.dram_tensor("den_d", [(1 + NT) * 128, TILE_COLS], F32).ap()
    g.r_d = nc.dram_tensor("r_d", [(1 + NT) * 128, TILE_COLS], F32).ap()
    g.y1_d = nc.dram_tensor("y1_d", [NT * 128, TILE_COLS], F32).ap()
    g.h_d = nc.dram_tensor("h_d", [NT * 128, TILE_COLS], F32).ap()
    g.wob_d = nc.dram_tensor("wob_d", [128, CH * C], BF16).ap()
    g.wcrb_d = nc.dram_tensor("wcrb_d", [128, CH * C], BF16).ap()
    g.wckb_d = nc.dram_tensor("wckb_d", [8 * 128, CH * 512], BF16).ap()
    g.wcvb_d = nc.dram_tensor("wcvb_d", [8 * 128, 4 * C], BF16).ap()
    g.wipb_d = nc.dram_tensor("wipb_d", [2 * 128, 12288], BF16).ap()
    g.wowb_d = nc.dram_tensor("wowb_d", [128, CH * C], BF16).ap()
    g.w1b_d = nc.dram_tensor("w1b_d", [NE * 128, CH * C], BF16).ap()
    g.w2b_d = nc.dram_tensor("w2b_d", [NE * 128, CH * C], BF16).ap()
    g.asg_d = nc.dram_tensor("asg_d", [1, 4096], F32).ap()
    g.cc_in = nc.dram_tensor("cc_in", [128, 256], F32).ap()
    g.cc_out = nc.dram_tensor("cc_out", [1024, 256], F32).ap()

    es = contextlib.ExitStack()
    with es:
        def sbt(name, shape, dt=F32):
            return es.enter_context(nc.sbuf_tensor(name, list(shape), dt))

        g.sVEC = sbt("sVEC", [128, 256])
        g.sC8 = sbt("sC8", [8, 64])
        g.sB2 = sbt("sB2", [8, C], BF16)
        g.sEID = sbt("sEID", [8, 512], BF16)
        g.sONES1 = sbt("sONES1", [1, 128])
        g.sONES8 = sbt("sONES8", [1, 8], BF16)
        g.sNEG9 = sbt("sNEG9", [8, 1024], BF16)
        g.sIDB = sbt("sIDB", [128, 128], BF16)
        g.sIDF = sbt("sIDF", [128, 128])
        g.sIOTA = sbt("sIOTA", [128, 257])
        g.sIOTA8 = sbt("sIOTA8", [128, 8])
        g.sGATE = sbt("sGATE", [128, 2])
        g.sGPRE = sbt("sGPRE", [128, 8])
        g.sGP256 = sbt("sGP256", [128, 8])
        g.sMA = sbt("sMA", [8, 128], BF16)
        g.sMB = sbt("sMB", [8, 128], BF16)
        g.sONE1c = sbt("sONE1c", [128, 1])
        g.sDRV = sbt("sDRV", [128, 104])
        g.sROW = sbt("sROW", [1, 1600])
        g.sSEED = sbt("sSEED", [128, 512])
        g.sGATH = sbt("sGATH", [128, 512])
        g.sSNSD = sbt("sSNSD", [128, 256])
        g.sXC = sbt("sXC", [128, 128])
        g.sASG = sbt("sASG", [1, 512], BF16)
        g.sM8 = sbt("sM8", [8, 512], BF16)
        g.sNM8 = sbt("sNM8", [8, 512], BF16)
        g.sSC = sbt("sSC", [128, 1088])
        g.sLREP = sbt("sLREP", [128, CH * SEG * B])
        g.sLAM = g.sLREP
        g.sLPRE = sbt("sLPRE", [128, CH * B * 2])
        g.ARF = sbt("ARF", [128, 21120])
        g.ARB = sbt("ARB", [128, 36864], BF16)
        g.PS = [es.enter_context(nc.psum_tensor("ps%d" % i, [128, 512], F32))
                for i in range(8)]

        g.s_ld = SemT(es.enter_context(nc.semaphore("s_ld")))
        g.s_gc = SemT(es.enter_context(nc.semaphore("s_gc")))
        g.s_t = SemT(es.enter_context(nc.semaphore("s_t")))
        g.s_v = SemT(es.enter_context(nc.semaphore("s_v")))
        g.s_s = SemT(es.enter_context(nc.semaphore("s_s")))
        g.s_st = SemT(es.enter_context(nc.semaphore("s_st")))
        g.s_cc = SemT(es.enter_context(nc.semaphore("s_cc")))

        g.P = {"sync": [], "tensor": [], "vector": [], "scalar": [], "gpsimd": []}

        def em(eng, fn):
            g.P[eng].append(fn)

        def wait(eng, sem, v=None):
            vv = sem.n if v is None else v
            em(eng, lambda e, h=sem.h, w=vv: e.wait_ge(h, w))

        def dma(eng, dst, src, sem, slow=False):
            sem.inc(16)
            em(eng, lambda e, d=dst, s=src, h=sem.h, sl=slow: e.dma_start(
                out=d, in_=s, allow_slow_non_contiguous=sl).then_inc(h, 16))

        def run(eng, fn, sem=None, amt=1):
            if sem is None:
                em(eng, fn)
            else:
                sem.inc(amt)
                em(eng, lambda e, f=fn, h=sem.h, a=amt: f(e).then_inc(h, a))

        g.em, g.wait, g.dma, g.run = em, wait, dma, run

        def vf(off, n):
            return g.ARF[:, off:off + n]

        def vb(off, n):
            return g.ARB[:, off:off + n]

        g.vf, g.vb = vf, vb

        VOFF = dict(g0=0, b0=8, g1=16, b1=24, tmk=32, tmv=40, tmr=48, cmk=56,
                    cmr=64, td=72, tf=80, ob=88, fg1=96, fb1=104, fg2=112,
                    fb2=120, ipb=128, b1e=152)
        DOFF = dict(w=0, lam=8, eu=16, bq8=24, fpre=32)

        def vcol(name, j=0):
            o = VOFF[name] + j
            return g.sVEC[:, o:o + 1]

        def dcol(name, j=0):
            o = DOFF[name] + j
            return g.sDRV[:, o:o + 1]

        g.vcol, g.dcol = vcol, dcol

        def emit_ln(get_src, put_out, sub_ranges, gcol=None, bcol=None):
            for (o, w) in sub_ranges:
                wait("tensor", g.s_v)
                wait("tensor", g.s_s)
                wait("scalar", g.s_v)
                for c in range(CH):
                    run("tensor", lambda e, ap=get_src(c, o, w), ww=w, cc=c: e.matmul(
                        g.PS[4][0:1, 0:ww], g.sONE1c[:], ap,
                        start=(cc == 0), stop=(cc == CH - 1)), g.s_t)
                for c in range(CH):
                    if c >= 2:
                        wait("scalar", g.s_t)
                    run("scalar", lambda e, ap=get_src(c, o, w), cc=c, ww=w: e.activation(
                        g.sSC[:, (cc % 2) * 544:(cc % 2) * 544 + ww],
                        ap, AF.Square), g.s_s)
                    wait("tensor", g.s_s)
                    run("tensor", lambda e, cc=c, ww=w: e.matmul(
                        g.PS[5][0:1, 0:ww], g.sONE1c[:],
                        g.sSC[:, (cc % 2) * 544:(cc % 2) * 544 + ww],
                        start=(cc == 0), stop=(cc == CH - 1)), g.s_t)
                tmm = g.s_t.n
                wait("vector", g.s_t, tmm)
                run("vector", lambda e, ww=w: e.tensor_copy(
                    g.sROW[0:1, 0:ww], g.PS[4][0:1, 0:ww]), g.s_v)
                em("vector", lambda e: e.drain())
                run("vector", lambda e, ww=w: e.tensor_tensor(
                    g.sROW[0:1, 1024:1024 + ww], g.sROW[0:1, 0:ww],
                    g.sROW[0:1, 0:ww], ALU.mult), g.s_v)
                em("vector", lambda e: e.drain())
                run("vector", lambda e, ww=w: e.tensor_tensor(
                    g.sROW[0:1, 512:512 + ww], g.PS[5][0:1, 0:ww],
                    g.sROW[0:1, 1024:1024 + ww], ALU.subtract), g.s_v)
                vrow = g.s_v.n
                wait("scalar", g.s_v, vrow)
                run("scalar", lambda e, ww=w: e.activation(
                    g.sROW[0:1, 512:512 + ww], g.sROW[0:1, 512:512 + ww],
                    AF.Ln, bias=g.sDRV[0:1, 96:97]), g.s_s)
                em("scalar", lambda e: e.drain())
                run("scalar", lambda e, ww=w: e.activation(
                    g.sROW[0:1, 512:512 + ww], g.sROW[0:1, 512:512 + ww],
                    AF.Exp, scale=-0.5), g.s_s)
                srow = g.s_s.n
                wait("tensor", g.s_s, srow)
                run("tensor", lambda e, ww=w: e.matmul(
                    g.PS[6][:, 0:ww], g.sONES1[:], g.sROW[0:1, 0:ww],
                    start=True, stop=True), g.s_t)
                run("tensor", lambda e, ww=w: e.matmul(
                    g.PS[7][:, 0:ww], g.sONES1[:], g.sROW[0:1, 512:512 + ww],
                    start=True, stop=True), g.s_t)
                tb = g.s_t.n
                wait("vector", g.s_t, tb)
                for c in range(CH):
                    run("vector", lambda e, po=put_out(c, o, w), gs=get_src(c, o, w), ww=w: e.tensor_tensor(
                        po, gs, g.PS[6][:, 0:ww], ALU.subtract), g.s_v)
                for c in range(CH):
                    run("vector", lambda e, po=put_out(c, o, w), ww=w: e.tensor_tensor(
                        po, po, g.PS[7][:, 0:ww], ALU.mult), g.s_v)
                if gcol is not None:
                    for c in range(CH):
                        run("vector", lambda e, po=put_out(c, o, w), gc=gcol(c), bc=bcol(c): e.tensor_scalar(
                            po, po, gc, bc, ALU.mult, ALU.add), g.s_v)

        g.emit_ln = emit_ln

        _preamble(g)
        _s1(g)
        _collective(g)
        _s2(g)
        _s3(g)
        _s4(g)

        wait("sync", g.s_st, g.s_st.n)
        wait("sync", g.s_gc, g.s_gc.n)

        with nc.Block() as block:
            @block.sync
            def _(q):
                for fn in g.P["sync"]:
                    fn(q)

            @block.tensor
            def _(q):
                for fn in g.P["tensor"]:
                    fn(q)

            @block.vector
            def _(q):
                for fn in g.P["vector"]:
                    fn(q)

            @block.scalar
            def _(q):
                for fn in g.P["scalar"]:
                    fn(q)

            @block.gpsimd
            def _(q):
                for fn in g.P["gpsimd"]:
                    fn(q)
    return nc


def _preamble(g):
    em, wait, dma, run = g.em, g.wait, g.dma, g.run
    for dst, src in [(g.sVEC, g.vec32), (g.sC8, g.c8f), (g.sONES1, g.ones1f),
                     (g.sONES8, g.ones8b), (g.sNEG9, g.neg9b),
                     (g.sIDB, g.identb), (g.sIDF, g.identf),
                     (g.sIOTA, g.iotarep), (g.sIOTA8, g.iota8),
                     (g.sGATE, g.gates), (g.sGPRE, g.gpre),
                     (g.sGP256, g.gpre256), (g.sMA, g.maskA),
                     (g.sMB, g.maskB), (g.sEID, g.eidb)]:
        dma("sync", dst[:], src, g.s_ld)
    dma("gpsimd", g.sB2[:], g.b2f, g.s_gc)

    run("vector", lambda e: e.memset(g.sONE1c[:], 1.0 / 1024.0), g.s_v)
    run("vector", lambda e: e.memset(g.sDRV[:, 96:97], float(EPS)), g.s_v)
    wait("scalar", g.s_ld)
    run("scalar", lambda e: e.activation(g.sDRV[:, 0:8], g.sVEC[:, 72:80], AF.Exp), g.s_s)
    wait("vector", g.s_s)
    run("vector", lambda e: e.tensor_scalar_mul(g.sDRV[:, 0:8], g.sDRV[:, 0:8], -1.0), g.s_v)
    em("vector", lambda e: e.drain())
    wait("scalar", g.s_v)
    run("scalar", lambda e: e.activation(g.sDRV[:, 8:16], g.sDRV[:, 0:8], AF.Exp), g.s_s)
    run("scalar", lambda e: e.activation(g.sDRV[:, 16:24], g.sVEC[:, 80:88], AF.Exp), g.s_s)
    for j in range(NC):
        run("scalar", lambda e, jj=j: e.activation(
            g.sDRV[:, 32 + 8 * jj:40 + 8 * jj], g.sDRV[:, 0:8], AF.Exp,
            scale=g.sGP256[:, jj:jj + 1]), g.s_s)
    sd = g.s_s.n
    wait("vector", g.s_s, sd)
    run("vector", lambda e: e.tensor_scalar_mul(g.sDRV[:, 24:32], g.sVEC[:, 128:136], 0.125), g.s_v)
    run("vector", lambda e: e.memset(g.sLREP[:], 1.0), g.s_v)
    run("vector", lambda e: e.memset(g.sLPRE[:], 1.0), g.s_v)
    em("vector", lambda e: e.drain())
    for chi in range(CH):
        run("vector", lambda e, c=chi: e.tensor_scalar_mul(
            g.sLREP[:, c * 528:(c + 1) * 528], g.sLREP[:, c * 528:(c + 1) * 528],
            g.sDRV[:, 8 + c:9 + c]), g.s_v)
        run("vector", lambda e, c=chi: e.tensor_scalar_mul(
            g.sLPRE[:, c * 32:(c + 1) * 32], g.sLPRE[:, c * 32:(c + 1) * 32],
            g.sDRV[:, 8 + c:9 + c]), g.s_v)
    em("vector", lambda e: e.drain())
    run("vector", lambda e: e.memset(
        g.sLREP[:].rearrange("p (c b t) -> p c b t", c=CH, b=B)[:, :, :, 0:1], 0.0), g.s_v)
    run("vector", lambda e: e.memset(
        g.sLPRE[:].rearrange("p (c b t) -> p c b t", c=CH, b=B, t=2)[:, :, :, 0:1], 0.0), g.s_v)
    run("vector", lambda e: e.memset(g.sSEED[:], 0.0), g.s_v)
    em("vector", lambda e: e.drain())
    WKB_, WVB_, WRB_ = 2 * TILE_COLS, 2 * TILE_COLS + CH * C, 2 * TILE_COLS + 2 * CH * C
    for dst_off, src in [(WKB_, g.wk_t), (WVB_, g.wv_t), (WRB_, g.wr_t)]:
        dma("gpsimd", g.vb(dst_off, CH * C), src, g.s_gc)
    g.s1w = g.s_gc.n
    dma("gpsimd", g.wob_d, g.wo_t, g.s_gc)
    dma("gpsimd", g.wcrb_d, g.wcr_t, g.s_gc)
    dma("gpsimd", g.wckb_d, g.wck_t, g.s_gc)
    dma("gpsimd", g.wcvb_d, g.wcv_t, g.s_gc)
    g.prep_s2 = g.s_gc.n
    dma("gpsimd", g.wipb_d, g.wip_t, g.s_gc)
    dma("gpsimd", g.wowb_d, g.wow_t, g.s_gc)
    g.prep_s3 = g.s_gc.n
    dma("gpsimd", g.w1b_d, g.w1_t, g.s_gc)
    dma("gpsimd", g.w2b_d, g.w2_t, g.s_gc)
    g.prep_done = g.s_gc.n


def _s1(g):
    em, wait, dma, run, vf, vb = g.em, g.wait, g.dma, g.run, g.vf, g.vb
    A_, Bf_, Cp_, Dn_, Ed_ = 0, HALO_COLS, 2 * HALO_COLS, 3 * HALO_COLS, 4 * HALO_COLS
    MX0_, MX1_, MX2_ = 0, TILE_COLS, 8 * TILE_COLS
    WKB_, WVB_, WRB_ = 2 * TILE_COLS, 2 * TILE_COLS + CH * C, 2 * TILE_COLS + 2 * CH * C

    dma("sync", vf(A_, CH * B * 2), g.x_t[0:128, 0:256], g.s_ld)
    g.ldx_next = g.s_ld.n
    fevac = {0: 0, 1: 0}
    pmult = {2: 0, 3: 0}
    revac = {4: 0, 5: 0}

    for kk in range(-1, NT):
        pre = (kk == -1)
        seg = 2 if pre else SEG
        cols = CH * B * seg
        tokw = B * (seg - 1)

        def r4(base, _c=cols, _s=seg):
            return g.ARF[:, base:base + _c].rearrange("p (c b t) -> p c b t", c=CH, b=B)

        def chsl(base, c, a, bb, _s=seg):
            return g.ARF[:, base + c * (B * _s): base + (c + 1) * (B * _s)].rearrange(
                "p (b t) -> p b t", b=B)[:, :, a:bb]

        ldx = g.ldx_next
        rt_prev = g.s_t.n
        st_prev = g.s_st.n

        wait("vector", g.s_ld, ldx)
        wait("scalar", g.s_ld, ldx)
        wait("tensor", g.s_ld, ldx)
        if kk == 0:
            wait("scalar", g.s_st, st_prev)
        subs = [(0, B * seg)] if pre else [(0, 512), (512, 16)]

        def src_xa(c, o, w, _base=A_, _s=seg):
            return g.ARF[:, _base + c * (B * _s) + o: _base + c * (B * _s) + o + w]
        g.emit_ln(src_xa, src_xa, subs, lambda c: g.vcol("g0", c), lambda c: g.vcol("b0", c))
        if pre:
            run("vector", lambda e, cc=cols: e.tensor_scalar_mul(vf(A_, cc), vf(A_, cc), g.sGATE[:, 0:1]), g.s_v)
        elif kk == 0:
            run("vector", lambda e, ap=r4(A_)[:, :, :, 0:1]: e.tensor_scalar_mul(
                ap, ap, g.sGATE[:, 0:1]), g.s_v)
        em("vector", lambda e: e.drain())
        run("vector", lambda e, ad=r4(Cp_)[:, :, :, 1:seg], ac=r4(A_)[:, :, :, 1:seg],
                ap_=r4(A_)[:, :, :, 0:seg - 1]: e.tensor_tensor(ad, ac, ap_, ALU.subtract), g.s_v)
        em("vector", lambda e: e.drain())

        wait("vector", g.s_t, rt_prev)
        for mi, (nm, moff) in enumerate([("tmr", MX2_), ("tmk", MX0_), ("tmv", MX1_)]):
            for c in range(CH):
                run("vector", lambda e, ot=vb(moff + c * tokw, tokw), i0=chsl(Cp_, c, 1, seg),
                        sc=g.vcol(nm, c), i1=chsl(A_, c, 0, seg - 1): e.scalar_tensor_tensor(
                    ot, i0, sc, i1, ALU.mult, ALU.add), g.s_v)
            mixv = g.s_v.n
            if mi == 1:
                wait("tensor", g.s_v, mixv)
                if kk <= 0:
                    wait("tensor", g.s_gc, g.s1w)
                wait("scalar", g.s_st, st_prev)
                for co in range(CH):
                    bank = co % 2
                    wait("tensor", g.s_s, fevac[bank])
                    for ci in range(CH):
                        run("tensor", lambda e, b_=bank, cc=ci, cco=co, tw=tokw: e.matmul(
                            g.PS[b_][:, 0:tw],
                            vb(WKB_ + cc * C + cco * 128, 128),
                            vb(MX0_ + cc * tw, tw),
                            start=(cc == 0), stop=(cc == CH - 1)),
                            g.s_t if ci == CH - 1 else None)
                    wait("scalar", g.s_t)
                    run("scalar", lambda e, b_=bank, ot=chsl(Bf_, co, 1, seg), tw=tokw: e.activation(
                        ot, g.PS[b_][:, 0:tw], AF.Exp), g.s_s)
                    fevac[bank] = g.s_s.n
            elif mi == 2:
                wait("tensor", g.s_v, mixv)
                for co in range(CH):
                    bank = 2 + co % 2
                    wait("tensor", g.s_v, pmult[bank])
                    for ci in range(CH):
                        run("tensor", lambda e, b_=bank, cc=ci, cco=co, tw=tokw: e.matmul(
                            g.PS[b_][:, 0:tw],
                            vb(WVB_ + cc * C + cco * 128, 128),
                            vb(MX1_ + cc * tw, tw),
                            start=(cc == 0), stop=(cc == CH - 1)),
                            g.s_t if ci == CH - 1 else None)
                    wait("vector", g.s_t)
                    wait("vector", g.s_s)
                    run("vector", lambda e, b_=bank, ot=chsl(Cp_, co, 1, seg),
                            i0=chsl(Bf_, co, 1, seg), tw=tokw: e.tensor_tensor(
                        ot, i0, g.PS[b_][:, 0:tw], ALU.mult), g.s_v)
                    pmult[bank] = g.s_v.n
        if kk < NT - 1:
            nxt = kk + 1
            wait("sync", g.s_v, mixv)
            dma("sync", vf(A_, HALO_COLS),
                g.x_t[(1 + nxt) * 128:(2 + nxt) * 128, :], g.s_ld)
            g.ldx_next = g.s_ld.n

        if pre:
            wait("vector", g.s_s)
            run("vector", lambda e, cc=cols: e.tensor_scalar_mul(vf(Cp_, cc), vf(Cp_, cc), g.sGATE[:, 0:1]), g.s_v)
            run("vector", lambda e, cc=cols: e.tensor_scalar_mul(vf(Bf_, cc), vf(Bf_, cc), g.sGATE[:, 0:1]), g.s_v)
            em("vector", lambda e: e.drain())
            run("vector", lambda e, ap=r4(Cp_)[:, :, :, 0:1]: e.memset(ap, 0.0), g.s_v)
            run("vector", lambda e, ap=r4(Bf_)[:, :, :, 0:1]: e.memset(ap, 0.0), g.s_v)
        else:
            wait("vector", g.s_s)
            run("vector", lambda e, ap=r4(Cp_)[:, :, :, 0:1]: e.tensor_copy(
                ap, g.sSEED[:, 0:128].rearrange("p (c b o) -> p c b o", c=CH, b=B)), g.s_v)
            run("vector", lambda e, ap=r4(Bf_)[:, :, :, 0:1]: e.tensor_copy(
                ap, g.sSEED[:, 128:256].rearrange("p (c b o) -> p c b o", c=CH, b=B)), g.s_v)
        em("vector", lambda e: e.drain())
        wait("vector", g.s_st, st_prev)
        lrep = g.sLPRE if pre else g.sLREP
        for c in range(CH):
            w_ = B * seg
            run("vector", lambda e, cc=c, ww=w_, lr=lrep: e.tensor_tensor_scan(
                vf(Dn_ + cc * ww, ww), lr[:, cc * ww:(cc + 1) * ww],
                vf(Cp_ + cc * ww, ww), 0.0, ALU.mult, ALU.add), g.s_v)
        for c in range(CH):
            w_ = B * seg
            run("vector", lambda e, cc=c, ww=w_, lr=lrep: e.tensor_tensor_scan(
                vf(Ed_ + cc * ww, ww), lr[:, cc * ww:(cc + 1) * ww],
                vf(Bf_ + cc * ww, ww), 0.0, ALU.mult, ALU.add), g.s_v)
        em("vector", lambda e: e.drain())
        run("vector", lambda e, ap=r4(Dn_)[:, :, :, seg - 1:seg]: e.tensor_copy(
            g.sSEED[:, 0:128].rearrange("p (c b o) -> p c b o", c=CH, b=B), ap), g.s_v)
        run("vector", lambda e, ap=r4(Ed_)[:, :, :, seg - 1:seg]: e.tensor_copy(
            g.sSEED[:, 128:256].rearrange("p (c b o) -> p c b o", c=CH, b=B), ap), g.s_v)
        if kk == NT - 1:
            run("vector", lambda e, ap=r4(Dn_)[:, :, :, seg - 2:seg - 1]: e.tensor_copy(
                g.sSEED[:, 256:384].rearrange("p (c b o) -> p c b o", c=CH, b=B), ap), g.s_v)
            run("vector", lambda e, ap=r4(Ed_)[:, :, :, seg - 2:seg - 1]: e.tensor_copy(
                g.sSEED[:, 384:512].rearrange("p (c b o) -> p c b o", c=CH, b=B), ap), g.s_v)
        em("vector", lambda e: e.drain())
        for c in range(CH):
            run("vector", lambda e, ot=chsl(Dn_, c, 0, seg - 1), i0=chsl(Cp_, c, 1, seg),
                    sc=g.dcol("eu", c): e.scalar_tensor_tensor(
                ot, i0, sc, ot, ALU.mult, ALU.add), g.s_v)
        for c in range(CH):
            run("vector", lambda e, ot=chsl(Ed_, c, 0, seg - 1), i0=chsl(Bf_, c, 1, seg),
                    sc=g.dcol("eu", c): e.scalar_tensor_tensor(
                ot, i0, sc, ot, ALU.mult, ALU.add), g.s_v)
        if pre:
            run("vector", lambda e, ap=r4(Ed_)[:, :, :, 0:1]: e.tensor_scalar(
                ap, ap, g.sGATE[:, 1:2], None, ALU.add), g.s_v)
        numden = g.s_v.n
        wait("tensor", g.s_v, numden)
        for co in range(CH):
            bank = 4 + co % 2
            wait("tensor", g.s_s, revac[bank])
            for ci in range(CH):
                run("tensor", lambda e, b_=bank, cc=ci, cco=co, tw=tokw: e.matmul(
                    g.PS[b_][:, 0:tw],
                    vb(WRB_ + cc * C + cco * 128, 128),
                    vb(MX2_ + cc * tw, tw),
                    start=(cc == 0), stop=(cc == CH - 1)),
                    g.s_t if ci == CH - 1 else None)
            wait("scalar", g.s_t)
            run("scalar", lambda e, b_=bank, ot=chsl(Cp_, co, 1, seg), tw=tokw: e.activation(
                ot, g.PS[b_][:, 0:tw], AF.Exp, scale=-1.0), g.s_s)
            revac[bank] = g.s_s.n
        wait("vector", g.s_s)
        run("vector", lambda e, cc=cols: e.tensor_scalar(
            vf(Cp_, cc), vf(Cp_, cc), 1.0, None, ALU.add), g.s_v)
        em("vector", lambda e: e.drain())
        run("vector", lambda e, cc=cols: e.reciprocal(vf(Bf_, cc), vf(Cp_, cc)), g.s_v)
        if pre:
            for si, (base, a) in enumerate([(Dn_, 0), (Ed_, 0), (Bf_, 1)]):
                run("vector", lambda e, aa=si, ap=r4(base)[:, :, :, a:a + 1]: e.tensor_copy(
                    g.sSC[:, aa * 128:(aa + 1) * 128].rearrange("p (c b o) -> p c b o", c=CH, b=B),
                    ap), g.s_v)
            em("vector", lambda e: e.drain())
            vr = g.s_v.n
            wait("sync", g.s_v, vr)
            for si, dram in enumerate([g.num_d, g.den_d, g.r_d]):
                dma("sync", dram[0:128, 0:128], g.sSC[:, si * 128:(si + 1) * 128], g.s_st)
        else:
            vr = g.s_v.n
            r0 = (1 + kk) * 128
            for (dram, base, a, bb), eng in [((g.r_d, Bf_, 1, seg), "sync"),
                                             ((g.num_d, Dn_, 0, seg - 1), "scalar"),
                                             ((g.den_d, Ed_, 0, seg - 1), "sync")]:
                wait(eng, g.s_v, vr)
                dma(eng, dram[r0:r0 + 128, :],
                    r4(base)[:, :, :, a:bb], g.s_st)


def _collective(g):
    em, wait, dma, run = g.em, g.wait, g.dma, g.run
    wait("gpsimd", g.s_v)
    dma("gpsimd", g.cc_in, g.sSEED[:, 256:512], g.s_gc)
    wait("gpsimd", g.s_gc)
    run("gpsimd", lambda e: e.collective_compute(
        "AllGather", ALU.bypass, replica_groups=[list(range(NC))],
        ins=[g.cc_in.opt()], outs=[g.cc_out.opt()]), g.s_cc)
    wait("gpsimd", g.s_cc)
    run("vector", lambda e: e.memset(g.sSNSD[:], 0.0), g.s_v)
    em("vector", lambda e: e.drain())
    for j in range(NC):
        dma("gpsimd", g.sGATH[:, 0:256], g.cc_out[j * 128:(j + 1) * 128, :], g.s_gc)
        gld = g.s_gc.n
        fj = g.sDRV[:, 32 + 8 * j:40 + 8 * j]
        wait("vector", g.s_gc, gld)
        for half in (0, 1):
            run("vector", lambda e, f_=fj, hh=half: e.tensor_tensor(
                g.sSC[:, hh * 128:hh * 128 + 128].rearrange("p (c b) -> p c b", c=CH),
                g.sSNSD[:, hh * 128:hh * 128 + 128].rearrange("p (c b) -> p c b", c=CH),
                f_.rearrange("p (c o) -> p c o", o=1).to_broadcast([128, CH, B]), ALU.mult), g.s_v)
        em("vector", lambda e: e.drain())
        for half in (0, 1):
            run("vector", lambda e, jj=j, hh=half: e.scalar_tensor_tensor(
                g.sSNSD[:, hh * 128:hh * 128 + 128],
                g.sGATH[:, hh * 128:hh * 128 + 128],
                g.sGPRE[:, jj:jj + 1],
                g.sSC[:, hh * 128:hh * 128 + 128], ALU.mult, ALU.add), g.s_v)
        em("vector", lambda e: e.drain())
        wait("gpsimd", g.s_v)
    if "seeds" in g.dumps:
        wait("sync", g.s_v)
        dma("sync", g.dumps["seeds"], g.sSNSD[:], g.s_st)
    for nm, dr in [("num", g.num_d), ("den", g.den_d), ("r", g.r_d)]:
        if nm in g.dumps:
            wait("sync", g.s_st, g.s_st.n)
            dma("sync", g.dumps[nm], dr, g.s_st)


def _s2(g):
    em, wait, dma, run, vf, vb = g.em, g.wait, g.dma, g.run, g.vf, g.vb
    A_, Bd_, Cr_, Dx_, Ekv_ = 0, 4224, 8320, 12416, 16512
    RW_, MXC_, HK_, WOC_, WG_ = 0, 4096, 8192, 10240, 18432
    FG = 4
    NG = 32 // FG

    wait("scalar", g.s_v)
    for chi in range(CH):
        run("scalar", lambda e, c=chi: e.activation(
            g.sLAM[:, c * 257:(c + 1) * 257], g.sIOTA[:], AF.Exp,
            scale=g.sDRV[:, c:c + 1]), g.s_s)
    lam_built = g.s_s.n

    def fg_load(gi, par):
        base = WG_ + par * 8192
        wait("sync", g.s_t)
        dma("sync", vb(base, 4096),
            g.wckb_d[gi * 128:(gi + 1) * 128, :], g.s_ld)
        wait("scalar", g.s_t)
        dma("scalar", vb(base + 4096, 4096),
            g.wcvb_d[gi * 128:(gi + 1) * 128, :], g.s_ld)
        return g.s_ld.n

    wait("sync", g.s_gc, g.prep_s2)
    wait("scalar", g.s_gc, g.prep_s2)
    wait("sync", g.s_st, g.s_st.n)
    wait("scalar", g.s_st, g.s_st.n)
    fg_ready = [None] * NG
    fg_ready[0] = fg_load(0, 0)
    woadd = {0: 0, 1: 0}
    rrev = {0: 0, 1: 0}
    hkread = {6: 0, 7: 0}

    for kk in range(-1, NT):
        pre = (kk == -1)
        ntok = 128 if pre else TILE_COLS
        tokc = B if pre else B * TP
        src_off = 0 if pre else 256 + kk * TILE_COLS

        def csl(base, c, _t=tokc):
            return g.ARF[:, base + c * _t: base + (c + 1) * _t]

        wait("sync", g.s_v)
        wait("sync", g.s_s)
        wait("sync", g.s_t)
        wait("sync", g.s_gc, g.s_gc.n)
        wait("scalar", g.s_v)
        wait("scalar", g.s_st, g.s_st.n)
        rb = (1 + kk) * 128 if not pre else 0
        dma("sync", vf(A_, ntok), g.num_d[rb:rb + 128, 0:ntok], g.s_ld)
        dma("scalar", vf(Bd_, ntok), g.den_d[rb:rb + 128, 0:ntok], g.s_ld)
        dma("sync", vf(Cr_, ntok), g.r_d[rb:rb + 128, 0:ntok], g.s_ld)
        if pre:
            dma("sync", vf(Ekv_, 256), g.x_t[0:128, 0:256], g.s_ld)
            prex = g.s_ld.n
            wait("vector", g.s_ld, prex)
            run("vector", lambda e: e.tensor_copy(
                vf(Dx_, 128).rearrange("p (c b o) -> p c b o", c=CH, b=B),
                vf(Ekv_, 256).rearrange("p (c b t) -> p c b t", c=CH, b=B)[:, :, :, 1:2]), g.s_v)
            em("vector", lambda e: e.drain())
        else:
            xs = g.x_t[(1 + kk) * 128:(2 + kk) * 128, :].rearrange(
                "p (c b t) -> p c b t", c=CH, b=B)[:, :, :, 1:SEG]
            dma("scalar", vf(Dx_, ntok).rearrange("p (c b t) -> p c b t", c=CH, b=B, t=TP),
                xs, g.s_ld)
        wait("scalar", g.s_t)
        dma("scalar", vb(WOC_, CH * C), g.wob_d, g.s_ld)
        ldt = g.s_ld.n

        t0 = 0 if pre else TP * kk + 1
        tw = 1 if pre else TP
        lam4 = g.sLAM.rearrange("p (c t) -> p c t", c=CH, t=528)[:, :, t0:t0 + tw] if False else \
            g.sLAM[:, 0:CH * 257].rearrange("p (c t) -> p c t", c=CH)[:, :, t0:t0 + tw]
        wait("vector", g.s_ld, ldt)
        if pre:
            wait("vector", g.s_s, lam_built)
        for (tgt, sn_off) in [(A_, 0), (Bd_, 128)]:
            run("vector", lambda e, so=sn_off, l4=lam4, _tw=tw: e.tensor_tensor(
                vf(Ekv_, CH * B * _tw).rearrange("p (c b t) -> p c b t", c=CH, b=B),
                g.sSNSD[:, so:so + 128].rearrange("p (c b o) -> p c b o", c=CH, b=B).to_broadcast([128, CH, B, _tw]),
                l4.unsqueeze(2).to_broadcast([128, CH, B, _tw]), ALU.mult), g.s_v)
            em("vector", lambda e: e.drain())
            run("vector", lambda e, tg=tgt, _n=ntok: e.tensor_tensor(
                vf(tg, _n), vf(tg, _n), vf(Ekv_, _n), ALU.add), g.s_v)
            em("vector", lambda e: e.drain())
        run("vector", lambda e, n_=ntok: e.reciprocal(vf(Ekv_, n_), vf(Bd_, n_)), g.s_v)
        em("vector", lambda e: e.drain())
        run("vector", lambda e, n_=ntok: e.tensor_tensor(vf(A_, n_), vf(A_, n_), vf(Ekv_, n_), ALU.mult), g.s_v)
        run("vector", lambda e, n_=ntok: e.tensor_tensor(vb(RW_, n_), vf(A_, n_), vf(Cr_, n_), ALU.mult), g.s_v)
        rwv = g.s_v.n
        if not pre:
            wait("scalar", g.s_v, rwv)
            dma("scalar", vf(Cr_, TILE_COLS).rearrange("p (c b t) -> p c b t", c=CH, b=B),
                g.x_t[(1 + kk) * 128:(2 + kk) * 128, :].rearrange(
                    "p (c b t) -> p c b t", c=CH, b=B)[:, :, :, 1:SEG], g.s_ld)
            g.xre = g.s_ld.n
        if kk == 0 and "wkv" in g.dumps:
            wait("sync", g.s_v, rwv)
            dma("sync", g.dumps["wkv"], vf(A_, TILE_COLS), g.s_st)
            wait("vector", g.s_st, g.s_st.n)
            wait("scalar", g.s_st, g.s_st.n)
        wait("tensor", g.s_v, rwv)
        wait("tensor", g.s_ld, ldt)
        for co in range(CH):
            bank = co % 2
            wait("tensor", g.s_v, woadd[bank])
            for ci in range(CH):
                run("tensor", lambda e, b_=bank, cc=ci, cco=co, _tk=tokc: e.matmul(
                    g.PS[b_][:, 0:_tk],
                    vb(WOC_ + cc * C + cco * 128, 128),
                    vb(RW_ + cc * _tk, _tk),
                    start=(cc == 0), stop=(cc == CH - 1)),
                    g.s_t if ci == CH - 1 else None)
            wait("vector", g.s_t)
            run("vector", lambda e, b_=bank, ap=csl(Dx_, co), _tk=tokc: e.tensor_tensor(
                ap, ap, g.PS[b_][:, 0:_tk], ALU.add), g.s_v)
            woadd[bank] = g.s_v.n
        if kk == 0 and "x1" in g.dumps:
            wait("sync", g.s_v)
            dma("sync", g.dumps["x1"], vf(Dx_, TILE_COLS), g.s_st)
            wait("vector", g.s_st, g.s_st.n)
            wait("scalar", g.s_st, g.s_st.n)
        if pre and "x1p" in g.dumps:
            wait("sync", g.s_v)
            dma("sync", g.dumps["x1p"], vf(Dx_, 128), g.s_st)
            dma("sync", g.dumps["wkvp"], vf(A_, 128), g.s_st)
            wait("vector", g.s_st, g.s_st.n)
            wait("scalar", g.s_st, g.s_st.n)
        if pre:
            def put_pre(c, o, w):
                return g.sXC[:, c * B + o: c * B + o + w]
            g.emit_ln(lambda c, o, w: g.ARF[:, Dx_ + c * tokc + o: Dx_ + c * tokc + o + w],
                      put_pre, [(0, B)],
                      lambda c: g.vcol("g1", c), lambda c: g.vcol("b1", c))
            run("vector", lambda e: e.tensor_scalar_mul(g.sXC[:], g.sXC[:], g.sGATE[:, 0:1]), g.s_v)
            em("vector", lambda e: e.drain())
            continue

        def put_xc(c, o, w):
            ap = g.ARF[:, A_ + c * (B * SEG): A_ + (c + 1) * (B * SEG)].rearrange(
                "p (b t) -> p b t", b=B)[:, :, 1:SEG]
            b0 = o // TP
            b1 = (o + w) // TP
            return ap[:, b0:b1, :]
        g.emit_ln(lambda c, o, w: g.ARF[:, Dx_ + c * tokc + o: Dx_ + c * tokc + o + w],
                  put_xc, [(0, 512)],
                  lambda c: g.vcol("g1", c), lambda c: g.vcol("b1", c))

        def xc4():
            return g.ARF[:, A_:A_ + HALO_COLS].rearrange("p (c b t) -> p c b t", c=CH, b=B)
        run("vector", lambda e, ap=xc4()[:, :, :, 0:1]: e.tensor_copy(
            ap, g.sXC[:].rearrange("p (c b o) -> p c b o", c=CH, b=B)), g.s_v)
        em("vector", lambda e: e.drain())
        run("vector", lambda e, ap=xc4()[:, :, :, SEG - 1:SEG]: e.tensor_copy(
            g.sXC[:].rearrange("p (c b o) -> p c b o", c=CH, b=B), ap), g.s_v)
        em("vector", lambda e: e.drain())
        if kk == 0 and "xc" in g.dumps:
            wait("sync", g.s_v)
            dma("sync", g.dumps["xc"], vf(A_, HALO_COLS), g.s_st)
            wait("vector", g.s_st, g.s_st.n)
            wait("scalar", g.s_st, g.s_st.n)
        run("vector", lambda e, a1=xc4()[:, :, :, 1:SEG], a0=xc4()[:, :, :, 0:SEG - 1]: e.tensor_tensor(
            vf(Bd_, TILE_COLS).rearrange("p (c b t) -> p c b t", c=CH, b=B),
            a1, a0, ALU.subtract), g.s_v)
        em("vector", lambda e: e.drain())
        wait("vector", g.s_t)
        for c in range(CH):
            run("vector", lambda e, cc=c: e.scalar_tensor_tensor(
                vb(RW_ + cc * tokc, tokc),
                g.ARF[:, Bd_ + cc * tokc: Bd_ + (cc + 1) * tokc], g.vcol("cmk", cc),
                g.ARF[:, A_ + cc * (B * SEG): A_ + (cc + 1) * (B * SEG)].rearrange(
                    "p (b t) -> p b t", b=B)[:, :, 0:SEG - 1],
                ALU.mult, ALU.add), g.s_v)
        for c in range(CH):
            run("vector", lambda e, cc=c: e.scalar_tensor_tensor(
                vb(MXC_ + cc * tokc, tokc),
                g.ARF[:, Bd_ + cc * tokc: Bd_ + (cc + 1) * tokc], g.vcol("cmr", cc),
                g.ARF[:, A_ + cc * (B * SEG): A_ + (cc + 1) * (B * SEG)].rearrange(
                    "p (b t) -> p b t", b=B)[:, :, 0:SEG - 1],
                ALU.mult, ALU.add), g.s_v)
        mixc = g.s_v.n
        wait("tensor", g.s_v, mixc)
        first_mm = [True] * CH
        for gi in range(NG):
            if fg_ready[gi] is None:
                wait("sync", g.s_t)
                fg_ready[gi] = fg_load(gi, gi % 2)
            if gi + 1 < NG and fg_ready[gi + 1] is None:
                wait("sync", g.s_t)
                fg_ready[gi + 1] = fg_load(gi + 1, (gi + 1) % 2)
            wait("tensor", g.s_ld, fg_ready[gi])
            wbase = WG_ + (gi % 2) * 8192
            prev_wcv_t = g.s_t.n
            for fi in range(FG):
                hbank = 6 + fi % 2
                wait("tensor", g.s_v, hkread[hbank])
                for ci in range(CH):
                    run("tensor", lambda e, b_=hbank, cc=ci, ff=fi, wb=wbase: e.matmul(
                        g.PS[b_][:, 0:512],
                        vb(wb + cc * (128 * FG) + ff * 128, 128),
                        vb(RW_ + cc * 512, 512),
                        start=(cc == 0), stop=(cc == CH - 1)),
                        g.s_t if ci == CH - 1 else None)
                wait("vector", g.s_t)
                run("vector", lambda e, b_=hbank, ff=fi: e.tensor_relu(
                    g.sSC[:, 544 * (ff % 2):544 * (ff % 2) + 512],
                    g.PS[b_][:, 0:512]), g.s_v)
                hkread[hbank] = g.s_v.n
                wait("vector", g.s_t, prev_wcv_t)
                run("vector", lambda e, ff=fi: e.tensor_tensor(
                    vb(HK_ + ff * 512, 512),
                    g.sSC[:, 544 * (ff % 2):544 * (ff % 2) + 512],
                    g.sSC[:, 544 * (ff % 2):544 * (ff % 2) + 512], ALU.mult), g.s_v)
            hkv = g.s_v.n
            if kk == 0 and gi == 0 and "hk0" in g.dumps:
                run("vector", lambda e: e.tensor_copy(g.sSC[:, 0:512], vb(HK_, 512)), g.s_v)
                run("vector", lambda e: e.tensor_copy(g.sSC[:, 544:1056], vb(HK_ + 512, 512)), g.s_v)
                em("vector", lambda e: e.drain())
                wait("sync", g.s_v)
                dma("sync", g.dumps["hk0"][:, 0:512], g.sSC[:, 0:512], g.s_st)
                dma("sync", g.dumps["hk0"][:, 512:1024], g.sSC[:, 544:1056], g.s_st)
                wait("vector", g.s_st, g.s_st.n)
                run("vector", lambda e: e.tensor_copy(g.sSC[:, 0:512], vb(HK_ + 1024, 512)), g.s_v)
                run("vector", lambda e: e.tensor_copy(g.sSC[:, 544:1056], vb(HK_ + 1536, 512)), g.s_v)
                em("vector", lambda e: e.drain())
                wait("sync", g.s_v)
                wait("sync", g.s_st, g.s_st.n)
                dma("sync", g.dumps["hk0"][:, 1024:1536], g.sSC[:, 0:512], g.s_st)
                dma("sync", g.dumps["hk0"][:, 1536:2048], g.sSC[:, 544:1056], g.s_st)
                wait("vector", g.s_st, g.s_st.n)
                wait("scalar", g.s_st, g.s_st.n)
            wait("tensor", g.s_v, hkv)
            for co in range(CH):
                if co < 6:
                    last = (gi == NG - 1)
                    for fi in range(FG):
                        run("tensor", lambda e, cc=co, ff=fi, wb=wbase,
                                st=first_mm[co] and fi == 0,
                                sp=(last and fi == FG - 1): e.matmul(
                            g.PS[cc][:, 0:512],
                            vb(wb + 4096 + ff * C + cc * 128, 128),
                            vb(HK_ + ff * 512, 512),
                            start=st, stop=sp),
                            g.s_t if fi == FG - 1 else None)
                    first_mm[co] = False
                else:
                    bank = 6 + (co - 6)
                    wait("tensor", g.s_v, hkread[bank])
                    for fi in range(FG):
                        run("tensor", lambda e, b_=bank, cc=co, ff=fi, wb=wbase: e.matmul(
                            g.PS[b_][:, 0:512],
                            vb(wb + 4096 + ff * C + cc * 128, 128),
                            vb(HK_ + ff * 512, 512),
                            start=(ff == 0), stop=(ff == FG - 1)),
                            g.s_t if fi == FG - 1 else None)
                    wait("vector", g.s_t)
                    if gi == 0:
                        run("vector", lambda e, b_=bank, ap=csl(Ekv_, co): e.tensor_copy(
                            ap, g.PS[b_][:, 0:512]), g.s_v)
                    else:
                        run("vector", lambda e, b_=bank, ap=csl(Ekv_, co): e.tensor_tensor(
                            ap, ap, g.PS[b_][:, 0:512], ALU.add), g.s_v)
                    hkread[bank] = g.s_v.n
        fg_ready = [None] * NG
        if kk < NT - 1:
            fg_ready[0] = fg_load(0, 0)
            fg_ready[1] = fg_load(1, 1)
        tall = g.s_t.n
        wait("vector", g.s_t, tall)
        for co in range(6):
            run("vector", lambda e, cc=co, ap=csl(Ekv_, co): e.tensor_copy(ap, g.PS[cc][:, 0:512]), g.s_v)
        kvev = g.s_v.n
        woadd[0] = woadd[1] = kvev
        wait("sync", g.s_v, kvev)
        dma("sync", vb(WOC_, CH * C), g.wcrb_d, g.s_ld)
        wcrl = g.s_ld.n
        wait("tensor", g.s_ld, wcrl)
        wait("tensor", g.s_v, kvev)
        for co in range(CH):
            bank = co % 2
            wait("tensor", g.s_s, rrev[bank])
            for ci in range(CH):
                run("tensor", lambda e, b_=bank, cc=ci, cco=co: e.matmul(
                    g.PS[b_][:, 0:512],
                    vb(WOC_ + cc * C + cco * 128, 128),
                    vb(MXC_ + cc * 512, 512),
                    start=(cc == 0), stop=(cc == CH - 1)),
                    g.s_t if ci == CH - 1 else None)
            wait("scalar", g.s_t)
            run("scalar", lambda e, b_=bank, ap=csl(A_, co): e.activation(
                ap, g.PS[b_][:, 0:512], AF.Exp, scale=-1.0), g.s_s)
            rrev[bank] = g.s_s.n
        wait("vector", g.s_s)
        run("vector", lambda e: e.tensor_scalar(vf(A_, TILE_COLS), vf(A_, TILE_COLS), 1.0, None, ALU.add), g.s_v)
        em("vector", lambda e: e.drain())
        run("vector", lambda e: e.reciprocal(vf(Bd_, TILE_COLS), vf(A_, TILE_COLS)), g.s_v)
        em("vector", lambda e: e.drain())
        if kk == 0 and "kv" in g.dumps:
            wait("sync", g.s_v)
            dma("sync", g.dumps["kv"], vf(Ekv_, TILE_COLS), g.s_st)
            dma("sync", g.dumps["rr"], vf(Bd_, TILE_COLS), g.s_st)
            wait("vector", g.s_st, g.s_st.n)
            wait("scalar", g.s_st, g.s_st.n)
        xre = g.xre
        run("vector", lambda e: e.tensor_tensor(vf(Ekv_, TILE_COLS), vf(Ekv_, TILE_COLS), vf(Bd_, TILE_COLS), ALU.mult), g.s_v)
        run("vector", lambda e: e.tensor_tensor(vf(Ekv_, TILE_COLS), vf(Ekv_, TILE_COLS), vf(Dx_, TILE_COLS), ALU.add), g.s_v)
        wait("vector", g.s_ld, xre)
        run("vector", lambda e: e.tensor_tensor(vf(Ekv_, TILE_COLS), vf(Ekv_, TILE_COLS), vf(Cr_, TILE_COLS), ALU.add), g.s_v)
        em("vector", lambda e: e.drain())

        if kk == 0 and "yp" in g.dumps:
            wait("sync", g.s_v)
            dma("sync", g.dumps["yp"], vf(Ekv_, TILE_COLS), g.s_st)
            wait("vector", g.s_st, g.s_st.n)
            wait("scalar", g.s_st, g.s_st.n)
        def src_y(c, o, w):
            return g.ARF[:, Ekv_ + c * tokc + o: Ekv_ + c * tokc + o + w]

        def put_y1(c, o, w):
            b0 = o // TP
            b1 = (o + w) // TP
            return g.ARF[:, Bd_ + c * tokc: Bd_ + (c + 1) * tokc].rearrange(
                "p (t b) -> p t b", t=TP)[:, :, b0:b1].transpose([0, 2, 1])
        for (o, w) in [(0, 512)]:
            wait("tensor", g.s_v)
            wait("tensor", g.s_s)
            wait("scalar", g.s_v)
            for c in range(CH):
                run("tensor", lambda e, cc=c, ap=src_y(c, o, w), ww=w: e.matmul(
                    g.PS[4][0:1, 0:ww], g.sONE1c[:], ap,
                    start=(cc == 0), stop=(cc == CH - 1)), g.s_t)
            for c in range(CH):
                if c >= 2:
                    wait("scalar", g.s_t)
                run("scalar", lambda e, cc=c, ap=src_y(c, o, w), ww=w: e.activation(
                    g.sSC[:, (cc % 2) * 544:(cc % 2) * 544 + ww], ap, AF.Square), g.s_s)
                wait("tensor", g.s_s)
                run("tensor", lambda e, cc=c, ww=w: e.matmul(
                    g.PS[5][0:1, 0:ww], g.sONE1c[:],
                    g.sSC[:, (cc % 2) * 544:(cc % 2) * 544 + ww],
                    start=(cc == 0), stop=(cc == CH - 1)), g.s_t)
            tmm = g.s_t.n
            wait("vector", g.s_t, tmm)
            run("vector", lambda e, ww=w: e.tensor_copy(g.sROW[0:1, 0:ww], g.PS[4][0:1, 0:ww]), g.s_v)
            em("vector", lambda e: e.drain())
            run("vector", lambda e, ww=w: e.tensor_tensor(
                g.sROW[0:1, 1024:1024 + ww], g.sROW[0:1, 0:ww], g.sROW[0:1, 0:ww], ALU.mult), g.s_v)
            em("vector", lambda e: e.drain())
            run("vector", lambda e, ww=w: e.tensor_tensor(
                g.sROW[0:1, 512:512 + ww], g.PS[5][0:1, 0:ww], g.sROW[0:1, 1024:1024 + ww], ALU.subtract), g.s_v)
            vrow = g.s_v.n
            wait("scalar", g.s_v, vrow)
            run("scalar", lambda e, ww=w: e.activation(
                g.sROW[0:1, 512:512 + ww], g.sROW[0:1, 512:512 + ww], AF.Ln, bias=g.sDRV[0:1, 96:97]), g.s_s)
            em("scalar", lambda e: e.drain())
            run("scalar", lambda e, ww=w: e.activation(
                g.sROW[0:1, 512:512 + ww], g.sROW[0:1, 512:512 + ww], AF.Exp, scale=-0.5), g.s_s)
            srow = g.s_s.n
            wait("tensor", g.s_s, srow)
            run("tensor", lambda e, ww=w: e.matmul(
                g.PS[6][:, 0:ww], g.sONES1[:], g.sROW[0:1, 0:ww], start=True, stop=True), g.s_t)
            run("tensor", lambda e, ww=w: e.matmul(
                g.PS[7][:, 0:ww], g.sONES1[:], g.sROW[0:1, 512:512 + ww], start=True, stop=True), g.s_t)
            tb = g.s_t.n
            wait("vector", g.s_t, tb)
            for c in range(CH):
                run("vector", lambda e, ap=src_y(c, o, w), ww=w: e.tensor_tensor(
                    ap, ap, g.PS[6][:, 0:ww], ALU.subtract), g.s_v)
            for c in range(CH):
                run("vector", lambda e, po=put_y1(c, o, w), ap=src_y(c, o, w), ww=w: e.tensor_tensor(
                    po, ap, g.PS[7][:, 0:ww], ALU.mult), g.s_v)
        y1v = g.s_v.n
        hkread[6] = hkread[7] = y1v
        wait("sync", g.s_v, y1v)
        dma("sync", g.y1_d[kk * 128:(kk + 1) * 128, :], vf(Bd_, TILE_COLS), g.s_st)
    if "y1" in g.dumps:
        wait("sync", g.s_st, g.s_st.n)
        dma("sync", g.dumps["y1"], g.y1_d, g.s_st)


def _s3(g):
    em, wait, dma, run, vf, vb = g.em, g.wait, g.dma, g.run, g.vf, g.vb
    Y_, Hh_ = 0, TILE_COLS
    QKV_, YB_, CTX_ = 0, 12288, 16384
    WIP_ = 20480
    CTX_ = QKV_
    WIP_ = 16384

    def wip_load(half):
        wait("scalar", g.s_t)
        dma("scalar", vb(WIP_, 12288),
            g.wipb_d[half * 128:(half + 1) * 128, :], g.s_ld)
        return g.s_ld.n

    wait("scalar", g.s_gc, g.prep_s3)
    s3w = wip_load(0)
    qevac = {0: 0, 1: 0}
    hpre = {0: 0, 1: 0}
    dma("sync", g.sGATH[:, 0:64], g.wrt_t, g.s_ld)
    rtw = g.s_ld.n

    for kk in range(NT):
        wait("sync", g.s_v)
        wait("sync", g.s_t)
        wait("sync", g.s_s)
        wait("sync", g.s_st, g.s_st.n)
        dma("sync", vf(Y_, TILE_COLS), g.y1_d[kk * 128:(kk + 1) * 128, :], g.s_ld)
        ldt = g.s_ld.n
        wait("vector", g.s_ld, ldt)
        run("vector", lambda e: e.tensor_copy(vb(YB_, TILE_COLS), vf(Y_, TILE_COLS)), g.s_v)
        em("vector", lambda e: e.drain())
        ybc = g.s_v.n
        wait("tensor", g.s_v, ybc)
        wait("tensor", g.s_ld, s3w)
        for oc in range(24):
            if oc == 12:
                s3w = wip_load(1)
                wait("tensor", g.s_ld, s3w)
            bank = oc % 2
            wait("tensor", g.s_s, qevac[bank])
            for ci in range(CH):
                run("tensor", lambda e, b_=bank, cc=ci, oo=oc % 12: e.matmul(
                    g.PS[b_][:, 0:512],
                    vb(WIP_ + cc * 1536 + oo * 128, 128),
                    vb(YB_ + cc * 512, 512),
                    start=(cc == 0), stop=(cc == CH - 1)),
                    g.s_t if ci == CH - 1 else None)
            wait("scalar", g.s_t)
            if oc < 8:
                run("scalar", lambda e, b_=bank, oo=oc: e.activation(
                    vb(QKV_ + oo * 512, 512), g.PS[b_][:, 0:512], AF.Identity,
                    bias=g.sDRV[:, 24 + oo:25 + oo], scale=0.125), g.s_s)
            else:
                run("scalar", lambda e, b_=bank, oo=oc: e.activation(
                    vb(QKV_ + oo * 512, 512), g.PS[b_][:, 0:512], AF.Identity,
                    bias=g.sVEC[:, 128 + oo:129 + oo]), g.s_s)
            qevac[bank] = g.s_s.n
        qkvs = g.s_s.n
        inproj_t = g.s_t.n
        wait("tensor", g.s_s, qkvs)
        snaps = [dict(t=0, v=0, s=0), dict(t=0, v=0, s=0)]
        SBK = [2, 0]
        TBK = [3, 1]
        CBK = [4, 6]
        AS_ = [YB_ + 2048, YB_ + 2816]
        XR = [1056, 1072]
        gidx = 0
        for pb in range(4):
            colo = pb * 128
            for j in range(CH):
                p = gidx % 2
                sb, tb, cb = SBK[p], TBK[p], CBK[p]
                a0 = AS_[p]
                xr = XR[p]
                qof = QKV_ + j * 512 + colo
                kof = QKV_ + (8 + j) * 512 + colo
                vof = QKV_ + (16 + j) * 512 + colo
                sn = snaps[p]
                wait("tensor", g.s_v, sn["v"])
                wait("tensor", g.s_s, sn["s"])
                for hh in range(2):
                    run("tensor", lambda e, h_=hh, b_=sb: e.matmul(
                        g.PS[b_][:, h_ * 128:h_ * 128 + 128], g.sMA[:], g.sMB[:],
                        start=True, stop=False), None)
                    run("tensor", lambda e, h_=hh, b_=sb, q_=qof, k_=kof: e.matmul(
                        g.PS[b_][:, h_ * 128:h_ * 128 + 128],
                        g.ARB[64 * h_:64 * h_ + 64, q_:q_ + 128],
                        g.ARB[64 * h_:64 * h_ + 64, k_:k_ + 128],
                        start=False, stop=True), g.s_t)
                sco = g.s_t.n
                run("tensor", lambda e, v_=vof, b_=tb: e.transpose(
                    g.PS[b_][:, 128:192].bitcast(BF16), g.ARB[:, v_:v_ + 128],
                    g.sIDB[:]), g.s_t)
                vt = g.s_t.n
                wait("vector", g.s_t, sco)
                run("vector", lambda e, b_=sb, x_=xr: e.reduce_max(
                    g.sSC[:, x_:x_ + 2].rearrange("p (s o) -> p s o", s=2),
                    g.PS[b_][:, 0:256].rearrange("p (s n) -> p s n", s=2),
                    axis=AX, negate=True), g.s_v)
                em("vector", lambda e: e.drain())
                smax = g.s_v.n
                wait("scalar", g.s_v, smax)
                for hh in range(2):
                    run("scalar", lambda e, h_=hh, b_=sb, a_=a0, x_=xr: e.activation(
                        g.ARB[:, a_ + h_ * 128: a_ + h_ * 128 + 128],
                        g.PS[b_][:, h_ * 128:h_ * 128 + 128],
                        AF.Exp, bias=g.sSC[:, x_ + h_:x_ + h_ + 1]), g.s_s)
                sexp = g.s_s.n
                wait("vector", g.s_s, sexp)
                run("vector", lambda e, a_=a0, x_=xr: e.tensor_reduce(
                    g.sSC[:, x_ + 2:x_ + 4].rearrange("p (s o) -> p s o", s=2),
                    g.ARB[:, a_:a_ + 256].rearrange("p (s n) -> p s n", s=2),
                    axis=AX, op=ALU.add), g.s_v)
                em("vector", lambda e: e.drain())
                run("vector", lambda e, x_=xr: e.reciprocal(
                    g.sSC[:, x_ + 4:x_ + 6], g.sSC[:, x_ + 2:x_ + 4]), g.s_v)
                em("vector", lambda e: e.drain())
                for hh in range(2):
                    run("vector", lambda e, h_=hh, a_=a0, x_=xr: e.tensor_scalar_mul(
                        g.ARB[:, a_ + h_ * 128: a_ + h_ * 128 + 128],
                        g.ARB[:, a_ + h_ * 128: a_ + h_ * 128 + 128],
                        g.sSC[:, x_ + 4 + h_:x_ + 5 + h_]), g.s_v)
                atn = g.s_v.n
                wait("tensor", g.s_v, atn)
                for hh in range(2):
                    run("tensor", lambda e, h_=hh, a_=a0, b_=tb: e.transpose(
                        g.PS[b_][:, h_ * 64:h_ * 64 + 64].bitcast(BF16),
                        g.ARB[:, a_ + h_ * 128: a_ + h_ * 128 + 128],
                        g.sIDB[:]), g.s_t)
                att = g.s_t.n
                wait("vector", g.s_t, att)
                run("vector", lambda e, a_=a0, b_=tb: e.tensor_copy(
                    g.ARB[:, a_ + 256: a_ + 512], g.PS[b_][:, 0:128].bitcast(BF16)), g.s_v)
                run("vector", lambda e, a_=a0, b_=tb: e.tensor_copy(
                    g.ARB[:, a_ + 512: a_ + 640], g.PS[b_][:, 128:192].bitcast(BF16)), g.s_v)
                em("vector", lambda e: e.drain())
                tev = g.s_v.n
                wait("tensor", g.s_v, tev)
                run("tensor", lambda e, a_=a0, b_=cb: e.matmul(
                    g.PS[b_][:, 0:256], g.ARB[:, a_ + 512: a_ + 640],
                    g.ARB[:, a_ + 256: a_ + 512], start=True, stop=True), g.s_t)
                ctm = g.s_t.n
                wait("vector", g.s_t, ctm)
                run("vector", lambda e, j_=j, c_=colo, b_=cb: e.tensor_copy(
                    g.ARB[0:64, CTX_ + j_ * 512 + c_: CTX_ + j_ * 512 + c_ + 128],
                    g.PS[b_][0:64, 0:128]), g.s_v)
                run("vector", lambda e, j_=j, c_=colo, b_=cb: e.tensor_copy(
                    g.ARB[64:128, CTX_ + j_ * 512 + c_: CTX_ + j_ * 512 + c_ + 128],
                    g.PS[b_][64:128, 128:256]), g.s_v)
                snaps[p] = dict(t=g.s_t.n, v=g.s_v.n, s=g.s_s.n)
                gidx += 1
        ctxv = g.s_v.n
        qevac[0] = qevac[1] = g.s_s.n
        wait("scalar", g.s_t, inproj_t)
        dma("scalar", vb(WIP_, CH * C), g.wowb_d, g.s_ld)
        oww = g.s_ld.n
        wait("tensor", g.s_ld, oww)
        wait("tensor", g.s_v, ctxv)
        for co in range(CH):
            bank = co % 2
            wait("tensor", g.s_v, hpre[bank])
            for ci in range(CH):
                run("tensor", lambda e, b_=bank, cc=ci, cco=co: e.matmul(
                    g.PS[b_][:, 0:512],
                    vb(WIP_ + cc * C + cco * 128, 128),
                    vb(CTX_ + cc * 512, 512),
                    start=(cc == 0), stop=(cc == CH - 1)),
                    g.s_t if ci == CH - 1 else None)
            wait("vector", g.s_t)
            run("vector", lambda e, b_=bank, cco=co: e.tensor_tensor(
                vf(Hh_ + cco * 512, 512), vf(Y_ + cco * 512, 512),
                g.PS[b_][:, 0:512], ALU.add), g.s_v)
            run("vector", lambda e, cco=co: e.tensor_scalar(
                vf(Hh_ + cco * 512, 512), vf(Hh_ + cco * 512, 512),
                g.vcol("ob", cco), None, ALU.add), g.s_v)
            hpre[bank] = g.s_v.n
        def src_h(c, o, w):
            return g.ARF[:, Hh_ + c * 512 + o: Hh_ + c * 512 + o + w]
        g.emit_ln(src_h, src_h, [(0, 512)],
                  lambda c: g.vcol("fg1", c), lambda c: g.vcol("fb1", c))
        hv = g.s_v.n
        wait("sync", g.s_v, hv)
        dma("sync", g.h_d[kk * 128:(kk + 1) * 128, :], vf(Hh_, TILE_COLS), g.s_st)
        wait("tensor", g.s_v, hv)
        wait("tensor", g.s_ld, rtw)
        for ci in range(CH):
            run("tensor", lambda e, cc=ci: e.matmul(
                g.PS[6][0:8, 0:512], g.sGATH[:, cc * 8:(cc + 1) * 8],
                vf(Hh_ + cc * 512, 512), start=(cc == 0), stop=(cc == CH - 1)),
                g.s_t if ci == CH - 1 else None)
        lg = g.s_t.n
        wait("vector", g.s_t, lg)
        run("vector", lambda e: e.tensor_scalar(
            g.sSC[0:8, 0:512], g.PS[6][0:8, 0:512], g.sC8[:, 0:1], None, ALU.add), g.s_v)
        em("vector", lambda e: e.drain())
        lgv = g.s_v.n
        wait("tensor", g.s_v, lgv)
        for bq in range(4):
            run("tensor", lambda e, b_=bq: e.transpose(
                g.PS[7][:, b_ * 8:b_ * 8 + 8], g.sSC[0:8, b_ * 128:(b_ + 1) * 128],
                g.sIDF[0:8, 0:8]), g.s_t)
        lgt = g.s_t.n
        wait("vector", g.s_t, lgt)
        run("vector", lambda e: e.tensor_copy(g.sSC[:, 544:544 + 32], g.PS[7][:, 0:32]), g.s_v)
        em("vector", lambda e: e.drain())
        for bq in range(4):
            lgq = g.sSC[:, 544 + bq * 8:544 + bq * 8 + 8]
            m1 = g.sSC[:, 1080:1081]
            msk = g.sSC[:, 0:8]
            xp = g.sSC[:, 16:24]
            m2 = g.sSC[:, 1081:1082]
            i1 = g.sSC[:, 1082:1083]
            run("vector", lambda e, a=lgq, b_=m1: e.reduce_max(b_, a, axis=AX), g.s_v)
            em("vector", lambda e: e.drain())
            run("vector", lambda e, a=lgq, b_=m1, c_=msk: e.tensor_tensor(
                c_, a, b_.to_broadcast([128, 8]), ALU.is_equal), g.s_v)
            em("vector", lambda e: e.drain())
            run("vector", lambda e, a=lgq, c_=msk, d_=xp: e.scalar_tensor_tensor(
                d_, c_, -1e30, a, ALU.mult, ALU.add), g.s_v)
            em("vector", lambda e: e.drain())
            run("vector", lambda e, d_=xp, b_=m2: e.reduce_max(b_, d_, axis=AX), g.s_v)
            em("vector", lambda e: e.drain())
            run("vector", lambda e, a=lgq, b_=m2, c_=msk: e.tensor_tensor(
                c_, b_.to_broadcast([128, 8]), a, ALU.is_le), g.s_v)
            em("vector", lambda e: e.drain())
            run("vector", lambda e, c_=msk: e.tensor_tensor(c_, c_, g.sIOTA8[:], ALU.mult), g.s_v)
            em("vector", lambda e: e.drain())
            run("vector", lambda e, c_=msk, i_=i1: e.reduce_max(i_, c_, axis=AX), g.s_v)
            em("vector", lambda e: e.drain())
            asn = g.s_v.n
            wait("tensor", g.s_v, asn)
            run("tensor", lambda e, i_=i1: e.transpose(
                g.PS[7][0:1, 128:256], i_, g.sIDF[:]), g.s_t)
            att2 = g.s_t.n
            wait("vector", g.s_t, att2)
            run("vector", lambda e: e.tensor_copy(
                g.sSC[0:1, 544:672], g.PS[7][0:1, 128:256]), g.s_v)
            em("vector", lambda e: e.drain())
            wait("sync", g.s_v)
            dma("sync", g.asg_d[0:1, kk * 512 + bq * 128: kk * 512 + (bq + 1) * 128],
                g.sSC[0:1, 544:672], g.s_st)
        if kk < NT - 1:
            s3w = wip_load(0)
    if "h" in g.dumps:
        wait("sync", g.s_st, g.s_st.n)
        dma("sync", g.dumps["h"], g.h_d, g.s_st)
    if "assign" in g.dumps:
        wait("sync", g.s_st, g.s_st.n)
        dma("sync", g.dumps["assign"], g.asg_d, g.s_st)


def _s4(g):
    em, wait, dma, run, vf, vb = g.em, g.wait, g.dma, g.run, g.vf, g.vb
    H_, Yt_, M67_ = 0, TILE_COLS, 2 * TILE_COLS
    HBV = g.ARF[:, 10240:12288].bitcast(BF16)
    HKe_ = 0
    W1_ = [4096, 12288]
    W2_ = [20480, 28672]

    def w_load(e_, par):
        wait("sync", g.s_t)
        dma("sync", vb(W1_[par], CH * C), g.w1b_d[e_ * 128:(e_ + 1) * 128, :], g.s_ld)
        wait("scalar", g.s_t)
        dma("scalar", vb(W2_[par], CH * C), g.w2b_d[e_ * 128:(e_ + 1) * 128, :], g.s_ld)
        return g.s_ld.n

    wait("sync", g.s_gc, g.prep_done)
    wait("scalar", g.s_gc, g.prep_done)
    wready = [None] * NE
    gelu_done = {6: 0, 7: 0}
    vadd_done = {6: 0, 7: 0}

    for kk in range(NT):
        wait("sync", g.s_v)
        wait("sync", g.s_t)
        wait("sync", g.s_s)
        wait("sync", g.s_st, g.s_st.n)
        dma("sync", vf(H_, TILE_COLS), g.h_d[kk * 128:(kk + 1) * 128, :], g.s_ld)
        wait("gpsimd", g.s_st, g.s_st.n)
        dma("gpsimd", g.sASG[:], g.asg_d[0:1, kk * 512:(kk + 1) * 512], g.s_gc)
        asgl = g.s_gc.n
        ldh = g.s_ld.n
        wait("vector", g.s_ld, ldh)
        run("vector", lambda e: e.tensor_copy(HBV, vf(H_, TILE_COLS)), g.s_v)
        em("vector", lambda e: e.drain())
        g.hbc = g.s_v.n
        wait("tensor", g.s_v)
        wait("tensor", g.s_gc, asgl)
        run("tensor", lambda e: e.matmul(
            g.PS[6][0:8, 0:512], g.sONES8[:],
            g.sASG[0:1, :], start=True, stop=True), g.s_t)
        mb = g.s_t.n
        wait("vector", g.s_t, mb)
        run("vector", lambda e: e.tensor_tensor(
            g.sM8[:], g.PS[6][0:8, 0:512], g.sEID[:], ALU.is_equal), g.s_v)
        em("vector", lambda e: e.drain())
        if kk == 0 and "m80" in g.dumps:
            run("vector", lambda e: e.tensor_copy(g.sSC[0:8, 0:512], g.sM8[:]), g.s_v)
            em("vector", lambda e: e.drain())
            wait("sync", g.s_v)
            dma("sync", g.dumps["m80"], g.sSC[0:8, 0:512], g.s_st)
            wait("vector", g.s_st, g.s_st.n)
            run("vector", lambda e: e.tensor_copy(g.sSC[0:8, 0:512], g.PS[6][0:8, 0:512]), g.s_v)
            em("vector", lambda e: e.drain())
            wait("sync", g.s_v)
            dma("sync", g.dumps["bc0"], g.sSC[0:8, 0:512], g.s_st)
            wait("vector", g.s_st, g.s_st.n)
            run("vector", lambda e: e.tensor_copy(g.sSC[0:1, 0:512], g.sASG[:]), g.s_v)
            em("vector", lambda e: e.drain())
            wait("sync", g.s_v)
            dma("sync", g.dumps["asg0"], g.sSC[0:1, 0:512], g.s_st)
            wait("vector", g.s_st, g.s_st.n)
        run("vector", lambda e: e.tensor_scalar(
            g.sNM8[:], g.sM8[:], -1.0, 1.0, ALU.mult, ALU.add), g.s_v)
        em("vector", lambda e: e.drain())
        m8v = g.s_v.n
        wait("tensor", g.s_v, m8v)
        for co in range(CH):
            if co < 6:
                run("tensor", lambda e, cc=co: e.matmul(
                    g.PS[cc][:, 0:512], g.sB2[:, cc * 128:(cc + 1) * 128],
                    g.sM8[:], start=True, stop=False), g.s_t)
            else:
                run("tensor", lambda e, cc=co: e.matmul(
                    g.PS[7][:, 0:512], g.sB2[:, cc * 128:(cc + 1) * 128],
                    g.sM8[:], start=True, stop=True), g.s_t)
                wait("vector", g.s_t)
                run("vector", lambda e, cc=co: e.tensor_copy(
                    vf(M67_ + (cc - 6) * 512, 512), g.PS[7][:, 0:512]), g.s_v)
                wait("tensor", g.s_v)
        vadd_done[6] = vadd_done[7] = g.s_v.n
        wait("tensor", g.s_v, g.hbc)
        for e_ in range(NE):
            if wready[e_] is None:
                wready[e_] = w_load(e_, e_ % 2)
            if e_ + 1 < NE and wready[e_ + 1] is None:
                wready[e_ + 1] = w_load(e_ + 1, (e_ + 1) % 2)
            wait("tensor", g.s_ld, wready[e_])
            w1b = W1_[e_ % 2]
            w2b = W2_[e_ % 2]
            gfi = []
            for fi in range(CH):
                hbank = 6 + fi % 2
                wait("tensor", g.s_s, gelu_done[hbank])
                wait("tensor", g.s_v, vadd_done[hbank])
                run("tensor", lambda e, b_=hbank, ee=e_: e.matmul(
                    g.PS[b_][:, 0:512], g.sNEG9[:, ee * 128:(ee + 1) * 128],
                    g.sNM8[:], start=True, stop=False), None)
                for ci in range(CH):
                    run("tensor", lambda e, b_=hbank, cc=ci, ff=fi, wb=w1b: e.matmul(
                        g.PS[b_][:, 0:512],
                        vb(wb + cc * C + ff * 128, 128),
                        HBV[:, cc * 512:(cc + 1) * 512],
                        start=False, stop=(cc == CH - 1)),
                        g.s_t if ci == CH - 1 else None)
                wait("scalar", g.s_t)
                run("scalar", lambda e, b_=hbank, ee=e_, ff=fi: e.activation(
                    vb(HKe_ + ff * 512, 512), g.PS[b_][:, 0:512], AF.Gelu,
                    bias=g.sVEC[:, 152 + ee * 8 + ff:153 + ee * 8 + ff]), g.s_s)
                gelu_done[hbank] = g.s_s.n
                gfi.append(g.s_s.n)
            hkv = g.s_s.n
            if kk == 0 and e_ == 0 and "he0" in g.dumps:
                wait("vector", g.s_s, hkv)
                run("vector", lambda e: e.tensor_copy(vf(Yt_, 4096), vb(HKe_, 4096)), g.s_v)
                em("vector", lambda e: e.drain())
                wait("sync", g.s_v)
                dma("sync", g.dumps["he0"], vf(Yt_, 4096), g.s_st)
                wait("vector", g.s_st, g.s_st.n)
                wait("scalar", g.s_st, g.s_st.n)
                wait("tensor", g.s_st, g.s_st.n)
            for fi in range(CH):
                wait("tensor", g.s_s, gfi[fi])
                for co in range(6):
                    run("tensor", lambda e, cc=co, ff=fi, wb=w2b,
                            sp=(e_ == NE - 1 and fi == CH - 1): e.matmul(
                        g.PS[cc][:, 0:512],
                        vb(wb + ff * C + cc * 128, 128),
                        vb(HKe_ + ff * 512, 512),
                        start=False, stop=sp),
                        g.s_t if co == 5 else None)
            for co in (6, 7):
                bank = co
                wait("tensor", g.s_v)
                for fi in range(CH):
                    run("tensor", lambda e, b_=bank, cc=co, ff=fi, wb=w2b: e.matmul(
                        g.PS[b_][:, 0:512],
                        vb(wb + ff * C + cc * 128, 128),
                        vb(HKe_ + ff * 512, 512),
                        start=(ff == 0), stop=(ff == CH - 1)),
                        g.s_t if fi == CH - 1 else None)
                wait("vector", g.s_t)
                run("vector", lambda e, cc=co: e.tensor_tensor(
                    vf(M67_ + (cc - 6) * 512, 512), vf(M67_ + (cc - 6) * 512, 512),
                    g.PS[6 + (cc - 6)][:, 0:512], ALU.add), g.s_v)
                vadd_done[bank] = g.s_v.n
        wready = [None] * NE
        tall = g.s_t.n
        wait("vector", g.s_t, tall)
        for co in range(CH):
            if co < 6:
                run("vector", lambda e, cc=co: e.tensor_tensor(
                    vf(H_ + cc * 512, 512), vf(H_ + cc * 512, 512),
                    g.PS[cc][:, 0:512], ALU.add), g.s_v)
            else:
                run("vector", lambda e, cc=co: e.tensor_tensor(
                    vf(H_ + cc * 512, 512), vf(H_ + cc * 512, 512),
                    vf(M67_ + (cc - 6) * 512, 512), ALU.add), g.s_v)
        em("vector", lambda e: e.drain())

        if kk == 0 and "hm0" in g.dumps:
            wait("sync", g.s_v)
            dma("sync", g.dumps["hm0"], vf(H_, TILE_COLS), g.s_st)
            wait("vector", g.s_st, g.s_st.n)
            wait("scalar", g.s_st, g.s_st.n)
            wait("tensor", g.s_st, g.s_st.n)
        def src_hm(c, o, w):
            return g.ARF[:, H_ + c * 512 + o: H_ + c * 512 + o + w]
        g.emit_ln(src_hm, src_hm, [(0, 512)],
                  lambda c: g.vcol("fg2", c), lambda c: g.vcol("fb2", c))
        dma("sync", vf(Yt_, TILE_COLS), g.y1_d[kk * 128:(kk + 1) * 128, :], g.s_ld)
        ldy = g.s_ld.n
        wait("vector", g.s_ld, ldy)
        run("vector", lambda e: e.tensor_tensor(
            vf(H_, TILE_COLS), vf(H_, TILE_COLS), vf(Yt_, TILE_COLS), ALU.add), g.s_v)
        em("vector", lambda e: e.drain())
        g.emit_ln(src_hm, src_hm, [(0, 512)], None, None)
        ov = g.s_v.n
        wait("sync", g.s_v, ov)
        dma("sync", g.out_d[kk * 128:(kk + 1) * 128, :], vf(H_, TILE_COLS), g.s_st)


def _pack_weight(w):
    kin, nout = w.shape
    return np.ascontiguousarray(
        w.reshape(kin // 128, 128, nout).transpose(1, 0, 2).reshape(128, -1))


def _pack_vec(v):
    return np.ascontiguousarray(np.asarray(v, np.float32).reshape(CH, 128).T)


def _stack_groups_ck(wp):
    blocks = []
    for gi in range(8):
        blocks.append(np.concatenate(
            [wp[:, cc * FF + 512 * gi: cc * FF + 512 * gi + 512] for cc in range(CH)], axis=1))
    return np.ascontiguousarray(np.concatenate(blocks, axis=0))


def _stack_halves_ip(wp):
    blocks = []
    for h in range(2):
        blocks.append(np.concatenate(
            [wp[:, ci * 3 * C + h * 1536: ci * 3 * C + (h + 1) * 1536] for ci in range(CH)], axis=1))
    return np.ascontiguousarray(np.concatenate(blocks, axis=0))


def _host_inputs(inp):
    x = np.asarray(inp["L0_attn_x"], np.float32)
    x_pad = np.concatenate([np.zeros((B, 2, C), np.float32), x], axis=1)
    vec = np.zeros((128, 256), np.float32)
    order = ["L0_attn_ln0_g", "L0_attn_ln0_b", "L0_attn_ln1_g", "L0_attn_ln1_b",
             "L0_attn_tm_k", "L0_attn_tm_v", "L0_attn_tm_r", "L0_attn_cm_k",
             "L0_attn_cm_r", "L0_attn_time_decay", "L0_attn_time_first",
             "L0_ffn_out_b", "L0_ffn_ln1_g", "L0_ffn_ln1_b", "L0_ffn_ln2_g",
             "L0_ffn_ln2_b"]
    for idx, key in enumerate(order):
        vec[:, idx * 8:(idx + 1) * 8] = _pack_vec(inp[key])
    vec[:, 128:152] = np.ascontiguousarray(
        np.asarray(inp["L0_ffn_in_proj_b"], np.float32).reshape(24, 128).T)
    vec[:, 152:216] = np.ascontiguousarray(
        np.asarray(inp["L0_ffn_b1"], np.float32).reshape(NE * CH, 128).T)
    c8 = np.zeros((8, 64), np.float32)
    c8[:, 0] = np.asarray(inp["L0_ffn_router_b"], np.float32)
    common = dict(
        wk_t=_pack_weight(np.asarray(inp["L0_attn_Wk"], np.float32)),
        wv_t=_pack_weight(np.asarray(inp["L0_attn_Wv"], np.float32)),
        wr_t=_pack_weight(np.asarray(inp["L0_attn_Wr"], np.float32)),
        wo_t=_pack_weight(np.asarray(inp["L0_attn_Wo"], np.float32)),
        wck_t=_stack_groups_ck(_pack_weight(np.asarray(inp["L0_attn_Wck"], np.float32))),
        wcv_t=np.concatenate([_pack_weight(np.asarray(inp["L0_attn_Wcv"], np.float32))[:, gi * 4096:(gi + 1) * 4096] for gi in range(8)], axis=0),
        wcr_t=_pack_weight(np.asarray(inp["L0_attn_Wcr"], np.float32)),
        wip_t=_stack_halves_ip(_pack_weight(np.ascontiguousarray(np.asarray(inp["L0_ffn_in_proj_w"], np.float32).T))),
        wow_t=_pack_weight(np.ascontiguousarray(np.asarray(inp["L0_ffn_out_w"], np.float32).T)),
        wrt_t=_pack_weight(np.ascontiguousarray(np.asarray(inp["L0_ffn_router_w"], np.float32).T)),
        w1_t=np.concatenate([_pack_weight(np.ascontiguousarray(np.asarray(inp["L0_ffn_w1"][e], np.float32).T)) for e in range(NE)], axis=0),
        w2_t=np.concatenate([_pack_weight(np.ascontiguousarray(np.asarray(inp["L0_ffn_w2"][e], np.float32).T)) for e in range(NE)], axis=0),
        vec32=vec, c8f=c8,
        b2f=np.asarray(inp["L0_ffn_b2"], np.float32),
        eidb=np.ascontiguousarray(np.broadcast_to(
            np.arange(8, dtype=np.float32)[:, None], (8, 512))).astype(NPBF),
        ones1f=np.ones((1, 128), np.float32),
        ones8b=np.ones((1, 8), NPBF),
        neg9b=np.asarray([[(-1e9 if k2 == (m // 128) else 0.0) for m in range(1024)]
                          for k2 in range(8)], NPBF),
        identb=np.eye(128, dtype=NPBF),
        identf=np.eye(128, dtype=np.float32),
        iotarep=np.ascontiguousarray(np.broadcast_to(np.arange(257, dtype=np.float32), (128, 257))),
        iota8=np.ascontiguousarray(np.broadcast_to(np.arange(8, dtype=np.float32), (128, 8))),
        maskA=np.asarray([[1.0 if m // 16 == gg else 0.0 for m in range(128)] for gg in range(8)], NPBF),
        maskB=np.asarray([[0.0 if n // 16 == gg else -1e30 for n in range(128)] for gg in range(8)], NPBF),
    )
    maps = []
    for i in range(NC):
        s = PPC * i
        slabs = [x_pad[:, s:s + 2, :]]
        for kk in range(NT):
            slabs.append(x_pad[:, s + 1 + TP * kk: s + 1 + TP * kk + SEG, :])
        x_core = np.zeros(((1 + NT) * 128, HALO_COLS), np.float32)
        for si, sl in enumerate(slabs):
            L = sl.shape[1]
            x_core[si * 128:(si + 1) * 128, 0:CH * B * L] = (
                sl.transpose(2, 0, 1).reshape(CH, 128, B * L).transpose(1, 0, 2).reshape(128, CH * B * L))
        gates = np.zeros((128, 2), np.float32)
        gates[:, 0] = 0.0 if i == 0 else 1.0
        gates[:, 1] = 1.0 - gates[:, 0]
        gp = np.zeros((128, 8), np.float32)
        gp[:, :i] = 1.0
        m = dict(common)
        m.update(x_t=x_core, gates=gates, gpre=gp, gpre256=np.ascontiguousarray(gp * 256.0))
        maps.append(m)
    return maps


def _assemble(results):
    out = np.zeros((B, TFULL, C), np.float32)
    for i in range(NC):
        arr = np.asarray(results[i]["out_d"], np.float32)
        a = arr.reshape(NT, 128, CH, TP, B)
        out[:, PPC * i:PPC * (i + 1), :] = a.transpose(4, 0, 3, 2, 1).reshape(B, PPC, C)
    return out


def kernel(**inputs):
    maps = _host_inputs(inputs)
    nc = build_program()
    res = run_bass_kernel_spmd(nc, maps, core_ids=list(range(NC)))
    return _assemble(res.results)


# version 13
# speedup vs baseline: 1.1136x; 1.0079x over previous
import numpy as np
import ml_dtypes
import contextlib
import concourse.bass as bass
import concourse.mybir as mybir
from concourse.bass_utils import run_bass_kernel_spmd

F32 = mybir.dt.float32
BF16 = mybir.dt.bfloat16
AF = mybir.ActivationFunctionType
ALU = mybir.AluOpType
AX = mybir.AxisListType.X
NPBF = np.dtype(ml_dtypes.bfloat16)

NC = 8
B = 16
C = 1024
TFULL = 2048
PPC = 256
TP = 32
NT = 8
CH = 8
FF = 4096
NE = 8
SEG = TP + 1
TILE_COLS = CH * B * TP
HALO_COLS = CH * B * SEG
EPS = 1e-5


class SemT:
    def __init__(self, h):
        self.h = h
        self.n = 0

    def inc(self, amt):
        self.n += amt
        return self.n


class Ctx:
    pass


def build_program(dump=()):
    nc = bass.Bass("TRN2", target_bir_lowering=False, debug=False)
    g = Ctx()
    g.nc = nc

    def din(name, shape, dt=F32):
        return nc.dram_tensor(name, list(shape), dt, kind="ExternalInput").ap()

    g.x_t = din("x_t", [(1 + NT) * 128, HALO_COLS])
    g.wk_t = din("wk_t", [128, CH * C])
    g.wv_t = din("wv_t", [128, CH * C])
    g.wr_t = din("wr_t", [128, CH * C])
    g.wo_t = din("wo_t", [128, CH * C])
    g.wck_t = din("wck_t", [8 * 128, CH * 512])
    g.wcv_t = din("wcv_t", [8 * 128, 4 * C])
    g.wcr_t = din("wcr_t", [128, CH * C])
    g.wip_t = din("wip_t", [2 * 128, 12288])
    g.wow_t = din("wow_t", [128, CH * C])
    g.wrt_t = din("wrt_t", [128, CH * 8])
    g.w1_t = din("w1_t", [NE * 128, CH * C])
    g.w2_t = din("w2_t", [NE * 128, CH * C])
    g.vec32 = din("vec32", [128, 256])
    g.c8f = din("c8f", [8, 64])
    g.b2f = din("b2f", [8, C])
    g.eidb = din("eidb", [8, 512], BF16)
    g.ones1f = din("ones1f", [1, 128])
    g.ones8b = din("ones8b", [1, 8], BF16)
    g.neg9b = din("neg9b", [8, 1024], BF16)
    g.identb = din("identb", [128, 128], BF16)
    g.identf = din("identf", [128, 128])
    g.iotarep = din("iotarep", [128, 257])
    g.iota8 = din("iota8", [128, 8])
    g.gates = din("gates", [128, 2])
    g.gpre = din("gpre", [128, 8])
    g.gpre256 = din("gpre256", [128, 8])
    g.maskA = din("maskA", [8, 128], BF16)
    g.selb = din("selb", [8, 1024], BF16)
    g.maskB = din("maskB", [8, 128], BF16)

    g.out_d = nc.dram_tensor("out_d", [NT * 128, TILE_COLS], F32, kind="ExternalOutput").ap()
    g.dumps = {}
    for nm in dump:
        shape = {"hm0": [128, TILE_COLS], "m80": [8, 512], "he0": [128, 4096],
                 "asg0": [1, 512], "bc0": [8, 512],
                 "hk0": [128, 2048], "x1p": [128, 128], "wkvp": [128, 128],
                 "x1": [128, TILE_COLS], "xc": [128, HALO_COLS],
                 "kv": [128, TILE_COLS], "rr": [128, TILE_COLS],
                 "yp": [128, TILE_COLS], "wkv": [128, TILE_COLS],
                 "dbgP": [128, HALO_COLS], "dbgF": [128, HALO_COLS],
                 "dbgN": [128, HALO_COLS], "dbgD": [128, HALO_COLS],
                 "num": [128, 256 + NT * TILE_COLS], "den": [128, 256 + NT * TILE_COLS],
                 "r": [128, 256 + NT * TILE_COLS], "y1": [128, NT * TILE_COLS],
                 "h": [128, NT * TILE_COLS], "seeds": [128, 256],
                 "assign": [1, 4096]}[nm]
        g.dumps[nm] = nc.dram_tensor("dump_" + nm, shape, F32, kind="ExternalOutput").ap()

    g.num_d = nc.dram_tensor("num_d", [(1 + NT) * 128, TILE_COLS], F32).ap()
    g.den_d = nc.dram_tensor("den_d", [(1 + NT) * 128, TILE_COLS], F32).ap()
    g.r_d = nc.dram_tensor("r_d", [(1 + NT) * 128, TILE_COLS], F32).ap()
    g.y1_d = nc.dram_tensor("y1_d", [NT * 128, TILE_COLS], F32).ap()
    g.h_d = nc.dram_tensor("h_d", [NT * 128, TILE_COLS], F32).ap()
    g.wob_d = nc.dram_tensor("wob_d", [128, CH * C], BF16).ap()
    g.wcrb_d = nc.dram_tensor("wcrb_d", [128, CH * C], BF16).ap()
    g.wckb_d = nc.dram_tensor("wckb_d", [8 * 128, CH * 512], BF16).ap()
    g.wcvb_d = nc.dram_tensor("wcvb_d", [8 * 128, 4 * C], BF16).ap()
    g.wipb_d = nc.dram_tensor("wipb_d", [2 * 128, 12288], BF16).ap()
    g.wowb_d = nc.dram_tensor("wowb_d", [128, CH * C], BF16).ap()
    g.w1b_d = nc.dram_tensor("w1b_d", [NE * 128, CH * C], BF16).ap()
    g.w2b_d = nc.dram_tensor("w2b_d", [NE * 128, CH * C], BF16).ap()
    g.asg_d = nc.dram_tensor("asg_d", [1, 4096], F32).ap()
    g.cc_in = nc.dram_tensor("cc_in", [128, 256], F32).ap()
    g.cc_out = nc.dram_tensor("cc_out", [1024, 256], F32).ap()

    es = contextlib.ExitStack()
    with es:
        def sbt(name, shape, dt=F32):
            return es.enter_context(nc.sbuf_tensor(name, list(shape), dt))

        g.sVEC = sbt("sVEC", [128, 256])
        g.sC8 = sbt("sC8", [8, 64])
        g.sB2 = sbt("sB2", [8, C], BF16)
        g.sEID = sbt("sEID", [8, 512], BF16)
        g.sONES1 = sbt("sONES1", [1, 128])
        g.sONES8 = sbt("sONES8", [1, 8], BF16)
        g.sSELB = sbt("sSELB", [8, 1024], BF16)
        g.sNEG9 = sbt("sNEG9", [8, 1024], BF16)
        g.sIDB = sbt("sIDB", [128, 128], BF16)
        g.sIDF = sbt("sIDF", [128, 128])
        g.sIOTA = sbt("sIOTA", [128, 257])
        g.sIOTA8 = sbt("sIOTA8", [128, 8])
        g.sGATE = sbt("sGATE", [128, 2])
        g.sGPRE = sbt("sGPRE", [128, 8])
        g.sGP256 = sbt("sGP256", [128, 8])
        g.sMA = sbt("sMA", [8, 128], BF16)
        g.sMB = sbt("sMB", [8, 128], BF16)
        g.sONE1c = sbt("sONE1c", [128, 1])
        g.sDRV = sbt("sDRV", [128, 104])
        g.sROW = sbt("sROW", [1, 1600])
        g.sSEED = sbt("sSEED", [128, 512])
        g.sGATH = sbt("sGATH", [128, 512])
        g.sSNSD = sbt("sSNSD", [128, 256])
        g.sXC = sbt("sXC", [128, 128])
        g.sASG = sbt("sASG", [1, 512], BF16)
        g.sM8 = sbt("sM8", [8, 512], BF16)
        g.sNM8 = sbt("sNM8", [8, 512], BF16)
        g.sSC = sbt("sSC", [128, 1088])
        g.sLREP = sbt("sLREP", [128, CH * SEG * B])
        g.sLAM = g.sLREP
        g.sLPRE = sbt("sLPRE", [128, CH * B * 2])
        g.ARF = sbt("ARF", [128, 21120])
        g.ARB = sbt("ARB", [128, 36864], BF16)
        g.PS = [es.enter_context(nc.psum_tensor("ps%d" % i, [128, 512], F32))
                for i in range(8)]

        g.s_ld = SemT(es.enter_context(nc.semaphore("s_ld")))
        g.s_gc = SemT(es.enter_context(nc.semaphore("s_gc")))
        g.s_t = SemT(es.enter_context(nc.semaphore("s_t")))
        g.s_v = SemT(es.enter_context(nc.semaphore("s_v")))
        g.s_s = SemT(es.enter_context(nc.semaphore("s_s")))
        g.s_st = SemT(es.enter_context(nc.semaphore("s_st")))
        g.s_cc = SemT(es.enter_context(nc.semaphore("s_cc")))

        g.P = {"sync": [], "tensor": [], "vector": [], "scalar": [], "gpsimd": []}

        def em(eng, fn):
            g.P[eng].append(fn)

        def wait(eng, sem, v=None):
            vv = sem.n if v is None else v
            em(eng, lambda e, h=sem.h, w=vv: e.wait_ge(h, w))

        def dma(eng, dst, src, sem, slow=False):
            sem.inc(16)
            em(eng, lambda e, d=dst, s=src, h=sem.h, sl=slow: e.dma_start(
                out=d, in_=s, allow_slow_non_contiguous=sl).then_inc(h, 16))

        def run(eng, fn, sem=None, amt=1):
            if sem is None:
                em(eng, fn)
            else:
                sem.inc(amt)
                em(eng, lambda e, f=fn, h=sem.h, a=amt: f(e).then_inc(h, a))

        g.em, g.wait, g.dma, g.run = em, wait, dma, run

        def vf(off, n):
            return g.ARF[:, off:off + n]

        def vb(off, n):
            return g.ARB[:, off:off + n]

        g.vf, g.vb = vf, vb

        VOFF = dict(g0=0, b0=8, g1=16, b1=24, tmk=32, tmv=40, tmr=48, cmk=56,
                    cmr=64, td=72, tf=80, ob=88, fg1=96, fb1=104, fg2=112,
                    fb2=120, ipb=128, b1e=152)
        DOFF = dict(w=0, lam=8, eu=16, bq8=24, fpre=32)

        def vcol(name, j=0):
            o = VOFF[name] + j
            return g.sVEC[:, o:o + 1]

        def dcol(name, j=0):
            o = DOFF[name] + j
            return g.sDRV[:, o:o + 1]

        g.vcol, g.dcol = vcol, dcol

        def emit_ln(get_src, put_out, sub_ranges, gcol=None, bcol=None):
            for (o, w) in sub_ranges:
                wait("tensor", g.s_v)
                wait("tensor", g.s_s)
                wait("scalar", g.s_v)
                for c in range(CH):
                    run("tensor", lambda e, ap=get_src(c, o, w), ww=w, cc=c: e.matmul(
                        g.PS[4][0:1, 0:ww], g.sONE1c[:], ap,
                        start=(cc == 0), stop=(cc == CH - 1)), g.s_t)
                for c in range(CH):
                    if c >= 2:
                        wait("scalar", g.s_t)
                    run("scalar", lambda e, ap=get_src(c, o, w), cc=c, ww=w: e.activation(
                        g.sSC[:, (cc % 2) * 544:(cc % 2) * 544 + ww],
                        ap, AF.Square), g.s_s)
                    wait("tensor", g.s_s)
                    run("tensor", lambda e, cc=c, ww=w: e.matmul(
                        g.PS[5][0:1, 0:ww], g.sONE1c[:],
                        g.sSC[:, (cc % 2) * 544:(cc % 2) * 544 + ww],
                        start=(cc == 0), stop=(cc == CH - 1)), g.s_t)
                tmm = g.s_t.n
                wait("vector", g.s_t, tmm)
                run("vector", lambda e, ww=w: e.tensor_copy(
                    g.sROW[0:1, 0:ww], g.PS[4][0:1, 0:ww]), g.s_v)
                em("vector", lambda e: e.drain())
                run("vector", lambda e, ww=w: e.tensor_tensor(
                    g.sROW[0:1, 1024:1024 + ww], g.sROW[0:1, 0:ww],
                    g.sROW[0:1, 0:ww], ALU.mult), g.s_v)
                em("vector", lambda e: e.drain())
                run("vector", lambda e, ww=w: e.tensor_tensor(
                    g.sROW[0:1, 512:512 + ww], g.PS[5][0:1, 0:ww],
                    g.sROW[0:1, 1024:1024 + ww], ALU.subtract), g.s_v)
                vrow = g.s_v.n
                wait("scalar", g.s_v, vrow)
                run("scalar", lambda e, ww=w: e.activation(
                    g.sROW[0:1, 512:512 + ww], g.sROW[0:1, 512:512 + ww],
                    AF.Ln, bias=g.sDRV[0:1, 96:97]), g.s_s)
                em("scalar", lambda e: e.drain())
                run("scalar", lambda e, ww=w: e.activation(
                    g.sROW[0:1, 512:512 + ww], g.sROW[0:1, 512:512 + ww],
                    AF.Exp, scale=-0.5), g.s_s)
                srow = g.s_s.n
                wait("tensor", g.s_s, srow)
                run("tensor", lambda e, ww=w: e.matmul(
                    g.PS[6][:, 0:ww], g.sONES1[:], g.sROW[0:1, 0:ww],
                    start=True, stop=True), g.s_t)
                run("tensor", lambda e, ww=w: e.matmul(
                    g.PS[7][:, 0:ww], g.sONES1[:], g.sROW[0:1, 512:512 + ww],
                    start=True, stop=True), g.s_t)
                tb = g.s_t.n
                wait("vector", g.s_t, tb)
                for c in range(CH):
                    run("vector", lambda e, po=put_out(c, o, w), gs=get_src(c, o, w), ww=w: e.tensor_tensor(
                        po, gs, g.PS[6][:, 0:ww], ALU.subtract), g.s_v)
                for c in range(CH):
                    run("vector", lambda e, po=put_out(c, o, w), ww=w: e.tensor_tensor(
                        po, po, g.PS[7][:, 0:ww], ALU.mult), g.s_v)
                if gcol is not None:
                    for c in range(CH):
                        run("vector", lambda e, po=put_out(c, o, w), gc=gcol(c), bc=bcol(c): e.tensor_scalar(
                            po, po, gc, bc, ALU.mult, ALU.add), g.s_v)

        g.emit_ln = emit_ln

        _preamble(g)
        _s1(g)
        _collective(g)
        _s2(g)
        _s3(g)
        _s4(g)

        wait("sync", g.s_st, g.s_st.n)
        wait("sync", g.s_gc, g.s_gc.n)

        with nc.Block() as block:
            @block.sync
            def _(q):
                for fn in g.P["sync"]:
                    fn(q)

            @block.tensor
            def _(q):
                for fn in g.P["tensor"]:
                    fn(q)

            @block.vector
            def _(q):
                for fn in g.P["vector"]:
                    fn(q)

            @block.scalar
            def _(q):
                for fn in g.P["scalar"]:
                    fn(q)

            @block.gpsimd
            def _(q):
                for fn in g.P["gpsimd"]:
                    fn(q)
    return nc


def _preamble(g):
    em, wait, dma, run = g.em, g.wait, g.dma, g.run
    for dst, src in [(g.sVEC, g.vec32), (g.sC8, g.c8f), (g.sONES1, g.ones1f),
                     (g.sONES8, g.ones8b), (g.sNEG9, g.neg9b),
                     (g.sIDB, g.identb), (g.sIDF, g.identf),
                     (g.sIOTA, g.iotarep), (g.sIOTA8, g.iota8),
                     (g.sGATE, g.gates), (g.sGPRE, g.gpre),
                     (g.sGP256, g.gpre256), (g.sMA, g.maskA),
                     (g.sMB, g.maskB), (g.sEID, g.eidb),
                     (g.sSELB, g.selb)]:
        dma("sync", dst[:], src, g.s_ld)
    dma("gpsimd", g.sB2[:], g.b2f, g.s_gc)

    run("vector", lambda e: e.memset(g.sONE1c[:], 1.0 / 1024.0), g.s_v)
    run("vector", lambda e: e.memset(g.sDRV[:, 96:97], float(EPS)), g.s_v)
    wait("scalar", g.s_ld)
    run("scalar", lambda e: e.activation(g.sDRV[:, 0:8], g.sVEC[:, 72:80], AF.Exp), g.s_s)
    wait("vector", g.s_s)
    run("vector", lambda e: e.tensor_scalar_mul(g.sDRV[:, 0:8], g.sDRV[:, 0:8], -1.0), g.s_v)
    em("vector", lambda e: e.drain())
    wait("scalar", g.s_v)
    run("scalar", lambda e: e.activation(g.sDRV[:, 8:16], g.sDRV[:, 0:8], AF.Exp), g.s_s)
    run("scalar", lambda e: e.activation(g.sDRV[:, 16:24], g.sVEC[:, 80:88], AF.Exp), g.s_s)
    for j in range(NC):
        run("scalar", lambda e, jj=j: e.activation(
            g.sDRV[:, 32 + 8 * jj:40 + 8 * jj], g.sDRV[:, 0:8], AF.Exp,
            scale=g.sGP256[:, jj:jj + 1]), g.s_s)
    sd = g.s_s.n
    wait("vector", g.s_s, sd)
    run("vector", lambda e: e.tensor_scalar_mul(g.sDRV[:, 24:32], g.sVEC[:, 128:136], 0.125), g.s_v)
    run("vector", lambda e: e.memset(g.sLREP[:], 1.0), g.s_v)
    run("vector", lambda e: e.memset(g.sLPRE[:], 1.0), g.s_v)
    em("vector", lambda e: e.drain())
    for chi in range(CH):
        run("vector", lambda e, c=chi: e.tensor_scalar_mul(
            g.sLREP[:, c * 528:(c + 1) * 528], g.sLREP[:, c * 528:(c + 1) * 528],
            g.sDRV[:, 8 + c:9 + c]), g.s_v)
        run("vector", lambda e, c=chi: e.tensor_scalar_mul(
            g.sLPRE[:, c * 32:(c + 1) * 32], g.sLPRE[:, c * 32:(c + 1) * 32],
            g.sDRV[:, 8 + c:9 + c]), g.s_v)
    em("vector", lambda e: e.drain())
    run("vector", lambda e: e.memset(
        g.sLREP[:].rearrange("p (c b t) -> p c b t", c=CH, b=B)[:, :, :, 0:1], 0.0), g.s_v)
    run("vector", lambda e: e.memset(
        g.sLPRE[:].rearrange("p (c b t) -> p c b t", c=CH, b=B, t=2)[:, :, :, 0:1], 0.0), g.s_v)
    run("vector", lambda e: e.memset(g.sSEED[:], 0.0), g.s_v)
    em("vector", lambda e: e.drain())
    WKB_, WVB_, WRB_ = 2 * TILE_COLS, 2 * TILE_COLS + CH * C, 2 * TILE_COLS + 2 * CH * C
    for dst_off, src in [(WKB_, g.wk_t), (WVB_, g.wv_t), (WRB_, g.wr_t)]:
        dma("gpsimd", g.vb(dst_off, CH * C), src, g.s_gc)
    g.s1w = g.s_gc.n
    dma("gpsimd", g.wob_d, g.wo_t, g.s_gc)
    dma("gpsimd", g.wcrb_d, g.wcr_t, g.s_gc)
    dma("gpsimd", g.wckb_d, g.wck_t, g.s_gc)
    dma("gpsimd", g.wcvb_d, g.wcv_t, g.s_gc)
    g.prep_s2 = g.s_gc.n
    dma("gpsimd", g.wipb_d, g.wip_t, g.s_gc)
    dma("gpsimd", g.wowb_d, g.wow_t, g.s_gc)
    g.prep_s3 = g.s_gc.n
    dma("gpsimd", g.w1b_d, g.w1_t, g.s_gc)
    dma("gpsimd", g.w2b_d, g.w2_t, g.s_gc)
    g.prep_done = g.s_gc.n


def _s1(g):
    em, wait, dma, run, vf, vb = g.em, g.wait, g.dma, g.run, g.vf, g.vb
    A_, Bf_, Cp_, Dn_, Ed_ = 0, HALO_COLS, 2 * HALO_COLS, 3 * HALO_COLS, 4 * HALO_COLS
    MX0_, MX1_, MX2_ = 0, TILE_COLS, 8 * TILE_COLS
    WKB_, WVB_, WRB_ = 2 * TILE_COLS, 2 * TILE_COLS + CH * C, 2 * TILE_COLS + 2 * CH * C

    dma("sync", vf(A_, CH * B * 2), g.x_t[0:128, 0:256], g.s_ld)
    g.ldx_next = g.s_ld.n
    fevac = {0: 0, 1: 0}
    pmult = {2: 0, 3: 0}
    revac = {4: 0, 5: 0}

    for kk in range(-1, NT):
        pre = (kk == -1)
        seg = 2 if pre else SEG
        cols = CH * B * seg
        tokw = B * (seg - 1)

        def r4(base, _c=cols, _s=seg):
            return g.ARF[:, base:base + _c].rearrange("p (c b t) -> p c b t", c=CH, b=B)

        def chsl(base, c, a, bb, _s=seg):
            return g.ARF[:, base + c * (B * _s): base + (c + 1) * (B * _s)].rearrange(
                "p (b t) -> p b t", b=B)[:, :, a:bb]

        ldx = g.ldx_next
        rt_prev = g.s_t.n
        st_prev = g.s_st.n

        wait("vector", g.s_ld, ldx)
        wait("scalar", g.s_ld, ldx)
        wait("tensor", g.s_ld, ldx)
        if kk == 0:
            wait("scalar", g.s_st, st_prev)
        subs = [(0, B * seg)] if pre else [(0, 512), (512, 16)]

        def src_xa(c, o, w, _base=A_, _s=seg):
            return g.ARF[:, _base + c * (B * _s) + o: _base + c * (B * _s) + o + w]
        g.emit_ln(src_xa, src_xa, subs, lambda c: g.vcol("g0", c), lambda c: g.vcol("b0", c))
        if pre:
            run("vector", lambda e, cc=cols: e.tensor_scalar_mul(vf(A_, cc), vf(A_, cc), g.sGATE[:, 0:1]), g.s_v)
        elif kk == 0:
            run("vector", lambda e, ap=r4(A_)[:, :, :, 0:1]: e.tensor_scalar_mul(
                ap, ap, g.sGATE[:, 0:1]), g.s_v)
        em("vector", lambda e: e.drain())
        run("vector", lambda e, ad=r4(Cp_)[:, :, :, 1:seg], ac=r4(A_)[:, :, :, 1:seg],
                ap_=r4(A_)[:, :, :, 0:seg - 1]: e.tensor_tensor(ad, ac, ap_, ALU.subtract), g.s_v)
        em("vector", lambda e: e.drain())

        wait("vector", g.s_t, rt_prev)
        for mi, (nm, moff) in enumerate([("tmr", MX2_), ("tmk", MX0_), ("tmv", MX1_)]):
            for c in range(CH):
                run("vector", lambda e, ot=vb(moff + c * tokw, tokw), i0=chsl(Cp_, c, 1, seg),
                        sc=g.vcol(nm, c), i1=chsl(A_, c, 0, seg - 1): e.scalar_tensor_tensor(
                    ot, i0, sc, i1, ALU.mult, ALU.add), g.s_v)
            mixv = g.s_v.n
            if mi == 1:
                wait("tensor", g.s_v, mixv)
                if kk <= 0:
                    wait("tensor", g.s_gc, g.s1w)
                wait("scalar", g.s_st, st_prev)
                for co in range(CH):
                    bank = co % 2
                    wait("tensor", g.s_s, fevac[bank])
                    for ci in range(CH):
                        run("tensor", lambda e, b_=bank, cc=ci, cco=co, tw=tokw: e.matmul(
                            g.PS[b_][:, 0:tw],
                            vb(WKB_ + cc * C + cco * 128, 128),
                            vb(MX0_ + cc * tw, tw),
                            start=(cc == 0), stop=(cc == CH - 1)),
                            g.s_t if ci == CH - 1 else None)
                    wait("scalar", g.s_t)
                    run("scalar", lambda e, b_=bank, ot=chsl(Bf_, co, 1, seg), tw=tokw: e.activation(
                        ot, g.PS[b_][:, 0:tw], AF.Exp), g.s_s)
                    fevac[bank] = g.s_s.n
            elif mi == 2:
                wait("tensor", g.s_v, mixv)
                for co in range(CH):
                    bank = 2 + co % 2
                    wait("tensor", g.s_v, pmult[bank])
                    for ci in range(CH):
                        run("tensor", lambda e, b_=bank, cc=ci, cco=co, tw=tokw: e.matmul(
                            g.PS[b_][:, 0:tw],
                            vb(WVB_ + cc * C + cco * 128, 128),
                            vb(MX1_ + cc * tw, tw),
                            start=(cc == 0), stop=(cc == CH - 1)),
                            g.s_t if ci == CH - 1 else None)
                    wait("vector", g.s_t)
                    wait("vector", g.s_s)
                    run("vector", lambda e, b_=bank, ot=chsl(Cp_, co, 1, seg),
                            i0=chsl(Bf_, co, 1, seg), tw=tokw: e.tensor_tensor(
                        ot, i0, g.PS[b_][:, 0:tw], ALU.mult), g.s_v)
                    pmult[bank] = g.s_v.n
        if kk < NT - 1:
            nxt = kk + 1
            wait("sync", g.s_v, mixv)
            dma("sync", vf(A_, HALO_COLS),
                g.x_t[(1 + nxt) * 128:(2 + nxt) * 128, :], g.s_ld)
            g.ldx_next = g.s_ld.n

        if pre:
            wait("vector", g.s_s)
            run("vector", lambda e, cc=cols: e.tensor_scalar_mul(vf(Cp_, cc), vf(Cp_, cc), g.sGATE[:, 0:1]), g.s_v)
            run("vector", lambda e, cc=cols: e.tensor_scalar_mul(vf(Bf_, cc), vf(Bf_, cc), g.sGATE[:, 0:1]), g.s_v)
            em("vector", lambda e: e.drain())
            run("vector", lambda e, ap=r4(Cp_)[:, :, :, 0:1]: e.memset(ap, 0.0), g.s_v)
            run("vector", lambda e, ap=r4(Bf_)[:, :, :, 0:1]: e.memset(ap, 0.0), g.s_v)
        else:
            wait("vector", g.s_s)
            run("vector", lambda e, ap=r4(Cp_)[:, :, :, 0:1]: e.tensor_copy(
                ap, g.sSEED[:, 0:128].rearrange("p (c b o) -> p c b o", c=CH, b=B)), g.s_v)
            run("vector", lambda e, ap=r4(Bf_)[:, :, :, 0:1]: e.tensor_copy(
                ap, g.sSEED[:, 128:256].rearrange("p (c b o) -> p c b o", c=CH, b=B)), g.s_v)
        em("vector", lambda e: e.drain())
        wait("vector", g.s_st, st_prev)
        lrep = g.sLPRE if pre else g.sLREP
        for c in range(CH):
            w_ = B * seg
            run("vector", lambda e, cc=c, ww=w_, lr=lrep: e.tensor_tensor_scan(
                vf(Dn_ + cc * ww, ww), lr[:, cc * ww:(cc + 1) * ww],
                vf(Cp_ + cc * ww, ww), 0.0, ALU.mult, ALU.add), g.s_v)
        for c in range(CH):
            w_ = B * seg
            run("vector", lambda e, cc=c, ww=w_, lr=lrep: e.tensor_tensor_scan(
                vf(Ed_ + cc * ww, ww), lr[:, cc * ww:(cc + 1) * ww],
                vf(Bf_ + cc * ww, ww), 0.0, ALU.mult, ALU.add), g.s_v)
        em("vector", lambda e: e.drain())
        run("vector", lambda e, ap=r4(Dn_)[:, :, :, seg - 1:seg]: e.tensor_copy(
            g.sSEED[:, 0:128].rearrange("p (c b o) -> p c b o", c=CH, b=B), ap), g.s_v)
        run("vector", lambda e, ap=r4(Ed_)[:, :, :, seg - 1:seg]: e.tensor_copy(
            g.sSEED[:, 128:256].rearrange("p (c b o) -> p c b o", c=CH, b=B), ap), g.s_v)
        if kk == NT - 1:
            run("vector", lambda e, ap=r4(Dn_)[:, :, :, seg - 2:seg - 1]: e.tensor_copy(
                g.sSEED[:, 256:384].rearrange("p (c b o) -> p c b o", c=CH, b=B), ap), g.s_v)
            run("vector", lambda e, ap=r4(Ed_)[:, :, :, seg - 2:seg - 1]: e.tensor_copy(
                g.sSEED[:, 384:512].rearrange("p (c b o) -> p c b o", c=CH, b=B), ap), g.s_v)
        em("vector", lambda e: e.drain())
        for c in range(CH):
            run("vector", lambda e, ot=chsl(Dn_, c, 0, seg - 1), i0=chsl(Cp_, c, 1, seg),
                    sc=g.dcol("eu", c): e.scalar_tensor_tensor(
                ot, i0, sc, ot, ALU.mult, ALU.add), g.s_v)
        for c in range(CH):
            run("vector", lambda e, ot=chsl(Ed_, c, 0, seg - 1), i0=chsl(Bf_, c, 1, seg),
                    sc=g.dcol("eu", c): e.scalar_tensor_tensor(
                ot, i0, sc, ot, ALU.mult, ALU.add), g.s_v)
        if pre:
            run("vector", lambda e, ap=r4(Ed_)[:, :, :, 0:1]: e.tensor_scalar(
                ap, ap, g.sGATE[:, 1:2], None, ALU.add), g.s_v)
        numden = g.s_v.n
        wait("tensor", g.s_v, numden)
        for co in range(CH):
            bank = 4 + co % 2
            wait("tensor", g.s_s, revac[bank])
            for ci in range(CH):
                run("tensor", lambda e, b_=bank, cc=ci, cco=co, tw=tokw: e.matmul(
                    g.PS[b_][:, 0:tw],
                    vb(WRB_ + cc * C + cco * 128, 128),
                    vb(MX2_ + cc * tw, tw),
                    start=(cc == 0), stop=(cc == CH - 1)),
                    g.s_t if ci == CH - 1 else None)
            wait("scalar", g.s_t)
            run("scalar", lambda e, b_=bank, ot=chsl(Cp_, co, 1, seg), tw=tokw: e.activation(
                ot, g.PS[b_][:, 0:tw], AF.Exp, scale=-1.0), g.s_s)
            revac[bank] = g.s_s.n
        wait("vector", g.s_s)
        run("vector", lambda e, cc=cols: e.tensor_scalar(
            vf(Cp_, cc), vf(Cp_, cc), 1.0, None, ALU.add), g.s_v)
        em("vector", lambda e: e.drain())
        run("vector", lambda e, cc=cols: e.reciprocal(vf(Bf_, cc), vf(Cp_, cc)), g.s_v)
        if pre:
            for si, (base, a) in enumerate([(Dn_, 0), (Ed_, 0), (Bf_, 1)]):
                run("vector", lambda e, aa=si, ap=r4(base)[:, :, :, a:a + 1]: e.tensor_copy(
                    g.sSC[:, aa * 128:(aa + 1) * 128].rearrange("p (c b o) -> p c b o", c=CH, b=B),
                    ap), g.s_v)
            em("vector", lambda e: e.drain())
            vr = g.s_v.n
            wait("sync", g.s_v, vr)
            for si, dram in enumerate([g.num_d, g.den_d, g.r_d]):
                dma("sync", dram[0:128, 0:128], g.sSC[:, si * 128:(si + 1) * 128], g.s_st)
        else:
            vr = g.s_v.n
            r0 = (1 + kk) * 128
            for (dram, base, a, bb), eng in [((g.r_d, Bf_, 1, seg), "sync"),
                                             ((g.num_d, Dn_, 0, seg - 1), "scalar"),
                                             ((g.den_d, Ed_, 0, seg - 1), "sync")]:
                wait(eng, g.s_v, vr)
                dma(eng, dram[r0:r0 + 128, :],
                    r4(base)[:, :, :, a:bb], g.s_st)


def _collective(g):
    em, wait, dma, run = g.em, g.wait, g.dma, g.run
    wait("gpsimd", g.s_v)
    dma("gpsimd", g.cc_in, g.sSEED[:, 256:512], g.s_gc)
    wait("gpsimd", g.s_gc)
    run("gpsimd", lambda e: e.collective_compute(
        "AllGather", ALU.bypass, replica_groups=[list(range(NC))],
        ins=[g.cc_in.opt()], outs=[g.cc_out.opt()]), g.s_cc)
    wait("gpsimd", g.s_cc)
    run("vector", lambda e: e.memset(g.sSNSD[:], 0.0), g.s_v)
    em("vector", lambda e: e.drain())
    for j in range(NC):
        dma("gpsimd", g.sGATH[:, 0:256], g.cc_out[j * 128:(j + 1) * 128, :], g.s_gc)
        gld = g.s_gc.n
        fj = g.sDRV[:, 32 + 8 * j:40 + 8 * j]
        wait("vector", g.s_gc, gld)
        for half in (0, 1):
            run("vector", lambda e, f_=fj, hh=half: e.tensor_tensor(
                g.sSC[:, hh * 128:hh * 128 + 128].rearrange("p (c b) -> p c b", c=CH),
                g.sSNSD[:, hh * 128:hh * 128 + 128].rearrange("p (c b) -> p c b", c=CH),
                f_.rearrange("p (c o) -> p c o", o=1).to_broadcast([128, CH, B]), ALU.mult), g.s_v)
        em("vector", lambda e: e.drain())
        for half in (0, 1):
            run("vector", lambda e, jj=j, hh=half: e.scalar_tensor_tensor(
                g.sSNSD[:, hh * 128:hh * 128 + 128],
                g.sGATH[:, hh * 128:hh * 128 + 128],
                g.sGPRE[:, jj:jj + 1],
                g.sSC[:, hh * 128:hh * 128 + 128], ALU.mult, ALU.add), g.s_v)
        em("vector", lambda e: e.drain())
        wait("gpsimd", g.s_v)
    if "seeds" in g.dumps:
        wait("sync", g.s_v)
        dma("sync", g.dumps["seeds"], g.sSNSD[:], g.s_st)
    for nm, dr in [("num", g.num_d), ("den", g.den_d), ("r", g.r_d)]:
        if nm in g.dumps:
            wait("sync", g.s_st, g.s_st.n)
            dma("sync", g.dumps[nm], dr, g.s_st)


def _s2(g):
    em, wait, dma, run, vf, vb = g.em, g.wait, g.dma, g.run, g.vf, g.vb
    A_, Bd_, Cr_, Dx_, Ekv_ = 0, 4224, 8320, 12416, 16512
    RW_, MXC_, HK_, WOC_, WG_ = 0, 4096, 8192, 10240, 18432
    FG = 4
    NG = 32 // FG

    wait("scalar", g.s_v)
    for chi in range(CH):
        run("scalar", lambda e, c=chi: e.activation(
            g.sLAM[:, c * 257:(c + 1) * 257], g.sIOTA[:], AF.Exp,
            scale=g.sDRV[:, c:c + 1]), g.s_s)
    lam_built = g.s_s.n

    def fg_load(gi, par):
        base = WG_ + par * 8192
        wait("sync", g.s_t)
        dma("sync", vb(base, 4096),
            g.wckb_d[gi * 128:(gi + 1) * 128, :], g.s_ld)
        wait("scalar", g.s_t)
        dma("scalar", vb(base + 4096, 4096),
            g.wcvb_d[gi * 128:(gi + 1) * 128, :], g.s_ld)
        return g.s_ld.n

    wait("sync", g.s_gc, g.prep_s2)
    wait("scalar", g.s_gc, g.prep_s2)
    wait("sync", g.s_st, g.s_st.n)
    wait("scalar", g.s_st, g.s_st.n)
    fg_ready = [None] * NG
    fg_ready[0] = fg_load(0, 0)
    woadd = {0: 0, 1: 0}
    rrev = {0: 0, 1: 0}
    hkread = {6: 0, 7: 0}

    for kk in range(-1, NT):
        pre = (kk == -1)
        ntok = 128 if pre else TILE_COLS
        tokc = B if pre else B * TP
        src_off = 0 if pre else 256 + kk * TILE_COLS

        def csl(base, c, _t=tokc):
            return g.ARF[:, base + c * _t: base + (c + 1) * _t]

        wait("sync", g.s_v)
        wait("sync", g.s_s)
        wait("sync", g.s_t)
        wait("sync", g.s_gc, g.s_gc.n)
        wait("scalar", g.s_v)
        wait("scalar", g.s_st, g.s_st.n)
        rb = (1 + kk) * 128 if not pre else 0
        dma("sync", vf(A_, ntok), g.num_d[rb:rb + 128, 0:ntok], g.s_ld)
        dma("scalar", vf(Bd_, ntok), g.den_d[rb:rb + 128, 0:ntok], g.s_ld)
        dma("sync", vf(Cr_, ntok), g.r_d[rb:rb + 128, 0:ntok], g.s_ld)
        if pre:
            dma("sync", vf(Ekv_, 256), g.x_t[0:128, 0:256], g.s_ld)
            prex = g.s_ld.n
            wait("vector", g.s_ld, prex)
            run("vector", lambda e: e.tensor_copy(
                vf(Dx_, 128).rearrange("p (c b o) -> p c b o", c=CH, b=B),
                vf(Ekv_, 256).rearrange("p (c b t) -> p c b t", c=CH, b=B)[:, :, :, 1:2]), g.s_v)
            em("vector", lambda e: e.drain())
        else:
            xs = g.x_t[(1 + kk) * 128:(2 + kk) * 128, :].rearrange(
                "p (c b t) -> p c b t", c=CH, b=B)[:, :, :, 1:SEG]
            dma("scalar", vf(Dx_, ntok).rearrange("p (c b t) -> p c b t", c=CH, b=B, t=TP),
                xs, g.s_ld)
        wait("scalar", g.s_t)
        dma("scalar", vb(WOC_, CH * C), g.wob_d, g.s_ld)
        ldt = g.s_ld.n

        t0 = 0 if pre else TP * kk + 1
        tw = 1 if pre else TP
        lam4 = g.sLAM.rearrange("p (c t) -> p c t", c=CH, t=528)[:, :, t0:t0 + tw] if False else \
            g.sLAM[:, 0:CH * 257].rearrange("p (c t) -> p c t", c=CH)[:, :, t0:t0 + tw]
        wait("vector", g.s_ld, ldt)
        if pre:
            wait("vector", g.s_s, lam_built)
        for (tgt, sn_off) in [(A_, 0), (Bd_, 128)]:
            run("vector", lambda e, so=sn_off, l4=lam4, _tw=tw: e.tensor_tensor(
                vf(Ekv_, CH * B * _tw).rearrange("p (c b t) -> p c b t", c=CH, b=B),
                g.sSNSD[:, so:so + 128].rearrange("p (c b o) -> p c b o", c=CH, b=B).to_broadcast([128, CH, B, _tw]),
                l4.unsqueeze(2).to_broadcast([128, CH, B, _tw]), ALU.mult), g.s_v)
            em("vector", lambda e: e.drain())
            run("vector", lambda e, tg=tgt, _n=ntok: e.tensor_tensor(
                vf(tg, _n), vf(tg, _n), vf(Ekv_, _n), ALU.add), g.s_v)
            em("vector", lambda e: e.drain())
        run("vector", lambda e, n_=ntok: e.reciprocal(vf(Ekv_, n_), vf(Bd_, n_)), g.s_v)
        em("vector", lambda e: e.drain())
        run("vector", lambda e, n_=ntok: e.tensor_tensor(vf(A_, n_), vf(A_, n_), vf(Ekv_, n_), ALU.mult), g.s_v)
        run("vector", lambda e, n_=ntok: e.tensor_tensor(vb(RW_, n_), vf(A_, n_), vf(Cr_, n_), ALU.mult), g.s_v)
        rwv = g.s_v.n
        if not pre:
            wait("scalar", g.s_v, rwv)
            dma("scalar", vf(Cr_, TILE_COLS).rearrange("p (c b t) -> p c b t", c=CH, b=B),
                g.x_t[(1 + kk) * 128:(2 + kk) * 128, :].rearrange(
                    "p (c b t) -> p c b t", c=CH, b=B)[:, :, :, 1:SEG], g.s_ld)
            g.xre = g.s_ld.n
        if kk == 0 and "wkv" in g.dumps:
            wait("sync", g.s_v, rwv)
            dma("sync", g.dumps["wkv"], vf(A_, TILE_COLS), g.s_st)
            wait("vector", g.s_st, g.s_st.n)
            wait("scalar", g.s_st, g.s_st.n)
        wait("tensor", g.s_v, rwv)
        wait("tensor", g.s_ld, ldt)
        for co in range(CH):
            bank = co % 2
            wait("tensor", g.s_v, woadd[bank])
            for ci in range(CH):
                run("tensor", lambda e, b_=bank, cc=ci, cco=co, _tk=tokc: e.matmul(
                    g.PS[b_][:, 0:_tk],
                    vb(WOC_ + cc * C + cco * 128, 128),
                    vb(RW_ + cc * _tk, _tk),
                    start=(cc == 0), stop=(cc == CH - 1)),
                    g.s_t if ci == CH - 1 else None)
            wait("vector", g.s_t)
            run("vector", lambda e, b_=bank, ap=csl(Dx_, co), _tk=tokc: e.tensor_tensor(
                ap, ap, g.PS[b_][:, 0:_tk], ALU.add), g.s_v)
            woadd[bank] = g.s_v.n
        if kk == 0 and "x1" in g.dumps:
            wait("sync", g.s_v)
            dma("sync", g.dumps["x1"], vf(Dx_, TILE_COLS), g.s_st)
            wait("vector", g.s_st, g.s_st.n)
            wait("scalar", g.s_st, g.s_st.n)
        if pre and "x1p" in g.dumps:
            wait("sync", g.s_v)
            dma("sync", g.dumps["x1p"], vf(Dx_, 128), g.s_st)
            dma("sync", g.dumps["wkvp"], vf(A_, 128), g.s_st)
            wait("vector", g.s_st, g.s_st.n)
            wait("scalar", g.s_st, g.s_st.n)
        if pre:
            def put_pre(c, o, w):
                return g.sXC[:, c * B + o: c * B + o + w]
            g.emit_ln(lambda c, o, w: g.ARF[:, Dx_ + c * tokc + o: Dx_ + c * tokc + o + w],
                      put_pre, [(0, B)],
                      lambda c: g.vcol("g1", c), lambda c: g.vcol("b1", c))
            run("vector", lambda e: e.tensor_scalar_mul(g.sXC[:], g.sXC[:], g.sGATE[:, 0:1]), g.s_v)
            em("vector", lambda e: e.drain())
            continue

        def put_xc(c, o, w):
            ap = g.ARF[:, A_ + c * (B * SEG): A_ + (c + 1) * (B * SEG)].rearrange(
                "p (b t) -> p b t", b=B)[:, :, 1:SEG]
            b0 = o // TP
            b1 = (o + w) // TP
            return ap[:, b0:b1, :]
        g.emit_ln(lambda c, o, w: g.ARF[:, Dx_ + c * tokc + o: Dx_ + c * tokc + o + w],
                  put_xc, [(0, 512)],
                  lambda c: g.vcol("g1", c), lambda c: g.vcol("b1", c))

        def xc4():
            return g.ARF[:, A_:A_ + HALO_COLS].rearrange("p (c b t) -> p c b t", c=CH, b=B)
        run("vector", lambda e, ap=xc4()[:, :, :, 0:1]: e.tensor_copy(
            ap, g.sXC[:].rearrange("p (c b o) -> p c b o", c=CH, b=B)), g.s_v)
        em("vector", lambda e: e.drain())
        run("vector", lambda e, ap=xc4()[:, :, :, SEG - 1:SEG]: e.tensor_copy(
            g.sXC[:].rearrange("p (c b o) -> p c b o", c=CH, b=B), ap), g.s_v)
        em("vector", lambda e: e.drain())
        if kk == 0 and "xc" in g.dumps:
            wait("sync", g.s_v)
            dma("sync", g.dumps["xc"], vf(A_, HALO_COLS), g.s_st)
            wait("vector", g.s_st, g.s_st.n)
            wait("scalar", g.s_st, g.s_st.n)
        run("vector", lambda e, a1=xc4()[:, :, :, 1:SEG], a0=xc4()[:, :, :, 0:SEG - 1]: e.tensor_tensor(
            vf(Bd_, TILE_COLS).rearrange("p (c b t) -> p c b t", c=CH, b=B),
            a1, a0, ALU.subtract), g.s_v)
        em("vector", lambda e: e.drain())
        wait("vector", g.s_t)
        for c in range(CH):
            run("vector", lambda e, cc=c: e.scalar_tensor_tensor(
                vb(RW_ + cc * tokc, tokc),
                g.ARF[:, Bd_ + cc * tokc: Bd_ + (cc + 1) * tokc], g.vcol("cmk", cc),
                g.ARF[:, A_ + cc * (B * SEG): A_ + (cc + 1) * (B * SEG)].rearrange(
                    "p (b t) -> p b t", b=B)[:, :, 0:SEG - 1],
                ALU.mult, ALU.add), g.s_v)
        for c in range(CH):
            run("vector", lambda e, cc=c: e.scalar_tensor_tensor(
                vb(MXC_ + cc * tokc, tokc),
                g.ARF[:, Bd_ + cc * tokc: Bd_ + (cc + 1) * tokc], g.vcol("cmr", cc),
                g.ARF[:, A_ + cc * (B * SEG): A_ + (cc + 1) * (B * SEG)].rearrange(
                    "p (b t) -> p b t", b=B)[:, :, 0:SEG - 1],
                ALU.mult, ALU.add), g.s_v)
        mixc = g.s_v.n
        wait("tensor", g.s_v, mixc)
        first_mm = [True] * CH
        for gi in range(NG):
            if fg_ready[gi] is None:
                wait("sync", g.s_t)
                fg_ready[gi] = fg_load(gi, gi % 2)
            if gi + 1 < NG and fg_ready[gi + 1] is None:
                wait("sync", g.s_t)
                fg_ready[gi + 1] = fg_load(gi + 1, (gi + 1) % 2)
            wait("tensor", g.s_ld, fg_ready[gi])
            wbase = WG_ + (gi % 2) * 8192
            prev_wcv_t = g.s_t.n
            for fi in range(FG):
                hbank = 6 + fi % 2
                wait("tensor", g.s_v, hkread[hbank])
                for ci in range(CH):
                    run("tensor", lambda e, b_=hbank, cc=ci, ff=fi, wb=wbase: e.matmul(
                        g.PS[b_][:, 0:512],
                        vb(wb + cc * (128 * FG) + ff * 128, 128),
                        vb(RW_ + cc * 512, 512),
                        start=(cc == 0), stop=(cc == CH - 1)),
                        g.s_t if ci == CH - 1 else None)
                wait("vector", g.s_t)
                run("vector", lambda e, b_=hbank, ff=fi: e.tensor_relu(
                    g.sSC[:, 544 * (ff % 2):544 * (ff % 2) + 512],
                    g.PS[b_][:, 0:512]), g.s_v)
                hkread[hbank] = g.s_v.n
                wait("vector", g.s_t, prev_wcv_t)
                run("vector", lambda e, ff=fi: e.tensor_tensor(
                    vb(HK_ + ff * 512, 512),
                    g.sSC[:, 544 * (ff % 2):544 * (ff % 2) + 512],
                    g.sSC[:, 544 * (ff % 2):544 * (ff % 2) + 512], ALU.mult), g.s_v)
            hkv = g.s_v.n
            if kk == 0 and gi == 0 and "hk0" in g.dumps:
                run("vector", lambda e: e.tensor_copy(g.sSC[:, 0:512], vb(HK_, 512)), g.s_v)
                run("vector", lambda e: e.tensor_copy(g.sSC[:, 544:1056], vb(HK_ + 512, 512)), g.s_v)
                em("vector", lambda e: e.drain())
                wait("sync", g.s_v)
                dma("sync", g.dumps["hk0"][:, 0:512], g.sSC[:, 0:512], g.s_st)
                dma("sync", g.dumps["hk0"][:, 512:1024], g.sSC[:, 544:1056], g.s_st)
                wait("vector", g.s_st, g.s_st.n)
                run("vector", lambda e: e.tensor_copy(g.sSC[:, 0:512], vb(HK_ + 1024, 512)), g.s_v)
                run("vector", lambda e: e.tensor_copy(g.sSC[:, 544:1056], vb(HK_ + 1536, 512)), g.s_v)
                em("vector", lambda e: e.drain())
                wait("sync", g.s_v)
                wait("sync", g.s_st, g.s_st.n)
                dma("sync", g.dumps["hk0"][:, 1024:1536], g.sSC[:, 0:512], g.s_st)
                dma("sync", g.dumps["hk0"][:, 1536:2048], g.sSC[:, 544:1056], g.s_st)
                wait("vector", g.s_st, g.s_st.n)
                wait("scalar", g.s_st, g.s_st.n)
            wait("tensor", g.s_v, hkv)
            for co in range(CH):
                if co < 6:
                    last = (gi == NG - 1)
                    for fi in range(FG):
                        run("tensor", lambda e, cc=co, ff=fi, wb=wbase,
                                st=first_mm[co] and fi == 0,
                                sp=(last and fi == FG - 1): e.matmul(
                            g.PS[cc][:, 0:512],
                            vb(wb + 4096 + ff * C + cc * 128, 128),
                            vb(HK_ + ff * 512, 512),
                            start=st, stop=sp),
                            g.s_t if fi == FG - 1 else None)
                    first_mm[co] = False
                else:
                    bank = 6 + (co - 6)
                    wait("tensor", g.s_v, hkread[bank])
                    for fi in range(FG):
                        run("tensor", lambda e, b_=bank, cc=co, ff=fi, wb=wbase: e.matmul(
                            g.PS[b_][:, 0:512],
                            vb(wb + 4096 + ff * C + cc * 128, 128),
                            vb(HK_ + ff * 512, 512),
                            start=(ff == 0), stop=(ff == FG - 1)),
                            g.s_t if fi == FG - 1 else None)
                    wait("vector", g.s_t)
                    if gi == 0:
                        run("vector", lambda e, b_=bank, ap=csl(Ekv_, co): e.tensor_copy(
                            ap, g.PS[b_][:, 0:512]), g.s_v)
                    else:
                        run("vector", lambda e, b_=bank, ap=csl(Ekv_, co): e.tensor_tensor(
                            ap, ap, g.PS[b_][:, 0:512], ALU.add), g.s_v)
                    hkread[bank] = g.s_v.n
        fg_ready = [None] * NG
        if kk < NT - 1:
            fg_ready[0] = fg_load(0, 0)
            fg_ready[1] = fg_load(1, 1)
        tall = g.s_t.n
        wait("vector", g.s_t, tall)
        for co in range(6):
            run("vector", lambda e, cc=co, ap=csl(Ekv_, co): e.tensor_copy(ap, g.PS[cc][:, 0:512]), g.s_v)
        kvev = g.s_v.n
        woadd[0] = woadd[1] = kvev
        wait("sync", g.s_v, kvev)
        dma("sync", vb(WOC_, CH * C), g.wcrb_d, g.s_ld)
        wcrl = g.s_ld.n
        wait("tensor", g.s_ld, wcrl)
        wait("tensor", g.s_v, kvev)
        for co in range(CH):
            bank = co % 2
            wait("tensor", g.s_s, rrev[bank])
            for ci in range(CH):
                run("tensor", lambda e, b_=bank, cc=ci, cco=co: e.matmul(
                    g.PS[b_][:, 0:512],
                    vb(WOC_ + cc * C + cco * 128, 128),
                    vb(MXC_ + cc * 512, 512),
                    start=(cc == 0), stop=(cc == CH - 1)),
                    g.s_t if ci == CH - 1 else None)
            wait("scalar", g.s_t)
            run("scalar", lambda e, b_=bank, ap=csl(A_, co): e.activation(
                ap, g.PS[b_][:, 0:512], AF.Exp, scale=-1.0), g.s_s)
            rrev[bank] = g.s_s.n
        wait("vector", g.s_s)
        run("vector", lambda e: e.tensor_scalar(vf(A_, TILE_COLS), vf(A_, TILE_COLS), 1.0, None, ALU.add), g.s_v)
        em("vector", lambda e: e.drain())
        run("vector", lambda e: e.reciprocal(vf(Bd_, TILE_COLS), vf(A_, TILE_COLS)), g.s_v)
        em("vector", lambda e: e.drain())
        if kk == 0 and "kv" in g.dumps:
            wait("sync", g.s_v)
            dma("sync", g.dumps["kv"], vf(Ekv_, TILE_COLS), g.s_st)
            dma("sync", g.dumps["rr"], vf(Bd_, TILE_COLS), g.s_st)
            wait("vector", g.s_st, g.s_st.n)
            wait("scalar", g.s_st, g.s_st.n)
        xre = g.xre
        run("vector", lambda e: e.tensor_tensor(vf(Ekv_, TILE_COLS), vf(Ekv_, TILE_COLS), vf(Bd_, TILE_COLS), ALU.mult), g.s_v)
        run("vector", lambda e: e.tensor_tensor(vf(Ekv_, TILE_COLS), vf(Ekv_, TILE_COLS), vf(Dx_, TILE_COLS), ALU.add), g.s_v)
        wait("vector", g.s_ld, xre)
        run("vector", lambda e: e.tensor_tensor(vf(Ekv_, TILE_COLS), vf(Ekv_, TILE_COLS), vf(Cr_, TILE_COLS), ALU.add), g.s_v)
        em("vector", lambda e: e.drain())

        if kk == 0 and "yp" in g.dumps:
            wait("sync", g.s_v)
            dma("sync", g.dumps["yp"], vf(Ekv_, TILE_COLS), g.s_st)
            wait("vector", g.s_st, g.s_st.n)
            wait("scalar", g.s_st, g.s_st.n)
        def src_y(c, o, w):
            return g.ARF[:, Ekv_ + c * tokc + o: Ekv_ + c * tokc + o + w]

        def put_y1(c, o, w):
            b0 = o // TP
            b1 = (o + w) // TP
            return g.ARF[:, Bd_ + c * tokc: Bd_ + (c + 1) * tokc].rearrange(
                "p (t b) -> p t b", t=TP)[:, :, b0:b1].transpose([0, 2, 1])
        for (o, w) in [(0, 512)]:
            wait("tensor", g.s_v)
            wait("tensor", g.s_s)
            wait("scalar", g.s_v)
            for c in range(CH):
                run("tensor", lambda e, cc=c, ap=src_y(c, o, w), ww=w: e.matmul(
                    g.PS[4][0:1, 0:ww], g.sONE1c[:], ap,
                    start=(cc == 0), stop=(cc == CH - 1)), g.s_t)
            for c in range(CH):
                if c >= 2:
                    wait("scalar", g.s_t)
                run("scalar", lambda e, cc=c, ap=src_y(c, o, w), ww=w: e.activation(
                    g.sSC[:, (cc % 2) * 544:(cc % 2) * 544 + ww], ap, AF.Square), g.s_s)
                wait("tensor", g.s_s)
                run("tensor", lambda e, cc=c, ww=w: e.matmul(
                    g.PS[5][0:1, 0:ww], g.sONE1c[:],
                    g.sSC[:, (cc % 2) * 544:(cc % 2) * 544 + ww],
                    start=(cc == 0), stop=(cc == CH - 1)), g.s_t)
            tmm = g.s_t.n
            wait("vector", g.s_t, tmm)
            run("vector", lambda e, ww=w: e.tensor_copy(g.sROW[0:1, 0:ww], g.PS[4][0:1, 0:ww]), g.s_v)
            em("vector", lambda e: e.drain())
            run("vector", lambda e, ww=w: e.tensor_tensor(
                g.sROW[0:1, 1024:1024 + ww], g.sROW[0:1, 0:ww], g.sROW[0:1, 0:ww], ALU.mult), g.s_v)
            em("vector", lambda e: e.drain())
            run("vector", lambda e, ww=w: e.tensor_tensor(
                g.sROW[0:1, 512:512 + ww], g.PS[5][0:1, 0:ww], g.sROW[0:1, 1024:1024 + ww], ALU.subtract), g.s_v)
            vrow = g.s_v.n
            wait("scalar", g.s_v, vrow)
            run("scalar", lambda e, ww=w: e.activation(
                g.sROW[0:1, 512:512 + ww], g.sROW[0:1, 512:512 + ww], AF.Ln, bias=g.sDRV[0:1, 96:97]), g.s_s)
            em("scalar", lambda e: e.drain())
            run("scalar", lambda e, ww=w: e.activation(
                g.sROW[0:1, 512:512 + ww], g.sROW[0:1, 512:512 + ww], AF.Exp, scale=-0.5), g.s_s)
            srow = g.s_s.n
            wait("tensor", g.s_s, srow)
            run("tensor", lambda e, ww=w: e.matmul(
                g.PS[6][:, 0:ww], g.sONES1[:], g.sROW[0:1, 0:ww], start=True, stop=True), g.s_t)
            run("tensor", lambda e, ww=w: e.matmul(
                g.PS[7][:, 0:ww], g.sONES1[:], g.sROW[0:1, 512:512 + ww], start=True, stop=True), g.s_t)
            tb = g.s_t.n
            wait("vector", g.s_t, tb)
            for c in range(CH):
                run("vector", lambda e, ap=src_y(c, o, w), ww=w: e.tensor_tensor(
                    ap, ap, g.PS[6][:, 0:ww], ALU.subtract), g.s_v)
            for c in range(CH):
                run("vector", lambda e, po=put_y1(c, o, w), ap=src_y(c, o, w), ww=w: e.tensor_tensor(
                    po, ap, g.PS[7][:, 0:ww], ALU.mult), g.s_v)
        y1v = g.s_v.n
        hkread[6] = hkread[7] = y1v
        wait("sync", g.s_v, y1v)
        dma("sync", g.y1_d[kk * 128:(kk + 1) * 128, :], vf(Bd_, TILE_COLS), g.s_st)
    if "y1" in g.dumps:
        wait("sync", g.s_st, g.s_st.n)
        dma("sync", g.dumps["y1"], g.y1_d, g.s_st)


def _s3(g):
    em, wait, dma, run, vf, vb = g.em, g.wait, g.dma, g.run, g.vf, g.vb
    Y_, Hh_ = 0, TILE_COLS
    QKV_, YB_, CTX_ = 0, 12288, 16384
    WIP_ = 20480
    CTX_ = QKV_
    WIP_ = 16384

    def wip_load(half):
        wait("scalar", g.s_t)
        dma("scalar", vb(WIP_, 12288),
            g.wipb_d[half * 128:(half + 1) * 128, :], g.s_ld)
        return g.s_ld.n

    wait("scalar", g.s_gc, g.prep_s3)
    s3w = wip_load(0)
    qevac = {0: 0, 1: 0}
    hpre = {0: 0, 1: 0}
    dma("sync", g.sGATH[:, 0:64], g.wrt_t, g.s_ld)
    rtw = g.s_ld.n

    for kk in range(NT):
        wait("sync", g.s_v)
        wait("sync", g.s_t)
        wait("sync", g.s_s)
        wait("sync", g.s_st, g.s_st.n)
        dma("sync", vf(Y_, TILE_COLS), g.y1_d[kk * 128:(kk + 1) * 128, :], g.s_ld)
        ldt = g.s_ld.n
        wait("vector", g.s_ld, ldt)
        run("vector", lambda e: e.tensor_copy(vb(YB_, TILE_COLS), vf(Y_, TILE_COLS)), g.s_v)
        em("vector", lambda e: e.drain())
        ybc = g.s_v.n
        wait("tensor", g.s_v, ybc)
        wait("tensor", g.s_ld, s3w)
        for oc in range(24):
            if oc == 12:
                s3w = wip_load(1)
                wait("tensor", g.s_ld, s3w)
            bank = oc % 2
            wait("tensor", g.s_s, qevac[bank])
            for ci in range(CH):
                run("tensor", lambda e, b_=bank, cc=ci, oo=oc % 12: e.matmul(
                    g.PS[b_][:, 0:512],
                    vb(WIP_ + cc * 1536 + oo * 128, 128),
                    vb(YB_ + cc * 512, 512),
                    start=(cc == 0), stop=(cc == CH - 1)),
                    g.s_t if ci == CH - 1 else None)
            wait("scalar", g.s_t)
            if oc < 8:
                run("scalar", lambda e, b_=bank, oo=oc: e.activation(
                    vb(QKV_ + oo * 512, 512), g.PS[b_][:, 0:512], AF.Identity,
                    bias=g.sDRV[:, 24 + oo:25 + oo], scale=0.125), g.s_s)
            else:
                run("scalar", lambda e, b_=bank, oo=oc: e.activation(
                    vb(QKV_ + oo * 512, 512), g.PS[b_][:, 0:512], AF.Identity,
                    bias=g.sVEC[:, 128 + oo:129 + oo]), g.s_s)
            qevac[bank] = g.s_s.n
        qkvs = g.s_s.n
        inproj_t = g.s_t.n
        wait("tensor", g.s_s, qkvs)
        snaps = [dict(t=0, v=0, s=0), dict(t=0, v=0, s=0)]
        SBK = [2, 0]
        TBK = [3, 1]
        CBK = [4, 6]
        AS_ = [YB_ + 2048, YB_ + 2816]
        XR = [1056, 1072]
        gidx = 0
        for pb in range(4):
            colo = pb * 128
            for j in range(CH):
                p = gidx % 2
                sb, tb, cb = SBK[p], TBK[p], CBK[p]
                a0 = AS_[p]
                xr = XR[p]
                qof = QKV_ + j * 512 + colo
                kof = QKV_ + (8 + j) * 512 + colo
                vof = QKV_ + (16 + j) * 512 + colo
                sn = snaps[p]
                wait("tensor", g.s_v, sn["v"])
                wait("tensor", g.s_s, sn["s"])
                for hh in range(2):
                    run("tensor", lambda e, h_=hh, b_=sb: e.matmul(
                        g.PS[b_][:, h_ * 128:h_ * 128 + 128], g.sMA[:], g.sMB[:],
                        start=True, stop=False), None)
                    run("tensor", lambda e, h_=hh, b_=sb, q_=qof, k_=kof: e.matmul(
                        g.PS[b_][:, h_ * 128:h_ * 128 + 128],
                        g.ARB[64 * h_:64 * h_ + 64, q_:q_ + 128],
                        g.ARB[64 * h_:64 * h_ + 64, k_:k_ + 128],
                        start=False, stop=True), g.s_t)
                sco = g.s_t.n
                run("tensor", lambda e, v_=vof, b_=tb: e.transpose(
                    g.PS[b_][:, 128:192].bitcast(BF16), g.ARB[:, v_:v_ + 128],
                    g.sIDB[:]), g.s_t)
                vt = g.s_t.n
                wait("vector", g.s_t, sco)
                run("vector", lambda e, b_=sb, x_=xr: e.reduce_max(
                    g.sSC[:, x_:x_ + 2].rearrange("p (s o) -> p s o", s=2),
                    g.PS[b_][:, 0:256].rearrange("p (s n) -> p s n", s=2),
                    axis=AX, negate=True), g.s_v)
                em("vector", lambda e: e.drain())
                smax = g.s_v.n
                wait("scalar", g.s_v, smax)
                for hh in range(2):
                    run("scalar", lambda e, h_=hh, b_=sb, a_=a0, x_=xr: e.activation(
                        g.ARB[:, a_ + h_ * 128: a_ + h_ * 128 + 128],
                        g.PS[b_][:, h_ * 128:h_ * 128 + 128],
                        AF.Exp, bias=g.sSC[:, x_ + h_:x_ + h_ + 1]), g.s_s)
                sexp = g.s_s.n
                wait("vector", g.s_s, sexp)
                run("vector", lambda e, a_=a0, x_=xr: e.tensor_reduce(
                    g.sSC[:, x_ + 2:x_ + 4].rearrange("p (s o) -> p s o", s=2),
                    g.ARB[:, a_:a_ + 256].rearrange("p (s n) -> p s n", s=2),
                    axis=AX, op=ALU.add), g.s_v)
                em("vector", lambda e: e.drain())
                run("vector", lambda e, x_=xr: e.reciprocal(
                    g.sSC[:, x_ + 4:x_ + 6], g.sSC[:, x_ + 2:x_ + 4]), g.s_v)
                em("vector", lambda e: e.drain())
                for hh in range(2):
                    run("vector", lambda e, h_=hh, a_=a0, x_=xr: e.tensor_scalar_mul(
                        g.ARB[:, a_ + h_ * 128: a_ + h_ * 128 + 128],
                        g.ARB[:, a_ + h_ * 128: a_ + h_ * 128 + 128],
                        g.sSC[:, x_ + 4 + h_:x_ + 5 + h_]), g.s_v)
                atn = g.s_v.n
                wait("tensor", g.s_v, atn)
                for hh in range(2):
                    run("tensor", lambda e, h_=hh, a_=a0, b_=tb: e.transpose(
                        g.PS[b_][:, h_ * 64:h_ * 64 + 64].bitcast(BF16),
                        g.ARB[:, a_ + h_ * 128: a_ + h_ * 128 + 128],
                        g.sIDB[:]), g.s_t)
                att = g.s_t.n
                wait("vector", g.s_t, att)
                run("vector", lambda e, a_=a0, b_=tb: e.tensor_copy(
                    g.ARB[:, a_ + 256: a_ + 512], g.PS[b_][:, 0:128].bitcast(BF16)), g.s_v)
                run("vector", lambda e, a_=a0, b_=tb: e.tensor_copy(
                    g.ARB[:, a_ + 512: a_ + 640], g.PS[b_][:, 128:192].bitcast(BF16)), g.s_v)
                em("vector", lambda e: e.drain())
                tev = g.s_v.n
                wait("tensor", g.s_v, tev)
                run("tensor", lambda e, a_=a0, b_=cb: e.matmul(
                    g.PS[b_][:, 0:256], g.ARB[:, a_ + 512: a_ + 640],
                    g.ARB[:, a_ + 256: a_ + 512], start=True, stop=True), g.s_t)
                ctm = g.s_t.n
                wait("vector", g.s_t, ctm)
                run("vector", lambda e, j_=j, c_=colo, b_=cb: e.tensor_copy(
                    g.ARB[0:64, CTX_ + j_ * 512 + c_: CTX_ + j_ * 512 + c_ + 128],
                    g.PS[b_][0:64, 0:128]), g.s_v)
                run("vector", lambda e, j_=j, c_=colo, b_=cb: e.tensor_copy(
                    g.ARB[64:128, CTX_ + j_ * 512 + c_: CTX_ + j_ * 512 + c_ + 128],
                    g.PS[b_][64:128, 128:256]), g.s_v)
                snaps[p] = dict(t=g.s_t.n, v=g.s_v.n, s=g.s_s.n)
                gidx += 1
        ctxv = g.s_v.n
        qevac[0] = qevac[1] = g.s_s.n
        wait("scalar", g.s_t, inproj_t)
        dma("scalar", vb(WIP_, CH * C), g.wowb_d, g.s_ld)
        oww = g.s_ld.n
        wait("tensor", g.s_ld, oww)
        wait("tensor", g.s_v, ctxv)
        for co in range(CH):
            bank = co % 2
            wait("tensor", g.s_v, hpre[bank])
            for ci in range(CH):
                run("tensor", lambda e, b_=bank, cc=ci, cco=co: e.matmul(
                    g.PS[b_][:, 0:512],
                    vb(WIP_ + cc * C + cco * 128, 128),
                    vb(CTX_ + cc * 512, 512),
                    start=(cc == 0), stop=(cc == CH - 1)),
                    g.s_t if ci == CH - 1 else None)
            wait("vector", g.s_t)
            run("vector", lambda e, b_=bank, cco=co: e.tensor_tensor(
                vf(Hh_ + cco * 512, 512), vf(Y_ + cco * 512, 512),
                g.PS[b_][:, 0:512], ALU.add), g.s_v)
            run("vector", lambda e, cco=co: e.tensor_scalar(
                vf(Hh_ + cco * 512, 512), vf(Hh_ + cco * 512, 512),
                g.vcol("ob", cco), None, ALU.add), g.s_v)
            hpre[bank] = g.s_v.n
        def src_h(c, o, w):
            return g.ARF[:, Hh_ + c * 512 + o: Hh_ + c * 512 + o + w]
        g.emit_ln(src_h, src_h, [(0, 512)],
                  lambda c: g.vcol("fg1", c), lambda c: g.vcol("fb1", c))
        hv = g.s_v.n
        wait("sync", g.s_v, hv)
        dma("sync", g.h_d[kk * 128:(kk + 1) * 128, :], vf(Hh_, TILE_COLS), g.s_st)
        wait("tensor", g.s_v, hv)
        wait("tensor", g.s_ld, rtw)
        for ci in range(CH):
            run("tensor", lambda e, cc=ci: e.matmul(
                g.PS[6][0:8, 0:512], g.sGATH[:, cc * 8:(cc + 1) * 8],
                vf(Hh_ + cc * 512, 512), start=(cc == 0), stop=(cc == CH - 1)),
                g.s_t if ci == CH - 1 else None)
        lg = g.s_t.n
        wait("vector", g.s_t, lg)
        run("vector", lambda e: e.tensor_scalar(
            g.sSC[0:8, 0:512], g.PS[6][0:8, 0:512], g.sC8[:, 0:1], None, ALU.add), g.s_v)
        em("vector", lambda e: e.drain())
        lgv = g.s_v.n
        wait("tensor", g.s_v, lgv)
        for bq in range(4):
            run("tensor", lambda e, b_=bq: e.transpose(
                g.PS[7][:, b_ * 8:b_ * 8 + 8], g.sSC[0:8, b_ * 128:(b_ + 1) * 128],
                g.sIDF[0:8, 0:8]), g.s_t)
        lgt = g.s_t.n
        wait("vector", g.s_t, lgt)
        run("vector", lambda e: e.tensor_copy(g.sSC[:, 544:544 + 32], g.PS[7][:, 0:32]), g.s_v)
        em("vector", lambda e: e.drain())
        for bq in range(4):
            lgq = g.sSC[:, 544 + bq * 8:544 + bq * 8 + 8]
            m1 = g.sSC[:, 1080:1081]
            msk = g.sSC[:, 0:8]
            xp = g.sSC[:, 16:24]
            m2 = g.sSC[:, 1081:1082]
            i1 = g.sSC[:, 1082:1083]
            run("vector", lambda e, a=lgq, b_=m1: e.reduce_max(b_, a, axis=AX), g.s_v)
            em("vector", lambda e: e.drain())
            run("vector", lambda e, a=lgq, b_=m1, c_=msk: e.tensor_tensor(
                c_, a, b_.to_broadcast([128, 8]), ALU.is_equal), g.s_v)
            em("vector", lambda e: e.drain())
            run("vector", lambda e, a=lgq, c_=msk, d_=xp: e.scalar_tensor_tensor(
                d_, c_, -1e30, a, ALU.mult, ALU.add), g.s_v)
            em("vector", lambda e: e.drain())
            run("vector", lambda e, d_=xp, b_=m2: e.reduce_max(b_, d_, axis=AX), g.s_v)
            em("vector", lambda e: e.drain())
            run("vector", lambda e, a=lgq, b_=m2, c_=msk: e.tensor_tensor(
                c_, b_.to_broadcast([128, 8]), a, ALU.is_le), g.s_v)
            em("vector", lambda e: e.drain())
            run("vector", lambda e, c_=msk: e.tensor_tensor(c_, c_, g.sIOTA8[:], ALU.mult), g.s_v)
            em("vector", lambda e: e.drain())
            run("vector", lambda e, c_=msk, i_=i1: e.reduce_max(i_, c_, axis=AX), g.s_v)
            em("vector", lambda e: e.drain())
            asn = g.s_v.n
            wait("tensor", g.s_v, asn)
            run("tensor", lambda e, i_=i1: e.transpose(
                g.PS[7][0:1, 128:256], i_, g.sIDF[:]), g.s_t)
            att2 = g.s_t.n
            wait("vector", g.s_t, att2)
            run("vector", lambda e: e.tensor_copy(
                g.sSC[0:1, 544:672], g.PS[7][0:1, 128:256]), g.s_v)
            em("vector", lambda e: e.drain())
            wait("sync", g.s_v)
            dma("sync", g.asg_d[0:1, kk * 512 + bq * 128: kk * 512 + (bq + 1) * 128],
                g.sSC[0:1, 544:672], g.s_st)
        if kk < NT - 1:
            s3w = wip_load(0)
    if "h" in g.dumps:
        wait("sync", g.s_st, g.s_st.n)
        dma("sync", g.dumps["h"], g.h_d, g.s_st)
    if "assign" in g.dumps:
        wait("sync", g.s_st, g.s_st.n)
        dma("sync", g.dumps["assign"], g.asg_d, g.s_st)


def _s4(g):
    em, wait, dma, run, vf, vb = g.em, g.wait, g.dma, g.run, g.vf, g.vb
    H_, Yt_, M67_ = 0, TILE_COLS, 2 * TILE_COLS
    HBV = g.ARF[:, 10240:12288].bitcast(BF16)
    SMB = g.sSC[:, 0:256].bitcast(BF16)
    HKe_ = 0
    W1_ = [4096, 12288]
    W2_ = [20480, 28672]

    def w_load(e_, par):
        wait("sync", g.s_t)
        dma("sync", vb(W1_[par], CH * C), g.w1b_d[e_ * 128:(e_ + 1) * 128, :], g.s_ld)
        wait("scalar", g.s_t)
        dma("scalar", vb(W2_[par], CH * C), g.w2b_d[e_ * 128:(e_ + 1) * 128, :], g.s_ld)
        return g.s_ld.n

    wait("sync", g.s_gc, g.prep_done)
    wait("scalar", g.s_gc, g.prep_done)
    wready = [None] * NE
    gelu_done = {6: 0, 7: 0}
    vadd_done = {6: 0, 7: 0}

    for kk in range(NT):
        wait("sync", g.s_v)
        wait("sync", g.s_t)
        wait("sync", g.s_s)
        wait("sync", g.s_st, g.s_st.n)
        dma("sync", vf(H_, TILE_COLS), g.h_d[kk * 128:(kk + 1) * 128, :], g.s_ld)
        wait("gpsimd", g.s_st, g.s_st.n)
        dma("gpsimd", g.sASG[:], g.asg_d[0:1, kk * 512:(kk + 1) * 512], g.s_gc)
        asgl = g.s_gc.n
        ldh = g.s_ld.n
        wait("vector", g.s_ld, ldh)
        run("vector", lambda e: e.tensor_copy(HBV, vf(H_, TILE_COLS)), g.s_v)
        em("vector", lambda e: e.drain())
        g.hbc = g.s_v.n
        wait("tensor", g.s_v)
        wait("tensor", g.s_gc, asgl)
        run("tensor", lambda e: e.matmul(
            g.PS[6][0:8, 0:512], g.sONES8[:],
            g.sASG[0:1, :], start=True, stop=True), g.s_t)
        mb = g.s_t.n
        wait("vector", g.s_t, mb)
        run("vector", lambda e: e.tensor_tensor(
            g.sM8[:], g.PS[6][0:8, 0:512], g.sEID[:], ALU.is_equal), g.s_v)
        em("vector", lambda e: e.drain())
        if kk == 0 and "m80" in g.dumps:
            run("vector", lambda e: e.tensor_copy(g.sSC[0:8, 0:512], g.sM8[:]), g.s_v)
            em("vector", lambda e: e.drain())
            wait("sync", g.s_v)
            dma("sync", g.dumps["m80"], g.sSC[0:8, 0:512], g.s_st)
            wait("vector", g.s_st, g.s_st.n)
            run("vector", lambda e: e.tensor_copy(g.sSC[0:8, 0:512], g.PS[6][0:8, 0:512]), g.s_v)
            em("vector", lambda e: e.drain())
            wait("sync", g.s_v)
            dma("sync", g.dumps["bc0"], g.sSC[0:8, 0:512], g.s_st)
            wait("vector", g.s_st, g.s_st.n)
            run("vector", lambda e: e.tensor_copy(g.sSC[0:1, 0:512], g.sASG[:]), g.s_v)
            em("vector", lambda e: e.drain())
            wait("sync", g.s_v)
            dma("sync", g.dumps["asg0"], g.sSC[0:1, 0:512], g.s_st)
            wait("vector", g.s_st, g.s_st.n)
        run("vector", lambda e: e.tensor_scalar(
            g.sNM8[:], g.sM8[:], -1.0, 1.0, ALU.mult, ALU.add), g.s_v)
        em("vector", lambda e: e.drain())
        m8v = g.s_v.n
        wait("tensor", g.s_v, m8v)
        for co in range(CH):
            if co < 6:
                run("tensor", lambda e, cc=co: e.matmul(
                    g.PS[cc][:, 0:512], g.sB2[:, cc * 128:(cc + 1) * 128],
                    g.sM8[:], start=True, stop=False), g.s_t)
            else:
                run("tensor", lambda e, cc=co: e.matmul(
                    g.PS[7][:, 0:512], g.sB2[:, cc * 128:(cc + 1) * 128],
                    g.sM8[:], start=True, stop=True), g.s_t)
                wait("vector", g.s_t)
                run("vector", lambda e, cc=co: e.tensor_copy(
                    vf(M67_ + (cc - 6) * 512, 512), g.PS[7][:, 0:512]), g.s_v)
                wait("tensor", g.s_v)
        vadd_done[6] = vadd_done[7] = g.s_v.n
        wait("tensor", g.s_v, g.hbc)
        for e_ in range(NE):
            if wready[e_] is None:
                wready[e_] = w_load(e_, e_ % 2)
            if e_ + 1 < NE and wready[e_ + 1] is None:
                wready[e_ + 1] = w_load(e_ + 1, (e_ + 1) % 2)
            wait("tensor", g.s_ld, wready[e_])
            w1b = W1_[e_ % 2]
            w2b = W2_[e_ % 2]
            wait("tensor", g.s_s, gelu_done[7])
            wait("tensor", g.s_v, vadd_done[7])
            run("tensor", lambda e, ee=e_: e.matmul(
                g.PS[7][:, 0:512], g.sSELB[:, ee * 128:(ee + 1) * 128], g.sM8[:],
                start=True, stop=True), g.s_t)
            bc8 = g.s_t.n
            wait("vector", g.s_t, bc8)
            run("vector", lambda e: e.tensor_copy(SMB, g.PS[7][:, 0:512]), g.s_v)
            vadd_done[7] = g.s_v.n
            gfi = []
            for fi in range(CH):
                hbank = 6 + fi % 2
                wait("tensor", g.s_s, gelu_done[hbank])
                wait("tensor", g.s_v, vadd_done[hbank])
                for ci in range(CH):
                    run("tensor", lambda e, b_=hbank, cc=ci, ff=fi, wb=w1b: e.matmul(
                        g.PS[b_][:, 0:512],
                        vb(wb + cc * C + ff * 128, 128),
                        HBV[:, cc * 512:(cc + 1) * 512],
                        start=(cc == 0), stop=(cc == CH - 1)),
                        g.s_t if ci == CH - 1 else None)
                wait("scalar", g.s_t)
                run("scalar", lambda e, b_=hbank, ee=e_, ff=fi: e.activation(
                    vb(HKe_ + ff * 512, 512), g.PS[b_][:, 0:512], AF.Gelu,
                    bias=g.sVEC[:, 152 + ee * 8 + ff:153 + ee * 8 + ff]), g.s_s)
                gelu_done[hbank] = g.s_s.n
                wait("vector", g.s_s)
                run("vector", lambda e, ff=fi: e.tensor_tensor(
                    vb(HKe_ + ff * 512, 512), vb(HKe_ + ff * 512, 512),
                    SMB, ALU.mult), g.s_v)
                gfi.append(g.s_v.n)
            hkv = g.s_s.n
            if kk == 0 and e_ == 0 and "he0" in g.dumps:
                wait("vector", g.s_s, hkv)
                run("vector", lambda e: e.tensor_copy(vf(Yt_, 4096), vb(HKe_, 4096)), g.s_v)
                em("vector", lambda e: e.drain())
                wait("sync", g.s_v)
                dma("sync", g.dumps["he0"], vf(Yt_, 4096), g.s_st)
                wait("vector", g.s_st, g.s_st.n)
                wait("scalar", g.s_st, g.s_st.n)
                wait("tensor", g.s_st, g.s_st.n)
            for fi in range(CH):
                wait("tensor", g.s_v, gfi[fi])
                for co in range(6):
                    run("tensor", lambda e, cc=co, ff=fi, wb=w2b,
                            sp=(e_ == NE - 1 and fi == CH - 1): e.matmul(
                        g.PS[cc][:, 0:512],
                        vb(wb + ff * C + cc * 128, 128),
                        vb(HKe_ + ff * 512, 512),
                        start=False, stop=sp),
                        g.s_t if co == 5 else None)
            for co in (6, 7):
                bank = co
                wait("tensor", g.s_v)
                for fi in range(CH):
                    run("tensor", lambda e, b_=bank, cc=co, ff=fi, wb=w2b: e.matmul(
                        g.PS[b_][:, 0:512],
                        vb(wb + ff * C + cc * 128, 128),
                        vb(HKe_ + ff * 512, 512),
                        start=(ff == 0), stop=(ff == CH - 1)),
                        g.s_t if fi == CH - 1 else None)
                wait("vector", g.s_t)
                run("vector", lambda e, cc=co: e.tensor_tensor(
                    vf(M67_ + (cc - 6) * 512, 512), vf(M67_ + (cc - 6) * 512, 512),
                    g.PS[6 + (cc - 6)][:, 0:512], ALU.add), g.s_v)
                vadd_done[bank] = g.s_v.n
        wready = [None] * NE
        tall = g.s_t.n
        wait("vector", g.s_t, tall)
        for co in range(CH):
            if co < 6:
                run("vector", lambda e, cc=co: e.tensor_tensor(
                    vf(H_ + cc * 512, 512), vf(H_ + cc * 512, 512),
                    g.PS[cc][:, 0:512], ALU.add), g.s_v)
            else:
                run("vector", lambda e, cc=co: e.tensor_tensor(
                    vf(H_ + cc * 512, 512), vf(H_ + cc * 512, 512),
                    vf(M67_ + (cc - 6) * 512, 512), ALU.add), g.s_v)
        em("vector", lambda e: e.drain())

        if kk == 0 and "hm0" in g.dumps:
            wait("sync", g.s_v)
            dma("sync", g.dumps["hm0"], vf(H_, TILE_COLS), g.s_st)
            wait("vector", g.s_st, g.s_st.n)
            wait("scalar", g.s_st, g.s_st.n)
            wait("tensor", g.s_st, g.s_st.n)
        def src_hm(c, o, w):
            return g.ARF[:, H_ + c * 512 + o: H_ + c * 512 + o + w]
        g.emit_ln(src_hm, src_hm, [(0, 512)],
                  lambda c: g.vcol("fg2", c), lambda c: g.vcol("fb2", c))
        dma("sync", vf(Yt_, TILE_COLS), g.y1_d[kk * 128:(kk + 1) * 128, :], g.s_ld)
        ldy = g.s_ld.n
        wait("vector", g.s_ld, ldy)
        run("vector", lambda e: e.tensor_tensor(
            vf(H_, TILE_COLS), vf(H_, TILE_COLS), vf(Yt_, TILE_COLS), ALU.add), g.s_v)
        em("vector", lambda e: e.drain())
        g.emit_ln(src_hm, src_hm, [(0, 512)], None, None)
        ov = g.s_v.n
        wait("sync", g.s_v, ov)
        dma("sync", g.out_d[kk * 128:(kk + 1) * 128, :], vf(H_, TILE_COLS), g.s_st)


def _pack_weight(w):
    kin, nout = w.shape
    return np.ascontiguousarray(
        w.reshape(kin // 128, 128, nout).transpose(1, 0, 2).reshape(128, -1))


def _pack_vec(v):
    return np.ascontiguousarray(np.asarray(v, np.float32).reshape(CH, 128).T)


def _stack_groups_ck(wp):
    blocks = []
    for gi in range(8):
        blocks.append(np.concatenate(
            [wp[:, cc * FF + 512 * gi: cc * FF + 512 * gi + 512] for cc in range(CH)], axis=1))
    return np.ascontiguousarray(np.concatenate(blocks, axis=0))


def _stack_halves_ip(wp):
    blocks = []
    for h in range(2):
        blocks.append(np.concatenate(
            [wp[:, ci * 3 * C + h * 1536: ci * 3 * C + (h + 1) * 1536] for ci in range(CH)], axis=1))
    return np.ascontiguousarray(np.concatenate(blocks, axis=0))


def _host_inputs(inp):
    x = np.asarray(inp["L0_attn_x"], np.float32)
    x_pad = np.concatenate([np.zeros((B, 2, C), np.float32), x], axis=1)
    vec = np.zeros((128, 256), np.float32)
    order = ["L0_attn_ln0_g", "L0_attn_ln0_b", "L0_attn_ln1_g", "L0_attn_ln1_b",
             "L0_attn_tm_k", "L0_attn_tm_v", "L0_attn_tm_r", "L0_attn_cm_k",
             "L0_attn_cm_r", "L0_attn_time_decay", "L0_attn_time_first",
             "L0_ffn_out_b", "L0_ffn_ln1_g", "L0_ffn_ln1_b", "L0_ffn_ln2_g",
             "L0_ffn_ln2_b"]
    for idx, key in enumerate(order):
        vec[:, idx * 8:(idx + 1) * 8] = _pack_vec(inp[key])
    vec[:, 128:152] = np.ascontiguousarray(
        np.asarray(inp["L0_ffn_in_proj_b"], np.float32).reshape(24, 128).T)
    vec[:, 152:216] = np.ascontiguousarray(
        np.asarray(inp["L0_ffn_b1"], np.float32).reshape(NE * CH, 128).T)
    c8 = np.zeros((8, 64), np.float32)
    c8[:, 0] = np.asarray(inp["L0_ffn_router_b"], np.float32)
    common = dict(
        wk_t=_pack_weight(np.asarray(inp["L0_attn_Wk"], np.float32)),
        wv_t=_pack_weight(np.asarray(inp["L0_attn_Wv"], np.float32)),
        wr_t=_pack_weight(np.asarray(inp["L0_attn_Wr"], np.float32)),
        wo_t=_pack_weight(np.asarray(inp["L0_attn_Wo"], np.float32)),
        wck_t=_stack_groups_ck(_pack_weight(np.asarray(inp["L0_attn_Wck"], np.float32))),
        wcv_t=np.concatenate([_pack_weight(np.asarray(inp["L0_attn_Wcv"], np.float32))[:, gi * 4096:(gi + 1) * 4096] for gi in range(8)], axis=0),
        wcr_t=_pack_weight(np.asarray(inp["L0_attn_Wcr"], np.float32)),
        wip_t=_stack_halves_ip(_pack_weight(np.ascontiguousarray(np.asarray(inp["L0_ffn_in_proj_w"], np.float32).T))),
        wow_t=_pack_weight(np.ascontiguousarray(np.asarray(inp["L0_ffn_out_w"], np.float32).T)),
        wrt_t=_pack_weight(np.ascontiguousarray(np.asarray(inp["L0_ffn_router_w"], np.float32).T)),
        w1_t=np.concatenate([_pack_weight(np.ascontiguousarray(np.asarray(inp["L0_ffn_w1"][e], np.float32).T)) for e in range(NE)], axis=0),
        w2_t=np.concatenate([_pack_weight(np.ascontiguousarray(np.asarray(inp["L0_ffn_w2"][e], np.float32).T)) for e in range(NE)], axis=0),
        vec32=vec, c8f=c8,
        b2f=np.asarray(inp["L0_ffn_b2"], np.float32),
        eidb=np.ascontiguousarray(np.broadcast_to(
            np.arange(8, dtype=np.float32)[:, None], (8, 512))).astype(NPBF),
        ones1f=np.ones((1, 128), np.float32),
        ones8b=np.ones((1, 8), NPBF),
        neg9b=np.asarray([[(-1e9 if k2 == (m // 128) else 0.0) for m in range(1024)]
                          for k2 in range(8)], NPBF),
        selb=np.asarray([[(1.0 if k2 == (m // 128) else 0.0) for m in range(1024)]
                         for k2 in range(8)], NPBF),
        identb=np.eye(128, dtype=NPBF),
        identf=np.eye(128, dtype=np.float32),
        iotarep=np.ascontiguousarray(np.broadcast_to(np.arange(257, dtype=np.float32), (128, 257))),
        iota8=np.ascontiguousarray(np.broadcast_to(np.arange(8, dtype=np.float32), (128, 8))),
        maskA=np.asarray([[1.0 if m // 16 == gg else 0.0 for m in range(128)] for gg in range(8)], NPBF),
        maskB=np.asarray([[0.0 if n // 16 == gg else -1e30 for n in range(128)] for gg in range(8)], NPBF),
    )
    maps = []
    for i in range(NC):
        s = PPC * i
        slabs = [x_pad[:, s:s + 2, :]]
        for kk in range(NT):
            slabs.append(x_pad[:, s + 1 + TP * kk: s + 1 + TP * kk + SEG, :])
        x_core = np.zeros(((1 + NT) * 128, HALO_COLS), np.float32)
        for si, sl in enumerate(slabs):
            L = sl.shape[1]
            x_core[si * 128:(si + 1) * 128, 0:CH * B * L] = (
                sl.transpose(2, 0, 1).reshape(CH, 128, B * L).transpose(1, 0, 2).reshape(128, CH * B * L))
        gates = np.zeros((128, 2), np.float32)
        gates[:, 0] = 0.0 if i == 0 else 1.0
        gates[:, 1] = 1.0 - gates[:, 0]
        gp = np.zeros((128, 8), np.float32)
        gp[:, :i] = 1.0
        m = dict(common)
        m.update(x_t=x_core, gates=gates, gpre=gp, gpre256=np.ascontiguousarray(gp * 256.0))
        maps.append(m)
    return maps


def _assemble(results):
    out = np.zeros((B, TFULL, C), np.float32)
    for i in range(NC):
        arr = np.asarray(results[i]["out_d"], np.float32)
        a = arr.reshape(NT, 128, CH, TP, B)
        out[:, PPC * i:PPC * (i + 1), :] = a.transpose(4, 0, 3, 2, 1).reshape(B, PPC, C)
    return out


def kernel(**inputs):
    maps = _host_inputs(inputs)
    nc = build_program()
    res = run_bass_kernel_spmd(nc, maps, core_ids=list(range(NC)))
    return _assemble(res.results)


# version 14 (final)
# speedup vs baseline: 1.1216x; 1.0072x over previous
import numpy as np
import ml_dtypes
import contextlib
import concourse.bass as bass
import concourse.mybir as mybir
from concourse.bass_utils import run_bass_kernel_spmd

F32 = mybir.dt.float32
BF16 = mybir.dt.bfloat16
AF = mybir.ActivationFunctionType
ALU = mybir.AluOpType
AX = mybir.AxisListType.X
NPBF = np.dtype(ml_dtypes.bfloat16)

NC = 8
B = 16
C = 1024
TFULL = 2048
PPC = 256
TP = 32
NT = 8
CH = 8
FF = 4096
NE = 8
SEG = TP + 1
TILE_COLS = CH * B * TP
HALO_COLS = CH * B * SEG
EPS = 1e-5


class SemT:
    def __init__(self, h):
        self.h = h
        self.n = 0

    def inc(self, amt):
        self.n += amt
        return self.n


class Ctx:
    pass


def build_program(dump=()):
    nc = bass.Bass("TRN2", target_bir_lowering=False, debug=False)
    g = Ctx()
    g.nc = nc

    def din(name, shape, dt=F32):
        return nc.dram_tensor(name, list(shape), dt, kind="ExternalInput").ap()

    g.x_t = din("x_t", [(1 + NT) * 128, HALO_COLS])
    g.wk_t = din("wk_t", [128, CH * C])
    g.wv_t = din("wv_t", [128, CH * C])
    g.wr_t = din("wr_t", [128, CH * C])
    g.wo_t = din("wo_t", [128, CH * C])
    g.wck_t = din("wck_t", [8 * 128, CH * 512])
    g.wcv_t = din("wcv_t", [8 * 128, 4 * C])
    g.wcr_t = din("wcr_t", [128, CH * C])
    g.wip_t = din("wip_t", [2 * 128, 12288])
    g.wow_t = din("wow_t", [128, CH * C])
    g.wrt_t = din("wrt_t", [128, CH * 8])
    g.w1_t = din("w1_t", [NE * 128, CH * C])
    g.w2_t = din("w2_t", [NE * 128, CH * C])
    g.vec32 = din("vec32", [128, 256])
    g.c8f = din("c8f", [8, 64])
    g.b2f = din("b2f", [8, C])
    g.eidb = din("eidb", [8, 512], BF16)
    g.ones1f = din("ones1f", [1, 128])
    g.ones8b = din("ones8b", [1, 8], BF16)
    g.neg9b = din("neg9b", [8, 1024], BF16)
    g.identb = din("identb", [128, 128], BF16)
    g.identf = din("identf", [128, 128])
    g.iotarep = din("iotarep", [128, 257])
    g.iota8 = din("iota8", [128, 8])
    g.gates = din("gates", [128, 2])
    g.gpre = din("gpre", [128, 8])
    g.gpre256 = din("gpre256", [128, 8])
    g.maskA = din("maskA", [8, 128], BF16)
    g.selb = din("selb", [8, 1024], BF16)
    g.maskB = din("maskB", [8, 128], BF16)

    g.out_d = nc.dram_tensor("out_d", [NT * 128, TILE_COLS], F32, kind="ExternalOutput").ap()
    g.dumps = {}
    for nm in dump:
        shape = {"hm0": [128, TILE_COLS], "m80": [8, 512], "he0": [128, 4096],
                 "asg0": [1, 512], "bc0": [8, 512],
                 "hk0": [128, 2048], "x1p": [128, 128], "wkvp": [128, 128],
                 "x1": [128, TILE_COLS], "xc": [128, HALO_COLS],
                 "kv": [128, TILE_COLS], "rr": [128, TILE_COLS],
                 "yp": [128, TILE_COLS], "wkv": [128, TILE_COLS],
                 "dbgP": [128, HALO_COLS], "dbgF": [128, HALO_COLS],
                 "dbgN": [128, HALO_COLS], "dbgD": [128, HALO_COLS],
                 "num": [128, 256 + NT * TILE_COLS], "den": [128, 256 + NT * TILE_COLS],
                 "r": [128, 256 + NT * TILE_COLS], "y1": [128, NT * TILE_COLS],
                 "h": [128, NT * TILE_COLS], "seeds": [128, 256],
                 "assign": [1, 4096]}[nm]
        g.dumps[nm] = nc.dram_tensor("dump_" + nm, shape, F32, kind="ExternalOutput").ap()

    g.num_d = nc.dram_tensor("num_d", [(1 + NT) * 128, TILE_COLS], F32).ap()
    g.den_d = nc.dram_tensor("den_d", [(1 + NT) * 128, TILE_COLS], F32).ap()
    g.r_d = nc.dram_tensor("r_d", [(1 + NT) * 128, TILE_COLS], F32).ap()
    g.y1_d = nc.dram_tensor("y1_d", [NT * 128, TILE_COLS], F32).ap()
    g.h_d = nc.dram_tensor("h_d", [NT * 128, TILE_COLS], F32).ap()
    g.wob_d = nc.dram_tensor("wob_d", [128, CH * C], BF16).ap()
    g.wcrb_d = nc.dram_tensor("wcrb_d", [128, CH * C], BF16).ap()
    g.wckb_d = nc.dram_tensor("wckb_d", [8 * 128, CH * 512], BF16).ap()
    g.wcvb_d = nc.dram_tensor("wcvb_d", [8 * 128, 4 * C], BF16).ap()
    g.wipb_d = nc.dram_tensor("wipb_d", [2 * 128, 12288], BF16).ap()
    g.wowb_d = nc.dram_tensor("wowb_d", [128, CH * C], BF16).ap()
    g.w1b_d = nc.dram_tensor("w1b_d", [NE * 128, CH * C], BF16).ap()
    g.w2b_d = nc.dram_tensor("w2b_d", [NE * 128, CH * C], BF16).ap()
    g.asg_d = nc.dram_tensor("asg_d", [1, 4096], F32).ap()
    g.cc_in = nc.dram_tensor("cc_in", [128, 256], F32).ap()
    g.cc_out = nc.dram_tensor("cc_out", [1024, 256], F32).ap()

    es = contextlib.ExitStack()
    with es:
        def sbt(name, shape, dt=F32):
            return es.enter_context(nc.sbuf_tensor(name, list(shape), dt))

        g.sVEC = sbt("sVEC", [128, 256])
        g.sC8 = sbt("sC8", [8, 64])
        g.sB2 = sbt("sB2", [8, C], BF16)
        g.sEID = sbt("sEID", [8, 512], BF16)
        g.sONES1 = sbt("sONES1", [1, 128])
        g.sONES8 = sbt("sONES8", [1, 8], BF16)
        g.sSELB = sbt("sSELB", [8, 1024], BF16)
        g.sNEG9 = sbt("sNEG9", [8, 1024], BF16)
        g.sIDB = sbt("sIDB", [128, 128], BF16)
        g.sIDF = sbt("sIDF", [128, 128])
        g.sIOTA = sbt("sIOTA", [128, 257])
        g.sIOTA8 = sbt("sIOTA8", [128, 8])
        g.sGATE = sbt("sGATE", [128, 2])
        g.sGPRE = sbt("sGPRE", [128, 8])
        g.sGP256 = sbt("sGP256", [128, 8])
        g.sMA = sbt("sMA", [8, 128], BF16)
        g.sMB = sbt("sMB", [8, 128], BF16)
        g.sONE1c = sbt("sONE1c", [128, 1])
        g.sDRV = sbt("sDRV", [128, 104])
        g.sROW = sbt("sROW", [1, 1600])
        g.sSEED = sbt("sSEED", [128, 512])
        g.sGATH = sbt("sGATH", [128, 512])
        g.sSNSD = sbt("sSNSD", [128, 256])
        g.sXC = sbt("sXC", [128, 128])
        g.sASG = sbt("sASG", [1, 512], BF16)
        g.sM8 = sbt("sM8", [8, 512], BF16)
        g.sNM8 = sbt("sNM8", [8, 512], BF16)
        g.sSC = sbt("sSC", [128, 1088])
        g.sLREP = sbt("sLREP", [128, CH * SEG * B])
        g.sLAM = g.sLREP
        g.sLPRE = sbt("sLPRE", [128, CH * B * 2])
        g.ARF = sbt("ARF", [128, 21120])
        g.ARB = sbt("ARB", [128, 36864], BF16)
        g.PS = [es.enter_context(nc.psum_tensor("ps%d" % i, [128, 512], F32))
                for i in range(8)]

        g.s_ld = SemT(es.enter_context(nc.semaphore("s_ld")))
        g.s_gc = SemT(es.enter_context(nc.semaphore("s_gc")))
        g.s_t = SemT(es.enter_context(nc.semaphore("s_t")))
        g.s_v = SemT(es.enter_context(nc.semaphore("s_v")))
        g.s_s = SemT(es.enter_context(nc.semaphore("s_s")))
        g.s_st = SemT(es.enter_context(nc.semaphore("s_st")))
        g.s_cc = SemT(es.enter_context(nc.semaphore("s_cc")))

        g.P = {"sync": [], "tensor": [], "vector": [], "scalar": [], "gpsimd": []}

        def em(eng, fn):
            g.P[eng].append(fn)

        def wait(eng, sem, v=None):
            vv = sem.n if v is None else v
            em(eng, lambda e, h=sem.h, w=vv: e.wait_ge(h, w))

        def dma(eng, dst, src, sem, slow=False):
            sem.inc(16)
            em(eng, lambda e, d=dst, s=src, h=sem.h, sl=slow: e.dma_start(
                out=d, in_=s, allow_slow_non_contiguous=sl).then_inc(h, 16))

        def run(eng, fn, sem=None, amt=1):
            if sem is None:
                em(eng, fn)
            else:
                sem.inc(amt)
                em(eng, lambda e, f=fn, h=sem.h, a=amt: f(e).then_inc(h, a))

        g.em, g.wait, g.dma, g.run = em, wait, dma, run

        def vf(off, n):
            return g.ARF[:, off:off + n]

        def vb(off, n):
            return g.ARB[:, off:off + n]

        g.vf, g.vb = vf, vb

        VOFF = dict(g0=0, b0=8, g1=16, b1=24, tmk=32, tmv=40, tmr=48, cmk=56,
                    cmr=64, td=72, tf=80, ob=88, fg1=96, fb1=104, fg2=112,
                    fb2=120, ipb=128, b1e=152)
        DOFF = dict(w=0, lam=8, eu=16, bq8=24, fpre=32)

        def vcol(name, j=0):
            o = VOFF[name] + j
            return g.sVEC[:, o:o + 1]

        def dcol(name, j=0):
            o = DOFF[name] + j
            return g.sDRV[:, o:o + 1]

        g.vcol, g.dcol = vcol, dcol

        def emit_ln(get_src, put_out, sub_ranges, gcol=None, bcol=None):
            for (o, w) in sub_ranges:
                wait("tensor", g.s_v)
                wait("tensor", g.s_s)
                wait("scalar", g.s_v)
                for c in range(CH):
                    run("tensor", lambda e, ap=get_src(c, o, w), ww=w, cc=c: e.matmul(
                        g.PS[4][0:1, 0:ww], g.sONE1c[:], ap,
                        start=(cc == 0), stop=(cc == CH - 1)), g.s_t)
                for c in range(CH):
                    if c >= 2:
                        wait("scalar", g.s_t)
                    run("scalar", lambda e, ap=get_src(c, o, w), cc=c, ww=w: e.activation(
                        g.sSC[:, (cc % 2) * 544:(cc % 2) * 544 + ww],
                        ap, AF.Square), g.s_s)
                    wait("tensor", g.s_s)
                    run("tensor", lambda e, cc=c, ww=w: e.matmul(
                        g.PS[5][0:1, 0:ww], g.sONE1c[:],
                        g.sSC[:, (cc % 2) * 544:(cc % 2) * 544 + ww],
                        start=(cc == 0), stop=(cc == CH - 1)), g.s_t)
                tmm = g.s_t.n
                wait("vector", g.s_t, tmm)
                run("vector", lambda e, ww=w: e.tensor_copy(
                    g.sROW[0:1, 0:ww], g.PS[4][0:1, 0:ww]), g.s_v)
                em("vector", lambda e: e.drain())
                run("vector", lambda e, ww=w: e.tensor_tensor(
                    g.sROW[0:1, 1024:1024 + ww], g.sROW[0:1, 0:ww],
                    g.sROW[0:1, 0:ww], ALU.mult), g.s_v)
                em("vector", lambda e: e.drain())
                run("vector", lambda e, ww=w: e.tensor_tensor(
                    g.sROW[0:1, 512:512 + ww], g.PS[5][0:1, 0:ww],
                    g.sROW[0:1, 1024:1024 + ww], ALU.subtract), g.s_v)
                vrow = g.s_v.n
                wait("scalar", g.s_v, vrow)
                run("scalar", lambda e, ww=w: e.activation(
                    g.sROW[0:1, 512:512 + ww], g.sROW[0:1, 512:512 + ww],
                    AF.Ln, bias=g.sDRV[0:1, 96:97]), g.s_s)
                em("scalar", lambda e: e.drain())
                run("scalar", lambda e, ww=w: e.activation(
                    g.sROW[0:1, 512:512 + ww], g.sROW[0:1, 512:512 + ww],
                    AF.Exp, scale=-0.5), g.s_s)
                srow = g.s_s.n
                wait("tensor", g.s_s, srow)
                run("tensor", lambda e, ww=w: e.matmul(
                    g.PS[6][:, 0:ww], g.sONES1[:], g.sROW[0:1, 0:ww],
                    start=True, stop=True), g.s_t)
                run("tensor", lambda e, ww=w: e.matmul(
                    g.PS[7][:, 0:ww], g.sONES1[:], g.sROW[0:1, 512:512 + ww],
                    start=True, stop=True), g.s_t)
                tb = g.s_t.n
                wait("vector", g.s_t, tb)
                for c in range(CH):
                    run("vector", lambda e, po=put_out(c, o, w), gs=get_src(c, o, w), ww=w: e.tensor_tensor(
                        po, gs, g.PS[6][:, 0:ww], ALU.subtract), g.s_v)
                for c in range(CH):
                    run("vector", lambda e, po=put_out(c, o, w), ww=w: e.tensor_tensor(
                        po, po, g.PS[7][:, 0:ww], ALU.mult), g.s_v)
                if gcol is not None:
                    for c in range(CH):
                        run("vector", lambda e, po=put_out(c, o, w), gc=gcol(c), bc=bcol(c): e.tensor_scalar(
                            po, po, gc, bc, ALU.mult, ALU.add), g.s_v)

        g.emit_ln = emit_ln

        _preamble(g)
        _s1(g)
        _collective(g)
        _s2(g)
        _s3(g)
        _s4(g)

        wait("sync", g.s_st, g.s_st.n)
        wait("sync", g.s_gc, g.s_gc.n)

        with nc.Block() as block:
            @block.sync
            def _(q):
                for fn in g.P["sync"]:
                    fn(q)

            @block.tensor
            def _(q):
                for fn in g.P["tensor"]:
                    fn(q)

            @block.vector
            def _(q):
                for fn in g.P["vector"]:
                    fn(q)

            @block.scalar
            def _(q):
                for fn in g.P["scalar"]:
                    fn(q)

            @block.gpsimd
            def _(q):
                for fn in g.P["gpsimd"]:
                    fn(q)
    return nc


def _preamble(g):
    em, wait, dma, run = g.em, g.wait, g.dma, g.run
    for dst, src in [(g.sVEC, g.vec32), (g.sC8, g.c8f), (g.sONES1, g.ones1f),
                     (g.sONES8, g.ones8b), (g.sNEG9, g.neg9b),
                     (g.sIDB, g.identb), (g.sIDF, g.identf),
                     (g.sIOTA, g.iotarep), (g.sIOTA8, g.iota8),
                     (g.sGATE, g.gates), (g.sGPRE, g.gpre),
                     (g.sGP256, g.gpre256), (g.sMA, g.maskA),
                     (g.sMB, g.maskB), (g.sEID, g.eidb),
                     (g.sSELB, g.selb)]:
        dma("sync", dst[:], src, g.s_ld)
    dma("gpsimd", g.sB2[:], g.b2f, g.s_gc)

    run("vector", lambda e: e.memset(g.sONE1c[:], 1.0 / 1024.0), g.s_v)
    run("vector", lambda e: e.memset(g.sDRV[:, 96:97], float(EPS)), g.s_v)
    wait("scalar", g.s_ld)
    run("scalar", lambda e: e.activation(g.sDRV[:, 0:8], g.sVEC[:, 72:80], AF.Exp), g.s_s)
    wait("vector", g.s_s)
    run("vector", lambda e: e.tensor_scalar_mul(g.sDRV[:, 0:8], g.sDRV[:, 0:8], -1.0), g.s_v)
    em("vector", lambda e: e.drain())
    wait("scalar", g.s_v)
    run("scalar", lambda e: e.activation(g.sDRV[:, 8:16], g.sDRV[:, 0:8], AF.Exp), g.s_s)
    run("scalar", lambda e: e.activation(g.sDRV[:, 16:24], g.sVEC[:, 80:88], AF.Exp), g.s_s)
    for j in range(NC):
        run("scalar", lambda e, jj=j: e.activation(
            g.sDRV[:, 32 + 8 * jj:40 + 8 * jj], g.sDRV[:, 0:8], AF.Exp,
            scale=g.sGP256[:, jj:jj + 1]), g.s_s)
    sd = g.s_s.n
    wait("vector", g.s_s, sd)
    run("vector", lambda e: e.tensor_scalar_mul(g.sDRV[:, 24:32], g.sVEC[:, 128:136], 0.125), g.s_v)
    run("vector", lambda e: e.memset(g.sLREP[:], 1.0), g.s_v)
    run("vector", lambda e: e.memset(g.sLPRE[:], 1.0), g.s_v)
    em("vector", lambda e: e.drain())
    for chi in range(CH):
        run("vector", lambda e, c=chi: e.tensor_scalar_mul(
            g.sLREP[:, c * 528:(c + 1) * 528], g.sLREP[:, c * 528:(c + 1) * 528],
            g.sDRV[:, 8 + c:9 + c]), g.s_v)
        run("vector", lambda e, c=chi: e.tensor_scalar_mul(
            g.sLPRE[:, c * 32:(c + 1) * 32], g.sLPRE[:, c * 32:(c + 1) * 32],
            g.sDRV[:, 8 + c:9 + c]), g.s_v)
    em("vector", lambda e: e.drain())
    run("vector", lambda e: e.memset(
        g.sLREP[:].rearrange("p (c b t) -> p c b t", c=CH, b=B)[:, :, :, 0:1], 0.0), g.s_v)
    run("vector", lambda e: e.memset(
        g.sLPRE[:].rearrange("p (c b t) -> p c b t", c=CH, b=B, t=2)[:, :, :, 0:1], 0.0), g.s_v)
    run("vector", lambda e: e.memset(g.sSEED[:], 0.0), g.s_v)
    em("vector", lambda e: e.drain())
    WKB_, WVB_, WRB_ = 2 * TILE_COLS, 2 * TILE_COLS + CH * C, 2 * TILE_COLS + 2 * CH * C
    for dst_off, src in [(WKB_, g.wk_t), (WVB_, g.wv_t), (WRB_, g.wr_t)]:
        dma("gpsimd", g.vb(dst_off, CH * C), src, g.s_gc)
    g.s1w = g.s_gc.n
    dma("gpsimd", g.wob_d, g.wo_t, g.s_gc)
    dma("gpsimd", g.wcrb_d, g.wcr_t, g.s_gc)
    dma("gpsimd", g.wckb_d, g.wck_t, g.s_gc)
    dma("gpsimd", g.wcvb_d, g.wcv_t, g.s_gc)
    g.prep_s2 = g.s_gc.n
    dma("gpsimd", g.wipb_d, g.wip_t, g.s_gc)
    dma("gpsimd", g.wowb_d, g.wow_t, g.s_gc)
    g.prep_s3 = g.s_gc.n
    dma("gpsimd", g.w1b_d, g.w1_t, g.s_gc)
    dma("gpsimd", g.w2b_d, g.w2_t, g.s_gc)
    g.prep_done = g.s_gc.n


def _s1(g):
    em, wait, dma, run, vf, vb = g.em, g.wait, g.dma, g.run, g.vf, g.vb
    A_, Bf_, Cp_, Dn_, Ed_ = 0, HALO_COLS, 2 * HALO_COLS, 3 * HALO_COLS, 4 * HALO_COLS
    MX0_, MX1_, MX2_ = 0, TILE_COLS, 8 * TILE_COLS
    WKB_, WVB_, WRB_ = 2 * TILE_COLS, 2 * TILE_COLS + CH * C, 2 * TILE_COLS + 2 * CH * C

    dma("sync", vf(A_, CH * B * 2), g.x_t[0:128, 0:256], g.s_ld)
    g.ldx_next = g.s_ld.n
    fevac = {0: 0, 1: 0}
    pmult = {2: 0, 3: 0}
    revac = {4: 0, 5: 0}

    for kk in range(-1, NT):
        pre = (kk == -1)
        seg = 2 if pre else SEG
        cols = CH * B * seg
        tokw = B * (seg - 1)

        def r4(base, _c=cols, _s=seg):
            return g.ARF[:, base:base + _c].rearrange("p (c b t) -> p c b t", c=CH, b=B)

        def chsl(base, c, a, bb, _s=seg):
            return g.ARF[:, base + c * (B * _s): base + (c + 1) * (B * _s)].rearrange(
                "p (b t) -> p b t", b=B)[:, :, a:bb]

        ldx = g.ldx_next
        rt_prev = g.s_t.n
        st_prev = g.s_st.n

        wait("vector", g.s_ld, ldx)
        wait("scalar", g.s_ld, ldx)
        wait("tensor", g.s_ld, ldx)
        if kk == 0:
            wait("scalar", g.s_st, st_prev)
        subs = [(0, B * seg)] if pre else [(0, 512), (512, 16)]

        def src_xa(c, o, w, _base=A_, _s=seg):
            return g.ARF[:, _base + c * (B * _s) + o: _base + c * (B * _s) + o + w]
        g.emit_ln(src_xa, src_xa, subs, lambda c: g.vcol("g0", c), lambda c: g.vcol("b0", c))
        if pre:
            run("vector", lambda e, cc=cols: e.tensor_scalar_mul(vf(A_, cc), vf(A_, cc), g.sGATE[:, 0:1]), g.s_v)
        elif kk == 0:
            run("vector", lambda e, ap=r4(A_)[:, :, :, 0:1]: e.tensor_scalar_mul(
                ap, ap, g.sGATE[:, 0:1]), g.s_v)
        em("vector", lambda e: e.drain())
        run("vector", lambda e, ad=r4(Cp_)[:, :, :, 1:seg], ac=r4(A_)[:, :, :, 1:seg],
                ap_=r4(A_)[:, :, :, 0:seg - 1]: e.tensor_tensor(ad, ac, ap_, ALU.subtract), g.s_v)
        em("vector", lambda e: e.drain())

        wait("vector", g.s_t, rt_prev)
        for mi, (nm, moff) in enumerate([("tmr", MX2_), ("tmk", MX0_), ("tmv", MX1_)]):
            for c in range(CH):
                run("vector", lambda e, ot=vb(moff + c * tokw, tokw), i0=chsl(Cp_, c, 1, seg),
                        sc=g.vcol(nm, c), i1=chsl(A_, c, 0, seg - 1): e.scalar_tensor_tensor(
                    ot, i0, sc, i1, ALU.mult, ALU.add), g.s_v)
            mixv = g.s_v.n
            if mi == 1:
                wait("tensor", g.s_v, mixv)
                if kk <= 0:
                    wait("tensor", g.s_gc, g.s1w)
                wait("scalar", g.s_st, st_prev)
                for co in range(CH):
                    bank = co % 2
                    wait("tensor", g.s_s, fevac[bank])
                    for ci in range(CH):
                        run("tensor", lambda e, b_=bank, cc=ci, cco=co, tw=tokw: e.matmul(
                            g.PS[b_][:, 0:tw],
                            vb(WKB_ + cc * C + cco * 128, 128),
                            vb(MX0_ + cc * tw, tw),
                            start=(cc == 0), stop=(cc == CH - 1)),
                            g.s_t if ci == CH - 1 else None)
                    wait("scalar", g.s_t)
                    run("scalar", lambda e, b_=bank, ot=chsl(Bf_, co, 1, seg), tw=tokw: e.activation(
                        ot, g.PS[b_][:, 0:tw], AF.Exp), g.s_s)
                    fevac[bank] = g.s_s.n
            elif mi == 2:
                wait("tensor", g.s_v, mixv)
                for co in range(CH):
                    bank = 2 + co % 2
                    wait("tensor", g.s_v, pmult[bank])
                    for ci in range(CH):
                        run("tensor", lambda e, b_=bank, cc=ci, cco=co, tw=tokw: e.matmul(
                            g.PS[b_][:, 0:tw],
                            vb(WVB_ + cc * C + cco * 128, 128),
                            vb(MX1_ + cc * tw, tw),
                            start=(cc == 0), stop=(cc == CH - 1)),
                            g.s_t if ci == CH - 1 else None)
                    wait("vector", g.s_t)
                    wait("vector", g.s_s)
                    run("vector", lambda e, b_=bank, ot=chsl(Cp_, co, 1, seg),
                            i0=chsl(Bf_, co, 1, seg), tw=tokw: e.tensor_tensor(
                        ot, i0, g.PS[b_][:, 0:tw], ALU.mult), g.s_v)
                    pmult[bank] = g.s_v.n
        if kk < NT - 1:
            nxt = kk + 1
            wait("sync", g.s_v, mixv)
            dma("sync", vf(A_, HALO_COLS),
                g.x_t[(1 + nxt) * 128:(2 + nxt) * 128, :], g.s_ld)
            g.ldx_next = g.s_ld.n

        if pre:
            wait("vector", g.s_s)
            run("vector", lambda e, cc=cols: e.tensor_scalar_mul(vf(Cp_, cc), vf(Cp_, cc), g.sGATE[:, 0:1]), g.s_v)
            run("vector", lambda e, cc=cols: e.tensor_scalar_mul(vf(Bf_, cc), vf(Bf_, cc), g.sGATE[:, 0:1]), g.s_v)
            em("vector", lambda e: e.drain())
            run("vector", lambda e, ap=r4(Cp_)[:, :, :, 0:1]: e.memset(ap, 0.0), g.s_v)
            run("vector", lambda e, ap=r4(Bf_)[:, :, :, 0:1]: e.memset(ap, 0.0), g.s_v)
        else:
            wait("vector", g.s_s)
            run("vector", lambda e, ap=r4(Cp_)[:, :, :, 0:1]: e.tensor_copy(
                ap, g.sSEED[:, 0:128].rearrange("p (c b o) -> p c b o", c=CH, b=B)), g.s_v)
            run("vector", lambda e, ap=r4(Bf_)[:, :, :, 0:1]: e.tensor_copy(
                ap, g.sSEED[:, 128:256].rearrange("p (c b o) -> p c b o", c=CH, b=B)), g.s_v)
        em("vector", lambda e: e.drain())
        wait("vector", g.s_st, st_prev)
        lrep = g.sLPRE if pre else g.sLREP
        for c in range(CH):
            w_ = B * seg
            run("vector", lambda e, cc=c, ww=w_, lr=lrep: e.tensor_tensor_scan(
                vf(Dn_ + cc * ww, ww), lr[:, cc * ww:(cc + 1) * ww],
                vf(Cp_ + cc * ww, ww), 0.0, ALU.mult, ALU.add), g.s_v)
        for c in range(CH):
            w_ = B * seg
            run("vector", lambda e, cc=c, ww=w_, lr=lrep: e.tensor_tensor_scan(
                vf(Ed_ + cc * ww, ww), lr[:, cc * ww:(cc + 1) * ww],
                vf(Bf_ + cc * ww, ww), 0.0, ALU.mult, ALU.add), g.s_v)
        em("vector", lambda e: e.drain())
        run("vector", lambda e, ap=r4(Dn_)[:, :, :, seg - 1:seg]: e.tensor_copy(
            g.sSEED[:, 0:128].rearrange("p (c b o) -> p c b o", c=CH, b=B), ap), g.s_v)
        run("vector", lambda e, ap=r4(Ed_)[:, :, :, seg - 1:seg]: e.tensor_copy(
            g.sSEED[:, 128:256].rearrange("p (c b o) -> p c b o", c=CH, b=B), ap), g.s_v)
        if kk == NT - 1:
            run("vector", lambda e, ap=r4(Dn_)[:, :, :, seg - 2:seg - 1]: e.tensor_copy(
                g.sSEED[:, 256:384].rearrange("p (c b o) -> p c b o", c=CH, b=B), ap), g.s_v)
            run("vector", lambda e, ap=r4(Ed_)[:, :, :, seg - 2:seg - 1]: e.tensor_copy(
                g.sSEED[:, 384:512].rearrange("p (c b o) -> p c b o", c=CH, b=B), ap), g.s_v)
        em("vector", lambda e: e.drain())
        for c in range(CH):
            run("vector", lambda e, ot=chsl(Dn_, c, 0, seg - 1), i0=chsl(Cp_, c, 1, seg),
                    sc=g.dcol("eu", c): e.scalar_tensor_tensor(
                ot, i0, sc, ot, ALU.mult, ALU.add), g.s_v)
        for c in range(CH):
            run("vector", lambda e, ot=chsl(Ed_, c, 0, seg - 1), i0=chsl(Bf_, c, 1, seg),
                    sc=g.dcol("eu", c): e.scalar_tensor_tensor(
                ot, i0, sc, ot, ALU.mult, ALU.add), g.s_v)
        if pre:
            run("vector", lambda e, ap=r4(Ed_)[:, :, :, 0:1]: e.tensor_scalar(
                ap, ap, g.sGATE[:, 1:2], None, ALU.add), g.s_v)
        numden = g.s_v.n
        wait("tensor", g.s_v, numden)
        for co in range(CH):
            bank = 4 + co % 2
            wait("tensor", g.s_s, revac[bank])
            for ci in range(CH):
                run("tensor", lambda e, b_=bank, cc=ci, cco=co, tw=tokw: e.matmul(
                    g.PS[b_][:, 0:tw],
                    vb(WRB_ + cc * C + cco * 128, 128),
                    vb(MX2_ + cc * tw, tw),
                    start=(cc == 0), stop=(cc == CH - 1)),
                    g.s_t if ci == CH - 1 else None)
            wait("scalar", g.s_t)
            run("scalar", lambda e, b_=bank, ot=chsl(Cp_, co, 1, seg), tw=tokw: e.activation(
                ot, g.PS[b_][:, 0:tw], AF.Exp, scale=-1.0), g.s_s)
            revac[bank] = g.s_s.n
        wait("vector", g.s_s)
        run("vector", lambda e, cc=cols: e.tensor_scalar(
            vf(Cp_, cc), vf(Cp_, cc), 1.0, None, ALU.add), g.s_v)
        em("vector", lambda e: e.drain())
        run("vector", lambda e, cc=cols: e.reciprocal(vf(Bf_, cc), vf(Cp_, cc)), g.s_v)
        if pre:
            for si, (base, a) in enumerate([(Dn_, 0), (Ed_, 0), (Bf_, 1)]):
                run("vector", lambda e, aa=si, ap=r4(base)[:, :, :, a:a + 1]: e.tensor_copy(
                    g.sSC[:, aa * 128:(aa + 1) * 128].rearrange("p (c b o) -> p c b o", c=CH, b=B),
                    ap), g.s_v)
            em("vector", lambda e: e.drain())
            vr = g.s_v.n
            wait("sync", g.s_v, vr)
            for si, dram in enumerate([g.num_d, g.den_d, g.r_d]):
                dma("sync", dram[0:128, 0:128], g.sSC[:, si * 128:(si + 1) * 128], g.s_st)
        else:
            vr = g.s_v.n
            r0 = (1 + kk) * 128
            for (dram, base, a, bb), eng in [((g.r_d, Bf_, 1, seg), "sync"),
                                             ((g.num_d, Dn_, 0, seg - 1), "scalar"),
                                             ((g.den_d, Ed_, 0, seg - 1), "sync")]:
                wait(eng, g.s_v, vr)
                dma(eng, dram[r0:r0 + 128, :],
                    r4(base)[:, :, :, a:bb], g.s_st)


def _collective(g):
    em, wait, dma, run = g.em, g.wait, g.dma, g.run
    wait("gpsimd", g.s_v)
    dma("gpsimd", g.cc_in, g.sSEED[:, 256:512], g.s_gc)
    wait("gpsimd", g.s_gc)
    run("gpsimd", lambda e: e.collective_compute(
        "AllGather", ALU.bypass, replica_groups=[list(range(NC))],
        ins=[g.cc_in.opt()], outs=[g.cc_out.opt()]), g.s_cc)
    wait("gpsimd", g.s_cc)
    run("vector", lambda e: e.memset(g.sSNSD[:], 0.0), g.s_v)
    em("vector", lambda e: e.drain())
    for j in range(NC):
        dma("gpsimd", g.sGATH[:, 0:256], g.cc_out[j * 128:(j + 1) * 128, :], g.s_gc)
        gld = g.s_gc.n
        fj = g.sDRV[:, 32 + 8 * j:40 + 8 * j]
        wait("vector", g.s_gc, gld)
        for half in (0, 1):
            run("vector", lambda e, f_=fj, hh=half: e.tensor_tensor(
                g.sSC[:, hh * 128:hh * 128 + 128].rearrange("p (c b) -> p c b", c=CH),
                g.sSNSD[:, hh * 128:hh * 128 + 128].rearrange("p (c b) -> p c b", c=CH),
                f_.rearrange("p (c o) -> p c o", o=1).to_broadcast([128, CH, B]), ALU.mult), g.s_v)
        em("vector", lambda e: e.drain())
        for half in (0, 1):
            run("vector", lambda e, jj=j, hh=half: e.scalar_tensor_tensor(
                g.sSNSD[:, hh * 128:hh * 128 + 128],
                g.sGATH[:, hh * 128:hh * 128 + 128],
                g.sGPRE[:, jj:jj + 1],
                g.sSC[:, hh * 128:hh * 128 + 128], ALU.mult, ALU.add), g.s_v)
        em("vector", lambda e: e.drain())
        wait("gpsimd", g.s_v)
    if "seeds" in g.dumps:
        wait("sync", g.s_v)
        dma("sync", g.dumps["seeds"], g.sSNSD[:], g.s_st)
    for nm, dr in [("num", g.num_d), ("den", g.den_d), ("r", g.r_d)]:
        if nm in g.dumps:
            wait("sync", g.s_st, g.s_st.n)
            dma("sync", g.dumps[nm], dr, g.s_st)


def _s2(g):
    em, wait, dma, run, vf, vb = g.em, g.wait, g.dma, g.run, g.vf, g.vb
    A_, Bd_, Cr_, Dx_, Ekv_ = 0, 4224, 8320, 12416, 16512
    RW_, MXC_, HK_, WOC_, WG_ = 0, 4096, 8192, 10240, 18432
    FG = 4
    NG = 32 // FG

    wait("scalar", g.s_v)
    for chi in range(CH):
        run("scalar", lambda e, c=chi: e.activation(
            g.sLAM[:, c * 257:(c + 1) * 257], g.sIOTA[:], AF.Exp,
            scale=g.sDRV[:, c:c + 1]), g.s_s)
    lam_built = g.s_s.n

    def fg_load(gi, par):
        base = WG_ + par * 8192
        wait("sync", g.s_t)
        dma("sync", vb(base, 4096),
            g.wckb_d[gi * 128:(gi + 1) * 128, :], g.s_ld)
        wait("scalar", g.s_t)
        dma("scalar", vb(base + 4096, 4096),
            g.wcvb_d[gi * 128:(gi + 1) * 128, :], g.s_ld)
        return g.s_ld.n

    wait("sync", g.s_gc, g.prep_s2)
    wait("scalar", g.s_gc, g.prep_s2)
    wait("sync", g.s_st, g.s_st.n)
    wait("scalar", g.s_st, g.s_st.n)
    fg_ready = [None] * NG
    fg_ready[0] = fg_load(0, 0)
    woadd = {0: 0, 1: 0}
    rrev = {0: 0, 1: 0}
    hkread = {6: 0, 7: 0}

    for kk in range(-1, NT):
        pre = (kk == -1)
        ntok = 128 if pre else TILE_COLS
        tokc = B if pre else B * TP
        src_off = 0 if pre else 256 + kk * TILE_COLS

        def csl(base, c, _t=tokc):
            return g.ARF[:, base + c * _t: base + (c + 1) * _t]

        wait("sync", g.s_v)
        wait("sync", g.s_s)
        wait("sync", g.s_t)
        wait("sync", g.s_gc, g.s_gc.n)
        wait("scalar", g.s_v)
        wait("scalar", g.s_st, g.s_st.n)
        rb = (1 + kk) * 128 if not pre else 0
        dma("sync", vf(A_, ntok), g.num_d[rb:rb + 128, 0:ntok], g.s_ld)
        dma("scalar", vf(Bd_, ntok), g.den_d[rb:rb + 128, 0:ntok], g.s_ld)
        dma("sync", vf(Cr_, ntok), g.r_d[rb:rb + 128, 0:ntok], g.s_ld)
        if pre:
            dma("sync", vf(Ekv_, 256), g.x_t[0:128, 0:256], g.s_ld)
            prex = g.s_ld.n
            wait("vector", g.s_ld, prex)
            run("vector", lambda e: e.tensor_copy(
                vf(Dx_, 128).rearrange("p (c b o) -> p c b o", c=CH, b=B),
                vf(Ekv_, 256).rearrange("p (c b t) -> p c b t", c=CH, b=B)[:, :, :, 1:2]), g.s_v)
            em("vector", lambda e: e.drain())
        else:
            xs = g.x_t[(1 + kk) * 128:(2 + kk) * 128, :].rearrange(
                "p (c b t) -> p c b t", c=CH, b=B)[:, :, :, 1:SEG]
            dma("scalar", vf(Dx_, ntok).rearrange("p (c b t) -> p c b t", c=CH, b=B, t=TP),
                xs, g.s_ld)
        wait("scalar", g.s_t)
        dma("scalar", vb(WOC_, CH * C), g.wob_d, g.s_ld)
        ldt = g.s_ld.n

        t0 = 0 if pre else TP * kk + 1
        tw = 1 if pre else TP
        lam4 = g.sLAM.rearrange("p (c t) -> p c t", c=CH, t=528)[:, :, t0:t0 + tw] if False else \
            g.sLAM[:, 0:CH * 257].rearrange("p (c t) -> p c t", c=CH)[:, :, t0:t0 + tw]
        wait("vector", g.s_ld, ldt)
        if pre:
            wait("vector", g.s_s, lam_built)
        for (tgt, sn_off) in [(A_, 0), (Bd_, 128)]:
            run("vector", lambda e, so=sn_off, l4=lam4, _tw=tw: e.tensor_tensor(
                vf(Ekv_, CH * B * _tw).rearrange("p (c b t) -> p c b t", c=CH, b=B),
                g.sSNSD[:, so:so + 128].rearrange("p (c b o) -> p c b o", c=CH, b=B).to_broadcast([128, CH, B, _tw]),
                l4.unsqueeze(2).to_broadcast([128, CH, B, _tw]), ALU.mult), g.s_v)
            em("vector", lambda e: e.drain())
            run("vector", lambda e, tg=tgt, _n=ntok: e.tensor_tensor(
                vf(tg, _n), vf(tg, _n), vf(Ekv_, _n), ALU.add), g.s_v)
            em("vector", lambda e: e.drain())
        run("vector", lambda e, n_=ntok: e.reciprocal(vf(Ekv_, n_), vf(Bd_, n_)), g.s_v)
        em("vector", lambda e: e.drain())
        run("vector", lambda e, n_=ntok: e.tensor_tensor(vf(A_, n_), vf(A_, n_), vf(Ekv_, n_), ALU.mult), g.s_v)
        run("vector", lambda e, n_=ntok: e.tensor_tensor(vb(RW_, n_), vf(A_, n_), vf(Cr_, n_), ALU.mult), g.s_v)
        rwv = g.s_v.n
        if not pre:
            wait("scalar", g.s_v, rwv)
            dma("scalar", vf(Cr_, TILE_COLS).rearrange("p (c b t) -> p c b t", c=CH, b=B),
                g.x_t[(1 + kk) * 128:(2 + kk) * 128, :].rearrange(
                    "p (c b t) -> p c b t", c=CH, b=B)[:, :, :, 1:SEG], g.s_ld)
            g.xre = g.s_ld.n
        if kk == 0 and "wkv" in g.dumps:
            wait("sync", g.s_v, rwv)
            dma("sync", g.dumps["wkv"], vf(A_, TILE_COLS), g.s_st)
            wait("vector", g.s_st, g.s_st.n)
            wait("scalar", g.s_st, g.s_st.n)
        wait("tensor", g.s_v, rwv)
        wait("tensor", g.s_ld, ldt)
        for co in range(CH):
            bank = co % 2
            wait("tensor", g.s_v, woadd[bank])
            for ci in range(CH):
                run("tensor", lambda e, b_=bank, cc=ci, cco=co, _tk=tokc: e.matmul(
                    g.PS[b_][:, 0:_tk],
                    vb(WOC_ + cc * C + cco * 128, 128),
                    vb(RW_ + cc * _tk, _tk),
                    start=(cc == 0), stop=(cc == CH - 1)),
                    g.s_t if ci == CH - 1 else None)
            wait("vector", g.s_t)
            run("vector", lambda e, b_=bank, ap=csl(Dx_, co), _tk=tokc: e.tensor_tensor(
                ap, ap, g.PS[b_][:, 0:_tk], ALU.add), g.s_v)
            woadd[bank] = g.s_v.n
        if kk == 0 and "x1" in g.dumps:
            wait("sync", g.s_v)
            dma("sync", g.dumps["x1"], vf(Dx_, TILE_COLS), g.s_st)
            wait("vector", g.s_st, g.s_st.n)
            wait("scalar", g.s_st, g.s_st.n)
        if pre and "x1p" in g.dumps:
            wait("sync", g.s_v)
            dma("sync", g.dumps["x1p"], vf(Dx_, 128), g.s_st)
            dma("sync", g.dumps["wkvp"], vf(A_, 128), g.s_st)
            wait("vector", g.s_st, g.s_st.n)
            wait("scalar", g.s_st, g.s_st.n)
        if pre:
            def put_pre(c, o, w):
                return g.sXC[:, c * B + o: c * B + o + w]
            g.emit_ln(lambda c, o, w: g.ARF[:, Dx_ + c * tokc + o: Dx_ + c * tokc + o + w],
                      put_pre, [(0, B)],
                      lambda c: g.vcol("g1", c), lambda c: g.vcol("b1", c))
            run("vector", lambda e: e.tensor_scalar_mul(g.sXC[:], g.sXC[:], g.sGATE[:, 0:1]), g.s_v)
            em("vector", lambda e: e.drain())
            continue

        def put_xc(c, o, w):
            ap = g.ARF[:, A_ + c * (B * SEG): A_ + (c + 1) * (B * SEG)].rearrange(
                "p (b t) -> p b t", b=B)[:, :, 1:SEG]
            b0 = o // TP
            b1 = (o + w) // TP
            return ap[:, b0:b1, :]
        g.emit_ln(lambda c, o, w: g.ARF[:, Dx_ + c * tokc + o: Dx_ + c * tokc + o + w],
                  put_xc, [(0, 512)],
                  lambda c: g.vcol("g1", c), lambda c: g.vcol("b1", c))

        def xc4():
            return g.ARF[:, A_:A_ + HALO_COLS].rearrange("p (c b t) -> p c b t", c=CH, b=B)
        run("vector", lambda e, ap=xc4()[:, :, :, 0:1]: e.tensor_copy(
            ap, g.sXC[:].rearrange("p (c b o) -> p c b o", c=CH, b=B)), g.s_v)
        em("vector", lambda e: e.drain())
        run("vector", lambda e, ap=xc4()[:, :, :, SEG - 1:SEG]: e.tensor_copy(
            g.sXC[:].rearrange("p (c b o) -> p c b o", c=CH, b=B), ap), g.s_v)
        em("vector", lambda e: e.drain())
        if kk == 0 and "xc" in g.dumps:
            wait("sync", g.s_v)
            dma("sync", g.dumps["xc"], vf(A_, HALO_COLS), g.s_st)
            wait("vector", g.s_st, g.s_st.n)
            wait("scalar", g.s_st, g.s_st.n)
        run("vector", lambda e, a1=xc4()[:, :, :, 1:SEG], a0=xc4()[:, :, :, 0:SEG - 1]: e.tensor_tensor(
            vf(Bd_, TILE_COLS).rearrange("p (c b t) -> p c b t", c=CH, b=B),
            a1, a0, ALU.subtract), g.s_v)
        em("vector", lambda e: e.drain())
        wait("vector", g.s_t)
        for c in range(CH):
            run("vector", lambda e, cc=c: e.scalar_tensor_tensor(
                vb(RW_ + cc * tokc, tokc),
                g.ARF[:, Bd_ + cc * tokc: Bd_ + (cc + 1) * tokc], g.vcol("cmk", cc),
                g.ARF[:, A_ + cc * (B * SEG): A_ + (cc + 1) * (B * SEG)].rearrange(
                    "p (b t) -> p b t", b=B)[:, :, 0:SEG - 1],
                ALU.mult, ALU.add), g.s_v)
        for c in range(CH):
            run("vector", lambda e, cc=c: e.scalar_tensor_tensor(
                vb(MXC_ + cc * tokc, tokc),
                g.ARF[:, Bd_ + cc * tokc: Bd_ + (cc + 1) * tokc], g.vcol("cmr", cc),
                g.ARF[:, A_ + cc * (B * SEG): A_ + (cc + 1) * (B * SEG)].rearrange(
                    "p (b t) -> p b t", b=B)[:, :, 0:SEG - 1],
                ALU.mult, ALU.add), g.s_v)
        mixc = g.s_v.n
        wait("tensor", g.s_v, mixc)
        first_mm = [True] * CH
        for gi in range(NG):
            if fg_ready[gi] is None:
                wait("sync", g.s_t)
                fg_ready[gi] = fg_load(gi, gi % 2)
            if gi + 1 < NG and fg_ready[gi + 1] is None:
                wait("sync", g.s_t)
                fg_ready[gi + 1] = fg_load(gi + 1, (gi + 1) % 2)
            wait("tensor", g.s_ld, fg_ready[gi])
            wbase = WG_ + (gi % 2) * 8192
            prev_wcv_t = g.s_t.n
            for fi in range(FG):
                hbank = 6 + fi % 2
                wait("tensor", g.s_v, hkread[hbank])
                for ci in range(CH):
                    run("tensor", lambda e, b_=hbank, cc=ci, ff=fi, wb=wbase: e.matmul(
                        g.PS[b_][:, 0:512],
                        vb(wb + cc * (128 * FG) + ff * 128, 128),
                        vb(RW_ + cc * 512, 512),
                        start=(cc == 0), stop=(cc == CH - 1)),
                        g.s_t if ci == CH - 1 else None)
                wait("vector", g.s_t)
                run("vector", lambda e, b_=hbank, ff=fi: e.tensor_relu(
                    g.sSC[:, 544 * (ff % 2):544 * (ff % 2) + 512],
                    g.PS[b_][:, 0:512]), g.s_v)
                hkread[hbank] = g.s_v.n
                wait("vector", g.s_t, prev_wcv_t)
                run("vector", lambda e, ff=fi: e.tensor_tensor(
                    vb(HK_ + ff * 512, 512),
                    g.sSC[:, 544 * (ff % 2):544 * (ff % 2) + 512],
                    g.sSC[:, 544 * (ff % 2):544 * (ff % 2) + 512], ALU.mult), g.s_v)
            hkv = g.s_v.n
            if kk == 0 and gi == 0 and "hk0" in g.dumps:
                run("vector", lambda e: e.tensor_copy(g.sSC[:, 0:512], vb(HK_, 512)), g.s_v)
                run("vector", lambda e: e.tensor_copy(g.sSC[:, 544:1056], vb(HK_ + 512, 512)), g.s_v)
                em("vector", lambda e: e.drain())
                wait("sync", g.s_v)
                dma("sync", g.dumps["hk0"][:, 0:512], g.sSC[:, 0:512], g.s_st)
                dma("sync", g.dumps["hk0"][:, 512:1024], g.sSC[:, 544:1056], g.s_st)
                wait("vector", g.s_st, g.s_st.n)
                run("vector", lambda e: e.tensor_copy(g.sSC[:, 0:512], vb(HK_ + 1024, 512)), g.s_v)
                run("vector", lambda e: e.tensor_copy(g.sSC[:, 544:1056], vb(HK_ + 1536, 512)), g.s_v)
                em("vector", lambda e: e.drain())
                wait("sync", g.s_v)
                wait("sync", g.s_st, g.s_st.n)
                dma("sync", g.dumps["hk0"][:, 1024:1536], g.sSC[:, 0:512], g.s_st)
                dma("sync", g.dumps["hk0"][:, 1536:2048], g.sSC[:, 544:1056], g.s_st)
                wait("vector", g.s_st, g.s_st.n)
                wait("scalar", g.s_st, g.s_st.n)
            wait("tensor", g.s_v, hkv)
            for co in range(CH):
                if co < 6:
                    last = (gi == NG - 1)
                    for fi in range(FG):
                        run("tensor", lambda e, cc=co, ff=fi, wb=wbase,
                                st=first_mm[co] and fi == 0,
                                sp=(last and fi == FG - 1): e.matmul(
                            g.PS[cc][:, 0:512],
                            vb(wb + 4096 + ff * C + cc * 128, 128),
                            vb(HK_ + ff * 512, 512),
                            start=st, stop=sp),
                            g.s_t if fi == FG - 1 else None)
                    first_mm[co] = False
                else:
                    bank = 6 + (co - 6)
                    wait("tensor", g.s_v, hkread[bank])
                    for fi in range(FG):
                        run("tensor", lambda e, b_=bank, cc=co, ff=fi, wb=wbase: e.matmul(
                            g.PS[b_][:, 0:512],
                            vb(wb + 4096 + ff * C + cc * 128, 128),
                            vb(HK_ + ff * 512, 512),
                            start=(ff == 0), stop=(ff == FG - 1)),
                            g.s_t if fi == FG - 1 else None)
                    wait("vector", g.s_t)
                    if gi == 0:
                        run("vector", lambda e, b_=bank, ap=csl(Ekv_, co): e.tensor_copy(
                            ap, g.PS[b_][:, 0:512]), g.s_v)
                    else:
                        run("vector", lambda e, b_=bank, ap=csl(Ekv_, co): e.tensor_tensor(
                            ap, ap, g.PS[b_][:, 0:512], ALU.add), g.s_v)
                    hkread[bank] = g.s_v.n
        fg_ready = [None] * NG
        if kk < NT - 1:
            fg_ready[0] = fg_load(0, 0)
            fg_ready[1] = fg_load(1, 1)
        tall = g.s_t.n
        wait("vector", g.s_t, tall)
        for co in range(6):
            run("vector", lambda e, cc=co, ap=csl(Ekv_, co): e.tensor_copy(ap, g.PS[cc][:, 0:512]), g.s_v)
        kvev = g.s_v.n
        woadd[0] = woadd[1] = kvev
        wait("sync", g.s_v, kvev)
        dma("sync", vb(WOC_, CH * C), g.wcrb_d, g.s_ld)
        wcrl = g.s_ld.n
        wait("tensor", g.s_ld, wcrl)
        wait("tensor", g.s_v, kvev)
        for co in range(CH):
            bank = co % 2
            wait("tensor", g.s_s, rrev[bank])
            for ci in range(CH):
                run("tensor", lambda e, b_=bank, cc=ci, cco=co: e.matmul(
                    g.PS[b_][:, 0:512],
                    vb(WOC_ + cc * C + cco * 128, 128),
                    vb(MXC_ + cc * 512, 512),
                    start=(cc == 0), stop=(cc == CH - 1)),
                    g.s_t if ci == CH - 1 else None)
            wait("scalar", g.s_t)
            run("scalar", lambda e, b_=bank, ap=csl(A_, co): e.activation(
                ap, g.PS[b_][:, 0:512], AF.Exp, scale=-1.0), g.s_s)
            rrev[bank] = g.s_s.n
        wait("vector", g.s_s)
        run("vector", lambda e: e.tensor_scalar(vf(A_, TILE_COLS), vf(A_, TILE_COLS), 1.0, None, ALU.add), g.s_v)
        em("vector", lambda e: e.drain())
        run("vector", lambda e: e.reciprocal(vf(Bd_, TILE_COLS), vf(A_, TILE_COLS)), g.s_v)
        em("vector", lambda e: e.drain())
        if kk == 0 and "kv" in g.dumps:
            wait("sync", g.s_v)
            dma("sync", g.dumps["kv"], vf(Ekv_, TILE_COLS), g.s_st)
            dma("sync", g.dumps["rr"], vf(Bd_, TILE_COLS), g.s_st)
            wait("vector", g.s_st, g.s_st.n)
            wait("scalar", g.s_st, g.s_st.n)
        xre = g.xre
        run("vector", lambda e: e.tensor_tensor(vf(Ekv_, TILE_COLS), vf(Ekv_, TILE_COLS), vf(Bd_, TILE_COLS), ALU.mult), g.s_v)
        run("vector", lambda e: e.tensor_tensor(vf(Ekv_, TILE_COLS), vf(Ekv_, TILE_COLS), vf(Dx_, TILE_COLS), ALU.add), g.s_v)
        wait("vector", g.s_ld, xre)
        run("vector", lambda e: e.tensor_tensor(vf(Ekv_, TILE_COLS), vf(Ekv_, TILE_COLS), vf(Cr_, TILE_COLS), ALU.add), g.s_v)
        em("vector", lambda e: e.drain())

        if kk == 0 and "yp" in g.dumps:
            wait("sync", g.s_v)
            dma("sync", g.dumps["yp"], vf(Ekv_, TILE_COLS), g.s_st)
            wait("vector", g.s_st, g.s_st.n)
            wait("scalar", g.s_st, g.s_st.n)
        def src_y(c, o, w):
            return g.ARF[:, Ekv_ + c * tokc + o: Ekv_ + c * tokc + o + w]

        def put_y1(c, o, w):
            b0 = o // TP
            b1 = (o + w) // TP
            return g.ARF[:, Bd_ + c * tokc: Bd_ + (c + 1) * tokc].rearrange(
                "p (t b) -> p t b", t=TP)[:, :, b0:b1].transpose([0, 2, 1])
        for (o, w) in [(0, 512)]:
            wait("tensor", g.s_v)
            wait("tensor", g.s_s)
            wait("scalar", g.s_v)
            for c in range(CH):
                run("tensor", lambda e, cc=c, ap=src_y(c, o, w), ww=w: e.matmul(
                    g.PS[4][0:1, 0:ww], g.sONE1c[:], ap,
                    start=(cc == 0), stop=(cc == CH - 1)), g.s_t)
            for c in range(CH):
                if c >= 2:
                    wait("scalar", g.s_t)
                run("scalar", lambda e, cc=c, ap=src_y(c, o, w), ww=w: e.activation(
                    g.sSC[:, (cc % 2) * 544:(cc % 2) * 544 + ww], ap, AF.Square), g.s_s)
                wait("tensor", g.s_s)
                run("tensor", lambda e, cc=c, ww=w: e.matmul(
                    g.PS[5][0:1, 0:ww], g.sONE1c[:],
                    g.sSC[:, (cc % 2) * 544:(cc % 2) * 544 + ww],
                    start=(cc == 0), stop=(cc == CH - 1)), g.s_t)
            tmm = g.s_t.n
            wait("vector", g.s_t, tmm)
            run("vector", lambda e, ww=w: e.tensor_copy(g.sROW[0:1, 0:ww], g.PS[4][0:1, 0:ww]), g.s_v)
            em("vector", lambda e: e.drain())
            run("vector", lambda e, ww=w: e.tensor_tensor(
                g.sROW[0:1, 1024:1024 + ww], g.sROW[0:1, 0:ww], g.sROW[0:1, 0:ww], ALU.mult), g.s_v)
            em("vector", lambda e: e.drain())
            run("vector", lambda e, ww=w: e.tensor_tensor(
                g.sROW[0:1, 512:512 + ww], g.PS[5][0:1, 0:ww], g.sROW[0:1, 1024:1024 + ww], ALU.subtract), g.s_v)
            vrow = g.s_v.n
            wait("scalar", g.s_v, vrow)
            run("scalar", lambda e, ww=w: e.activation(
                g.sROW[0:1, 512:512 + ww], g.sROW[0:1, 512:512 + ww], AF.Ln, bias=g.sDRV[0:1, 96:97]), g.s_s)
            em("scalar", lambda e: e.drain())
            run("scalar", lambda e, ww=w: e.activation(
                g.sROW[0:1, 512:512 + ww], g.sROW[0:1, 512:512 + ww], AF.Exp, scale=-0.5), g.s_s)
            srow = g.s_s.n
            wait("tensor", g.s_s, srow)
            run("tensor", lambda e, ww=w: e.matmul(
                g.PS[6][:, 0:ww], g.sONES1[:], g.sROW[0:1, 0:ww], start=True, stop=True), g.s_t)
            run("tensor", lambda e, ww=w: e.matmul(
                g.PS[7][:, 0:ww], g.sONES1[:], g.sROW[0:1, 512:512 + ww], start=True, stop=True), g.s_t)
            tb = g.s_t.n
            wait("vector", g.s_t, tb)
            for c in range(CH):
                run("vector", lambda e, ap=src_y(c, o, w), ww=w: e.tensor_tensor(
                    ap, ap, g.PS[6][:, 0:ww], ALU.subtract), g.s_v)
            for c in range(CH):
                run("vector", lambda e, po=put_y1(c, o, w), ap=src_y(c, o, w), ww=w: e.tensor_tensor(
                    po, ap, g.PS[7][:, 0:ww], ALU.mult), g.s_v)
        y1v = g.s_v.n
        hkread[6] = hkread[7] = y1v
        wait("sync", g.s_v, y1v)
        dma("sync", g.y1_d[kk * 128:(kk + 1) * 128, :], vf(Bd_, TILE_COLS), g.s_st)
    if "y1" in g.dumps:
        wait("sync", g.s_st, g.s_st.n)
        dma("sync", g.dumps["y1"], g.y1_d, g.s_st)


def _s3(g):
    em, wait, dma, run, vf, vb = g.em, g.wait, g.dma, g.run, g.vf, g.vb
    Y_, Hh_ = 0, TILE_COLS
    QKV_, YB_, CTX_ = 0, 12288, 16384
    WIP_ = 20480
    CTX_ = QKV_
    WIP_ = 16384
    OW2_ = 28672

    def wip_load(half):
        wait("scalar", g.s_t)
        dma("scalar", vb(WIP_, 12288),
            g.wipb_d[half * 128:(half + 1) * 128, :], g.s_ld)
        return g.s_ld.n

    wait("scalar", g.s_gc, g.prep_s3)
    s3w = wip_load(0)
    qevac = {0: 0, 1: 0}
    hpre = {0: 0, 1: 0}
    dma("sync", g.sGATH[:, 0:64], g.wrt_t, g.s_ld)
    rtw = g.s_ld.n

    for kk in range(NT):
        wait("sync", g.s_v)
        wait("sync", g.s_t)
        wait("sync", g.s_s)
        wait("sync", g.s_st, g.s_st.n)
        dma("sync", vf(Y_, TILE_COLS), g.y1_d[kk * 128:(kk + 1) * 128, :], g.s_ld)
        wait("scalar", g.s_t)
        dma("scalar", vb(OW2_, CH * C), g.wowb_d, g.s_ld)
        oww = g.s_ld.n
        ldt = g.s_ld.n
        wait("vector", g.s_ld, ldt)
        run("vector", lambda e: e.tensor_copy(vb(YB_, TILE_COLS), vf(Y_, TILE_COLS)), g.s_v)
        em("vector", lambda e: e.drain())
        ybc = g.s_v.n
        wait("tensor", g.s_v, ybc)
        wait("tensor", g.s_ld, s3w)
        for oc in range(24):
            if oc == 12:
                s3w = wip_load(1)
                wait("tensor", g.s_ld, s3w)
            bank = oc % 2
            wait("tensor", g.s_s, qevac[bank])
            for ci in range(CH):
                run("tensor", lambda e, b_=bank, cc=ci, oo=oc % 12: e.matmul(
                    g.PS[b_][:, 0:512],
                    vb(WIP_ + cc * 1536 + oo * 128, 128),
                    vb(YB_ + cc * 512, 512),
                    start=(cc == 0), stop=(cc == CH - 1)),
                    g.s_t if ci == CH - 1 else None)
            wait("scalar", g.s_t)
            if oc < 8:
                run("scalar", lambda e, b_=bank, oo=oc: e.activation(
                    vb(QKV_ + oo * 512, 512), g.PS[b_][:, 0:512], AF.Identity,
                    bias=g.sDRV[:, 24 + oo:25 + oo], scale=0.125), g.s_s)
            else:
                run("scalar", lambda e, b_=bank, oo=oc: e.activation(
                    vb(QKV_ + oo * 512, 512), g.PS[b_][:, 0:512], AF.Identity,
                    bias=g.sVEC[:, 128 + oo:129 + oo]), g.s_s)
            qevac[bank] = g.s_s.n
        qkvs = g.s_s.n
        inproj_t = g.s_t.n
        if kk < NT - 1:
            s3w_next = wip_load(0)
        wait("tensor", g.s_s, qkvs)
        snaps = [dict(t=0, v=0, s=0), dict(t=0, v=0, s=0)]
        SBK = [2, 0]
        TBK = [3, 1]
        CBK = [4, 6]
        AS_ = [YB_ + 2048, YB_ + 2816]
        XR = [1056, 1072]
        gidx = 0
        for pb in range(4):
            colo = pb * 128
            for j in range(CH):
                p = gidx % 2
                sb, tb, cb = SBK[p], TBK[p], CBK[p]
                a0 = AS_[p]
                xr = XR[p]
                qof = QKV_ + j * 512 + colo
                kof = QKV_ + (8 + j) * 512 + colo
                vof = QKV_ + (16 + j) * 512 + colo
                sn = snaps[p]
                wait("tensor", g.s_v, sn["v"])
                wait("tensor", g.s_s, sn["s"])
                for hh in range(2):
                    run("tensor", lambda e, h_=hh, b_=sb: e.matmul(
                        g.PS[b_][:, h_ * 128:h_ * 128 + 128], g.sMA[:], g.sMB[:],
                        start=True, stop=False), None)
                    run("tensor", lambda e, h_=hh, b_=sb, q_=qof, k_=kof: e.matmul(
                        g.PS[b_][:, h_ * 128:h_ * 128 + 128],
                        g.ARB[64 * h_:64 * h_ + 64, q_:q_ + 128],
                        g.ARB[64 * h_:64 * h_ + 64, k_:k_ + 128],
                        start=False, stop=True), g.s_t)
                sco = g.s_t.n
                run("tensor", lambda e, v_=vof, b_=tb: e.transpose(
                    g.PS[b_][:, 128:192].bitcast(BF16), g.ARB[:, v_:v_ + 128],
                    g.sIDB[:]), g.s_t)
                vt = g.s_t.n
                wait("vector", g.s_t, sco)
                run("vector", lambda e, b_=sb, x_=xr: e.reduce_max(
                    g.sSC[:, x_:x_ + 2].rearrange("p (s o) -> p s o", s=2),
                    g.PS[b_][:, 0:256].rearrange("p (s n) -> p s n", s=2),
                    axis=AX, negate=True), g.s_v)
                em("vector", lambda e: e.drain())
                smax = g.s_v.n
                wait("scalar", g.s_v, smax)
                for hh in range(2):
                    run("scalar", lambda e, h_=hh, b_=sb, a_=a0, x_=xr: e.activation(
                        g.ARB[:, a_ + h_ * 128: a_ + h_ * 128 + 128],
                        g.PS[b_][:, h_ * 128:h_ * 128 + 128],
                        AF.Exp, bias=g.sSC[:, x_ + h_:x_ + h_ + 1]), g.s_s)
                sexp = g.s_s.n
                wait("vector", g.s_s, sexp)
                run("vector", lambda e, a_=a0, x_=xr: e.tensor_reduce(
                    g.sSC[:, x_ + 2:x_ + 4].rearrange("p (s o) -> p s o", s=2),
                    g.ARB[:, a_:a_ + 256].rearrange("p (s n) -> p s n", s=2),
                    axis=AX, op=ALU.add), g.s_v)
                em("vector", lambda e: e.drain())
                run("vector", lambda e, x_=xr: e.reciprocal(
                    g.sSC[:, x_ + 4:x_ + 6], g.sSC[:, x_ + 2:x_ + 4]), g.s_v)
                em("vector", lambda e: e.drain())
                for hh in range(2):
                    run("vector", lambda e, h_=hh, a_=a0, x_=xr: e.tensor_scalar_mul(
                        g.ARB[:, a_ + h_ * 128: a_ + h_ * 128 + 128],
                        g.ARB[:, a_ + h_ * 128: a_ + h_ * 128 + 128],
                        g.sSC[:, x_ + 4 + h_:x_ + 5 + h_]), g.s_v)
                atn = g.s_v.n
                wait("tensor", g.s_v, atn)
                for hh in range(2):
                    run("tensor", lambda e, h_=hh, a_=a0, b_=tb: e.transpose(
                        g.PS[b_][:, h_ * 64:h_ * 64 + 64].bitcast(BF16),
                        g.ARB[:, a_ + h_ * 128: a_ + h_ * 128 + 128],
                        g.sIDB[:]), g.s_t)
                att = g.s_t.n
                wait("vector", g.s_t, att)
                run("vector", lambda e, a_=a0, b_=tb: e.tensor_copy(
                    g.ARB[:, a_ + 256: a_ + 512], g.PS[b_][:, 0:128].bitcast(BF16)), g.s_v)
                run("vector", lambda e, a_=a0, b_=tb: e.tensor_copy(
                    g.ARB[:, a_ + 512: a_ + 640], g.PS[b_][:, 128:192].bitcast(BF16)), g.s_v)
                em("vector", lambda e: e.drain())
                tev = g.s_v.n
                wait("tensor", g.s_v, tev)
                run("tensor", lambda e, a_=a0, b_=cb: e.matmul(
                    g.PS[b_][:, 0:256], g.ARB[:, a_ + 512: a_ + 640],
                    g.ARB[:, a_ + 256: a_ + 512], start=True, stop=True), g.s_t)
                ctm = g.s_t.n
                wait("vector", g.s_t, ctm)
                run("vector", lambda e, j_=j, c_=colo, b_=cb: e.tensor_copy(
                    g.ARB[0:64, CTX_ + j_ * 512 + c_: CTX_ + j_ * 512 + c_ + 128],
                    g.PS[b_][0:64, 0:128]), g.s_v)
                run("vector", lambda e, j_=j, c_=colo, b_=cb: e.tensor_copy(
                    g.ARB[64:128, CTX_ + j_ * 512 + c_: CTX_ + j_ * 512 + c_ + 128],
                    g.PS[b_][64:128, 128:256]), g.s_v)
                snaps[p] = dict(t=g.s_t.n, v=g.s_v.n, s=g.s_s.n)
                gidx += 1
        ctxv = g.s_v.n
        qevac[0] = qevac[1] = g.s_s.n
        wait("tensor", g.s_ld, oww)
        wait("tensor", g.s_v, ctxv)
        for co in range(CH):
            bank = co % 2
            wait("tensor", g.s_v, hpre[bank])
            for ci in range(CH):
                run("tensor", lambda e, b_=bank, cc=ci, cco=co: e.matmul(
                    g.PS[b_][:, 0:512],
                    vb(OW2_ + cc * C + cco * 128, 128),
                    vb(CTX_ + cc * 512, 512),
                    start=(cc == 0), stop=(cc == CH - 1)),
                    g.s_t if ci == CH - 1 else None)
            wait("vector", g.s_t)
            run("vector", lambda e, b_=bank, cco=co: e.tensor_tensor(
                vf(Hh_ + cco * 512, 512), vf(Y_ + cco * 512, 512),
                g.PS[b_][:, 0:512], ALU.add), g.s_v)
            run("vector", lambda e, cco=co: e.tensor_scalar(
                vf(Hh_ + cco * 512, 512), vf(Hh_ + cco * 512, 512),
                g.vcol("ob", cco), None, ALU.add), g.s_v)
            hpre[bank] = g.s_v.n
        def src_h(c, o, w):
            return g.ARF[:, Hh_ + c * 512 + o: Hh_ + c * 512 + o + w]
        g.emit_ln(src_h, src_h, [(0, 512)],
                  lambda c: g.vcol("fg1", c), lambda c: g.vcol("fb1", c))
        hv = g.s_v.n
        wait("sync", g.s_v, hv)
        dma("sync", g.h_d[kk * 128:(kk + 1) * 128, :], vf(Hh_, TILE_COLS), g.s_st)
        wait("tensor", g.s_v, hv)
        wait("tensor", g.s_ld, rtw)
        for ci in range(CH):
            run("tensor", lambda e, cc=ci: e.matmul(
                g.PS[6][0:8, 0:512], g.sGATH[:, cc * 8:(cc + 1) * 8],
                vf(Hh_ + cc * 512, 512), start=(cc == 0), stop=(cc == CH - 1)),
                g.s_t if ci == CH - 1 else None)
        lg = g.s_t.n
        wait("vector", g.s_t, lg)
        run("vector", lambda e: e.tensor_scalar(
            g.sSC[0:8, 0:512], g.PS[6][0:8, 0:512], g.sC8[:, 0:1], None, ALU.add), g.s_v)
        em("vector", lambda e: e.drain())
        lgv = g.s_v.n
        wait("tensor", g.s_v, lgv)
        for bq in range(4):
            run("tensor", lambda e, b_=bq: e.transpose(
                g.PS[7][:, b_ * 8:b_ * 8 + 8], g.sSC[0:8, b_ * 128:(b_ + 1) * 128],
                g.sIDF[0:8, 0:8]), g.s_t)
        lgt = g.s_t.n
        wait("vector", g.s_t, lgt)
        run("vector", lambda e: e.tensor_copy(g.sSC[:, 544:544 + 32], g.PS[7][:, 0:32]), g.s_v)
        em("vector", lambda e: e.drain())
        for bq in range(4):
            lgq = g.sSC[:, 544 + bq * 8:544 + bq * 8 + 8]
            m1 = g.sSC[:, 1080:1081]
            msk = g.sSC[:, 0:8]
            xp = g.sSC[:, 16:24]
            m2 = g.sSC[:, 1081:1082]
            i1 = g.sSC[:, 1082:1083]
            run("vector", lambda e, a=lgq, b_=m1: e.reduce_max(b_, a, axis=AX), g.s_v)
            em("vector", lambda e: e.drain())
            run("vector", lambda e, a=lgq, b_=m1, c_=msk: e.tensor_tensor(
                c_, a, b_.to_broadcast([128, 8]), ALU.is_equal), g.s_v)
            em("vector", lambda e: e.drain())
            run("vector", lambda e, a=lgq, c_=msk, d_=xp: e.scalar_tensor_tensor(
                d_, c_, -1e30, a, ALU.mult, ALU.add), g.s_v)
            em("vector", lambda e: e.drain())
            run("vector", lambda e, d_=xp, b_=m2: e.reduce_max(b_, d_, axis=AX), g.s_v)
            em("vector", lambda e: e.drain())
            run("vector", lambda e, a=lgq, b_=m2, c_=msk: e.tensor_tensor(
                c_, b_.to_broadcast([128, 8]), a, ALU.is_le), g.s_v)
            em("vector", lambda e: e.drain())
            run("vector", lambda e, c_=msk: e.tensor_tensor(c_, c_, g.sIOTA8[:], ALU.mult), g.s_v)
            em("vector", lambda e: e.drain())
            run("vector", lambda e, c_=msk, i_=i1: e.reduce_max(i_, c_, axis=AX), g.s_v)
            em("vector", lambda e: e.drain())
            asn = g.s_v.n
            wait("tensor", g.s_v, asn)
            run("tensor", lambda e, i_=i1: e.transpose(
                g.PS[7][0:1, 128:256], i_, g.sIDF[:]), g.s_t)
            att2 = g.s_t.n
            wait("vector", g.s_t, att2)
            run("vector", lambda e: e.tensor_copy(
                g.sSC[0:1, 544:672], g.PS[7][0:1, 128:256]), g.s_v)
            em("vector", lambda e: e.drain())
            wait("sync", g.s_v)
            dma("sync", g.asg_d[0:1, kk * 512 + bq * 128: kk * 512 + (bq + 1) * 128],
                g.sSC[0:1, 544:672], g.s_st)
        if kk < NT - 1:
            s3w = s3w_next
    if "h" in g.dumps:
        wait("sync", g.s_st, g.s_st.n)
        dma("sync", g.dumps["h"], g.h_d, g.s_st)
    if "assign" in g.dumps:
        wait("sync", g.s_st, g.s_st.n)
        dma("sync", g.dumps["assign"], g.asg_d, g.s_st)


def _s4(g):
    em, wait, dma, run, vf, vb = g.em, g.wait, g.dma, g.run, g.vf, g.vb
    H_, Yt_, M67_ = 0, TILE_COLS, 2 * TILE_COLS
    HBV = g.ARF[:, 10240:12288].bitcast(BF16)
    SMB = g.sSC[:, 0:256].bitcast(BF16)
    HKe_ = 0
    W1_ = [4096, 12288]
    W2_ = [20480, 28672]

    def w_load(e_, par):
        wait("sync", g.s_t)
        dma("sync", vb(W1_[par], CH * C), g.w1b_d[e_ * 128:(e_ + 1) * 128, :], g.s_ld)
        wait("scalar", g.s_t)
        dma("scalar", vb(W2_[par], CH * C), g.w2b_d[e_ * 128:(e_ + 1) * 128, :], g.s_ld)
        return g.s_ld.n

    wait("sync", g.s_gc, g.prep_done)
    wait("scalar", g.s_gc, g.prep_done)
    wready = [None] * NE
    gelu_done = {6: 0, 7: 0}
    vadd_done = {6: 0, 7: 0}

    for kk in range(NT):
        wait("sync", g.s_v)
        wait("sync", g.s_t)
        wait("sync", g.s_s)
        wait("sync", g.s_st, g.s_st.n)
        dma("sync", vf(H_, TILE_COLS), g.h_d[kk * 128:(kk + 1) * 128, :], g.s_ld)
        wait("gpsimd", g.s_st, g.s_st.n)
        dma("gpsimd", g.sASG[:], g.asg_d[0:1, kk * 512:(kk + 1) * 512], g.s_gc)
        asgl = g.s_gc.n
        ldh = g.s_ld.n
        wait("vector", g.s_ld, ldh)
        run("vector", lambda e: e.tensor_copy(HBV, vf(H_, TILE_COLS)), g.s_v)
        em("vector", lambda e: e.drain())
        g.hbc = g.s_v.n
        wait("tensor", g.s_v)
        wait("tensor", g.s_gc, asgl)
        run("tensor", lambda e: e.matmul(
            g.PS[6][0:8, 0:512], g.sONES8[:],
            g.sASG[0:1, :], start=True, stop=True), g.s_t)
        mb = g.s_t.n
        wait("vector", g.s_t, mb)
        run("vector", lambda e: e.tensor_tensor(
            g.sM8[:], g.PS[6][0:8, 0:512], g.sEID[:], ALU.is_equal), g.s_v)
        em("vector", lambda e: e.drain())
        if kk == 0 and "m80" in g.dumps:
            run("vector", lambda e: e.tensor_copy(g.sSC[0:8, 0:512], g.sM8[:]), g.s_v)
            em("vector", lambda e: e.drain())
            wait("sync", g.s_v)
            dma("sync", g.dumps["m80"], g.sSC[0:8, 0:512], g.s_st)
            wait("vector", g.s_st, g.s_st.n)
            run("vector", lambda e: e.tensor_copy(g.sSC[0:8, 0:512], g.PS[6][0:8, 0:512]), g.s_v)
            em("vector", lambda e: e.drain())
            wait("sync", g.s_v)
            dma("sync", g.dumps["bc0"], g.sSC[0:8, 0:512], g.s_st)
            wait("vector", g.s_st, g.s_st.n)
            run("vector", lambda e: e.tensor_copy(g.sSC[0:1, 0:512], g.sASG[:]), g.s_v)
            em("vector", lambda e: e.drain())
            wait("sync", g.s_v)
            dma("sync", g.dumps["asg0"], g.sSC[0:1, 0:512], g.s_st)
            wait("vector", g.s_st, g.s_st.n)
        run("vector", lambda e: e.tensor_scalar(
            g.sNM8[:], g.sM8[:], -1.0, 1.0, ALU.mult, ALU.add), g.s_v)
        em("vector", lambda e: e.drain())
        m8v = g.s_v.n
        wait("tensor", g.s_v, m8v)
        for co in range(CH):
            if co < 6:
                run("tensor", lambda e, cc=co: e.matmul(
                    g.PS[cc][:, 0:512], g.sB2[:, cc * 128:(cc + 1) * 128],
                    g.sM8[:], start=True, stop=False), g.s_t)
            else:
                run("tensor", lambda e, cc=co: e.matmul(
                    g.PS[7][:, 0:512], g.sB2[:, cc * 128:(cc + 1) * 128],
                    g.sM8[:], start=True, stop=True), g.s_t)
                wait("vector", g.s_t)
                run("vector", lambda e, cc=co: e.tensor_copy(
                    vf(M67_ + (cc - 6) * 512, 512), g.PS[7][:, 0:512]), g.s_v)
                wait("tensor", g.s_v)
        vadd_done[6] = vadd_done[7] = g.s_v.n
        wait("tensor", g.s_v, g.hbc)
        for e_ in range(NE):
            if wready[e_] is None:
                wready[e_] = w_load(e_, e_ % 2)
            if e_ + 1 < NE and wready[e_ + 1] is None:
                wready[e_ + 1] = w_load(e_ + 1, (e_ + 1) % 2)
            wait("tensor", g.s_ld, wready[e_])
            w1b = W1_[e_ % 2]
            w2b = W2_[e_ % 2]
            wait("tensor", g.s_s, gelu_done[7])
            wait("tensor", g.s_v, vadd_done[7])
            run("tensor", lambda e, ee=e_: e.matmul(
                g.PS[7][:, 0:512], g.sSELB[:, ee * 128:(ee + 1) * 128], g.sM8[:],
                start=True, stop=True), g.s_t)
            bc8 = g.s_t.n
            wait("vector", g.s_t, bc8)
            run("vector", lambda e: e.tensor_copy(SMB, g.PS[7][:, 0:512]), g.s_v)
            vadd_done[7] = g.s_v.n
            gfi = []
            for fi in range(CH):
                hbank = 6 + fi % 2
                wait("tensor", g.s_s, gelu_done[hbank])
                wait("tensor", g.s_v, vadd_done[hbank])
                for ci in range(CH):
                    run("tensor", lambda e, b_=hbank, cc=ci, ff=fi, wb=w1b: e.matmul(
                        g.PS[b_][:, 0:512],
                        vb(wb + cc * C + ff * 128, 128),
                        HBV[:, cc * 512:(cc + 1) * 512],
                        start=(cc == 0), stop=(cc == CH - 1)),
                        g.s_t if ci == CH - 1 else None)
                wait("scalar", g.s_t)
                run("scalar", lambda e, b_=hbank, ee=e_, ff=fi: e.activation(
                    vb(HKe_ + ff * 512, 512), g.PS[b_][:, 0:512], AF.Gelu,
                    bias=g.sVEC[:, 152 + ee * 8 + ff:153 + ee * 8 + ff]), g.s_s)
                gelu_done[hbank] = g.s_s.n
                wait("vector", g.s_s)
                run("vector", lambda e, ff=fi: e.tensor_tensor(
                    vb(HKe_ + ff * 512, 512), vb(HKe_ + ff * 512, 512),
                    SMB, ALU.mult), g.s_v)
                gfi.append(g.s_v.n)
            hkv = g.s_s.n
            if kk == 0 and e_ == 0 and "he0" in g.dumps:
                wait("vector", g.s_s, hkv)
                run("vector", lambda e: e.tensor_copy(vf(Yt_, 4096), vb(HKe_, 4096)), g.s_v)
                em("vector", lambda e: e.drain())
                wait("sync", g.s_v)
                dma("sync", g.dumps["he0"], vf(Yt_, 4096), g.s_st)
                wait("vector", g.s_st, g.s_st.n)
                wait("scalar", g.s_st, g.s_st.n)
                wait("tensor", g.s_st, g.s_st.n)
            for fi in range(CH):
                wait("tensor", g.s_v, gfi[fi])
                for co in range(6):
                    run("tensor", lambda e, cc=co, ff=fi, wb=w2b,
                            sp=(e_ == NE - 1 and fi == CH - 1): e.matmul(
                        g.PS[cc][:, 0:512],
                        vb(wb + ff * C + cc * 128, 128),
                        vb(HKe_ + ff * 512, 512),
                        start=False, stop=sp),
                        g.s_t if co == 5 else None)
            for co in (6, 7):
                bank = co
                wait("tensor", g.s_v)
                for fi in range(CH):
                    run("tensor", lambda e, b_=bank, cc=co, ff=fi, wb=w2b: e.matmul(
                        g.PS[b_][:, 0:512],
                        vb(wb + ff * C + cc * 128, 128),
                        vb(HKe_ + ff * 512, 512),
                        start=(ff == 0), stop=(ff == CH - 1)),
                        g.s_t if fi == CH - 1 else None)
                wait("vector", g.s_t)
                run("vector", lambda e, cc=co: e.tensor_tensor(
                    vf(M67_ + (cc - 6) * 512, 512), vf(M67_ + (cc - 6) * 512, 512),
                    g.PS[6 + (cc - 6)][:, 0:512], ALU.add), g.s_v)
                vadd_done[bank] = g.s_v.n
        wready = [None] * NE
        tall = g.s_t.n
        wait("vector", g.s_t, tall)
        for co in range(CH):
            if co < 6:
                run("vector", lambda e, cc=co: e.tensor_tensor(
                    vf(H_ + cc * 512, 512), vf(H_ + cc * 512, 512),
                    g.PS[cc][:, 0:512], ALU.add), g.s_v)
            else:
                run("vector", lambda e, cc=co: e.tensor_tensor(
                    vf(H_ + cc * 512, 512), vf(H_ + cc * 512, 512),
                    vf(M67_ + (cc - 6) * 512, 512), ALU.add), g.s_v)
        em("vector", lambda e: e.drain())

        if kk == 0 and "hm0" in g.dumps:
            wait("sync", g.s_v)
            dma("sync", g.dumps["hm0"], vf(H_, TILE_COLS), g.s_st)
            wait("vector", g.s_st, g.s_st.n)
            wait("scalar", g.s_st, g.s_st.n)
            wait("tensor", g.s_st, g.s_st.n)
        def src_hm(c, o, w):
            return g.ARF[:, H_ + c * 512 + o: H_ + c * 512 + o + w]
        g.emit_ln(src_hm, src_hm, [(0, 512)],
                  lambda c: g.vcol("fg2", c), lambda c: g.vcol("fb2", c))
        dma("sync", vf(Yt_, TILE_COLS), g.y1_d[kk * 128:(kk + 1) * 128, :], g.s_ld)
        ldy = g.s_ld.n
        wait("vector", g.s_ld, ldy)
        run("vector", lambda e: e.tensor_tensor(
            vf(H_, TILE_COLS), vf(H_, TILE_COLS), vf(Yt_, TILE_COLS), ALU.add), g.s_v)
        em("vector", lambda e: e.drain())
        g.emit_ln(src_hm, src_hm, [(0, 512)], None, None)
        ov = g.s_v.n
        wait("sync", g.s_v, ov)
        dma("sync", g.out_d[kk * 128:(kk + 1) * 128, :], vf(H_, TILE_COLS), g.s_st)


def _pack_weight(w):
    kin, nout = w.shape
    return np.ascontiguousarray(
        w.reshape(kin // 128, 128, nout).transpose(1, 0, 2).reshape(128, -1))


def _pack_vec(v):
    return np.ascontiguousarray(np.asarray(v, np.float32).reshape(CH, 128).T)


def _stack_groups_ck(wp):
    blocks = []
    for gi in range(8):
        blocks.append(np.concatenate(
            [wp[:, cc * FF + 512 * gi: cc * FF + 512 * gi + 512] for cc in range(CH)], axis=1))
    return np.ascontiguousarray(np.concatenate(blocks, axis=0))


def _stack_halves_ip(wp):
    blocks = []
    for h in range(2):
        blocks.append(np.concatenate(
            [wp[:, ci * 3 * C + h * 1536: ci * 3 * C + (h + 1) * 1536] for ci in range(CH)], axis=1))
    return np.ascontiguousarray(np.concatenate(blocks, axis=0))


def _host_inputs(inp):
    x = np.asarray(inp["L0_attn_x"], np.float32)
    x_pad = np.concatenate([np.zeros((B, 2, C), np.float32), x], axis=1)
    vec = np.zeros((128, 256), np.float32)
    order = ["L0_attn_ln0_g", "L0_attn_ln0_b", "L0_attn_ln1_g", "L0_attn_ln1_b",
             "L0_attn_tm_k", "L0_attn_tm_v", "L0_attn_tm_r", "L0_attn_cm_k",
             "L0_attn_cm_r", "L0_attn_time_decay", "L0_attn_time_first",
             "L0_ffn_out_b", "L0_ffn_ln1_g", "L0_ffn_ln1_b", "L0_ffn_ln2_g",
             "L0_ffn_ln2_b"]
    for idx, key in enumerate(order):
        vec[:, idx * 8:(idx + 1) * 8] = _pack_vec(inp[key])
    vec[:, 128:152] = np.ascontiguousarray(
        np.asarray(inp["L0_ffn_in_proj_b"], np.float32).reshape(24, 128).T)
    vec[:, 152:216] = np.ascontiguousarray(
        np.asarray(inp["L0_ffn_b1"], np.float32).reshape(NE * CH, 128).T)
    c8 = np.zeros((8, 64), np.float32)
    c8[:, 0] = np.asarray(inp["L0_ffn_router_b"], np.float32)
    common = dict(
        wk_t=_pack_weight(np.asarray(inp["L0_attn_Wk"], np.float32)),
        wv_t=_pack_weight(np.asarray(inp["L0_attn_Wv"], np.float32)),
        wr_t=_pack_weight(np.asarray(inp["L0_attn_Wr"], np.float32)),
        wo_t=_pack_weight(np.asarray(inp["L0_attn_Wo"], np.float32)),
        wck_t=_stack_groups_ck(_pack_weight(np.asarray(inp["L0_attn_Wck"], np.float32))),
        wcv_t=np.concatenate([_pack_weight(np.asarray(inp["L0_attn_Wcv"], np.float32))[:, gi * 4096:(gi + 1) * 4096] for gi in range(8)], axis=0),
        wcr_t=_pack_weight(np.asarray(inp["L0_attn_Wcr"], np.float32)),
        wip_t=_stack_halves_ip(_pack_weight(np.ascontiguousarray(np.asarray(inp["L0_ffn_in_proj_w"], np.float32).T))),
        wow_t=_pack_weight(np.ascontiguousarray(np.asarray(inp["L0_ffn_out_w"], np.float32).T)),
        wrt_t=_pack_weight(np.ascontiguousarray(np.asarray(inp["L0_ffn_router_w"], np.float32).T)),
        w1_t=np.concatenate([_pack_weight(np.ascontiguousarray(np.asarray(inp["L0_ffn_w1"][e], np.float32).T)) for e in range(NE)], axis=0),
        w2_t=np.concatenate([_pack_weight(np.ascontiguousarray(np.asarray(inp["L0_ffn_w2"][e], np.float32).T)) for e in range(NE)], axis=0),
        vec32=vec, c8f=c8,
        b2f=np.asarray(inp["L0_ffn_b2"], np.float32),
        eidb=np.ascontiguousarray(np.broadcast_to(
            np.arange(8, dtype=np.float32)[:, None], (8, 512))).astype(NPBF),
        ones1f=np.ones((1, 128), np.float32),
        ones8b=np.ones((1, 8), NPBF),
        neg9b=np.asarray([[(-1e9 if k2 == (m // 128) else 0.0) for m in range(1024)]
                          for k2 in range(8)], NPBF),
        selb=np.asarray([[(1.0 if k2 == (m // 128) else 0.0) for m in range(1024)]
                         for k2 in range(8)], NPBF),
        identb=np.eye(128, dtype=NPBF),
        identf=np.eye(128, dtype=np.float32),
        iotarep=np.ascontiguousarray(np.broadcast_to(np.arange(257, dtype=np.float32), (128, 257))),
        iota8=np.ascontiguousarray(np.broadcast_to(np.arange(8, dtype=np.float32), (128, 8))),
        maskA=np.asarray([[1.0 if m // 16 == gg else 0.0 for m in range(128)] for gg in range(8)], NPBF),
        maskB=np.asarray([[0.0 if n // 16 == gg else -1e30 for n in range(128)] for gg in range(8)], NPBF),
    )
    maps = []
    for i in range(NC):
        s = PPC * i
        slabs = [x_pad[:, s:s + 2, :]]
        for kk in range(NT):
            slabs.append(x_pad[:, s + 1 + TP * kk: s + 1 + TP * kk + SEG, :])
        x_core = np.zeros(((1 + NT) * 128, HALO_COLS), np.float32)
        for si, sl in enumerate(slabs):
            L = sl.shape[1]
            x_core[si * 128:(si + 1) * 128, 0:CH * B * L] = (
                sl.transpose(2, 0, 1).reshape(CH, 128, B * L).transpose(1, 0, 2).reshape(128, CH * B * L))
        gates = np.zeros((128, 2), np.float32)
        gates[:, 0] = 0.0 if i == 0 else 1.0
        gates[:, 1] = 1.0 - gates[:, 0]
        gp = np.zeros((128, 8), np.float32)
        gp[:, :i] = 1.0
        m = dict(common)
        m.update(x_t=x_core, gates=gates, gpre=gp, gpre256=np.ascontiguousarray(gp * 256.0))
        maps.append(m)
    return maps


def _assemble(results):
    out = np.zeros((B, TFULL, C), np.float32)
    for i in range(NC):
        arr = np.asarray(results[i]["out_d"], np.float32)
        a = arr.reshape(NT, 128, CH, TP, B)
        out[:, PPC * i:PPC * (i + 1), :] = a.transpose(4, 0, 3, 2, 1).reshape(B, PPC, C)
    return out


def kernel(**inputs):
    maps = _host_inputs(inputs)
    nc = build_program()
    res = run_bass_kernel_spmd(nc, maps, core_ids=list(range(NC)))
    return _assemble(res.results)
